# version 02
# baseline (speedup 1.0000x reference)
import numpy as np
import ml_dtypes
from contextlib import ExitStack

import concourse.bass as bass
import concourse.mybir as mybir
from concourse.bass_utils import run_bass_kernel_spmd

F32 = mybir.dt.float32
BF16 = mybir.dt.bfloat16
AF = mybir.ActivationFunctionType
ALU = mybir.AluOpType
AX = mybir.AxisListType

D = 2048
DC = 16
T = 2048
HALO = 512
TK = 2560
NT = 16
NTA = 20
H = 16
DH = 128
FF = 1408
FCN = 11
NE = 18
SCALE = float(128 ** -0.5)
EPS = 1e-5

_DEBUG_TAPS = set()
_MAX_PHASE = 99
_SKIP_QK = False
_SKIP_V = False
_QK_NCK = 99


class Ledger:
    def __init__(self, nc, ctx):
        self.nc, self.ctx = nc, ctx
        self.h = {}
        self.c = {}

    def sem(self, name):
        if name not in self.h:
            self.h[name] = self.ctx.enter_context(self.nc.semaphore(name))
            self.c[name] = 0
        return self.h[name]

    def inc(self, name, instr, k):
        instr.then_inc(self.sem(name), k)
        self.c[name] += k

    def n(self, name):
        return self.c.get(name, 0)


def build_program():
    nc = bass.Bass("TRN2", target_bir_lowering=False)

    x_own = nc.dram_tensor("x_own", [T, D], F32, kind="ExternalInput").ap()
    x_halo = nc.dram_tensor("x_halo", [HALO, D], F32, kind="ExternalInput").ap()
    wqT = nc.dram_tensor("wqT", [D, D], F32, kind="ExternalInput").ap()
    wkT = nc.dram_tensor("wkT", [D, D], F32, kind="ExternalInput").ap()
    wvT = nc.dram_tensor("wvT", [D, D], F32, kind="ExternalInput").ap()
    woT = nc.dram_tensor("woT", [D, D], F32, kind="ExternalInput").ap()
    w1h = nc.dram_tensor("w1", [NE, D, FF], F32, kind="ExternalInput").ap()
    w2h = nc.dram_tensor("w2", [NE, FF, D], F32, kind="ExternalInput").ap()
    b1ch = nc.dram_tensor("b1c", [128, NE * FCN], F32, kind="ExternalInput").ap()
    b2rh = nc.dram_tensor("b2r", [16, D], F32, kind="ExternalInput").ap()
    b2sh = nc.dram_tensor("b2s", [2, D], F32, kind="ExternalInput").ap()
    cenh = nc.dram_tensor("cenT", [128, DC * 16], F32, kind="ExternalInput").ap()
    gbh = nc.dram_tensor("gbias", [1, 16], F32, kind="ExternalInput").ap()
    cosh = nc.dram_tensor("cosf", [128, TK], F32, kind="ExternalInput").ap()
    sinh = nc.dram_tensor("sinf", [128, TK], F32, kind="ExternalInput").ap()
    p128h = nc.dram_tensor("p128", [128, 128], F32, kind="ExternalInput").ap()
    idh = nc.dram_tensor("iden", [128, 128], F32, kind="ExternalInput").ap()
    maskh = nc.dram_tensor("masksb", [128, 25 * 128], F32, kind="ExternalInput").ap()

    outh = nc.dram_tensor("out", [T, D], F32, kind="ExternalOutput").ap()

    def skind(nm):
        return "ExternalOutput" if nm in _DEBUG_TAPS else "Internal"

    qt_d = nc.dram_tensor("qt_d", [NT * 128, H * 128], BF16, kind=skind("qt_d")).ap()
    kt_d = nc.dram_tensor("kt_d", [NTA * 128, H * 128], BF16, kind=skind("kt_d")).ap()
    v_d = nc.dram_tensor("v_d", [TK, D], BF16, kind=skind("v_d")).ap()
    s_d = nc.dram_tensor("s_d", [T, D], BF16, kind=skind("s_d")).ap()
    acc_d = nc.dram_tensor("acc_d", [T, D], F32, kind=skind("acc_d")).ap()
    xnt_dbg = g_dbg = None
    if "xnt_d" in _DEBUG_TAPS:
        xnt_dbg = nc.dram_tensor("xnt_d", [128, DC * T], BF16, kind="ExternalOutput").ap()
    if "g_d" in _DEBUG_TAPS:
        g_dbg = nc.dram_tensor("g_d", [128, NT * 16], F32, kind="ExternalOutput").ap()
    st_dbg = None
    if "st_d" in _DEBUG_TAPS:
        st_dbg = nc.dram_tensor("st_d", [128, NTA * 2], F32, kind="ExternalOutput").ap()

    gctx = ExitStack()
    L = Ledger(nc, gctx)
    PH = L.sem("ph")
    pno = [0]

    iden32 = gctx.enter_context(nc.sbuf_tensor("iden32", [128, 128], F32))
    idenb = gctx.enter_context(nc.sbuf_tensor("idenb", [128, 128], BF16))
    p128b = gctx.enter_context(nc.sbuf_tensor("p128b", [128, 128], BF16))
    stats = gctx.enter_context(nc.sbuf_tensor("stats", [128, NTA, 2], F32))
    b1c = gctx.enter_context(nc.sbuf_tensor("b1c_s", [128, NE * FCN], F32))
    cenT = gctx.enter_context(nc.sbuf_tensor("cenT_s", [128, DC, 16], F32))
    gbias = gctx.enter_context(nc.sbuf_tensor("gbias_s", [1, 16], F32))
    bplane = gctx.enter_context(nc.sbuf_tensor("bplane", [128, 16], F32))
    ones1 = gctx.enter_context(nc.sbuf_tensor("ones1", [1, 128], F32))
    ones2 = gctx.enter_context(nc.sbuf_tensor("ones2", [2, 128], BF16))
    Gm = gctx.enter_context(nc.sbuf_tensor("Gm", [128, NT, 16], F32))
    GTb = gctx.enter_context(nc.sbuf_tensor("GTb", [16, T], BF16))
    b2rb = gctx.enter_context(nc.sbuf_tensor("b2rb", [16, D], BF16))
    b2sb = gctx.enter_context(nc.sbuf_tensor("b2sb", [2, D], BF16))
    epsb = gctx.enter_context(nc.sbuf_tensor("epsb", [128, 1], F32))

    block = gctx.enter_context(nc.Block())

    def barrier(names):
        pno[0] += 1

        @block.sync
        def _(sy):
            for s in names:
                sy.wait_ge(L.sem(s), L.n(s))
            sy.sem_inc(PH, 1)
        L.c["ph"] += 1

    def prol(eng):
        eng.wait_ge(PH, pno[0])

    with ExitStack() as p0:
        b2r32 = p0.enter_context(nc.sbuf_tensor("b2r32", [16, D], F32))
        b2s32 = p0.enter_context(nc.sbuf_tensor("b2s32", [2, D], F32))
        p128s = p0.enter_context(nc.sbuf_tensor("p128s", [128, 128], F32))
        psbp = p0.enter_context(nc.psum_tensor("psbp", [128, 16], F32))

        @block.sync
        def _(sy):
            for dst, src in [(iden32, idh), (b1c, b1ch),
                             (gbias, gbh), (b2r32, b2rh), (b2s32, b2sh),
                             (p128s, p128h)]:
                i = sy.dma_start(out=dst[:], in_=src)
                L.inc("c_dma", i, 16)
            i = sy.dma_start(out=cenT[:], in_=cenh.rearrange("p (dc j) -> p dc j", dc=DC))
            L.inc("c_dma", i, 16)

        @block.gpsimd
        def _(g):
            g.wait_ge(L.sem("c_dma"), L.n("c_dma"))
            g.memset(ones1[:], 1.0)
            g.memset(ones2[:], 1.0)
            g.memset(epsb[:], EPS)
            g.tensor_copy(out=idenb[:], in_=iden32[:])
            g.tensor_copy(out=b2rb[:], in_=b2r32[:])
            g.tensor_copy(out=b2sb[:], in_=b2s32[:])
            i = g.tensor_copy(out=p128b[:], in_=p128s[:])
            L.inc("c_cast", i, 1)

        @block.tensor
        def _(t):
            t.wait_ge(L.sem("c_cast"), 1)
            i = t.matmul(psbp[:], ones1[:], gbias[:], start=True, stop=True)
            L.inc("c_bp", i, 1)

        @block.scalar
        def _(s):
            s.wait_ge(L.sem("c_bp"), 1)
            i = s.copy(out=bplane[:], in_=psbp[:])
            L.inc("c_bpe", i, 1)

        barrier(["c_dma", "c_cast", "c_bpe"])

    sA = ExitStack()
    xnT = sA.enter_context(nc.sbuf_tensor("xnT", [128, DC, T], BF16))
    sB = ExitStack()
    xnTh = sB.enter_context(nc.sbuf_tensor("xnTh", [128, DC, HALO], BF16))
    cosf = sB.enter_context(nc.sbuf_tensor("cosf_s", [128, TK], F32))
    sinf = sB.enter_context(nc.sbuf_tensor("sinf_s", [128, TK], F32))

    def xnT_sl(tt, dc, lo=0, hi=128):
        if tt < 4:
            return xnTh[:, dc, tt * 128 + lo:tt * 128 + hi]
        return xnT[:, dc, (tt - 4) * 128 + lo:(tt - 4) * 128 + hi]

    def xnT_grp(dc, g):
        if g == 0:
            return xnTh[:, dc, :]
        return xnT[:, dc, (g - 1) * 512:g * 512]

    with ExitStack() as p1:
        xb = [p1.enter_context(nc.sbuf_tensor(f"xb{i}", [128, D], F32)) for i in range(2)]
        xn32 = [p1.enter_context(nc.sbuf_tensor(f"xn32_{i}", [128, D], F32)) for i in range(2)]
        sqd = p1.enter_context(nc.sbuf_tensor("sqd", [128, D], F32))
        s1b = p1.enter_context(nc.sbuf_tensor("s1b", [128, 1], F32))
        s2b = p1.enter_context(nc.sbuf_tensor("s2b", [128, 1], F32))
        ex2b = p1.enter_context(nc.sbuf_tensor("ex2b", [128, 1], F32))
        nvb = p1.enter_context(nc.sbuf_tensor("nvb", [128, 1], F32))
        sdb = p1.enter_context(nc.sbuf_tensor("sdb", [128, 1], F32))
        xnT32 = p1.enter_context(nc.sbuf_tensor("xnT32", [128, DC, 128], F32))
        sgb = [p1.enter_context(nc.sbuf_tensor(f"sgb{i}", [128, 16], F32)) for i in range(2)]
        scb = p1.enter_context(nc.sbuf_tensor("scb", [128, 16], F32))
        scur = [p1.enter_context(nc.sbuf_tensor(f"scur{i}", [128, 16], F32)) for i in range(2)]
        geb = p1.enter_context(nc.sbuf_tensor("geb", [128, 16], F32))
        gmb = p1.enter_context(nc.sbuf_tensor("gmb", [128, 16], F32))
        mrb = [p1.enter_context(nc.sbuf_tensor(f"mrb{i}", [128, 1], F32)) for i in range(4)]
        msb = p1.enter_context(nc.sbuf_tensor("msb", [128, 1], F32))
        recb = p1.enter_context(nc.sbuf_tensor("recb", [128, 1], F32))
        pst = [p1.enter_context(nc.psum_tensor(f"pst{i}", [128, 128], F32)) for i in range(2)]
        pssc = [p1.enter_context(nc.psum_tensor(f"pssc{i}", [128, 16], F32)) for i in range(2)]
        psgt = p1.enter_context(nc.psum_tensor("psgt", [16, 128], F32))

        @block.sync
        def _(sy):
            prol(sy)
            for tt in range(NTA):
                if tt >= 2:
                    sy.wait_ge(L.sem("p1_xn"), tt - 1)
                src = x_halo[tt * 128:(tt + 1) * 128, :] if tt < 4 else \
                    x_own[(tt - 4) * 128:(tt - 3) * 128, :]
                i = sy.dma_start(out=xb[tt % 2][:], in_=src)
                L.inc(f"p1_xd{tt % 2}", i, 16)

        @block.scalar
        def _(s):
            prol(s)
            for tt in range(NTA):
                ot = tt - 4
                s.wait_ge(L.sem(f"p1_xd{tt % 2}"), 16 * (tt // 2 + 1))
                if tt >= 1:
                    s.wait_ge(L.sem("p1_nv"), tt)
                i = s.activation(out=sqd[:], in_=xb[tt % 2][:], func=AF.Square,
                                 accum_out=s2b[:])
                L.inc("p1_sq", i, 1)
                s.wait_ge(L.sem("p1_nv"), tt + 1)
                if tt >= 1:
                    s.wait_ge(L.sem("p1_rc"), tt)
                i = s.activation(out=sdb[:], in_=nvb[:], func=AF.Sqrt,
                                 bias=epsb[:, 0:1], scale=-1.0)
                L.inc("p1_sd", i, 1)
                for dc in range(16):
                    gk = tt * 16 + dc
                    s.wait_ge(L.sem("p1_tr"), gk + 1)
                    if ot >= 0:
                        s.copy(out=xnT_sl(tt, dc), in_=pst[gk % 2][:])
                        i = s.copy(out=xnT32[:, dc, :], in_=pst[gk % 2][:])
                    else:
                        i = s.copy(out=xnT_sl(tt, dc), in_=pst[gk % 2][:])
                    L.inc("p1_ev", i, 1)
                if ot >= 0:
                    s.wait_ge(L.sem("p1_scm"), ot + 1)
                    if ot >= 2:
                        s.wait_ge(L.sem("p1_g"), ot - 1)
                    i = s.activation(out=sgb[ot % 2][:], in_=pssc[ot % 2][:],
                                     func=AF.Sigmoid)
                    L.inc("p1_sg", i, 1)
                if ot >= 1:
                    s.wait_ge(L.sem("p1_gt"), ot)
                    i = s.copy(out=GTb[:, (ot - 1) * 128:ot * 128], in_=psgt[:])
                    L.inc("p1_gte", i, 1)
            s.wait_ge(L.sem("p1_gt"), NT)
            i = s.copy(out=GTb[:, (NT - 1) * 128:NT * 128], in_=psgt[:])
            L.inc("p1_gte", i, 1)

        @block.vector
        def _(v):
            prol(v)
            for tt in range(NTA):
                ot = tt - 4
                v.wait_ge(L.sem(f"p1_xd{tt % 2}"), 16 * (tt // 2 + 1))
                v.reduce_sum(out=s1b[:], in_=xb[tt % 2][:], axis=AX.X)
                v.drain()
                v.tensor_single_scalar(out=stats[:, tt, 0:1], in_=s1b[:],
                                       scalar=1.0 / D, op=ALU.mult)
                v.wait_ge(L.sem("p1_sq"), tt + 1)
                v.tensor_single_scalar(out=ex2b[:], in_=s2b[:], scalar=1.0 / D,
                                       op=ALU.mult)
                v.drain()
                i = v.scalar_tensor_tensor(out=nvb[:], in0=stats[:, tt, 0:1],
                                           scalar=stats[:, tt, 0:1], in1=ex2b[:],
                                           op0=ALU.mult, op1=ALU.subtract)
                L.inc("p1_nv", i, 1)
                v.wait_ge(L.sem("p1_sd"), tt + 1)
                i = v.reciprocal(out=stats[:, tt, 1:2], in_=sdb[:])
                L.inc("p1_rc", i, 1)
                v.drain()
                if tt >= 2:
                    v.wait_ge(L.sem("p1_tr"), 16 * (tt - 1))
                i = v.tensor_scalar(out=xn32[tt % 2][:], in0=xb[tt % 2][:],
                                    scalar1=stats[:, tt, 0:1], scalar2=stats[:, tt, 1:2],
                                    op0=ALU.subtract, op1=ALU.mult)
                L.inc("p1_xn", i, 1)
                if ot >= 0:
                    v.wait_ge(L.sem("p1_sg"), ot + 1)
                    v.tensor_tensor(out=scb[:], in0=sgb[ot % 2][:], in1=bplane[:], op=ALU.add)
                    v.drain()
                    v.tensor_copy(out=scur[0][:], in_=scb[:])
                    v.drain()
                    v.reduce_max(out=mrb[0][:], in_=scur[0][:], axis=AX.X)
                    v.drain()
                    for r in range(1, 4):
                        v.tensor_single_scalar(out=geb[:], in_=scur[(r - 1) % 2][:],
                                               scalar=mrb[r - 1][:, 0:1], op=ALU.is_ge)
                        v.drain()
                        v.scalar_tensor_tensor(out=scur[r % 2][:], in0=geb[:],
                                               scalar=-2e9, in1=scur[(r - 1) % 2][:],
                                               op0=ALU.mult, op1=ALU.add)
                        v.drain()
                        v.reduce_max(out=mrb[r][:], in_=scur[r % 2][:], axis=AX.X)
                        v.drain()
                    v.tensor_tensor(out=msb[:], in0=mrb[0][:], in1=mrb[1][:], op=ALU.add)
                    v.drain()
                    v.tensor_tensor(out=msb[:], in0=msb[:], in1=mrb[2][:], op=ALU.add)
                    v.drain()
                    v.tensor_tensor(out=msb[:], in0=msb[:], in1=mrb[3][:], op=ALU.add)
                    v.drain()
                    v.tensor_single_scalar(out=msb[:], in_=msb[:], scalar=1e-8, op=ALU.max)
                    v.drain()
                    v.reciprocal(out=recb[:], in_=msb[:])
                    v.tensor_single_scalar(out=geb[:], in_=scb[:], scalar=mrb[3][:, 0:1],
                                           op=ALU.is_ge)
                    v.drain()
                    v.tensor_tensor(out=gmb[:], in0=scb[:], in1=geb[:], op=ALU.mult)
                    v.drain()
                    i = v.tensor_single_scalar(out=Gm[:, ot, :], in_=gmb[:],
                                               scalar=recb[:, 0:1], op=ALU.mult)
                    L.inc("p1_g", i, 1)

        def emit_scores(t, ot):
            t.wait_ge(L.sem("p1_ev"), 16 * (ot + 5))
            if ot >= 2:
                t.wait_ge(L.sem("p1_sg"), ot - 1)
            for dc in range(16):
                i = t.matmul(pssc[ot % 2][:], xnT32[:, dc, :], cenT[:, dc, :],
                             start=(dc == 0), stop=(dc == 15))
            L.inc("p1_scm", i, 1)

        def emit_gt(t, j):
            t.wait_ge(L.sem("p1_g"), j + 1)
            if j >= 1:
                t.wait_ge(L.sem("p1_gte"), j)
            i = t.transpose(out=psgt[:], in_=Gm[:, j, :], identity=iden32[:])
            L.inc("p1_gt", i, 1)

        @block.tensor
        def _(t):
            prol(t)
            for tt in range(NTA):
                ot = tt - 4
                t.wait_ge(L.sem("p1_xn"), tt + 1)
                for dc in range(16):
                    gk = tt * 16 + dc
                    if gk >= 2:
                        t.wait_ge(L.sem("p1_ev"), gk - 1)
                    i = t.transpose(out=pst[gk % 2][:],
                                    in_=xn32[tt % 2][:, dc * 128:(dc + 1) * 128],
                                    identity=iden32[:])
                    L.inc("p1_tr", i, 1)
                if ot >= 0:
                    emit_scores(t, ot)
                if ot >= 1:
                    emit_gt(t, ot - 1)
            emit_gt(t, NT - 1)

        names = ["p1_ev", "p1_g", "p1_gte", "p1_xd0", "p1_xd1", "p1_xn",
                 "p1_scm", "p1_gt"]
        if "xnt_d" in _DEBUG_TAPS or "g_d" in _DEBUG_TAPS:
            @block.gpsimd
            def _(g):
                prol(g)
                g.wait_ge(L.sem("p1_ev"), L.n("p1_ev"))
                g.wait_ge(L.sem("p1_g"), L.n("p1_g"))
                if xnt_dbg is not None:
                    i = g.dma_start(out=xnt_dbg,
                                    in_=xnT[:].rearrange("p dc t -> p (dc t)"))
                    L.inc("p1_dbg", i, 16)
                if g_dbg is not None:
                    i = g.dma_start(out=g_dbg, in_=Gm[:].rearrange("p t j -> p (t j)"))
                    L.inc("p1_dbg", i, 16)
                if st_dbg is not None:
                    g.wait_ge(L.sem("p1_xn"), L.n("p1_xn"))
                    i = g.dma_start(out=st_dbg, in_=stats[:].rearrange("p t k -> p (t k)"))
                    L.inc("p1_dbg", i, 16)
                g.wait_ge(L.sem("p1_dbg"), L.n("p1_dbg"))
            names.append("p1_dbg")
        barrier(names)

    if _MAX_PHASE >= 2:
      with ExitStack() as p2:
        wst = [p2.enter_context(nc.sbuf_tensor(f"wst{i}", [128, DC, 128], F32)) for i in range(2)]
        wbf = [p2.enter_context(nc.sbuf_tensor(f"wbf{i}", [128, DC, 128], BF16)) for i in range(2)]
        qa32 = [p2.enter_context(nc.sbuf_tensor(f"qa32_{i}", [128, 512], F32)) for i in range(2)]
        qab = [p2.enter_context(nc.sbuf_tensor(f"qab{i}", [128, 512], BF16)) for i in range(2)]
        t1b = [p2.enter_context(nc.sbuf_tensor(f"t1b{i}", [128, 512], F32)) for i in range(2)]
        t2b = p2.enter_context(nc.sbuf_tensor("t2b", [128, 512], F32))
        qstg = [p2.enter_context(nc.sbuf_tensor(f"qstg{i}", [128, 512], BF16)) for i in range(2)]
        psq = [p2.enter_context(nc.psum_tensor(f"psq{i}", [128, 512], F32)) for i in range(2)]
        pshf = [p2.enter_context(nc.psum_tensor(f"pshf{i}", [128, 512], F32)) for i in range(2)]

        chunks = []
        groups = []
        for h in range(H):
            chunks.append((wqT, h * 128))
            for g in range(4):
                groups.append((h, g, 0))
        for h in range(H):
            chunks.append((wkT, h * 128))
            for g in range(5):
                groups.append((16 + h, g, 1))
        if _QK_NCK < len(chunks):
            chunks = chunks[:_QK_NCK]
            groups = [g for g in groups if g[0] < _QK_NCK]
        NCK = len(chunks)
        lastg = {}
        firstg = {}
        for gi, (ck, g, part) in enumerate(groups):
            lastg[ck] = gi
            firstg.setdefault(ck, gi)

        def grp_meta(gi):
            ck, g, part = groups[gi]
            h = ck if part == 0 else ck - 16
            if part == 0:
                off = HALO + g * 512
                dst = qt_d
                scale = SCALE
                rhs_g = g + 1
            else:
                off = g * 512
                dst = kt_d
                scale = 1.0
                rhs_g = g
            return ck, h, g, off, dst, scale, rhs_g

        @block.sync
        def _(sy):
            if _SKIP_QK:
                return
            prol(sy)
            i = sy.dma_start(out=cosf[:], in_=cosh)
            L.inc("p2_tb", i, 16)
            i = sy.dma_start(out=sinf[:], in_=sinh)
            L.inc("p2_tb", i, 16)
            for ck in range(NCK):
                if ck >= 2:
                    sy.wait_ge(L.sem("p2_wc"), ck - 1)
                wh, col = chunks[ck]
                src = bass.AP(wh.tensor, col,
                              [[D, 128], [128 * D, DC], [1, 128]])
                i = sy.dma_start(out=wst[ck % 2][:], in_=src)
                L.inc(f"p2_wd{ck % 2}", i, 16)

        gbyck = {}
        for gi, (ck, g, part) in enumerate(groups):
            gbyck.setdefault(ck, []).append(gi)

        def emit_qw(g_, gi):
            ck, h, g, off, dst, scale, rhs_g = grp_meta(gi)
            g_.wait_ge(L.sem("p2_rt"), gi + 1)
            t0 = off if dst is kt_d else g * 512
            dsta = bass.AP(dst.tensor, (t0 // 128) * 128 * (H * 128) + h * 128,
                           [[H * 128, 128], [128 * H * 128, 4], [1, 128]])
            i = g_.dma_start(out=dsta,
                             in_=qstg[gi % 2][:].rearrange("p (a t) -> p a t", a=4))
            L.inc(f"p2_qw{gi % 2}", i, 16)

        @block.gpsimd
        def _(g_):
            if _SKIP_QK:
                return
            prol(g_)
            for ck in range(NCK):
                g_.wait_ge(L.sem(f"p2_wd{ck % 2}"), 16 * (ck // 2 + 1))
                if ck >= 2:
                    g_.wait_ge(L.sem("p2_mm"), lastg[ck - 2] + 1)
                i = g_.tensor_copy(out=wbf[ck % 2][:], in_=wst[ck % 2][:])
                L.inc("p2_wc", i, 1)
                if ck >= 1:
                    for gi in gbyck[ck - 1]:
                        emit_qw(g_, gi)
            for gi in gbyck[NCK - 1]:
                emit_qw(g_, gi)

        @block.tensor
        def _(t):
            if _SKIP_QK:
                return
            prol(t)
            for gi in range(len(groups)):
                ck, h, g, off, dst, scale, rhs_g = grp_meta(gi)
                if gi == firstg[ck]:
                    t.wait_ge(L.sem("p2_wc"), ck + 1)
                if gi >= 2:
                    t.wait_ge(L.sem("p2_ea"), gi - 1)
                    t.wait_ge(L.sem("p2_ev"), gi - 1)
                for dc in range(DC):
                    i = t.matmul(psq[gi % 2][:], wbf[ck % 2][:, dc, :],
                                 xnT_grp(dc, rhs_g), start=(dc == 0), stop=(dc == 15))
                L.inc("p2_mm", i, 1)
                if gi >= 1:
                    j = gi - 1
                    t.wait_ge(L.sem("p2_ev"), j + 1)
                    if j >= 2:
                        t.wait_ge(L.sem("p2_t2"), j - 1)
                    i = t.matmul(pshf[j % 2][:], p128b[:], qab[j % 2][:],
                                 start=True, stop=True)
                    L.inc("p2_sh", i, 1)
            j = len(groups) - 1
            t.wait_ge(L.sem("p2_ev"), j + 1)
            t.wait_ge(L.sem("p2_t2"), j - 1)
            i = t.matmul(pshf[j % 2][:], p128b[:], qab[j % 2][:], start=True, stop=True)
            L.inc("p2_sh", i, 1)

        @block.scalar
        def _(s):
            if _SKIP_QK:
                return
            prol(s)
            for gi in range(len(groups)):
                ck, h, g, off, dst, scale, rhs_g = grp_meta(gi)
                s.wait_ge(L.sem("p2_mm"), gi + 1)
                if gi >= 2:
                    s.wait_ge(L.sem("p2_rt"), gi - 1)
                i = s.activation(out=qa32[gi % 2][:], in_=psq[gi % 2][:],
                                 func=AF.Copy, scale=scale)
                L.inc("p2_ea", i, 1)
                if gi >= 2:
                    s.wait_ge(L.sem("p2_sh"), gi - 1)
                i = s.activation(out=qab[gi % 2][:], in_=psq[gi % 2][:],
                                 func=AF.Copy, scale=scale)
                L.inc("p2_ev", i, 1)

        def rope_finish(v, j):
            _, _, _, off, _, _, _ = grp_meta(j)
            v.wait_ge(L.sem("p2_sh"), j + 1)
            i = v.tensor_tensor(out=t2b[:], in0=pshf[j % 2][:],
                                in1=sinf[:, off:off + 512], op=ALU.mult)
            L.inc("p2_t2", i, 1)
            v.wait_ge(L.sem("p2_ea"), j + 1)
            v.tensor_tensor(out=t1b[0][:], in0=qa32[j % 2][:],
                            in1=cosf[:, off:off + 512], op=ALU.mult)
            v.drain()
            if j >= 2:
                v.wait_ge(L.sem(f"p2_qw{j % 2}"), 16 * ((j - 2) // 2 + 1))
            i = v.tensor_tensor(out=qstg[j % 2][:], in0=t1b[0][:],
                                in1=t2b[:], op=ALU.add)
            L.inc("p2_rt", i, 1)
            v.drain()

        @block.vector
        def _(v):
            if _SKIP_QK:
                return
            prol(v)
            v.wait_ge(L.sem("p2_tb"), 32)
            for gi in range(len(groups)):
                if gi >= 1:
                    rope_finish(v, gi - 1)
            rope_finish(v, len(groups) - 1)

        barrier(["p2_tb", "p2_wd0", "p2_wd1", "p2_wc", "p2_mm", "p2_ea",
                 "p2_ev", "p2_sh", "p2_t2", "p2_rt", "p2_qw0", "p2_qw1"])

        vst = [p2.enter_context(nc.sbuf_tensor(f"vst{i}", [128, 512], F32)) for i in range(2)]
        wvbf = p2.enter_context(nc.sbuf_tensor("wvbf", [128, DC, 512], BF16))
        vstg = [p2.enter_context(nc.sbuf_tensor(f"vstg{i}", [128, 512], BF16)) for i in range(2)]
        psv = [p2.enter_context(nc.psum_tensor(f"psv{i}", [128, 512], F32)) for i in range(2)]

        @block.sync
        def _(sy):
            if _SKIP_V:
                return
            prol(sy)
            for vck in range(4 * DC):
                do, dcc = divmod(vck, DC)
                if vck >= 2:
                    sy.wait_ge(L.sem("p2v_wc"), vck - 1)
                src = bass.AP(wvT.tensor, dcc * 128 * D + do * 512,
                              [[D, 128], [1, 512]])
                i = sy.dma_start(out=vst[vck % 2][:], in_=src)
                L.inc(f"p2v_wd{vck % 2}", i, 16)

        @block.gpsimd
        def _(g_):
            if _SKIP_V:
                return
            prol(g_)

            def emit_vw(gv):
                do_, tt_ = divmod(gv, NTA)
                g_.wait_ge(L.sem("p2v_ev"), gv + 1)
                dsta = bass.AP(v_d.tensor, tt_ * 128 * D + do_ * 512,
                               [[D, 128], [1, 512]])
                i = g_.dma_start(out=dsta, in_=vstg[gv % 2][:])
                L.inc(f"p2v_vw{gv % 2}", i, 16)

            for do in range(4):
                if do >= 1:
                    for gv in range((do - 1) * NTA, do * NTA):
                        emit_vw(gv)
                for dcc in range(DC):
                    vck = do * DC + dcc
                    g_.wait_ge(L.sem(f"p2v_wd{vck % 2}"), 16 * (vck // 2 + 1))
                    if do >= 1:
                        g_.wait_ge(L.sem("p2v_mm"), NTA * do)
                    i = g_.tensor_copy(out=wvbf[:, dcc, :], in_=vst[vck % 2][:])
                    L.inc("p2v_wc", i, 1)
            for gv in range(3 * NTA, 4 * NTA):
                emit_vw(gv)

        @block.tensor
        def _(t):
            if _SKIP_V:
                return
            prol(t)
            for gv in range(4 * NTA):
                do, tt = divmod(gv, NTA)
                if gv >= 2:
                    t.wait_ge(L.sem("p2v_ev"), gv - 1)
                for dc in range(DC):
                    if tt == 0:
                        t.wait_ge(L.sem("p2v_wc"), do * DC + dc + 1)
                    i = t.matmul(psv[gv % 2][:], xnT_sl(tt, dc), wvbf[:, dc, :],
                                 start=(dc == 0), stop=(dc == 15))
                L.inc("p2v_mm", i, 1)

        @block.scalar
        def _(s):
            if _SKIP_V:
                return
            prol(s)
            for gv in range(4 * NTA):
                s.wait_ge(L.sem("p2v_mm"), gv + 1)
                if gv >= 2:
                    s.wait_ge(L.sem(f"p2v_vw{gv % 2}"), 16 * ((gv - 2) // 2 + 1))
                i = s.copy(out=vstg[gv % 2][:], in_=psv[gv % 2][:])
                L.inc("p2v_ev", i, 1)

        barrier(["p2v_wd0", "p2v_wd1", "p2v_wc", "p2v_mm", "p2v_ev",
                 "p2v_vw0", "p2v_vw1"])

    sB.close()

    if _MAX_PHASE >= 3:
      with ExitStack() as p5:
        w1st = [p5.enter_context(nc.sbuf_tensor(f"w1st{i}", [128, DC, 128], F32)) for i in range(2)]
        w1bf = [p5.enter_context(nc.sbuf_tensor(f"w1bf{i}", [128, DC, 128], BF16)) for i in range(2)]
        w2st = [p5.enter_context(nc.sbuf_tensor(f"w2st{i}", [128, 512], F32)) for i in range(2)]
        w2bd = p5.enter_context(nc.sbuf_tensor("w2bd", [128, FCN, 512], BF16))
        hT = p5.enter_context(nc.sbuf_tensor("hT", [128, FCN, 1024], BF16))
        acc = p5.enter_context(nc.sbuf_tensor("acc", [128, 8, D], F32))
        psh = [p5.enter_context(nc.psum_tensor(f"psh{i}", [128, 512], F32)) for i in range(4)]
        psy = [p5.enter_context(nc.psum_tensor(f"psy{i}", [128, 512], F32)) for i in range(4)]

        NJ = NE
        n_hgrp = 2 * FCN
        n_ygrp = 4 * 8

        @block.sync
        def _(sy):
            prol(sy)
            for half in range(2):
                for j in range(NJ):
                    jj = half * NJ + j
                    for fc in range(FCN):
                        ck = jj * FCN + fc
                        if ck >= 2:
                            sy.wait_ge(L.sem("p5_w1c"), ck - 1)
                        src = bass.AP(w1h.tensor, j * D * FF + fc * 128,
                                      [[FF, 128], [128 * FF, DC], [1, 128]])
                        i = sy.dma_start(out=w1st[ck % 2][:], in_=src)
                        L.inc(f"p5_w1d{ck % 2}", i, 16)
                    for do in range(4):
                        gd = jj * 4 + do
                        for fc in range(FCN):
                            cw = gd * FCN + fc
                            if cw >= 2:
                                sy.wait_ge(L.sem("p5_w2c"), cw - 1)
                            src = bass.AP(w2h.tensor, j * FF * D + fc * 128 * D + do * 512,
                                          [[D, 128], [1, 512]])
                            i = sy.dma_start(out=w2st[cw % 2][:], in_=src)
                            L.inc(f"p5_w2d{cw % 2}", i, 16)

        @block.vector
        def _(v):
            prol(v)
            for half in range(2):
                for j in range(NJ):
                    jj = half * NJ + j
                    for fc in range(FCN):
                        ck = jj * FCN + fc
                        v.wait_ge(L.sem(f"p5_w1d{ck % 2}"), 16 * (ck // 2 + 1))
                        if ck >= 2:
                            v.wait_ge(L.sem("p5_hmm"), 2 * (ck - 1))
                        i = v.tensor_copy(out=w1bf[ck % 2][:], in_=w1st[ck % 2][:])
                        L.inc("p5_w1c", i, 1)
                    for do in range(4):
                        gd = jj * 4 + do
                        for tile in range(8):
                            gy = gd * 8 + tile
                            v.wait_ge(L.sem("p5_ymm"), gy + 1)
                            if half == 1 and j == 0 and do == 0:
                                v.wait_ge(L.sem("p5_acw"), 128)
                            dsl = acc[:, tile, do * 512:(do + 1) * 512]
                            if j == 0:
                                i = v.tensor_single_scalar(out=dsl, in_=psy[gy % 4][:],
                                                           scalar=1.0, op=ALU.mult)
                            elif j == 1:
                                i = v.scalar_tensor_tensor(out=dsl, in0=psy[gy % 4][:],
                                                           scalar=1.0, in1=dsl,
                                                           op0=ALU.mult, op1=ALU.add)
                            else:
                                gate = Gm[:, half * 8 + tile, j - 2:j - 1]
                                i = v.scalar_tensor_tensor(out=dsl, in0=psy[gy % 4][:],
                                                           scalar=gate, in1=dsl,
                                                           op0=ALU.mult, op1=ALU.add)
                            L.inc("p5_yev", i, 1)

        @block.scalar
        def _(s):
            prol(s)
            for half in range(2):
                for j in range(NJ):
                    jj = half * NJ + j
                    for fc in range(FCN):
                        ck = jj * FCN + fc
                        for g2 in range(2):
                            gh = ck * 2 + g2
                            s.wait_ge(L.sem("p5_hmm"), gh + 1)
                            i = s.activation(out=hT[:, fc, g2 * 512:(g2 + 1) * 512],
                                             in_=psh[gh % 4][:], func=AF.Relu,
                                             bias=b1c[:, j * FCN + fc:j * FCN + fc + 1])
                            L.inc("p5_hev", i, 1)
                    for do in range(4):
                        gd = jj * 4 + do
                        for fc in range(FCN):
                            cw = gd * FCN + fc
                            s.wait_ge(L.sem(f"p5_w2d{cw % 2}"), 16 * (cw // 2 + 1))
                            if gd >= 1:
                                s.wait_ge(L.sem("p5_ymm"), 8 * gd)
                            i = s.activation(out=w2bd[:, fc, :],
                                             in_=w2st[cw % 2][:], func=AF.Copy)
                            L.inc("p5_w2c", i, 1)

        @block.tensor
        def _(t):
            prol(t)
            for half in range(2):
                for j in range(NJ):
                    jj = half * NJ + j
                    for fc in range(FCN):
                        ck = jj * FCN + fc
                        t.wait_ge(L.sem("p5_w1c"), ck + 1)
                        for g2 in range(2):
                            gh = ck * 2 + g2
                            if gh >= 4:
                                t.wait_ge(L.sem("p5_hev"), gh - 3)
                            for dc in range(DC):
                                i = t.matmul(psh[gh % 4][:], w1bf[ck % 2][:, dc, :],
                                             xnT[:, dc, half * 1024 + g2 * 512:
                                                 half * 1024 + (g2 + 1) * 512],
                                             start=(dc == 0), stop=(dc == 15))
                            L.inc("p5_hmm", i, 1)
                    for do in range(4):
                        gd = jj * 4 + do
                        for tile in range(8):
                            gy = gd * 8 + tile
                            if gy >= 4:
                                t.wait_ge(L.sem("p5_yev"), gy - 3)
                            nmm = FCN + (2 if j == 0 else 0)
                            k = 0
                            for fc in range(FCN):
                                if tile == 0:
                                    t.wait_ge(L.sem("p5_w2c"), gd * FCN + fc + 1)
                                if do == 0 and tile == 0:
                                    t.wait_ge(L.sem("p5_hev"), 2 * (jj * FCN + fc) + 2)
                                i = t.matmul(psy[gy % 4][:],
                                             hT[:, fc, tile * 128:(tile + 1) * 128],
                                             w2bd[:, fc, :],
                                             start=(fc == 0), stop=(fc == FCN - 1 and j != 0))
                                k += 1
                            if j == 0:
                                tok0 = (half * 8 + tile) * 128
                                t.matmul(psy[gy % 4][:], GTb[:, tok0:tok0 + 128],
                                         b2rb[:, do * 512:(do + 1) * 512],
                                         start=False, stop=False)
                                i = t.matmul(psy[gy % 4][:], ones2[:],
                                             b2sb[:, do * 512:(do + 1) * 512],
                                             start=False, stop=True)
                            L.inc("p5_ymm", i, 1)

        @block.gpsimd
        def _(g_):
            prol(g_)
            for half in range(2):
                g_.wait_ge(L.sem("p5_yev"), (half + 1) * NJ * 32)
                for tile in range(8):
                    i = g_.dma_start(out=acc_d[(half * 8 + tile) * 128:
                                               (half * 8 + tile + 1) * 128, :],
                                     in_=acc[:, tile, :])
                    L.inc("p5_acw", i, 16)

        barrier(["p5_w1d0", "p5_w1d1", "p5_w1c", "p5_w2d0", "p5_w2d1",
                 "p5_w2c", "p5_hmm", "p5_hev", "p5_ymm", "p5_yev", "p5_acw"])

    sA.close()

    if _MAX_PHASE >= 4:
      aoS = ExitStack()
      aoT = aoS.enter_context(nc.sbuf_tensor("aoT", [128, H, T], BF16))
      with ExitStack() as p4:
        masks = p4.enter_context(nc.sbuf_tensor("masks", [128, 25, 128], BF16))
        masks32 = p4.enter_context(nc.sbuf_tensor("masks32", [128, 25 * 128], F32))
        kts = [p4.enter_context(nc.sbuf_tensor(f"kts{i}", [128, H * 128], BF16)) for i in range(6)]
        vs = [p4.enter_context(nc.sbuf_tensor(f"vs{i}", [128, D], BF16)) for i in range(6)]
        qts = [p4.enter_context(nc.sbuf_tensor(f"qts{i}", [128, H * 128], BF16)) for i in range(3)]
        pa = [p4.enter_context(nc.sbuf_tensor(f"pa{i}", [128, 640], BF16)) for i in range(2)]
        panb = [p4.enter_context(nc.sbuf_tensor(f"panb{i}", [128, 640], BF16)) for i in range(2)]
        ptb = [p4.enter_context(nc.sbuf_tensor(f"ptb{i}", [128, 640], BF16)) for i in range(2)]
        negmb = [p4.enter_context(nc.sbuf_tensor(f"negmb{i}", [128, 1], F32)) for i in range(2)]
        sumb = p4.enter_context(nc.sbuf_tensor("sumb", [128, 1], F32))
        rcb = p4.enter_context(nc.sbuf_tensor("rcb", [128, 1], F32))
        pss = [p4.enter_context(nc.psum_tensor(f"pss{i}", [128, 640], F32)) for i in range(2)]
        pspt = [p4.enter_context(nc.psum_tensor(f"pspt{i}", [128, 640], BF16)) for i in range(2)]
        psao = [p4.enter_context(nc.psum_tensor(f"psao{i}", [128, 128], F32)) for i in range(2)]

        NIDX = NT * H

        @block.gpsimd
        def _(g_):
            prol(g_)
            g_.wait_ge(L.sem("p4_mk"), 16)
            i = g_.tensor_copy(out=masks[:].rearrange("p a t -> p (a t)"), in_=masks32[:])
            L.inc("p4_mkc", i, 1)
            for kt in range(NTA):
                if kt >= 6:
                    g_.wait_ge(L.sem("p4_avm"), 16 * (kt - 5))
                sl = kt % 6
                i = g_.dma_start(out=kts[sl][:], in_=kt_d[kt * 128:(kt + 1) * 128, :])
                L.inc(f"p4_kv{sl}", i, 16)
                i = g_.dma_start(out=vs[sl][:], in_=v_d[kt * 128:(kt + 1) * 128, :])
                L.inc(f"p4_kv{sl}", i, 16)

        @block.sync
        def _(sy):
            prol(sy)
            i = sy.dma_start(out=masks32[:], in_=maskh)
            L.inc("p4_mk", i, 16)
            for qt in range(NT):
                if qt >= 3:
                    sy.wait_ge(L.sem("p4_smm"), 16 * (qt - 2))
                i = sy.dma_start(out=qts[qt % 3][:], in_=qt_d[qt * 128:(qt + 1) * 128, :])
                L.inc(f"p4_qt{qt % 3}", i, 16)

        def pe_T(t, j):
            t.wait_ge(L.sem("p4_pan"), j + 1)
            if j >= 2:
                t.wait_ge(L.sem("p4_pte"), j - 1)
            for d in range(5):
                i = t.transpose(out=pspt[j % 2][:, d * 128:(d + 1) * 128],
                                in_=panb[j % 2][:, d * 128:(d + 1) * 128],
                                identity=idenb[:])
            L.inc("p4_tmm", i, 1)

        def pe_AV(t, j):
            qt, h = divmod(j, H)
            t.wait_ge(L.sem("p4_pte"), j + 1)
            if j >= 2:
                t.wait_ge(L.sem("p4_aoe"), j - 1)
            for d in range(5):
                sl = (qt + d) % 6
                i = t.matmul(psao[j % 2][:], vs[sl][:, h * 128:(h + 1) * 128],
                             ptb[j % 2][:, d * 128:(d + 1) * 128],
                             start=(d == 0), stop=(d == 4))
            L.inc("p4_avm", i, 1)

        @block.tensor
        def _(t):
            prol(t)
            for idx in range(NIDX):
                qt, h = divmod(idx, H)
                if idx == 0:
                    t.wait_ge(L.sem("p4_mkc"), 1)
                if h == 0:
                    for kt in range(qt, qt + 5):
                        t.wait_ge(L.sem(f"p4_kv{kt % 6}"), 32 * (kt // 6 + 1))
                    t.wait_ge(L.sem(f"p4_qt{qt % 3}"), 16 * (qt // 3 + 1))
                if idx >= 2:
                    t.wait_ge(L.sem("p4_mx"), idx - 1)
                    t.wait_ge(L.sem("p4_ex"), idx - 1)
                sl = idx % 2
                for d in range(5):
                    ks = (qt + d) % 6
                    t.matmul(pss[sl][:, d * 128:(d + 1) * 128],
                             qts[qt % 3][:, h * 128:(h + 1) * 128],
                             kts[ks][:, h * 128:(h + 1) * 128],
                             start=(d == 0 or d == 4), stop=False)
                dlist = range(5) if qt < 4 else (0, 4)
                tc = qt if qt < 4 else 4
                for d in dlist:
                    last_b0 = (d == max(x for x in dlist if x < 4))
                    i = t.matmul(pss[sl][:, d * 128:(d + 1) * 128], idenb[:],
                                 masks[:, tc * 5 + d, :],
                                 start=False, stop=(last_b0 or d == 4))
                L.inc("p4_smm", i, 1)
                if idx >= 1:
                    pe_T(t, idx - 1)
                    pe_AV(t, idx - 1)
            pe_T(t, NIDX - 1)
            pe_AV(t, NIDX - 1)

        @block.vector
        def _(v):
            prol(v)
            for idx in range(NIDX):
                v.wait_ge(L.sem("p4_smm"), idx + 1)
                i = v.reduce_max(out=negmb[idx % 2][:], in_=pss[idx % 2][:],
                                 axis=AX.X, negate=True)
                L.inc("p4_mx", i, 1)
                if idx >= 1:
                    j = idx - 1
                    v.wait_ge(L.sem("p4_ex"), j + 1)
                    v.reduce_sum(out=sumb[:], in_=pa[j % 2][:], axis=AX.X)
                    v.drain()
                    v.reciprocal(out=rcb[:], in_=sumb[:])
                    v.drain()
                    if j >= 2:
                        v.wait_ge(L.sem("p4_tmm"), j - 1)
                    i = v.tensor_single_scalar(out=panb[j % 2][:], in_=pa[j % 2][:],
                                               scalar=rcb[:, 0:1], op=ALU.mult)
                    L.inc("p4_pan", i, 1)
            j = NIDX - 1
            v.wait_ge(L.sem("p4_ex"), j + 1)
            v.reduce_sum(out=sumb[:], in_=pa[j % 2][:], axis=AX.X)
            v.drain()
            v.reciprocal(out=rcb[:], in_=sumb[:])
            v.drain()
            v.wait_ge(L.sem("p4_tmm"), j - 1)
            i = v.tensor_single_scalar(out=panb[j % 2][:], in_=pa[j % 2][:],
                                       scalar=rcb[:, 0:1], op=ALU.mult)
            L.inc("p4_pan", i, 1)

        def act_tail(s, j):
            qt, h = divmod(j, H)
            s.wait_ge(L.sem("p4_tmm"), j + 1)
            if j >= 2:
                s.wait_ge(L.sem("p4_avm"), j - 1)
            i = s.copy(out=ptb[j % 2][:], in_=pspt[j % 2][:])
            L.inc("p4_pte", i, 1)
            s.wait_ge(L.sem("p4_avm"), j + 1)
            i = s.copy(out=aoT[:, h, qt * 128:(qt + 1) * 128], in_=psao[j % 2][:])
            L.inc("p4_aoe", i, 1)

        @block.scalar
        def _(s):
            prol(s)
            for idx in range(NIDX):
                s.wait_ge(L.sem("p4_mx"), idx + 1)
                if idx >= 2:
                    s.wait_ge(L.sem("p4_pan"), idx - 1)
                i = s.activation(out=pa[idx % 2][:], in_=pss[idx % 2][:], func=AF.Exp,
                                 bias=negmb[idx % 2][:, 0:1])
                L.inc("p4_ex", i, 1)
                if idx >= 1:
                    act_tail(s, idx - 1)
            act_tail(s, NIDX - 1)

        barrier([f"p4_kv{s}" for s in range(6)] + [f"p4_qt{s}" for s in range(3)]
                + ["p4_smm", "p4_mx", "p4_ex", "p4_pan",
                   "p4_tmm", "p4_pte", "p4_avm", "p4_aoe"])

      with ExitStack() as p4w:
        wost = [p4w.enter_context(nc.sbuf_tensor(f"wost{i}", [128, 512], F32)) for i in range(2)]
        wodb = p4w.enter_context(nc.sbuf_tensor("wodb", [128, H, 512], BF16))
        sstg = [p4w.enter_context(nc.sbuf_tensor(f"sstg{i}", [128, 512], BF16)) for i in range(2)]
        pss2 = [p4w.enter_context(nc.psum_tensor(f"pss2_{i}", [128, 512], F32)) for i in range(2)]

        @block.sync
        def _(sy):
            prol(sy)
            for cwo in range(4 * H):
                if cwo >= 2:
                    sy.wait_ge(L.sem("p4w_wc"), cwo - 1)
                do, h2 = divmod(cwo, H)
                src = bass.AP(woT.tensor, h2 * 128 * D + do * 512, [[D, 128], [1, 512]])
                i = sy.dma_start(out=wost[cwo % 2][:], in_=src)
                L.inc(f"p4w_wd{cwo % 2}", i, 16)

        @block.gpsimd
        def _(g_):
            prol(g_)

            def emit_sw(gw):
                do_, tile_ = divmod(gw, NT)
                g_.wait_ge(L.sem("p4w_ev"), gw + 1)
                dsta = bass.AP(s_d.tensor, tile_ * 128 * D + do_ * 512,
                               [[D, 128], [1, 512]])
                i = g_.dma_start(out=dsta, in_=sstg[gw % 2][:])
                L.inc(f"p4w_sw{gw % 2}", i, 16)

            for do in range(4):
                if do >= 1:
                    for gw in range((do - 1) * NT, do * NT):
                        emit_sw(gw)
                for h2 in range(H):
                    cwo = do * H + h2
                    g_.wait_ge(L.sem(f"p4w_wd{cwo % 2}"), 16 * (cwo // 2 + 1))
                    if do >= 1:
                        g_.wait_ge(L.sem("p4w_mm"), NT * do)
                    i = g_.tensor_copy(out=wodb[:, h2, :], in_=wost[cwo % 2][:])
                    L.inc("p4w_wc", i, 1)
            for gw in range(3 * NT, 4 * NT):
                emit_sw(gw)

        @block.tensor
        def _(t):
            prol(t)
            for gw in range(4 * NT):
                do, tile = divmod(gw, NT)
                if gw >= 2:
                    t.wait_ge(L.sem("p4w_ev"), gw - 1)
                for h2 in range(H):
                    if tile == 0:
                        t.wait_ge(L.sem("p4w_wc"), do * H + h2 + 1)
                    i = t.matmul(pss2[gw % 2][:], aoT[:, h2, tile * 128:(tile + 1) * 128],
                                 wodb[:, h2, :], start=(h2 == 0), stop=(h2 == H - 1))
                L.inc("p4w_mm", i, 1)

        @block.scalar
        def _(s):
            prol(s)
            for gw in range(4 * NT):
                s.wait_ge(L.sem("p4w_mm"), gw + 1)
                if gw >= 2:
                    s.wait_ge(L.sem(f"p4w_sw{gw % 2}"), 16 * ((gw - 2) // 2 + 1))
                i = s.copy(out=sstg[gw % 2][:], in_=pss2[gw % 2][:])
                L.inc("p4w_ev", i, 1)

        barrier(["p4w_wd0", "p4w_wd1", "p4w_wc", "p4w_mm", "p4w_ev",
                 "p4w_sw0", "p4w_sw1"])
      aoS.close()

    if _MAX_PHASE >= 5:
      with ExitStack() as p6:
        xb6 = [p6.enter_context(nc.sbuf_tensor(f"xb6_{i}", [128, D], F32)) for i in range(2)]
        ab6 = [p6.enter_context(nc.sbuf_tensor(f"ab6_{i}", [128, D], F32)) for i in range(2)]
        sb6 = [p6.enter_context(nc.sbuf_tensor(f"sb6_{i}", [128, D], BF16)) for i in range(2)]
        xn6 = p6.enter_context(nc.sbuf_tensor("xn6", [128, D], F32))
        ffb = p6.enter_context(nc.sbuf_tensor("ffb", [128, D], F32))
        g6b = p6.enter_context(nc.sbuf_tensor("g6b", [128, D], F32))
        sgb6 = p6.enter_context(nc.sbuf_tensor("sgb6", [128, D], F32))
        outb = [p6.enter_context(nc.sbuf_tensor(f"outb{i}", [128, D], F32)) for i in range(2)]

        @block.sync
        def _(sy):
            prol(sy)
            for tt in range(NT):
                if tt >= 2:
                    sy.wait_ge(L.sem("p6_ot"), tt - 1)
                i = sy.dma_start(out=xb6[tt % 2][:], in_=x_own[tt * 128:(tt + 1) * 128, :])
                L.inc(f"p6_xd{tt % 2}", i, 16)
                i = sy.dma_start(out=ab6[tt % 2][:], in_=acc_d[tt * 128:(tt + 1) * 128, :])
                L.inc(f"p6_ad{tt % 2}", i, 16)

        @block.gpsimd
        def _(g_):
            prol(g_)

            def emit_ow(tt):
                g_.wait_ge(L.sem("p6_ot"), tt + 1)
                i = g_.dma_start(out=outh[tt * 128:(tt + 1) * 128, :], in_=outb[tt % 2][:])
                L.inc(f"p6_ow{tt % 2}", i, 16)

            for tt in range(NT):
                if tt >= 2:
                    g_.wait_ge(L.sem("p6_ot"), tt - 1)
                i = g_.dma_start(out=sb6[tt % 2][:], in_=s_d[tt * 128:(tt + 1) * 128, :])
                L.inc(f"p6_sd{tt % 2}", i, 16)
                if tt >= 2:
                    emit_ow(tt - 2)
            emit_ow(NT - 2)
            emit_ow(NT - 1)
            g_.wait_ge(L.sem("p6_ow0"), 16 * (NT // 2))
            g_.wait_ge(L.sem("p6_ow1"), 16 * (NT // 2))

        @block.vector
        def _(v):
            prol(v)
            for tt in range(NT):
                st = tt + 4
                v.wait_ge(L.sem(f"p6_xd{tt % 2}"), 16 * (tt // 2 + 1))
                v.wait_ge(L.sem(f"p6_ad{tt % 2}"), 16 * (tt // 2 + 1))
                v.tensor_scalar(out=xn6[:], in0=xb6[tt % 2][:],
                                scalar1=stats[:, st, 0:1], scalar2=stats[:, st, 1:2],
                                op0=ALU.subtract, op1=ALU.mult)
                v.drain()
                if tt >= 1:
                    v.wait_ge(L.sem("p6_sl"), tt)
                i = v.tensor_tensor(out=ffb[:], in0=xn6[:], in1=ab6[tt % 2][:], op=ALU.add)
                L.inc("p6_ff", i, 1)
                v.wait_ge(L.sem("p6_sl"), tt + 1)
                v.wait_ge(L.sem(f"p6_sd{tt % 2}"), 16 * (tt // 2 + 1))
                v.tensor_tensor(out=sgb6[:], in0=sb6[tt % 2][:], in1=g6b[:], op=ALU.mult)
                v.drain()
                if tt >= 2:
                    v.wait_ge(L.sem(f"p6_ow{tt % 2}"), 16 * ((tt - 2) // 2 + 1))
                i = v.tensor_tensor(out=outb[tt % 2][:], in0=xb6[tt % 2][:],
                                    in1=sgb6[:], op=ALU.add)
                L.inc("p6_ot", i, 1)

        @block.scalar
        def _(s):
            prol(s)
            for tt in range(NT):
                s.wait_ge(L.sem("p6_ff"), tt + 1)
                i = s.activation(out=g6b[:], in_=ffb[:], func=AF.Silu)
                L.inc("p6_sl", i, 1)

        @block.sync
        def _(sy):
            sy.wait_ge(L.sem("p6_ow0"), 16 * (NT // 2))
            sy.wait_ge(L.sem("p6_ow1"), 16 * (NT // 2))

    gctx.close()
    return nc


def _host_constants(hf):
    P0 = hf * 2048
    pos = np.maximum(P0 - HALO + np.arange(TK), 0).astype(np.float64)
    inv = 10000.0 ** (-(np.arange(0, DH, 2, dtype=np.float64)) / DH)
    ang = pos[None, :] * inv[:, None]
    c = np.cos(ang)
    sn = np.sin(ang)
    cosf = np.empty((128, TK), np.float32)
    sinf = np.empty((128, TK), np.float32)
    cosf[0::2] = c
    cosf[1::2] = c
    sinf[0::2] = -sn
    sinf[1::2] = sn

    ii = np.arange(128)[:, None]
    jj = np.arange(128)[None, :]
    masks = np.zeros((128, 25, 128), np.float32)
    for tc in range(5):
        for d in range(5):
            m = np.zeros((128, 128), np.float32)
            if d == 0:
                m[~(jj > ii) * np.ones((128, 128), bool)] = 0.0
                m = np.where(jj > ii, 0.0, -1e9).astype(np.float32)
            elif d == 4:
                m = np.where(jj <= ii, 0.0, -1e9).astype(np.float32)
            if hf == 0 and tc < 4 and tc + d <= 3:
                m[:] = -1e9
            masks[:, tc * 5 + d, :] = m
    masksb = masks.reshape(128, 25 * 128)
    return cosf, sinf, masksb


def prep_inputs(L0_attn_x, L0_attn_Wq, L0_attn_Wk, L0_attn_Wv, L0_attn_Wo,
                L0_ffn_shared_w1, L0_ffn_shared_b1, L0_ffn_shared_w2, L0_ffn_shared_b2,
                L0_ffn_routed_w1, L0_ffn_routed_b1, L0_ffn_routed_w2, L0_ffn_routed_b2,
                L0_ffn_centroids, L0_ffn_bias):
    x = np.ascontiguousarray(np.asarray(L0_attn_x, dtype=np.float32))
    wqT = np.ascontiguousarray(np.asarray(L0_attn_Wq, np.float32).T)
    wkT = np.ascontiguousarray(np.asarray(L0_attn_Wk, np.float32).T)
    wvT = np.ascontiguousarray(np.asarray(L0_attn_Wv, np.float32).T)
    woT = np.ascontiguousarray(np.asarray(L0_attn_Wo, np.float32).T)
    w1 = np.ascontiguousarray(np.concatenate([np.asarray(L0_ffn_shared_w1, np.float32),
                                              np.asarray(L0_ffn_routed_w1, np.float32)], 0))
    w2 = np.ascontiguousarray(np.concatenate([np.asarray(L0_ffn_shared_w2, np.float32),
                                              np.asarray(L0_ffn_routed_w2, np.float32)], 0))
    b1all = np.concatenate([np.asarray(L0_ffn_shared_b1, np.float32),
                            np.asarray(L0_ffn_routed_b1, np.float32)], 0)
    b1c = np.ascontiguousarray(b1all.reshape(NE, FCN, 128).transpose(2, 0, 1)
                               .reshape(128, NE * FCN))
    b2r = np.ascontiguousarray(np.asarray(L0_ffn_routed_b2, np.float32))
    b2s = np.ascontiguousarray(np.asarray(L0_ffn_shared_b2, np.float32))
    cen = np.asarray(L0_ffn_centroids, np.float32)
    cenT = np.ascontiguousarray(cen.T.reshape(DC, 128, 16).transpose(1, 0, 2)
                                .reshape(128, DC * 16))
    gbias = np.ascontiguousarray(np.asarray(L0_ffn_bias, np.float32).reshape(1, 16))
    p128 = np.zeros((128, 128), np.float32)
    for i in range(64):
        p128[2 * i, 2 * i + 1] = 1.0
        p128[2 * i + 1, 2 * i] = 1.0
    iden = np.eye(128, dtype=np.float32)
    consts = {hf: _host_constants(hf) for hf in range(2)}

    in_maps = []
    for c in range(8):
        b, hf = divmod(c, 2)
        P0 = hf * 2048
        cosf, sinf, masksb = consts[hf]
        x_own = np.ascontiguousarray(x[b, P0:P0 + T])
        x_halo = np.ascontiguousarray(x[b, P0 - HALO:P0]) if hf == 1 else \
            np.zeros((HALO, D), np.float32)
        in_maps.append(dict(
            x_own=x_own, x_halo=x_halo, wqT=wqT, wkT=wkT, wvT=wvT, woT=woT,
            w1=w1, w2=w2, b1c=b1c, b2r=b2r, b2s=b2s, cenT=cenT, gbias=gbias,
            cosf=cosf, sinf=sinf, p128=p128, iden=iden, masksb=masksb))
    return in_maps


def kernel(**inputs):
    in_maps = prep_inputs(**inputs)
    nc = build_program()
    res = run_bass_kernel_spmd(nc, in_maps, core_ids=list(range(8)))
    out = np.empty((4, 4096, D), np.float32)
    for c in range(8):
        b, hf = divmod(c, 2)
        out[b, hf * 2048:(hf + 1) * 2048] = res.results[c]["out"]
    return out


# version 05
# speedup vs baseline: 1.0630x; 1.0630x over previous
import numpy as np
import ml_dtypes
from contextlib import ExitStack

import concourse.bass as bass
import concourse.mybir as mybir
from concourse.bass_utils import run_bass_kernel_spmd

F32 = mybir.dt.float32
BF16 = mybir.dt.bfloat16
AF = mybir.ActivationFunctionType
ALU = mybir.AluOpType
AX = mybir.AxisListType

D = 2048
DC = 16
T = 2048
HALO = 512
TK = 2560
NT = 16
NTA = 20
H = 16
DH = 128
FF = 1408
FCN = 11
NE = 18
SCALE = float(128 ** -0.5)
EPS = 1e-5

_DEBUG_TAPS = set()
_MAX_PHASE = 99
_SKIP_QK = False
_SKIP_V = False
_QK_NCK = 99


class Ledger:
    def __init__(self, nc, ctx):
        self.nc, self.ctx = nc, ctx
        self.h = {}
        self.c = {}

    def sem(self, name):
        if name not in self.h:
            self.h[name] = self.ctx.enter_context(self.nc.semaphore(name))
            self.c[name] = 0
        return self.h[name]

    def inc(self, name, instr, k):
        instr.then_inc(self.sem(name), k)
        self.c[name] += k

    def n(self, name):
        return self.c.get(name, 0)


def build_program():
    nc = bass.Bass("TRN2", target_bir_lowering=False)

    x_own = nc.dram_tensor("x_own", [T, D], F32, kind="ExternalInput").ap()
    x_halo = nc.dram_tensor("x_halo", [HALO, D], F32, kind="ExternalInput").ap()
    wqT = nc.dram_tensor("wqT", [D, D], F32, kind="ExternalInput").ap()
    wkT = nc.dram_tensor("wkT", [D, D], F32, kind="ExternalInput").ap()
    wvT = nc.dram_tensor("wvT", [D, D], F32, kind="ExternalInput").ap()
    woT = nc.dram_tensor("woT", [D, D], F32, kind="ExternalInput").ap()
    w1h = nc.dram_tensor("w1", [NE, D, FF], F32, kind="ExternalInput").ap()
    w2h = nc.dram_tensor("w2", [NE, FF, D], F32, kind="ExternalInput").ap()
    b1ch = nc.dram_tensor("b1c", [128, NE * FCN], F32, kind="ExternalInput").ap()
    b2rh = nc.dram_tensor("b2r", [16, D], F32, kind="ExternalInput").ap()
    b2sh = nc.dram_tensor("b2s", [2, D], F32, kind="ExternalInput").ap()
    cenh = nc.dram_tensor("cenT", [128, DC * 16], F32, kind="ExternalInput").ap()
    gbh = nc.dram_tensor("gbias", [1, 16], F32, kind="ExternalInput").ap()
    cosh = nc.dram_tensor("cosf", [128, TK], F32, kind="ExternalInput").ap()
    sinh = nc.dram_tensor("sinf", [128, TK], F32, kind="ExternalInput").ap()
    p128h = nc.dram_tensor("p128", [128, 128], F32, kind="ExternalInput").ap()
    idh = nc.dram_tensor("iden", [128, 128], F32, kind="ExternalInput").ap()
    maskh = nc.dram_tensor("masksb", [128, 25 * 128], F32, kind="ExternalInput").ap()

    outh = nc.dram_tensor("out", [T, D], F32, kind="ExternalOutput").ap()

    def skind(nm):
        return "ExternalOutput" if nm in _DEBUG_TAPS else "Internal"

    qt_d = nc.dram_tensor("qt_d", [NT * 128, H * 128], BF16, kind=skind("qt_d")).ap()
    kt_d = nc.dram_tensor("kt_d", [NTA * 128, H * 128], BF16, kind=skind("kt_d")).ap()
    v_d = nc.dram_tensor("v_d", [TK, D], BF16, kind=skind("v_d")).ap()
    s_d = nc.dram_tensor("s_d", [T, D], BF16, kind=skind("s_d")).ap()
    acc_d = nc.dram_tensor("acc_d", [T, D], F32, kind=skind("acc_d")).ap()
    xnt_dbg = g_dbg = None
    if "xnt_d" in _DEBUG_TAPS:
        xnt_dbg = nc.dram_tensor("xnt_d", [128, DC * T], BF16, kind="ExternalOutput").ap()
    if "g_d" in _DEBUG_TAPS:
        g_dbg = nc.dram_tensor("g_d", [128, NT * 16], F32, kind="ExternalOutput").ap()
    st_dbg = None
    if "st_d" in _DEBUG_TAPS:
        st_dbg = nc.dram_tensor("st_d", [128, NTA * 2], F32, kind="ExternalOutput").ap()

    gctx = ExitStack()
    L = Ledger(nc, gctx)
    PH = L.sem("ph")
    pno = [0]

    idenb = gctx.enter_context(nc.sbuf_tensor("idenb", [128, 128], BF16))
    p128b = gctx.enter_context(nc.sbuf_tensor("p128b", [128, 128], BF16))
    stats = gctx.enter_context(nc.sbuf_tensor("stats", [128, NTA, 2], F32))
    b1c = gctx.enter_context(nc.sbuf_tensor("b1c_s", [128, NE * FCN], F32))
    gbias = gctx.enter_context(nc.sbuf_tensor("gbias_s", [1, 16], F32))
    bplane = gctx.enter_context(nc.sbuf_tensor("bplane", [128, 16], F32))
    ones2 = gctx.enter_context(nc.sbuf_tensor("ones2", [2, 128], BF16))
    Gm = gctx.enter_context(nc.sbuf_tensor("Gm", [128, NT, 16], F32))
    GTb = gctx.enter_context(nc.sbuf_tensor("GTb", [16, T], BF16))
    b2rb = gctx.enter_context(nc.sbuf_tensor("b2rb", [16, D], BF16))
    b2sb = gctx.enter_context(nc.sbuf_tensor("b2sb", [2, D], BF16))
    epsb = gctx.enter_context(nc.sbuf_tensor("epsb", [128, 1], F32))

    sA = ExitStack()
    xnT = sA.enter_context(nc.sbuf_tensor("xnT", [128, DC, T], BF16))
    sB = ExitStack()
    xnTh = sB.enter_context(nc.sbuf_tensor("xnTh", [128, DC, HALO], BF16))
    cosf = sB.enter_context(nc.sbuf_tensor("cosf_s", [128, TK], F32))
    sinf = sB.enter_context(nc.sbuf_tensor("sinf_s", [128, TK], F32))
    iden32 = sB.enter_context(nc.sbuf_tensor("iden32", [128, 128], F32))
    cenT = sB.enter_context(nc.sbuf_tensor("cenT_s", [128, DC, 16], F32))
    ones1 = sB.enter_context(nc.sbuf_tensor("ones1", [1, 128], F32))

    block = gctx.enter_context(nc.Block())

    def barrier(names):
        pno[0] += 1

        @block.sync
        def _(sy):
            for s in names:
                sy.wait_ge(L.sem(s), L.n(s))
            sy.sem_inc(PH, 1)
        L.c["ph"] += 1

    def prol(eng):
        eng.wait_ge(PH, pno[0])

    with ExitStack() as p0:
        b2r32 = p0.enter_context(nc.sbuf_tensor("b2r32", [16, D], F32))
        b2s32 = p0.enter_context(nc.sbuf_tensor("b2s32", [2, D], F32))
        p128s = p0.enter_context(nc.sbuf_tensor("p128s", [128, 128], F32))
        psbp = p0.enter_context(nc.psum_tensor("psbp", [128, 16], F32))

        @block.sync
        def _(sy):
            for dst, src in [(iden32, idh), (b1c, b1ch),
                             (gbias, gbh), (b2r32, b2rh), (b2s32, b2sh),
                             (p128s, p128h)]:
                i = sy.dma_start(out=dst[:], in_=src)
                L.inc("c_dma", i, 16)
            i = sy.dma_start(out=cenT[:], in_=cenh.rearrange("p (dc j) -> p dc j", dc=DC))
            L.inc("c_dma", i, 16)

        @block.gpsimd
        def _(g):
            g.wait_ge(L.sem("c_dma"), L.n("c_dma"))
            g.memset(ones1[:], 1.0)
            g.memset(ones2[:], 1.0)
            g.memset(epsb[:], EPS)
            g.tensor_copy(out=idenb[:], in_=iden32[:])
            g.tensor_copy(out=b2rb[:], in_=b2r32[:])
            g.tensor_copy(out=b2sb[:], in_=b2s32[:])
            i = g.tensor_copy(out=p128b[:], in_=p128s[:])
            L.inc("c_cast", i, 1)

        @block.tensor
        def _(t):
            t.wait_ge(L.sem("c_cast"), 1)
            i = t.matmul(psbp[:], ones1[:], gbias[:], start=True, stop=True)
            L.inc("c_bp", i, 1)

        @block.scalar
        def _(s):
            s.wait_ge(L.sem("c_bp"), 1)
            i = s.copy(out=bplane[:], in_=psbp[:])
            L.inc("c_bpe", i, 1)

        barrier(["c_dma", "c_cast", "c_bpe"])

    def xnT_sl(tt, dc, lo=0, hi=128):
        if tt < 4:
            return xnTh[:, dc, tt * 128 + lo:tt * 128 + hi]
        return xnT[:, dc, (tt - 4) * 128 + lo:(tt - 4) * 128 + hi]

    def xnT_grp(dc, g):
        if g == 0:
            return xnTh[:, dc, :]
        return xnT[:, dc, (g - 1) * 512:g * 512]

    with ExitStack() as p1:
        xb = [p1.enter_context(nc.sbuf_tensor(f"xb{i}", [128, D], F32)) for i in range(2)]
        xn32 = [p1.enter_context(nc.sbuf_tensor(f"xn32_{i}", [128, D], F32)) for i in range(2)]
        sqd = p1.enter_context(nc.sbuf_tensor("sqd", [128, D], F32))
        s1b = p1.enter_context(nc.sbuf_tensor("s1b", [128, 1], F32))
        s2b = p1.enter_context(nc.sbuf_tensor("s2b", [128, 1], F32))
        ex2b = p1.enter_context(nc.sbuf_tensor("ex2b", [128, 1], F32))
        nvb = p1.enter_context(nc.sbuf_tensor("nvb", [128, 1], F32))
        sdb = p1.enter_context(nc.sbuf_tensor("sdb", [128, 1], F32))
        xnT32 = p1.enter_context(nc.sbuf_tensor("xnT32", [128, DC, 128], F32))
        sgb = [p1.enter_context(nc.sbuf_tensor(f"sgb{i}", [128, 16], F32)) for i in range(2)]
        scb = p1.enter_context(nc.sbuf_tensor("scb", [128, 16], F32))
        scur = [p1.enter_context(nc.sbuf_tensor(f"scur{i}", [128, 16], F32)) for i in range(2)]
        geb = p1.enter_context(nc.sbuf_tensor("geb", [128, 16], F32))
        gmb = p1.enter_context(nc.sbuf_tensor("gmb", [128, 16], F32))
        mrb = [p1.enter_context(nc.sbuf_tensor(f"mrb{i}", [128, 1], F32)) for i in range(4)]
        msb = p1.enter_context(nc.sbuf_tensor("msb", [128, 1], F32))
        recb = p1.enter_context(nc.sbuf_tensor("recb", [128, 1], F32))
        pst = [p1.enter_context(nc.psum_tensor(f"pst{i}", [128, 128], F32)) for i in range(2)]
        pssc = [p1.enter_context(nc.psum_tensor(f"pssc{i}", [128, 16], F32)) for i in range(2)]
        psgt = p1.enter_context(nc.psum_tensor("psgt", [16, 128], F32))

        @block.sync
        def _(sy):
            prol(sy)
            for tt in range(NTA):
                if tt >= 2:
                    sy.wait_ge(L.sem("p1_xn"), tt - 1)
                src = x_halo[tt * 128:(tt + 1) * 128, :] if tt < 4 else \
                    x_own[(tt - 4) * 128:(tt - 3) * 128, :]
                i = sy.dma_start(out=xb[tt % 2][:], in_=src)
                L.inc(f"p1_xd{tt % 2}", i, 16)

        @block.scalar
        def _(s):
            prol(s)
            for tt in range(NTA):
                ot = tt - 4
                s.wait_ge(L.sem(f"p1_xd{tt % 2}"), 16 * (tt // 2 + 1))
                if tt >= 1:
                    s.wait_ge(L.sem("p1_nv"), tt)
                i = s.activation(out=sqd[:], in_=xb[tt % 2][:], func=AF.Square,
                                 accum_out=s2b[:])
                L.inc("p1_sq", i, 1)
                s.wait_ge(L.sem("p1_nv"), tt + 1)
                if tt >= 1:
                    s.wait_ge(L.sem("p1_rc"), tt)
                i = s.activation(out=sdb[:], in_=nvb[:], func=AF.Sqrt,
                                 bias=epsb[:, 0:1], scale=-1.0)
                L.inc("p1_sd", i, 1)
                for dc in range(16):
                    gk = tt * 16 + dc
                    s.wait_ge(L.sem("p1_tr"), gk + 1)
                    if ot >= 0:
                        s.copy(out=xnT_sl(tt, dc), in_=pst[gk % 2][:])
                        i = s.copy(out=xnT32[:, dc, :], in_=pst[gk % 2][:])
                    else:
                        i = s.copy(out=xnT_sl(tt, dc), in_=pst[gk % 2][:])
                    L.inc("p1_ev", i, 1)
                if ot >= 0:
                    s.wait_ge(L.sem("p1_scm"), ot + 1)
                    if ot >= 2:
                        s.wait_ge(L.sem("p1_g"), ot - 1)
                    i = s.activation(out=sgb[ot % 2][:], in_=pssc[ot % 2][:],
                                     func=AF.Sigmoid)
                    L.inc("p1_sg", i, 1)
                if ot >= 1:
                    s.wait_ge(L.sem("p1_gt"), ot)
                    i = s.copy(out=GTb[:, (ot - 1) * 128:ot * 128], in_=psgt[:])
                    L.inc("p1_gte", i, 1)
            s.wait_ge(L.sem("p1_gt"), NT)
            i = s.copy(out=GTb[:, (NT - 1) * 128:NT * 128], in_=psgt[:])
            L.inc("p1_gte", i, 1)

        @block.vector
        def _(v):
            prol(v)
            for tt in range(NTA):
                ot = tt - 4
                v.wait_ge(L.sem(f"p1_xd{tt % 2}"), 16 * (tt // 2 + 1))
                v.reduce_sum(out=s1b[:], in_=xb[tt % 2][:], axis=AX.X)
                v.drain()
                v.tensor_single_scalar(out=stats[:, tt, 0:1], in_=s1b[:],
                                       scalar=1.0 / D, op=ALU.mult)
                v.wait_ge(L.sem("p1_sq"), tt + 1)
                v.tensor_single_scalar(out=ex2b[:], in_=s2b[:], scalar=1.0 / D,
                                       op=ALU.mult)
                v.drain()
                i = v.scalar_tensor_tensor(out=nvb[:], in0=stats[:, tt, 0:1],
                                           scalar=stats[:, tt, 0:1], in1=ex2b[:],
                                           op0=ALU.mult, op1=ALU.subtract)
                L.inc("p1_nv", i, 1)
                v.wait_ge(L.sem("p1_sd"), tt + 1)
                i = v.reciprocal(out=stats[:, tt, 1:2], in_=sdb[:])
                L.inc("p1_rc", i, 1)
                v.drain()
                if tt >= 2:
                    v.wait_ge(L.sem("p1_tr"), 16 * (tt - 1))
                i = v.tensor_scalar(out=xn32[tt % 2][:], in0=xb[tt % 2][:],
                                    scalar1=stats[:, tt, 0:1], scalar2=stats[:, tt, 1:2],
                                    op0=ALU.subtract, op1=ALU.mult)
                L.inc("p1_xn", i, 1)
                if ot >= 0:
                    v.wait_ge(L.sem("p1_sg"), ot + 1)
                    v.tensor_tensor(out=scb[:], in0=sgb[ot % 2][:], in1=bplane[:], op=ALU.add)
                    v.drain()
                    v.tensor_copy(out=scur[0][:], in_=scb[:])
                    v.drain()
                    v.reduce_max(out=mrb[0][:], in_=scur[0][:], axis=AX.X)
                    v.drain()
                    for r in range(1, 4):
                        v.tensor_single_scalar(out=geb[:], in_=scur[(r - 1) % 2][:],
                                               scalar=mrb[r - 1][:, 0:1], op=ALU.is_ge)
                        v.drain()
                        v.scalar_tensor_tensor(out=scur[r % 2][:], in0=geb[:],
                                               scalar=-2e9, in1=scur[(r - 1) % 2][:],
                                               op0=ALU.mult, op1=ALU.add)
                        v.drain()
                        v.reduce_max(out=mrb[r][:], in_=scur[r % 2][:], axis=AX.X)
                        v.drain()
                    v.tensor_tensor(out=msb[:], in0=mrb[0][:], in1=mrb[1][:], op=ALU.add)
                    v.drain()
                    v.tensor_tensor(out=msb[:], in0=msb[:], in1=mrb[2][:], op=ALU.add)
                    v.drain()
                    v.tensor_tensor(out=msb[:], in0=msb[:], in1=mrb[3][:], op=ALU.add)
                    v.drain()
                    v.tensor_single_scalar(out=msb[:], in_=msb[:], scalar=1e-8, op=ALU.max)
                    v.drain()
                    v.reciprocal(out=recb[:], in_=msb[:])
                    v.tensor_single_scalar(out=geb[:], in_=scb[:], scalar=mrb[3][:, 0:1],
                                           op=ALU.is_ge)
                    v.drain()
                    v.tensor_tensor(out=gmb[:], in0=scb[:], in1=geb[:], op=ALU.mult)
                    v.drain()
                    i = v.tensor_single_scalar(out=Gm[:, ot, :], in_=gmb[:],
                                               scalar=recb[:, 0:1], op=ALU.mult)
                    L.inc("p1_g", i, 1)

        def emit_scores(t, ot):
            t.wait_ge(L.sem("p1_ev"), 16 * (ot + 5))
            if ot >= 2:
                t.wait_ge(L.sem("p1_sg"), ot - 1)
            for dc in range(16):
                i = t.matmul(pssc[ot % 2][:], xnT32[:, dc, :], cenT[:, dc, :],
                             start=(dc == 0), stop=(dc == 15))
            L.inc("p1_scm", i, 1)

        def emit_gt(t, j):
            t.wait_ge(L.sem("p1_g"), j + 1)
            if j >= 1:
                t.wait_ge(L.sem("p1_gte"), j)
            i = t.transpose(out=psgt[:], in_=Gm[:, j, :], identity=iden32[:])
            L.inc("p1_gt", i, 1)

        @block.tensor
        def _(t):
            prol(t)
            for tt in range(NTA):
                ot = tt - 4
                t.wait_ge(L.sem("p1_xn"), tt + 1)
                for dc in range(16):
                    gk = tt * 16 + dc
                    if gk >= 2:
                        t.wait_ge(L.sem("p1_ev"), gk - 1)
                    i = t.transpose(out=pst[gk % 2][:],
                                    in_=xn32[tt % 2][:, dc * 128:(dc + 1) * 128],
                                    identity=iden32[:])
                    L.inc("p1_tr", i, 1)
                if ot >= 0:
                    emit_scores(t, ot)
                if ot >= 1:
                    emit_gt(t, ot - 1)
            emit_gt(t, NT - 1)

        names = ["p1_ev", "p1_g", "p1_gte", "p1_xd0", "p1_xd1", "p1_xn",
                 "p1_scm", "p1_gt"]
        if "xnt_d" in _DEBUG_TAPS or "g_d" in _DEBUG_TAPS:
            @block.gpsimd
            def _(g):
                prol(g)
                g.wait_ge(L.sem("p1_ev"), L.n("p1_ev"))
                g.wait_ge(L.sem("p1_g"), L.n("p1_g"))
                if xnt_dbg is not None:
                    i = g.dma_start(out=xnt_dbg,
                                    in_=xnT[:].rearrange("p dc t -> p (dc t)"))
                    L.inc("p1_dbg", i, 16)
                if g_dbg is not None:
                    i = g.dma_start(out=g_dbg, in_=Gm[:].rearrange("p t j -> p (t j)"))
                    L.inc("p1_dbg", i, 16)
                if st_dbg is not None:
                    g.wait_ge(L.sem("p1_xn"), L.n("p1_xn"))
                    i = g.dma_start(out=st_dbg, in_=stats[:].rearrange("p t k -> p (t k)"))
                    L.inc("p1_dbg", i, 16)
                g.wait_ge(L.sem("p1_dbg"), L.n("p1_dbg"))
            names.append("p1_dbg")
        barrier(names)

    if _MAX_PHASE >= 2:
      with ExitStack() as p2:
        wst = [p2.enter_context(nc.sbuf_tensor(f"wst{i}", [128, DC, 128], F32)) for i in range(2)]
        wbf = [p2.enter_context(nc.sbuf_tensor(f"wbf{i}", [128, DC, 128], BF16)) for i in range(2)]
        qa32 = [p2.enter_context(nc.sbuf_tensor(f"qa32_{i}", [128, 512], F32)) for i in range(2)]
        qab = [p2.enter_context(nc.sbuf_tensor(f"qab{i}", [128, 512], BF16)) for i in range(2)]
        t1b = [p2.enter_context(nc.sbuf_tensor(f"t1b{i}", [128, 512], F32)) for i in range(2)]
        t2b = p2.enter_context(nc.sbuf_tensor("t2b", [128, 512], F32))
        qstg = [p2.enter_context(nc.sbuf_tensor(f"qstg{i}", [128, 512], BF16)) for i in range(2)]
        psq = [p2.enter_context(nc.psum_tensor(f"psq{i}", [128, 512], F32)) for i in range(2)]
        pshf = [p2.enter_context(nc.psum_tensor(f"pshf{i}", [128, 512], F32)) for i in range(2)]

        chunks = []
        groups = []
        for h in range(H):
            chunks.append((wqT, h * 128))
            for g in range(4):
                groups.append((h, g, 0))
        for h in range(H):
            chunks.append((wkT, h * 128))
            for g in range(5):
                groups.append((16 + h, g, 1))
        if _QK_NCK < len(chunks):
            chunks = chunks[:_QK_NCK]
            groups = [g for g in groups if g[0] < _QK_NCK]
        NCK = len(chunks)
        lastg = {}
        firstg = {}
        for gi, (ck, g, part) in enumerate(groups):
            lastg[ck] = gi
            firstg.setdefault(ck, gi)

        def grp_meta(gi):
            ck, g, part = groups[gi]
            h = ck if part == 0 else ck - 16
            if part == 0:
                off = HALO + g * 512
                dst = qt_d
                scale = SCALE
                rhs_g = g + 1
            else:
                off = g * 512
                dst = kt_d
                scale = 1.0
                rhs_g = g
            return ck, h, g, off, dst, scale, rhs_g

        @block.sync
        def _(sy):
            if _SKIP_QK:
                return
            prol(sy)
            i = sy.dma_start(out=cosf[:], in_=cosh)
            L.inc("p2_tb", i, 16)
            i = sy.dma_start(out=sinf[:], in_=sinh)
            L.inc("p2_tb", i, 16)
            for ck in range(NCK):
                if ck >= 2:
                    sy.wait_ge(L.sem("p2_wc"), ck - 1)
                wh, col = chunks[ck]
                src = bass.AP(wh.tensor, col,
                              [[D, 128], [128 * D, DC], [1, 128]])
                i = sy.dma_start(out=wst[ck % 2][:], in_=src)
                L.inc(f"p2_wd{ck % 2}", i, 16)

        gbyck = {}
        for gi, (ck, g, part) in enumerate(groups):
            gbyck.setdefault(ck, []).append(gi)

        def emit_qw(g_, gi):
            ck, h, g, off, dst, scale, rhs_g = grp_meta(gi)
            g_.wait_ge(L.sem("p2_rt"), gi + 1)
            t0 = off if dst is kt_d else g * 512
            dsta = bass.AP(dst.tensor, (t0 // 128) * 128 * (H * 128) + h * 128,
                           [[H * 128, 128], [128 * H * 128, 4], [1, 128]])
            i = g_.dma_start(out=dsta,
                             in_=qstg[gi % 2][:].rearrange("p (a t) -> p a t", a=4))
            L.inc(f"p2_qw{gi % 2}", i, 16)

        @block.gpsimd
        def _(g_):
            if _SKIP_QK:
                return
            prol(g_)
            for ck in range(NCK):
                g_.wait_ge(L.sem(f"p2_wd{ck % 2}"), 16 * (ck // 2 + 1))
                if ck >= 2:
                    g_.wait_ge(L.sem("p2_mm"), lastg[ck - 2] + 1)
                i = g_.tensor_copy(out=wbf[ck % 2][:], in_=wst[ck % 2][:])
                L.inc("p2_wc", i, 1)
                if ck >= 1:
                    for gi in gbyck[ck - 1]:
                        emit_qw(g_, gi)
            for gi in gbyck[NCK - 1]:
                emit_qw(g_, gi)

        @block.tensor
        def _(t):
            if _SKIP_QK:
                return
            prol(t)
            for gi in range(len(groups)):
                ck, h, g, off, dst, scale, rhs_g = grp_meta(gi)
                if gi == firstg[ck]:
                    t.wait_ge(L.sem("p2_wc"), ck + 1)
                if gi >= 2:
                    t.wait_ge(L.sem("p2_ea"), gi - 1)
                    t.wait_ge(L.sem("p2_ev"), gi - 1)
                for dc in range(DC):
                    i = t.matmul(psq[gi % 2][:], wbf[ck % 2][:, dc, :],
                                 xnT_grp(dc, rhs_g), start=(dc == 0), stop=(dc == 15))
                L.inc("p2_mm", i, 1)
                if gi >= 1:
                    j = gi - 1
                    t.wait_ge(L.sem("p2_ev"), j + 1)
                    if j >= 2:
                        t.wait_ge(L.sem("p2_t2"), j - 1)
                    i = t.matmul(pshf[j % 2][:], p128b[:], qab[j % 2][:],
                                 start=True, stop=True)
                    L.inc("p2_sh", i, 1)
            j = len(groups) - 1
            t.wait_ge(L.sem("p2_ev"), j + 1)
            t.wait_ge(L.sem("p2_t2"), j - 1)
            i = t.matmul(pshf[j % 2][:], p128b[:], qab[j % 2][:], start=True, stop=True)
            L.inc("p2_sh", i, 1)

        @block.scalar
        def _(s):
            if _SKIP_QK:
                return
            prol(s)
            for gi in range(len(groups)):
                ck, h, g, off, dst, scale, rhs_g = grp_meta(gi)
                s.wait_ge(L.sem("p2_mm"), gi + 1)
                if gi >= 2:
                    s.wait_ge(L.sem("p2_rt"), gi - 1)
                i = s.activation(out=qa32[gi % 2][:], in_=psq[gi % 2][:],
                                 func=AF.Copy, scale=scale)
                L.inc("p2_ea", i, 1)
                if gi >= 2:
                    s.wait_ge(L.sem("p2_sh"), gi - 1)
                i = s.activation(out=qab[gi % 2][:], in_=psq[gi % 2][:],
                                 func=AF.Copy, scale=scale)
                L.inc("p2_ev", i, 1)

        def rope_finish(v, j):
            _, _, _, off, _, _, _ = grp_meta(j)
            v.wait_ge(L.sem("p2_sh"), j + 1)
            i = v.tensor_tensor(out=t2b[:], in0=pshf[j % 2][:],
                                in1=sinf[:, off:off + 512], op=ALU.mult)
            L.inc("p2_t2", i, 1)
            v.wait_ge(L.sem("p2_ea"), j + 1)
            v.tensor_tensor(out=t1b[0][:], in0=qa32[j % 2][:],
                            in1=cosf[:, off:off + 512], op=ALU.mult)
            v.drain()
            if j >= 2:
                v.wait_ge(L.sem(f"p2_qw{j % 2}"), 16 * ((j - 2) // 2 + 1))
            i = v.tensor_tensor(out=qstg[j % 2][:], in0=t1b[0][:],
                                in1=t2b[:], op=ALU.add)
            L.inc("p2_rt", i, 1)
            v.drain()

        @block.vector
        def _(v):
            if _SKIP_QK:
                return
            prol(v)
            v.wait_ge(L.sem("p2_tb"), 32)
            for gi in range(len(groups)):
                if gi >= 1:
                    rope_finish(v, gi - 1)
            rope_finish(v, len(groups) - 1)

        barrier(["p2_tb", "p2_wd0", "p2_wd1", "p2_wc", "p2_mm", "p2_ea",
                 "p2_ev", "p2_sh", "p2_t2", "p2_rt", "p2_qw0", "p2_qw1"])

        vst = [p2.enter_context(nc.sbuf_tensor(f"vst{i}", [128, 512], F32)) for i in range(2)]
        wvbf = p2.enter_context(nc.sbuf_tensor("wvbf", [128, DC, 512], BF16))
        vstg = [p2.enter_context(nc.sbuf_tensor(f"vstg{i}", [128, 512], BF16)) for i in range(2)]
        psv = [p2.enter_context(nc.psum_tensor(f"psv{i}", [128, 512], F32)) for i in range(2)]

        @block.sync
        def _(sy):
            if _SKIP_V:
                return
            prol(sy)
            for vck in range(4 * DC):
                do, dcc = divmod(vck, DC)
                if vck >= 2:
                    sy.wait_ge(L.sem("p2v_wc"), vck - 1)
                src = bass.AP(wvT.tensor, dcc * 128 * D + do * 512,
                              [[D, 128], [1, 512]])
                i = sy.dma_start(out=vst[vck % 2][:], in_=src)
                L.inc(f"p2v_wd{vck % 2}", i, 16)

        @block.gpsimd
        def _(g_):
            if _SKIP_V:
                return
            prol(g_)

            def emit_vw(gv):
                do_, tt_ = divmod(gv, NTA)
                g_.wait_ge(L.sem("p2v_ev"), gv + 1)
                dsta = bass.AP(v_d.tensor, tt_ * 128 * D + do_ * 512,
                               [[D, 128], [1, 512]])
                i = g_.dma_start(out=dsta, in_=vstg[gv % 2][:])
                L.inc(f"p2v_vw{gv % 2}", i, 16)

            for do in range(4):
                if do >= 1:
                    for gv in range((do - 1) * NTA, do * NTA):
                        emit_vw(gv)
                for dcc in range(DC):
                    vck = do * DC + dcc
                    g_.wait_ge(L.sem(f"p2v_wd{vck % 2}"), 16 * (vck // 2 + 1))
                    if do >= 1:
                        g_.wait_ge(L.sem("p2v_mm"), NTA * do)
                    i = g_.tensor_copy(out=wvbf[:, dcc, :], in_=vst[vck % 2][:])
                    L.inc("p2v_wc", i, 1)
            for gv in range(3 * NTA, 4 * NTA):
                emit_vw(gv)

        @block.tensor
        def _(t):
            if _SKIP_V:
                return
            prol(t)
            for gv in range(4 * NTA):
                do, tt = divmod(gv, NTA)
                if gv >= 2:
                    t.wait_ge(L.sem("p2v_ev"), gv - 1)
                for dc in range(DC):
                    if tt == 0:
                        t.wait_ge(L.sem("p2v_wc"), do * DC + dc + 1)
                    i = t.matmul(psv[gv % 2][:], xnT_sl(tt, dc), wvbf[:, dc, :],
                                 start=(dc == 0), stop=(dc == 15))
                L.inc("p2v_mm", i, 1)

        @block.scalar
        def _(s):
            if _SKIP_V:
                return
            prol(s)
            for gv in range(4 * NTA):
                s.wait_ge(L.sem("p2v_mm"), gv + 1)
                if gv >= 2:
                    s.wait_ge(L.sem(f"p2v_vw{gv % 2}"), 16 * ((gv - 2) // 2 + 1))
                i = s.copy(out=vstg[gv % 2][:], in_=psv[gv % 2][:])
                L.inc("p2v_ev", i, 1)

        barrier(["p2v_wd0", "p2v_wd1", "p2v_wc", "p2v_mm", "p2v_ev",
                 "p2v_vw0", "p2v_vw1"])

    sB.close()

    if _MAX_PHASE >= 3:
      with ExitStack() as p5:
        w1st = [p5.enter_context(nc.sbuf_tensor(f"w1st{i}", [128, 8, 128], F32)) for i in range(2)]
        w1bf = [p5.enter_context(nc.sbuf_tensor(f"w1bf{i}", [128, DC, 128], BF16)) for i in range(2)]
        w2st = [p5.enter_context(nc.sbuf_tensor(f"w2st{i}", [128, 512], F32)) for i in range(2)]
        w2bd = [p5.enter_context(nc.sbuf_tensor(f"w2bd{i}", [128, FCN, 512], BF16)) for i in range(2)]
        hT = p5.enter_context(nc.sbuf_tensor("hT", [128, FCN, 1024], BF16))
        acc = p5.enter_context(nc.sbuf_tensor("acc", [128, 8, D], F32))
        psh = [p5.enter_context(nc.psum_tensor(f"psh{i}", [128, 512], F32)) for i in range(4)]
        psy = [p5.enter_context(nc.psum_tensor(f"psy{i}", [128, 512], F32)) for i in range(4)]

        NJ = NE
        n_hgrp = 2 * FCN
        n_ygrp = 4 * 8

        @block.sync
        def _(sy):
            prol(sy)
            for half in range(2):
                for j in range(NJ):
                    jj = half * NJ + j
                    for fc in range(FCN):
                        ck = jj * FCN + fc
                        for hp in range(2):
                            ck2 = ck * 2 + hp
                            if ck2 >= 2:
                                sy.wait_ge(L.sem("p5_w1h"), ck2 - 1)
                            src = bass.AP(w1h.tensor,
                                          j * D * FF + hp * 8 * 128 * FF + fc * 128,
                                          [[FF, 128], [128 * FF, 8], [1, 128]])
                            i = sy.dma_start(out=w1st[ck2 % 2][:], in_=src)
                            L.inc(f"p5_w1d{ck2 % 2}", i, 16)
                    for do in range(4):
                        gd = jj * 4 + do
                        for fc in range(FCN):
                            cw = gd * FCN + fc
                            if cw >= 2:
                                sy.wait_ge(L.sem("p5_w2c"), cw - 1)
                            src = bass.AP(w2h.tensor, j * FF * D + fc * 128 * D + do * 512,
                                          [[D, 128], [1, 512]])
                            i = sy.dma_start(out=w2st[cw % 2][:], in_=src)
                            L.inc(f"p5_w2d{cw % 2}", i, 16)

        @block.vector
        def _(v):
            prol(v)
            for half in range(2):
                for j in range(NJ):
                    jj = half * NJ + j
                    for do in range(4):
                        gd = jj * 4 + do
                        for tile in range(8):
                            gy = gd * 8 + tile
                            v.wait_ge(L.sem("p5_ymm"), gy + 1)
                            if half == 1 and j == 0 and do == 0:
                                v.wait_ge(L.sem("p5_acw"), 128)
                            dsl = acc[:, tile, do * 512:(do + 1) * 512]
                            if j == 0:
                                i = v.tensor_single_scalar(out=dsl, in_=psy[gy % 4][:],
                                                           scalar=1.0, op=ALU.mult)
                            elif j == 1:
                                i = v.scalar_tensor_tensor(out=dsl, in0=psy[gy % 4][:],
                                                           scalar=1.0, in1=dsl,
                                                           op0=ALU.mult, op1=ALU.add)
                            else:
                                gate = Gm[:, half * 8 + tile, j - 2:j - 1]
                                i = v.scalar_tensor_tensor(out=dsl, in0=psy[gy % 4][:],
                                                           scalar=gate, in1=dsl,
                                                           op0=ALU.mult, op1=ALU.add)
                            L.inc("p5_yev", i, 1)

        @block.scalar
        def _(s):
            prol(s)
            for half in range(2):
                for j in range(NJ):
                    jj = half * NJ + j
                    for fc in range(FCN):
                        ck = jj * FCN + fc
                        for g2 in range(2):
                            gh = ck * 2 + g2
                            s.wait_ge(L.sem("p5_hmm"), gh + 1)
                            i = s.activation(out=hT[:, fc, g2 * 512:(g2 + 1) * 512],
                                             in_=psh[gh % 4][:], func=AF.Relu,
                                             bias=b1c[:, j * FCN + fc:j * FCN + fc + 1])
                            L.inc("p5_hev", i, 1)


        @block.tensor
        def _(t):
            prol(t)
            for half in range(2):
                for j in range(NJ):
                    jj = half * NJ + j
                    for fc in range(FCN):
                        ck = jj * FCN + fc
                        t.wait_ge(L.sem("p5_w1h"), 2 * (ck + 1))
                        for g2 in range(2):
                            gh = ck * 2 + g2
                            if gh >= 4:
                                t.wait_ge(L.sem("p5_hev"), gh - 3)
                            for dc in range(DC):
                                i = t.matmul(psh[gh % 4][:], w1bf[ck % 2][:, dc, :],
                                             xnT[:, dc, half * 1024 + g2 * 512:
                                                 half * 1024 + (g2 + 1) * 512],
                                             start=(dc == 0), stop=(dc == 15))
                            L.inc("p5_hmm", i, 1)
                    for do in range(4):
                        gd = jj * 4 + do
                        for tile in range(8):
                            gy = gd * 8 + tile
                            if gy >= 4:
                                t.wait_ge(L.sem("p5_yev"), gy - 3)
                            nmm = FCN + (2 if j == 0 else 0)
                            k = 0
                            for fc in range(FCN):
                                if tile == 0:
                                    t.wait_ge(L.sem("p5_w2c"), gd * FCN + fc + 1)
                                if do == 0 and tile == 0:
                                    t.wait_ge(L.sem("p5_hev"), 2 * (jj * FCN + fc) + 2)
                                i = t.matmul(psy[gy % 4][:],
                                             hT[:, fc, tile * 128:(tile + 1) * 128],
                                             w2bd[gd % 2][:, fc, :],
                                             start=(fc == 0), stop=(fc == FCN - 1 and j != 0))
                                k += 1
                            if j == 0:
                                tok0 = (half * 8 + tile) * 128
                                t.matmul(psy[gy % 4][:], GTb[:, tok0:tok0 + 128],
                                         b2rb[:, do * 512:(do + 1) * 512],
                                         start=False, stop=False)
                                i = t.matmul(psy[gy % 4][:], ones2[:],
                                             b2sb[:, do * 512:(do + 1) * 512],
                                             start=False, stop=True)
                            L.inc("p5_ymm", i, 1)

        @block.gpsimd
        def _(g_):
            prol(g_)

            def cast_w1(jj2):
                for fc in range(FCN):
                    ck = jj2 * FCN + fc
                    if ck >= 2:
                        g_.wait_ge(L.sem("p5_hmm"), 2 * (ck - 1))
                    for hp in range(2):
                        ck2 = ck * 2 + hp
                        g_.wait_ge(L.sem(f"p5_w1d{ck2 % 2}"), 16 * (ck2 // 2 + 1))
                        i = g_.tensor_copy(out=w1bf[ck % 2][:, hp * 8:(hp + 1) * 8, :],
                                           in_=w1st[ck2 % 2][:])
                        L.inc("p5_w1h", i, 1)

            def cast_w2(jj2):
                for do in range(4):
                    gd = jj2 * 4 + do
                    for fc in range(FCN):
                        cw = gd * FCN + fc
                        g_.wait_ge(L.sem(f"p5_w2d{cw % 2}"), 16 * (cw // 2 + 1))
                        if gd >= 2:
                            g_.wait_ge(L.sem("p5_ymm"), 8 * (gd - 1))
                        i = g_.tensor_copy(out=w2bd[gd % 2][:, fc, :],
                                           in_=w2st[cw % 2][:])
                        L.inc("p5_w2c", i, 1)

            def acc_out(half):
                g_.wait_ge(L.sem("p5_yev"), (half + 1) * NJ * 32)
                for tile in range(8):
                    i = g_.dma_start(out=acc_d[(half * 8 + tile) * 128:
                                               (half * 8 + tile + 1) * 128, :],
                                     in_=acc[:, tile, :])
                    L.inc("p5_acw", i, 16)

            cast_w1(0)
            for jj2 in range(2 * NJ):
                cast_w2(jj2)
                if jj2 == NJ - 1:
                    acc_out(0)
                if jj2 + 1 < 2 * NJ:
                    cast_w1(jj2 + 1)
            acc_out(1)

        barrier(["p5_w1d0", "p5_w1d1", "p5_w1h", "p5_w2d0",
                 "p5_w2d1", "p5_w2c", "p5_hmm", "p5_hev", "p5_ymm", "p5_yev",
                 "p5_acw"])

    sA.close()

    if _MAX_PHASE >= 4:
      aoS = ExitStack()
      aoT = aoS.enter_context(nc.sbuf_tensor("aoT", [128, H, T], BF16))
      with ExitStack() as p4:
        masks = p4.enter_context(nc.sbuf_tensor("masks", [128, 25, 128], BF16))
        masks32 = p4.enter_context(nc.sbuf_tensor("masks32", [128, 25 * 128], F32))
        kts = [p4.enter_context(nc.sbuf_tensor(f"kts{i}", [128, H * 128], BF16)) for i in range(6)]
        vs = [p4.enter_context(nc.sbuf_tensor(f"vs{i}", [128, D], BF16)) for i in range(6)]
        qts = [p4.enter_context(nc.sbuf_tensor(f"qts{i}", [128, H * 128], BF16)) for i in range(3)]
        pa = [p4.enter_context(nc.sbuf_tensor(f"pa{i}", [128, 640], BF16)) for i in range(2)]
        panb = [p4.enter_context(nc.sbuf_tensor(f"panb{i}", [128, 640], BF16)) for i in range(2)]
        ptb = [p4.enter_context(nc.sbuf_tensor(f"ptb{i}", [128, 640], BF16)) for i in range(2)]
        negmb = [p4.enter_context(nc.sbuf_tensor(f"negmb{i}", [128, 1], F32)) for i in range(2)]
        sumb = p4.enter_context(nc.sbuf_tensor("sumb", [128, 1], F32))
        rcb = p4.enter_context(nc.sbuf_tensor("rcb", [128, 1], F32))
        pss = [p4.enter_context(nc.psum_tensor(f"pss{i}", [128, 640], F32)) for i in range(2)]
        pspt = [p4.enter_context(nc.psum_tensor(f"pspt{i}", [128, 640], BF16)) for i in range(2)]
        psao = [p4.enter_context(nc.psum_tensor(f"psao{i}", [128, 128], F32)) for i in range(2)]

        NIDX = NT * H

        @block.gpsimd
        def _(g_):
            prol(g_)
            g_.wait_ge(L.sem("p4_mk"), 16)
            i = g_.tensor_copy(out=masks[:].rearrange("p a t -> p (a t)"), in_=masks32[:])
            L.inc("p4_mkc", i, 1)
            for kt in range(NTA):
                if kt >= 6:
                    g_.wait_ge(L.sem("p4_avm"), 16 * (kt - 5))
                sl = kt % 6
                i = g_.dma_start(out=kts[sl][:], in_=kt_d[kt * 128:(kt + 1) * 128, :])
                L.inc(f"p4_kv{sl}", i, 16)
                i = g_.dma_start(out=vs[sl][:], in_=v_d[kt * 128:(kt + 1) * 128, :])
                L.inc(f"p4_kv{sl}", i, 16)

        @block.sync
        def _(sy):
            prol(sy)
            i = sy.dma_start(out=masks32[:], in_=maskh)
            L.inc("p4_mk", i, 16)
            for qt in range(NT):
                if qt >= 3:
                    sy.wait_ge(L.sem("p4_smm"), 16 * (qt - 2))
                i = sy.dma_start(out=qts[qt % 3][:], in_=qt_d[qt * 128:(qt + 1) * 128, :])
                L.inc(f"p4_qt{qt % 3}", i, 16)

        def pe_T(t, j):
            t.wait_ge(L.sem("p4_pan"), j + 1)
            if j >= 2:
                t.wait_ge(L.sem("p4_pte"), j - 1)
            for d in range(5):
                i = t.transpose(out=pspt[j % 2][:, d * 128:(d + 1) * 128],
                                in_=panb[j % 2][:, d * 128:(d + 1) * 128],
                                identity=idenb[:])
            L.inc("p4_tmm", i, 1)

        def pe_AV(t, j):
            qt, h = divmod(j, H)
            t.wait_ge(L.sem("p4_pte"), j + 1)
            if j >= 2:
                t.wait_ge(L.sem("p4_aoe"), j - 1)
            for d in range(5):
                sl = (qt + d) % 6
                i = t.matmul(psao[j % 2][:], vs[sl][:, h * 128:(h + 1) * 128],
                             ptb[j % 2][:, d * 128:(d + 1) * 128],
                             start=(d == 0), stop=(d == 4))
            L.inc("p4_avm", i, 1)

        @block.tensor
        def _(t):
            prol(t)
            for idx in range(NIDX):
                qt, h = divmod(idx, H)
                if idx == 0:
                    t.wait_ge(L.sem("p4_mkc"), 1)
                if h == 0:
                    for kt in range(qt, qt + 5):
                        t.wait_ge(L.sem(f"p4_kv{kt % 6}"), 32 * (kt // 6 + 1))
                    t.wait_ge(L.sem(f"p4_qt{qt % 3}"), 16 * (qt // 3 + 1))
                if idx >= 2:
                    t.wait_ge(L.sem("p4_mx"), idx - 1)
                    t.wait_ge(L.sem("p4_ex"), idx - 1)
                sl = idx % 2
                for d in range(5):
                    ks = (qt + d) % 6
                    t.matmul(pss[sl][:, d * 128:(d + 1) * 128],
                             qts[qt % 3][:, h * 128:(h + 1) * 128],
                             kts[ks][:, h * 128:(h + 1) * 128],
                             start=(d == 0 or d == 4), stop=False)
                dlist = range(5) if qt < 4 else (0, 4)
                tc = qt if qt < 4 else 4
                for d in dlist:
                    last_b0 = (d == max(x for x in dlist if x < 4))
                    i = t.matmul(pss[sl][:, d * 128:(d + 1) * 128], idenb[:],
                                 masks[:, tc * 5 + d, :],
                                 start=False, stop=(last_b0 or d == 4))
                L.inc("p4_smm", i, 1)
                if idx >= 1:
                    pe_T(t, idx - 1)
                    pe_AV(t, idx - 1)
            pe_T(t, NIDX - 1)
            pe_AV(t, NIDX - 1)

        @block.vector
        def _(v):
            prol(v)
            for idx in range(NIDX):
                v.wait_ge(L.sem("p4_smm"), idx + 1)
                i = v.reduce_max(out=negmb[idx % 2][:], in_=pss[idx % 2][:],
                                 axis=AX.X, negate=True)
                L.inc("p4_mx", i, 1)
                if idx >= 1:
                    j = idx - 1
                    v.wait_ge(L.sem("p4_ex"), j + 1)
                    v.reduce_sum(out=sumb[:], in_=pa[j % 2][:], axis=AX.X)
                    v.drain()
                    v.reciprocal(out=rcb[:], in_=sumb[:])
                    v.drain()
                    if j >= 2:
                        v.wait_ge(L.sem("p4_tmm"), j - 1)
                    i = v.tensor_single_scalar(out=panb[j % 2][:], in_=pa[j % 2][:],
                                               scalar=rcb[:, 0:1], op=ALU.mult)
                    L.inc("p4_pan", i, 1)
            j = NIDX - 1
            v.wait_ge(L.sem("p4_ex"), j + 1)
            v.reduce_sum(out=sumb[:], in_=pa[j % 2][:], axis=AX.X)
            v.drain()
            v.reciprocal(out=rcb[:], in_=sumb[:])
            v.drain()
            v.wait_ge(L.sem("p4_tmm"), j - 1)
            i = v.tensor_single_scalar(out=panb[j % 2][:], in_=pa[j % 2][:],
                                       scalar=rcb[:, 0:1], op=ALU.mult)
            L.inc("p4_pan", i, 1)

        def act_tail(s, j):
            qt, h = divmod(j, H)
            s.wait_ge(L.sem("p4_tmm"), j + 1)
            if j >= 2:
                s.wait_ge(L.sem("p4_avm"), j - 1)
            i = s.copy(out=ptb[j % 2][:], in_=pspt[j % 2][:])
            L.inc("p4_pte", i, 1)
            s.wait_ge(L.sem("p4_avm"), j + 1)
            i = s.copy(out=aoT[:, h, qt * 128:(qt + 1) * 128], in_=psao[j % 2][:])
            L.inc("p4_aoe", i, 1)

        @block.scalar
        def _(s):
            prol(s)
            for idx in range(NIDX):
                s.wait_ge(L.sem("p4_mx"), idx + 1)
                if idx >= 2:
                    s.wait_ge(L.sem("p4_pan"), idx - 1)
                i = s.activation(out=pa[idx % 2][:], in_=pss[idx % 2][:], func=AF.Exp,
                                 bias=negmb[idx % 2][:, 0:1])
                L.inc("p4_ex", i, 1)
                if idx >= 1:
                    act_tail(s, idx - 1)
            act_tail(s, NIDX - 1)

        barrier([f"p4_kv{s}" for s in range(6)] + [f"p4_qt{s}" for s in range(3)]
                + ["p4_smm", "p4_mx", "p4_ex", "p4_pan",
                   "p4_tmm", "p4_pte", "p4_avm", "p4_aoe"])

      with ExitStack() as p4w:
        wost = [p4w.enter_context(nc.sbuf_tensor(f"wost{i}", [128, 512], F32)) for i in range(2)]
        wodb = p4w.enter_context(nc.sbuf_tensor("wodb", [128, H, 512], BF16))
        sstg = [p4w.enter_context(nc.sbuf_tensor(f"sstg{i}", [128, 512], BF16)) for i in range(2)]
        pss2 = [p4w.enter_context(nc.psum_tensor(f"pss2_{i}", [128, 512], F32)) for i in range(2)]

        @block.sync
        def _(sy):
            prol(sy)
            for cwo in range(4 * H):
                if cwo >= 2:
                    sy.wait_ge(L.sem("p4w_wc"), cwo - 1)
                do, h2 = divmod(cwo, H)
                src = bass.AP(woT.tensor, h2 * 128 * D + do * 512, [[D, 128], [1, 512]])
                i = sy.dma_start(out=wost[cwo % 2][:], in_=src)
                L.inc(f"p4w_wd{cwo % 2}", i, 16)

        @block.gpsimd
        def _(g_):
            prol(g_)

            def emit_sw(gw):
                do_, tile_ = divmod(gw, NT)
                g_.wait_ge(L.sem("p4w_ev"), gw + 1)
                dsta = bass.AP(s_d.tensor, tile_ * 128 * D + do_ * 512,
                               [[D, 128], [1, 512]])
                i = g_.dma_start(out=dsta, in_=sstg[gw % 2][:])
                L.inc(f"p4w_sw{gw % 2}", i, 16)

            for do in range(4):
                if do >= 1:
                    for gw in range((do - 1) * NT, do * NT):
                        emit_sw(gw)
                for h2 in range(H):
                    cwo = do * H + h2
                    g_.wait_ge(L.sem(f"p4w_wd{cwo % 2}"), 16 * (cwo // 2 + 1))
                    if do >= 1:
                        g_.wait_ge(L.sem("p4w_mm"), NT * do)
                    i = g_.tensor_copy(out=wodb[:, h2, :], in_=wost[cwo % 2][:])
                    L.inc("p4w_wc", i, 1)
            for gw in range(3 * NT, 4 * NT):
                emit_sw(gw)

        @block.tensor
        def _(t):
            prol(t)
            for gw in range(4 * NT):
                do, tile = divmod(gw, NT)
                if gw >= 2:
                    t.wait_ge(L.sem("p4w_ev"), gw - 1)
                for h2 in range(H):
                    if tile == 0:
                        t.wait_ge(L.sem("p4w_wc"), do * H + h2 + 1)
                    i = t.matmul(pss2[gw % 2][:], aoT[:, h2, tile * 128:(tile + 1) * 128],
                                 wodb[:, h2, :], start=(h2 == 0), stop=(h2 == H - 1))
                L.inc("p4w_mm", i, 1)

        @block.scalar
        def _(s):
            prol(s)
            for gw in range(4 * NT):
                s.wait_ge(L.sem("p4w_mm"), gw + 1)
                if gw >= 2:
                    s.wait_ge(L.sem(f"p4w_sw{gw % 2}"), 16 * ((gw - 2) // 2 + 1))
                i = s.copy(out=sstg[gw % 2][:], in_=pss2[gw % 2][:])
                L.inc("p4w_ev", i, 1)

        barrier(["p4w_wd0", "p4w_wd1", "p4w_wc", "p4w_mm", "p4w_ev",
                 "p4w_sw0", "p4w_sw1"])
      aoS.close()

    if _MAX_PHASE >= 5:
      with ExitStack() as p6:
        xb6 = [p6.enter_context(nc.sbuf_tensor(f"xb6_{i}", [128, D], F32)) for i in range(2)]
        ab6 = [p6.enter_context(nc.sbuf_tensor(f"ab6_{i}", [128, D], F32)) for i in range(2)]
        sb6 = [p6.enter_context(nc.sbuf_tensor(f"sb6_{i}", [128, D], BF16)) for i in range(2)]
        xn6 = p6.enter_context(nc.sbuf_tensor("xn6", [128, D], F32))
        ffb = p6.enter_context(nc.sbuf_tensor("ffb", [128, D], F32))
        g6b = p6.enter_context(nc.sbuf_tensor("g6b", [128, D], F32))
        sgb6 = p6.enter_context(nc.sbuf_tensor("sgb6", [128, D], F32))
        outb = [p6.enter_context(nc.sbuf_tensor(f"outb{i}", [128, D], F32)) for i in range(2)]

        @block.sync
        def _(sy):
            prol(sy)
            for tt in range(NT):
                if tt >= 2:
                    sy.wait_ge(L.sem("p6_ot"), tt - 1)
                i = sy.dma_start(out=xb6[tt % 2][:], in_=x_own[tt * 128:(tt + 1) * 128, :])
                L.inc(f"p6_xd{tt % 2}", i, 16)
                i = sy.dma_start(out=ab6[tt % 2][:], in_=acc_d[tt * 128:(tt + 1) * 128, :])
                L.inc(f"p6_ad{tt % 2}", i, 16)

        @block.gpsimd
        def _(g_):
            prol(g_)

            def emit_ow(tt):
                g_.wait_ge(L.sem("p6_ot"), tt + 1)
                i = g_.dma_start(out=outh[tt * 128:(tt + 1) * 128, :], in_=outb[tt % 2][:])
                L.inc(f"p6_ow{tt % 2}", i, 16)

            for tt in range(NT):
                if tt >= 2:
                    g_.wait_ge(L.sem("p6_ot"), tt - 1)
                i = g_.dma_start(out=sb6[tt % 2][:], in_=s_d[tt * 128:(tt + 1) * 128, :])
                L.inc(f"p6_sd{tt % 2}", i, 16)
                if tt >= 2:
                    emit_ow(tt - 2)
            emit_ow(NT - 2)
            emit_ow(NT - 1)
            g_.wait_ge(L.sem("p6_ow0"), 16 * (NT // 2))
            g_.wait_ge(L.sem("p6_ow1"), 16 * (NT // 2))

        @block.vector
        def _(v):
            prol(v)
            for tt in range(NT):
                st = tt + 4
                v.wait_ge(L.sem(f"p6_xd{tt % 2}"), 16 * (tt // 2 + 1))
                v.wait_ge(L.sem(f"p6_ad{tt % 2}"), 16 * (tt // 2 + 1))
                v.tensor_scalar(out=xn6[:], in0=xb6[tt % 2][:],
                                scalar1=stats[:, st, 0:1], scalar2=stats[:, st, 1:2],
                                op0=ALU.subtract, op1=ALU.mult)
                v.drain()
                if tt >= 1:
                    v.wait_ge(L.sem("p6_sl"), tt)
                i = v.tensor_tensor(out=ffb[:], in0=xn6[:], in1=ab6[tt % 2][:], op=ALU.add)
                L.inc("p6_ff", i, 1)
                v.wait_ge(L.sem("p6_sl"), tt + 1)
                v.wait_ge(L.sem(f"p6_sd{tt % 2}"), 16 * (tt // 2 + 1))
                v.tensor_tensor(out=sgb6[:], in0=sb6[tt % 2][:], in1=g6b[:], op=ALU.mult)
                v.drain()
                if tt >= 2:
                    v.wait_ge(L.sem(f"p6_ow{tt % 2}"), 16 * ((tt - 2) // 2 + 1))
                i = v.tensor_tensor(out=outb[tt % 2][:], in0=xb6[tt % 2][:],
                                    in1=sgb6[:], op=ALU.add)
                L.inc("p6_ot", i, 1)

        @block.scalar
        def _(s):
            prol(s)
            for tt in range(NT):
                s.wait_ge(L.sem("p6_ff"), tt + 1)
                i = s.activation(out=g6b[:], in_=ffb[:], func=AF.Silu)
                L.inc("p6_sl", i, 1)

        @block.sync
        def _(sy):
            sy.wait_ge(L.sem("p6_ow0"), 16 * (NT // 2))
            sy.wait_ge(L.sem("p6_ow1"), 16 * (NT // 2))

    gctx.close()
    return nc


def _host_constants(hf):
    P0 = hf * 2048
    pos = np.maximum(P0 - HALO + np.arange(TK), 0).astype(np.float64)
    inv = 10000.0 ** (-(np.arange(0, DH, 2, dtype=np.float64)) / DH)
    ang = pos[None, :] * inv[:, None]
    c = np.cos(ang)
    sn = np.sin(ang)
    cosf = np.empty((128, TK), np.float32)
    sinf = np.empty((128, TK), np.float32)
    cosf[0::2] = c
    cosf[1::2] = c
    sinf[0::2] = -sn
    sinf[1::2] = sn

    ii = np.arange(128)[:, None]
    jj = np.arange(128)[None, :]
    masks = np.zeros((128, 25, 128), np.float32)
    for tc in range(5):
        for d in range(5):
            m = np.zeros((128, 128), np.float32)
            if d == 0:
                m[~(jj > ii) * np.ones((128, 128), bool)] = 0.0
                m = np.where(jj > ii, 0.0, -1e9).astype(np.float32)
            elif d == 4:
                m = np.where(jj <= ii, 0.0, -1e9).astype(np.float32)
            if hf == 0 and tc < 4 and tc + d <= 3:
                m[:] = -1e9
            masks[:, tc * 5 + d, :] = m
    masksb = masks.reshape(128, 25 * 128)
    return cosf, sinf, masksb


def prep_inputs(L0_attn_x, L0_attn_Wq, L0_attn_Wk, L0_attn_Wv, L0_attn_Wo,
                L0_ffn_shared_w1, L0_ffn_shared_b1, L0_ffn_shared_w2, L0_ffn_shared_b2,
                L0_ffn_routed_w1, L0_ffn_routed_b1, L0_ffn_routed_w2, L0_ffn_routed_b2,
                L0_ffn_centroids, L0_ffn_bias):
    x = np.ascontiguousarray(np.asarray(L0_attn_x, dtype=np.float32))
    wqT = np.ascontiguousarray(np.asarray(L0_attn_Wq, np.float32).T)
    wkT = np.ascontiguousarray(np.asarray(L0_attn_Wk, np.float32).T)
    wvT = np.ascontiguousarray(np.asarray(L0_attn_Wv, np.float32).T)
    woT = np.ascontiguousarray(np.asarray(L0_attn_Wo, np.float32).T)
    w1 = np.ascontiguousarray(np.concatenate([np.asarray(L0_ffn_shared_w1, np.float32),
                                              np.asarray(L0_ffn_routed_w1, np.float32)], 0))
    w2 = np.ascontiguousarray(np.concatenate([np.asarray(L0_ffn_shared_w2, np.float32),
                                              np.asarray(L0_ffn_routed_w2, np.float32)], 0))
    b1all = np.concatenate([np.asarray(L0_ffn_shared_b1, np.float32),
                            np.asarray(L0_ffn_routed_b1, np.float32)], 0)
    b1c = np.ascontiguousarray(b1all.reshape(NE, FCN, 128).transpose(2, 0, 1)
                               .reshape(128, NE * FCN))
    b2r = np.ascontiguousarray(np.asarray(L0_ffn_routed_b2, np.float32))
    b2s = np.ascontiguousarray(np.asarray(L0_ffn_shared_b2, np.float32))
    cen = np.asarray(L0_ffn_centroids, np.float32)
    cenT = np.ascontiguousarray(cen.T.reshape(DC, 128, 16).transpose(1, 0, 2)
                                .reshape(128, DC * 16))
    gbias = np.ascontiguousarray(np.asarray(L0_ffn_bias, np.float32).reshape(1, 16))
    p128 = np.zeros((128, 128), np.float32)
    for i in range(64):
        p128[2 * i, 2 * i + 1] = 1.0
        p128[2 * i + 1, 2 * i] = 1.0
    iden = np.eye(128, dtype=np.float32)
    consts = {hf: _host_constants(hf) for hf in range(2)}

    in_maps = []
    for c in range(8):
        b, hf = divmod(c, 2)
        P0 = hf * 2048
        cosf, sinf, masksb = consts[hf]
        x_own = np.ascontiguousarray(x[b, P0:P0 + T])
        x_halo = np.ascontiguousarray(x[b, P0 - HALO:P0]) if hf == 1 else \
            np.zeros((HALO, D), np.float32)
        in_maps.append(dict(
            x_own=x_own, x_halo=x_halo, wqT=wqT, wkT=wkT, wvT=wvT, woT=woT,
            w1=w1, w2=w2, b1c=b1c, b2r=b2r, b2s=b2s, cenT=cenT, gbias=gbias,
            cosf=cosf, sinf=sinf, p128=p128, iden=iden, masksb=masksb))
    return in_maps


def kernel(**inputs):
    in_maps = prep_inputs(**inputs)
    nc = build_program()
    res = run_bass_kernel_spmd(nc, in_maps, core_ids=list(range(8)))
    out = np.empty((4, 4096, D), np.float32)
    for c in range(8):
        b, hf = divmod(c, 2)
        out[b, hf * 2048:(hf + 1) * 2048] = res.results[c]["out"]
    return out


# version 07
# speedup vs baseline: 1.0861x; 1.0217x over previous
import numpy as np
import ml_dtypes
from contextlib import ExitStack

import concourse.bass as bass
import concourse.mybir as mybir
from concourse.bass_utils import run_bass_kernel_spmd

F32 = mybir.dt.float32
BF16 = mybir.dt.bfloat16
AF = mybir.ActivationFunctionType
ALU = mybir.AluOpType
AX = mybir.AxisListType

D = 2048
DC = 16
T = 2048
HALO = 512
TK = 2560
NT = 16
NTA = 20
H = 16
DH = 128
FF = 1408
FCN = 11
NE = 18
SCALE = float(128 ** -0.5)
EPS = 1e-5

_DEBUG_TAPS = set()
_MAX_PHASE = 99
_SKIP_QK = False
_SKIP_V = False
_QK_NCK = 99


class Ledger:
    def __init__(self, nc, ctx):
        self.nc, self.ctx = nc, ctx
        self.h = {}
        self.c = {}

    def sem(self, name):
        if name not in self.h:
            self.h[name] = self.ctx.enter_context(self.nc.semaphore(name))
            self.c[name] = 0
        return self.h[name]

    def inc(self, name, instr, k):
        instr.then_inc(self.sem(name), k)
        self.c[name] += k

    def n(self, name):
        return self.c.get(name, 0)


def build_program():
    nc = bass.Bass("TRN2", target_bir_lowering=False)

    x_own = nc.dram_tensor("x_own", [T, D], F32, kind="ExternalInput").ap()
    x_halo = nc.dram_tensor("x_halo", [HALO, D], F32, kind="ExternalInput").ap()
    wqT = nc.dram_tensor("wqT", [D, D], F32, kind="ExternalInput").ap()
    wkT = nc.dram_tensor("wkT", [D, D], F32, kind="ExternalInput").ap()
    wvT = nc.dram_tensor("wvT", [D, D], F32, kind="ExternalInput").ap()
    woT = nc.dram_tensor("woT", [D, D], F32, kind="ExternalInput").ap()
    w1h = nc.dram_tensor("w1", [NE, D, FF], F32, kind="ExternalInput").ap()
    w2h = nc.dram_tensor("w2", [NE, FF, D], F32, kind="ExternalInput").ap()
    b1ch = nc.dram_tensor("b1c", [128, NE * FCN], F32, kind="ExternalInput").ap()
    b2rh = nc.dram_tensor("b2r", [16, D], F32, kind="ExternalInput").ap()
    b2sh = nc.dram_tensor("b2s", [2, D], F32, kind="ExternalInput").ap()
    cenh = nc.dram_tensor("cenT", [128, DC * 16], F32, kind="ExternalInput").ap()
    gbh = nc.dram_tensor("gbias", [1, 16], F32, kind="ExternalInput").ap()
    cosh = nc.dram_tensor("cosf", [128, TK], F32, kind="ExternalInput").ap()
    sinh = nc.dram_tensor("sinf", [128, TK], F32, kind="ExternalInput").ap()
    p128h = nc.dram_tensor("p128", [128, 128], F32, kind="ExternalInput").ap()
    idh = nc.dram_tensor("iden", [128, 128], F32, kind="ExternalInput").ap()
    maskh = nc.dram_tensor("masksb", [128, 25 * 128], F32, kind="ExternalInput").ap()

    outh = nc.dram_tensor("out", [T, D], F32, kind="ExternalOutput").ap()

    def skind(nm):
        return "ExternalOutput" if nm in _DEBUG_TAPS else "Internal"

    qt_d = nc.dram_tensor("qt_d", [NT * 128, H * 128], BF16, kind=skind("qt_d")).ap()
    kt_d = nc.dram_tensor("kt_d", [NTA * 128, H * 128], BF16, kind=skind("kt_d")).ap()
    v_d = nc.dram_tensor("v_d", [TK, D], BF16, kind=skind("v_d")).ap()
    s_d = nc.dram_tensor("s_d", [T, D], BF16, kind=skind("s_d")).ap()
    acc_d = nc.dram_tensor("acc_d", [T, D], F32, kind=skind("acc_d")).ap()
    xnt_dbg = g_dbg = None
    if "xnt_d" in _DEBUG_TAPS:
        xnt_dbg = nc.dram_tensor("xnt_d", [128, DC * T], BF16, kind="ExternalOutput").ap()
    if "g_d" in _DEBUG_TAPS:
        g_dbg = nc.dram_tensor("g_d", [128, NT * 16], F32, kind="ExternalOutput").ap()
    st_dbg = None
    if "st_d" in _DEBUG_TAPS:
        st_dbg = nc.dram_tensor("st_d", [128, NTA * 2], F32, kind="ExternalOutput").ap()

    gctx = ExitStack()
    L = Ledger(nc, gctx)
    PH = L.sem("ph")
    pno = [0]

    idenb = gctx.enter_context(nc.sbuf_tensor("idenb", [128, 128], BF16))
    p128b = gctx.enter_context(nc.sbuf_tensor("p128b", [128, 128], BF16))
    stats = gctx.enter_context(nc.sbuf_tensor("stats", [128, NTA, 2], F32))
    b1c = gctx.enter_context(nc.sbuf_tensor("b1c_s", [128, NE * FCN], F32))
    gbias = gctx.enter_context(nc.sbuf_tensor("gbias_s", [1, 16], F32))
    bplane = gctx.enter_context(nc.sbuf_tensor("bplane", [128, 16], F32))
    ones2 = gctx.enter_context(nc.sbuf_tensor("ones2", [2, 128], BF16))
    Gm = gctx.enter_context(nc.sbuf_tensor("Gm", [128, NT, 16], F32))
    GTb = gctx.enter_context(nc.sbuf_tensor("GTb", [16, T], BF16))
    b2rb = gctx.enter_context(nc.sbuf_tensor("b2rb", [16, D], BF16))
    b2sb = gctx.enter_context(nc.sbuf_tensor("b2sb", [2, D], BF16))
    epsb = gctx.enter_context(nc.sbuf_tensor("epsb", [128, 1], F32))

    sA = ExitStack()
    xnT = sA.enter_context(nc.sbuf_tensor("xnT", [128, DC, T], BF16))
    sB = ExitStack()
    xnTh = sB.enter_context(nc.sbuf_tensor("xnTh", [128, DC, HALO], BF16))
    cosf = sB.enter_context(nc.sbuf_tensor("cosf_s", [128, TK], F32))
    sinf = sB.enter_context(nc.sbuf_tensor("sinf_s", [128, TK], F32))
    iden32 = sB.enter_context(nc.sbuf_tensor("iden32", [128, 128], F32))
    cenT = sB.enter_context(nc.sbuf_tensor("cenT_s", [128, DC, 16], F32))
    ones1 = sB.enter_context(nc.sbuf_tensor("ones1", [1, 128], F32))

    block = gctx.enter_context(nc.Block())

    def barrier(names):
        pno[0] += 1

        @block.sync
        def _(sy):
            for s in names:
                sy.wait_ge(L.sem(s), L.n(s))
            sy.sem_inc(PH, 1)
        L.c["ph"] += 1

    def prol(eng):
        eng.wait_ge(PH, pno[0])

    with ExitStack() as p0:
        b2r32 = p0.enter_context(nc.sbuf_tensor("b2r32", [16, D], F32))
        b2s32 = p0.enter_context(nc.sbuf_tensor("b2s32", [2, D], F32))
        p128s = p0.enter_context(nc.sbuf_tensor("p128s", [128, 128], F32))
        psbp = p0.enter_context(nc.psum_tensor("psbp", [128, 16], F32))

        @block.sync
        def _(sy):
            for dst, src in [(iden32, idh), (b1c, b1ch),
                             (gbias, gbh), (b2r32, b2rh), (b2s32, b2sh),
                             (p128s, p128h)]:
                i = sy.dma_start(out=dst[:], in_=src)
                L.inc("c_dma", i, 16)
            i = sy.dma_start(out=cenT[:], in_=cenh.rearrange("p (dc j) -> p dc j", dc=DC))
            L.inc("c_dma", i, 16)

        @block.gpsimd
        def _(g):
            g.wait_ge(L.sem("c_dma"), L.n("c_dma"))
            g.memset(ones1[:], 1.0)
            g.memset(ones2[:], 1.0)
            g.memset(epsb[:], EPS)
            g.tensor_copy(out=idenb[:], in_=iden32[:])
            g.tensor_copy(out=b2rb[:], in_=b2r32[:])
            g.tensor_copy(out=b2sb[:], in_=b2s32[:])
            i = g.tensor_copy(out=p128b[:], in_=p128s[:])
            L.inc("c_cast", i, 1)

        @block.tensor
        def _(t):
            t.wait_ge(L.sem("c_cast"), 1)
            i = t.matmul(psbp[:], ones1[:], gbias[:], start=True, stop=True)
            L.inc("c_bp", i, 1)

        @block.scalar
        def _(s):
            s.wait_ge(L.sem("c_bp"), 1)
            i = s.copy(out=bplane[:], in_=psbp[:])
            L.inc("c_bpe", i, 1)

        barrier(["c_dma", "c_cast", "c_bpe"])

    def xnT_sl(tt, dc, lo=0, hi=128):
        if tt < 4:
            return xnTh[:, dc, tt * 128 + lo:tt * 128 + hi]
        return xnT[:, dc, (tt - 4) * 128 + lo:(tt - 4) * 128 + hi]

    def xnT_grp(dc, g):
        if g == 0:
            return xnTh[:, dc, :]
        return xnT[:, dc, (g - 1) * 512:g * 512]

    with ExitStack() as p1:
        xb = [p1.enter_context(nc.sbuf_tensor(f"xb{i}", [128, D], F32)) for i in range(2)]
        xn32 = [p1.enter_context(nc.sbuf_tensor(f"xn32_{i}", [128, D], F32)) for i in range(2)]
        sqd = p1.enter_context(nc.sbuf_tensor("sqd", [128, D], F32))
        s1b = p1.enter_context(nc.sbuf_tensor("s1b", [128, 1], F32))
        s2b = p1.enter_context(nc.sbuf_tensor("s2b", [128, 1], F32))
        ex2b = p1.enter_context(nc.sbuf_tensor("ex2b", [128, 1], F32))
        nvb = p1.enter_context(nc.sbuf_tensor("nvb", [128, 1], F32))
        sdb = p1.enter_context(nc.sbuf_tensor("sdb", [128, 1], F32))
        xnT32 = p1.enter_context(nc.sbuf_tensor("xnT32", [128, DC, 128], F32))
        sgb = [p1.enter_context(nc.sbuf_tensor(f"sgb{i}", [128, 16], F32)) for i in range(2)]
        scb = p1.enter_context(nc.sbuf_tensor("scb", [128, 16], F32))
        scur = [p1.enter_context(nc.sbuf_tensor(f"scur{i}", [128, 16], F32)) for i in range(2)]
        geb = p1.enter_context(nc.sbuf_tensor("geb", [128, 16], F32))
        gmb = p1.enter_context(nc.sbuf_tensor("gmb", [128, 16], F32))
        mrb = [p1.enter_context(nc.sbuf_tensor(f"mrb{i}", [128, 1], F32)) for i in range(4)]
        msb = p1.enter_context(nc.sbuf_tensor("msb", [128, 1], F32))
        recb = p1.enter_context(nc.sbuf_tensor("recb", [128, 1], F32))
        pst = [p1.enter_context(nc.psum_tensor(f"pst{i}", [128, 128], F32)) for i in range(2)]
        pssc = [p1.enter_context(nc.psum_tensor(f"pssc{i}", [128, 16], F32)) for i in range(2)]
        psgt = p1.enter_context(nc.psum_tensor("psgt", [16, 128], F32))

        @block.sync
        def _(sy):
            prol(sy)
            for tt in range(NTA):
                if tt >= 2:
                    sy.wait_ge(L.sem("p1_xn"), tt - 1)
                src = x_halo[tt * 128:(tt + 1) * 128, :] if tt < 4 else \
                    x_own[(tt - 4) * 128:(tt - 3) * 128, :]
                i = sy.dma_start(out=xb[tt % 2][:], in_=src)
                L.inc(f"p1_xd{tt % 2}", i, 16)

        @block.scalar
        def _(s):
            prol(s)
            for tt in range(NTA):
                ot = tt - 4
                s.wait_ge(L.sem(f"p1_xd{tt % 2}"), 16 * (tt // 2 + 1))
                if tt >= 1:
                    s.wait_ge(L.sem("p1_nv"), tt)
                i = s.activation(out=sqd[:], in_=xb[tt % 2][:], func=AF.Square,
                                 accum_out=s2b[:])
                L.inc("p1_sq", i, 1)
                s.wait_ge(L.sem("p1_nv"), tt + 1)
                if tt >= 1:
                    s.wait_ge(L.sem("p1_rc"), tt)
                i = s.activation(out=sdb[:], in_=nvb[:], func=AF.Sqrt,
                                 bias=epsb[:, 0:1], scale=-1.0)
                L.inc("p1_sd", i, 1)
                for dc in range(16):
                    gk = tt * 16 + dc
                    s.wait_ge(L.sem("p1_tr"), gk + 1)
                    if ot >= 0:
                        s.copy(out=xnT_sl(tt, dc), in_=pst[gk % 2][:])
                        i = s.copy(out=xnT32[:, dc, :], in_=pst[gk % 2][:])
                    else:
                        i = s.copy(out=xnT_sl(tt, dc), in_=pst[gk % 2][:])
                    L.inc("p1_ev", i, 1)
                if ot >= 0:
                    s.wait_ge(L.sem("p1_scm"), ot + 1)
                    if ot >= 2:
                        s.wait_ge(L.sem("p1_g"), ot - 1)
                    i = s.activation(out=sgb[ot % 2][:], in_=pssc[ot % 2][:],
                                     func=AF.Sigmoid)
                    L.inc("p1_sg", i, 1)
                if ot >= 1:
                    s.wait_ge(L.sem("p1_gt"), ot)
                    i = s.copy(out=GTb[:, (ot - 1) * 128:ot * 128], in_=psgt[:])
                    L.inc("p1_gte", i, 1)
            s.wait_ge(L.sem("p1_gt"), NT)
            i = s.copy(out=GTb[:, (NT - 1) * 128:NT * 128], in_=psgt[:])
            L.inc("p1_gte", i, 1)

        @block.vector
        def _(v):
            prol(v)
            for tt in range(NTA):
                ot = tt - 4
                v.wait_ge(L.sem(f"p1_xd{tt % 2}"), 16 * (tt // 2 + 1))
                v.reduce_sum(out=s1b[:], in_=xb[tt % 2][:], axis=AX.X)
                v.drain()
                v.tensor_single_scalar(out=stats[:, tt, 0:1], in_=s1b[:],
                                       scalar=1.0 / D, op=ALU.mult)
                v.wait_ge(L.sem("p1_sq"), tt + 1)
                v.tensor_single_scalar(out=ex2b[:], in_=s2b[:], scalar=1.0 / D,
                                       op=ALU.mult)
                v.drain()
                i = v.scalar_tensor_tensor(out=nvb[:], in0=stats[:, tt, 0:1],
                                           scalar=stats[:, tt, 0:1], in1=ex2b[:],
                                           op0=ALU.mult, op1=ALU.subtract)
                L.inc("p1_nv", i, 1)
                v.wait_ge(L.sem("p1_sd"), tt + 1)
                i = v.reciprocal(out=stats[:, tt, 1:2], in_=sdb[:])
                L.inc("p1_rc", i, 1)
                v.drain()
                if tt >= 2:
                    v.wait_ge(L.sem("p1_tr"), 16 * (tt - 1))
                i = v.tensor_scalar(out=xn32[tt % 2][:], in0=xb[tt % 2][:],
                                    scalar1=stats[:, tt, 0:1], scalar2=stats[:, tt, 1:2],
                                    op0=ALU.subtract, op1=ALU.mult)
                L.inc("p1_xn", i, 1)
                if ot >= 0:
                    v.wait_ge(L.sem("p1_sg"), ot + 1)
                    v.tensor_tensor(out=scb[:], in0=sgb[ot % 2][:], in1=bplane[:], op=ALU.add)
                    v.drain()
                    v.tensor_copy(out=scur[0][:], in_=scb[:])
                    v.drain()
                    v.reduce_max(out=mrb[0][:], in_=scur[0][:], axis=AX.X)
                    v.drain()
                    for r in range(1, 4):
                        v.tensor_single_scalar(out=geb[:], in_=scur[(r - 1) % 2][:],
                                               scalar=mrb[r - 1][:, 0:1], op=ALU.is_ge)
                        v.drain()
                        v.scalar_tensor_tensor(out=scur[r % 2][:], in0=geb[:],
                                               scalar=-2e9, in1=scur[(r - 1) % 2][:],
                                               op0=ALU.mult, op1=ALU.add)
                        v.drain()
                        v.reduce_max(out=mrb[r][:], in_=scur[r % 2][:], axis=AX.X)
                        v.drain()
                    v.tensor_tensor(out=msb[:], in0=mrb[0][:], in1=mrb[1][:], op=ALU.add)
                    v.drain()
                    v.tensor_tensor(out=msb[:], in0=msb[:], in1=mrb[2][:], op=ALU.add)
                    v.drain()
                    v.tensor_tensor(out=msb[:], in0=msb[:], in1=mrb[3][:], op=ALU.add)
                    v.drain()
                    v.tensor_single_scalar(out=msb[:], in_=msb[:], scalar=1e-8, op=ALU.max)
                    v.drain()
                    v.reciprocal(out=recb[:], in_=msb[:])
                    v.tensor_single_scalar(out=geb[:], in_=scb[:], scalar=mrb[3][:, 0:1],
                                           op=ALU.is_ge)
                    v.drain()
                    v.tensor_tensor(out=gmb[:], in0=scb[:], in1=geb[:], op=ALU.mult)
                    v.drain()
                    i = v.tensor_single_scalar(out=Gm[:, ot, :], in_=gmb[:],
                                               scalar=recb[:, 0:1], op=ALU.mult)
                    L.inc("p1_g", i, 1)

        def emit_scores(t, ot):
            t.wait_ge(L.sem("p1_ev"), 16 * (ot + 5))
            if ot >= 2:
                t.wait_ge(L.sem("p1_sg"), ot - 1)
            for dc in range(16):
                i = t.matmul(pssc[ot % 2][:], xnT32[:, dc, :], cenT[:, dc, :],
                             start=(dc == 0), stop=(dc == 15))
            L.inc("p1_scm", i, 1)

        def emit_gt(t, j):
            t.wait_ge(L.sem("p1_g"), j + 1)
            if j >= 1:
                t.wait_ge(L.sem("p1_gte"), j)
            i = t.transpose(out=psgt[:], in_=Gm[:, j, :], identity=iden32[:])
            L.inc("p1_gt", i, 1)

        @block.tensor
        def _(t):
            prol(t)
            for tt in range(NTA):
                ot = tt - 4
                t.wait_ge(L.sem("p1_xn"), tt + 1)
                for dc in range(16):
                    gk = tt * 16 + dc
                    if gk >= 2:
                        t.wait_ge(L.sem("p1_ev"), gk - 1)
                    i = t.transpose(out=pst[gk % 2][:],
                                    in_=xn32[tt % 2][:, dc * 128:(dc + 1) * 128],
                                    identity=iden32[:])
                    L.inc("p1_tr", i, 1)
                if ot >= 0:
                    emit_scores(t, ot)
                if ot >= 1:
                    emit_gt(t, ot - 1)
            emit_gt(t, NT - 1)

        names = ["p1_ev", "p1_g", "p1_gte", "p1_xd0", "p1_xd1", "p1_xn",
                 "p1_scm", "p1_gt"]
        if "xnt_d" in _DEBUG_TAPS or "g_d" in _DEBUG_TAPS:
            @block.gpsimd
            def _(g):
                prol(g)
                g.wait_ge(L.sem("p1_ev"), L.n("p1_ev"))
                g.wait_ge(L.sem("p1_g"), L.n("p1_g"))
                if xnt_dbg is not None:
                    i = g.dma_start(out=xnt_dbg,
                                    in_=xnT[:].rearrange("p dc t -> p (dc t)"))
                    L.inc("p1_dbg", i, 16)
                if g_dbg is not None:
                    i = g.dma_start(out=g_dbg, in_=Gm[:].rearrange("p t j -> p (t j)"))
                    L.inc("p1_dbg", i, 16)
                if st_dbg is not None:
                    g.wait_ge(L.sem("p1_xn"), L.n("p1_xn"))
                    i = g.dma_start(out=st_dbg, in_=stats[:].rearrange("p t k -> p (t k)"))
                    L.inc("p1_dbg", i, 16)
                g.wait_ge(L.sem("p1_dbg"), L.n("p1_dbg"))
            names.append("p1_dbg")
        barrier(names)

    if _MAX_PHASE >= 2:
      with ExitStack() as p2:
        wst = [p2.enter_context(nc.sbuf_tensor(f"wst{i}", [128, DC, 128], F32)) for i in range(2)]
        wbf = [p2.enter_context(nc.sbuf_tensor(f"wbf{i}", [128, DC, 128], BF16)) for i in range(2)]
        qa32 = [p2.enter_context(nc.sbuf_tensor(f"qa32_{i}", [128, 512], F32)) for i in range(2)]
        qab = [p2.enter_context(nc.sbuf_tensor(f"qab{i}", [128, 512], BF16)) for i in range(2)]
        t1b = [p2.enter_context(nc.sbuf_tensor(f"t1b{i}", [128, 512], F32)) for i in range(2)]
        t2b = p2.enter_context(nc.sbuf_tensor("t2b", [128, 512], F32))
        qstg = [p2.enter_context(nc.sbuf_tensor(f"qstg{i}", [128, 512], BF16)) for i in range(2)]
        psq = [p2.enter_context(nc.psum_tensor(f"psq{i}", [128, 512], F32)) for i in range(2)]
        pshf = [p2.enter_context(nc.psum_tensor(f"pshf{i}", [128, 512], F32)) for i in range(2)]

        chunks = []
        groups = []
        for h in range(H):
            chunks.append((wqT, h * 128))
            for g in range(4):
                groups.append((h, g, 0))
        for h in range(H):
            chunks.append((wkT, h * 128))
            for g in range(5):
                groups.append((16 + h, g, 1))
        if _QK_NCK < len(chunks):
            chunks = chunks[:_QK_NCK]
            groups = [g for g in groups if g[0] < _QK_NCK]
        NCK = len(chunks)
        lastg = {}
        firstg = {}
        for gi, (ck, g, part) in enumerate(groups):
            lastg[ck] = gi
            firstg.setdefault(ck, gi)

        def grp_meta(gi):
            ck, g, part = groups[gi]
            h = ck if part == 0 else ck - 16
            if part == 0:
                off = HALO + g * 512
                dst = qt_d
                scale = SCALE
                rhs_g = g + 1
            else:
                off = g * 512
                dst = kt_d
                scale = 1.0
                rhs_g = g
            return ck, h, g, off, dst, scale, rhs_g

        @block.sync
        def _(sy):
            if _SKIP_QK:
                return
            prol(sy)
            i = sy.dma_start(out=cosf[:], in_=cosh)
            L.inc("p2_tb", i, 16)
            i = sy.dma_start(out=sinf[:], in_=sinh)
            L.inc("p2_tb", i, 16)
            for ck in range(NCK):
                if ck >= 2:
                    sy.wait_ge(L.sem("p2_wc"), ck - 1)
                wh, col = chunks[ck]
                src = bass.AP(wh.tensor, col,
                              [[D, 128], [128 * D, DC], [1, 128]])
                i = sy.dma_start(out=wst[ck % 2][:], in_=src)
                L.inc(f"p2_wd{ck % 2}", i, 16)

        gbyck = {}
        for gi, (ck, g, part) in enumerate(groups):
            gbyck.setdefault(ck, []).append(gi)

        def emit_qw(g_, gi):
            ck, h, g, off, dst, scale, rhs_g = grp_meta(gi)
            g_.wait_ge(L.sem("p2_rt"), gi + 1)
            t0 = off if dst is kt_d else g * 512
            dsta = bass.AP(dst.tensor, (t0 // 128) * 128 * (H * 128) + h * 128,
                           [[H * 128, 128], [128 * H * 128, 4], [1, 128]])
            i = g_.dma_start(out=dsta,
                             in_=qstg[gi % 2][:].rearrange("p (a t) -> p a t", a=4))
            L.inc(f"p2_qw{gi % 2}", i, 16)

        @block.gpsimd
        def _(g_):
            if _SKIP_QK:
                return
            prol(g_)
            for ck in range(NCK):
                g_.wait_ge(L.sem(f"p2_wd{ck % 2}"), 16 * (ck // 2 + 1))
                if ck >= 2:
                    g_.wait_ge(L.sem("p2_mm"), lastg[ck - 2] + 1)
                i = g_.tensor_copy(out=wbf[ck % 2][:], in_=wst[ck % 2][:])
                L.inc("p2_wc", i, 1)
                if ck >= 1:
                    for gi in gbyck[ck - 1]:
                        emit_qw(g_, gi)
            for gi in gbyck[NCK - 1]:
                emit_qw(g_, gi)

        @block.tensor
        def _(t):
            if _SKIP_QK:
                return
            prol(t)
            for gi in range(len(groups)):
                ck, h, g, off, dst, scale, rhs_g = grp_meta(gi)
                if gi == firstg[ck]:
                    t.wait_ge(L.sem("p2_wc"), ck + 1)
                if gi >= 2:
                    t.wait_ge(L.sem("p2_ea"), gi - 1)
                    t.wait_ge(L.sem("p2_ev"), gi - 1)
                for dc in range(DC):
                    i = t.matmul(psq[gi % 2][:], wbf[ck % 2][:, dc, :],
                                 xnT_grp(dc, rhs_g), start=(dc == 0), stop=(dc == 15))
                L.inc("p2_mm", i, 1)
                if gi >= 1:
                    j = gi - 1
                    t.wait_ge(L.sem("p2_ev"), j + 1)
                    if j >= 2:
                        t.wait_ge(L.sem("p2_t2"), j - 1)
                    i = t.matmul(pshf[j % 2][:], p128b[:], qab[j % 2][:],
                                 start=True, stop=True)
                    L.inc("p2_sh", i, 1)
            j = len(groups) - 1
            t.wait_ge(L.sem("p2_ev"), j + 1)
            t.wait_ge(L.sem("p2_t2"), j - 1)
            i = t.matmul(pshf[j % 2][:], p128b[:], qab[j % 2][:], start=True, stop=True)
            L.inc("p2_sh", i, 1)

        @block.scalar
        def _(s):
            if _SKIP_QK:
                return
            prol(s)
            for gi in range(len(groups)):
                ck, h, g, off, dst, scale, rhs_g = grp_meta(gi)
                s.wait_ge(L.sem("p2_mm"), gi + 1)
                if gi >= 2:
                    s.wait_ge(L.sem("p2_rt"), gi - 1)
                i = s.activation(out=qa32[gi % 2][:], in_=psq[gi % 2][:],
                                 func=AF.Copy, scale=scale)
                L.inc("p2_ea", i, 1)
                if gi >= 2:
                    s.wait_ge(L.sem("p2_sh"), gi - 1)
                i = s.activation(out=qab[gi % 2][:], in_=psq[gi % 2][:],
                                 func=AF.Copy, scale=scale)
                L.inc("p2_ev", i, 1)

        def rope_finish(v, j):
            _, _, _, off, _, _, _ = grp_meta(j)
            v.wait_ge(L.sem("p2_sh"), j + 1)
            i = v.tensor_tensor(out=t2b[:], in0=pshf[j % 2][:],
                                in1=sinf[:, off:off + 512], op=ALU.mult)
            L.inc("p2_t2", i, 1)
            v.wait_ge(L.sem("p2_ea"), j + 1)
            v.tensor_tensor(out=t1b[0][:], in0=qa32[j % 2][:],
                            in1=cosf[:, off:off + 512], op=ALU.mult)
            v.drain()
            if j >= 2:
                v.wait_ge(L.sem(f"p2_qw{j % 2}"), 16 * ((j - 2) // 2 + 1))
            i = v.tensor_tensor(out=qstg[j % 2][:], in0=t1b[0][:],
                                in1=t2b[:], op=ALU.add)
            L.inc("p2_rt", i, 1)
            v.drain()

        @block.vector
        def _(v):
            if _SKIP_QK:
                return
            prol(v)
            v.wait_ge(L.sem("p2_tb"), 32)
            for gi in range(len(groups)):
                if gi >= 1:
                    rope_finish(v, gi - 1)
            rope_finish(v, len(groups) - 1)

        barrier(["p2_tb", "p2_wd0", "p2_wd1", "p2_wc", "p2_mm", "p2_ea",
                 "p2_ev", "p2_sh", "p2_t2", "p2_rt", "p2_qw0", "p2_qw1"])

        vst = [p2.enter_context(nc.sbuf_tensor(f"vst{i}", [128, 512], F32)) for i in range(2)]
        wvbf = p2.enter_context(nc.sbuf_tensor("wvbf", [128, DC, 512], BF16))
        vstg = [p2.enter_context(nc.sbuf_tensor(f"vstg{i}", [128, 512], BF16)) for i in range(2)]
        psv = [p2.enter_context(nc.psum_tensor(f"psv{i}", [128, 512], F32)) for i in range(2)]

        @block.sync
        def _(sy):
            if _SKIP_V:
                return
            prol(sy)
            for vck in range(4 * DC):
                do, dcc = divmod(vck, DC)
                if vck >= 2:
                    sy.wait_ge(L.sem("p2v_wc"), vck - 1)
                src = bass.AP(wvT.tensor, dcc * 128 * D + do * 512,
                              [[D, 128], [1, 512]])
                i = sy.dma_start(out=vst[vck % 2][:], in_=src)
                L.inc(f"p2v_wd{vck % 2}", i, 16)

        @block.gpsimd
        def _(g_):
            if _SKIP_V:
                return
            prol(g_)

            def emit_vw(gv):
                do_, tt_ = divmod(gv, NTA)
                g_.wait_ge(L.sem("p2v_ev"), gv + 1)
                dsta = bass.AP(v_d.tensor, tt_ * 128 * D + do_ * 512,
                               [[D, 128], [1, 512]])
                i = g_.dma_start(out=dsta, in_=vstg[gv % 2][:])
                L.inc(f"p2v_vw{gv % 2}", i, 16)

            for do in range(4):
                if do >= 1:
                    for gv in range((do - 1) * NTA, do * NTA):
                        emit_vw(gv)
                for dcc in range(DC):
                    vck = do * DC + dcc
                    g_.wait_ge(L.sem(f"p2v_wd{vck % 2}"), 16 * (vck // 2 + 1))
                    if do >= 1:
                        g_.wait_ge(L.sem("p2v_mm"), NTA * do)
                    i = g_.tensor_copy(out=wvbf[:, dcc, :], in_=vst[vck % 2][:])
                    L.inc("p2v_wc", i, 1)
            for gv in range(3 * NTA, 4 * NTA):
                emit_vw(gv)

        @block.tensor
        def _(t):
            if _SKIP_V:
                return
            prol(t)
            for gv in range(4 * NTA):
                do, tt = divmod(gv, NTA)
                if gv >= 2:
                    t.wait_ge(L.sem("p2v_ev"), gv - 1)
                for dc in range(DC):
                    if tt == 0:
                        t.wait_ge(L.sem("p2v_wc"), do * DC + dc + 1)
                    i = t.matmul(psv[gv % 2][:], xnT_sl(tt, dc), wvbf[:, dc, :],
                                 start=(dc == 0), stop=(dc == 15))
                L.inc("p2v_mm", i, 1)

        @block.scalar
        def _(s):
            if _SKIP_V:
                return
            prol(s)
            for gv in range(4 * NTA):
                s.wait_ge(L.sem("p2v_mm"), gv + 1)
                if gv >= 2:
                    s.wait_ge(L.sem(f"p2v_vw{gv % 2}"), 16 * ((gv - 2) // 2 + 1))
                i = s.copy(out=vstg[gv % 2][:], in_=psv[gv % 2][:])
                L.inc("p2v_ev", i, 1)

        barrier(["p2v_wd0", "p2v_wd1", "p2v_wc", "p2v_mm", "p2v_ev",
                 "p2v_vw0", "p2v_vw1"])

    sB.close()

    if _MAX_PHASE >= 3:
      with ExitStack() as p5:
        w1st = [p5.enter_context(nc.sbuf_tensor(f"w1st{i}", [128, 8, 128], F32)) for i in range(2)]
        w1bf = [p5.enter_context(nc.sbuf_tensor(f"w1bf{i}", [128, DC, 128], BF16)) for i in range(2)]
        w2st = [p5.enter_context(nc.sbuf_tensor(f"w2st{i}", [128, 512], F32)) for i in range(2)]
        w2bd = [p5.enter_context(nc.sbuf_tensor(f"w2bd{i}", [128, FCN, 512], BF16)) for i in range(2)]
        hT = p5.enter_context(nc.sbuf_tensor("hT", [128, FCN, 1024], BF16))
        acc = p5.enter_context(nc.sbuf_tensor("acc", [128, 8, D], F32))
        psh = [p5.enter_context(nc.psum_tensor(f"psh{i}", [128, 512], F32)) for i in range(4)]
        psy = [p5.enter_context(nc.psum_tensor(f"psy{i}", [128, 512], F32)) for i in range(4)]

        NJ = NE
        n_hgrp = 2 * FCN
        n_ygrp = 4 * 8

        @block.sync
        def _(sy):
            prol(sy)
            for half in range(2):
                for j in range(NJ):
                    jj = half * NJ + j
                    for do in range(4):
                        gd = jj * 4 + do
                        for fc in range(FCN):
                            cw = gd * FCN + fc
                            if cw >= 2:
                                sy.wait_ge(L.sem("p5_w2c"), cw - 1)
                            src = bass.AP(w2h.tensor, j * FF * D + fc * 128 * D + do * 512,
                                          [[D, 128], [1, 512]])
                            i = sy.dma_start(out=w2st[cw % 2][:], in_=src)
                            L.inc(f"p5_w2d{cw % 2}", i, 16)

        @block.vector
        def _(v):
            prol(v)
            for half in range(2):
                for j in range(NJ):
                    jj = half * NJ + j
                    for do in range(4):
                        gd = jj * 4 + do
                        for tile in range(8):
                            gy = gd * 8 + tile
                            v.wait_ge(L.sem("p5_ymm"), gy + 1)
                            if half == 1 and j == 0 and do == 0:
                                v.wait_ge(L.sem("p5_acw"), 128)
                            dsl = acc[:, tile, do * 512:(do + 1) * 512]
                            if j == 0:
                                i = v.tensor_single_scalar(out=dsl, in_=psy[gy % 4][:],
                                                           scalar=1.0, op=ALU.mult)
                            elif j == 1:
                                i = v.scalar_tensor_tensor(out=dsl, in0=psy[gy % 4][:],
                                                           scalar=1.0, in1=dsl,
                                                           op0=ALU.mult, op1=ALU.add)
                            else:
                                gate = Gm[:, half * 8 + tile, j - 2:j - 1]
                                i = v.scalar_tensor_tensor(out=dsl, in0=psy[gy % 4][:],
                                                           scalar=gate, in1=dsl,
                                                           op0=ALU.mult, op1=ALU.add)
                            L.inc("p5_yev", i, 1)

        @block.scalar
        def _(s):
            prol(s)

            def cast_w2(jj2, dos):
                for do in dos:
                    gd = jj2 * 4 + do
                    for fc in range(FCN):
                        cw = gd * FCN + fc
                        s.wait_ge(L.sem(f"p5_w2d{cw % 2}"), 16 * (cw // 2 + 1))
                        if gd >= 2:
                            s.wait_ge(L.sem("p5_ymm"), 8 * (gd - 1))
                        i = s.activation(out=w2bd[gd % 2][:, fc, :],
                                         in_=w2st[cw % 2][:], func=AF.Copy)
                        L.inc("p5_w2c", i, 1)

            for half in range(2):
                for j in range(NJ):
                    jj = half * NJ + j
                    cast_w2(jj, (0, 1))
                    for fc in range(FCN):
                        ck = jj * FCN + fc
                        for g2 in range(2):
                            gh = ck * 2 + g2
                            s.wait_ge(L.sem("p5_hmm"), gh + 1)
                            i = s.activation(out=hT[:, fc, g2 * 512:(g2 + 1) * 512],
                                             in_=psh[gh % 4][:], func=AF.Relu,
                                             bias=b1c[:, j * FCN + fc:j * FCN + fc + 1])
                            L.inc("p5_hev", i, 1)
                    cast_w2(jj, (2, 3))


        @block.tensor
        def _(t):
            prol(t)
            for half in range(2):
                for j in range(NJ):
                    jj = half * NJ + j
                    for fc in range(FCN):
                        ck = jj * FCN + fc
                        t.wait_ge(L.sem("p5_w1h"), 2 * (ck + 1))
                        for g2 in range(2):
                            gh = ck * 2 + g2
                            if gh >= 4:
                                t.wait_ge(L.sem("p5_hev"), gh - 3)
                            for dc in range(DC):
                                i = t.matmul(psh[gh % 4][:], w1bf[ck % 2][:, dc, :],
                                             xnT[:, dc, half * 1024 + g2 * 512:
                                                 half * 1024 + (g2 + 1) * 512],
                                             start=(dc == 0), stop=(dc == 15))
                            L.inc("p5_hmm", i, 1)
                    for do in range(4):
                        gd = jj * 4 + do
                        for tile in range(8):
                            gy = gd * 8 + tile
                            if gy >= 4:
                                t.wait_ge(L.sem("p5_yev"), gy - 3)
                            nmm = FCN + (2 if j == 0 else 0)
                            k = 0
                            for fc in range(FCN):
                                if tile == 0:
                                    t.wait_ge(L.sem("p5_w2c"), gd * FCN + fc + 1)
                                if do == 0 and tile == 0:
                                    t.wait_ge(L.sem("p5_hev"), 2 * (jj * FCN + fc) + 2)
                                i = t.matmul(psy[gy % 4][:],
                                             hT[:, fc, tile * 128:(tile + 1) * 128],
                                             w2bd[gd % 2][:, fc, :],
                                             start=(fc == 0), stop=(fc == FCN - 1 and j != 0))
                                k += 1
                            if j == 0:
                                tok0 = (half * 8 + tile) * 128
                                t.matmul(psy[gy % 4][:], GTb[:, tok0:tok0 + 128],
                                         b2rb[:, do * 512:(do + 1) * 512],
                                         start=False, stop=False)
                                i = t.matmul(psy[gy % 4][:], ones2[:],
                                             b2sb[:, do * 512:(do + 1) * 512],
                                             start=False, stop=True)
                            L.inc("p5_ymm", i, 1)

        @block.gpsimd
        def _(g_):
            prol(g_)

            NCK2 = 2 * NJ * FCN * 2

            def issue_w1(ck2, after=0):
                if ck2 >= NCK2:
                    return
                if after:
                    g_.wait_ge(L.sem("p5_w1h"), after)
                ck, hp = divmod(ck2, 2)
                jj2, fc = divmod(ck, FCN)
                j = jj2 % NJ
                src = bass.AP(w1h.tensor,
                              j * D * FF + hp * 8 * 128 * FF + fc * 128,
                              [[FF, 128], [128 * FF, 8], [1, 128]])
                i = g_.dma_start(out=w1st[ck2 % 2][:], in_=src)
                L.inc(f"p5_w1d{ck2 % 2}", i, 16)

            def cast_w1(jj2):
                for fc in range(FCN):
                    ck = jj2 * FCN + fc
                    if ck >= 2:
                        g_.wait_ge(L.sem("p5_hmm"), 2 * (ck - 1))
                    for hp in range(2):
                        ck2 = ck * 2 + hp
                        g_.wait_ge(L.sem(f"p5_w1d{ck2 % 2}"), 16 * (ck2 // 2 + 1))
                        i = g_.tensor_copy(out=w1bf[ck % 2][:, hp * 8:(hp + 1) * 8, :],
                                           in_=w1st[ck2 % 2][:])
                        L.inc("p5_w1h", i, 1)
                        issue_w1(ck2 + 2, after=L.n("p5_w1h"))

            def acc_out(half):
                g_.wait_ge(L.sem("p5_yev"), (half + 1) * NJ * 32)
                for tile in range(8):
                    i = g_.dma_start(out=acc_d[(half * 8 + tile) * 128:
                                               (half * 8 + tile + 1) * 128, :],
                                     in_=acc[:, tile, :])
                    L.inc("p5_acw", i, 16)

            issue_w1(0)
            issue_w1(1)
            cast_w1(0)
            for jj2 in range(2 * NJ):
                if jj2 == NJ:
                    acc_out(0)
                if jj2 + 1 < 2 * NJ:
                    cast_w1(jj2 + 1)
            acc_out(1)

        barrier(["p5_w1d0", "p5_w1d1", "p5_w1h", "p5_w2d0",
                 "p5_w2d1", "p5_w2c", "p5_hmm", "p5_hev", "p5_ymm", "p5_yev",
                 "p5_acw"])

    sA.close()

    if _MAX_PHASE >= 4:
      aoS = ExitStack()
      aoT = aoS.enter_context(nc.sbuf_tensor("aoT", [128, H, T], BF16))
      with ExitStack() as p4:
        masks = p4.enter_context(nc.sbuf_tensor("masks", [128, 25, 128], BF16))
        masks32 = p4.enter_context(nc.sbuf_tensor("masks32", [128, 25 * 128], F32))
        kts = [p4.enter_context(nc.sbuf_tensor(f"kts{i}", [128, H * 128], BF16)) for i in range(6)]
        vs = [p4.enter_context(nc.sbuf_tensor(f"vs{i}", [128, D], BF16)) for i in range(6)]
        qts = [p4.enter_context(nc.sbuf_tensor(f"qts{i}", [128, H * 128], BF16)) for i in range(3)]
        pa = [p4.enter_context(nc.sbuf_tensor(f"pa{i}", [128, 640], BF16)) for i in range(2)]
        panb = [p4.enter_context(nc.sbuf_tensor(f"panb{i}", [128, 640], BF16)) for i in range(2)]
        ptb = [p4.enter_context(nc.sbuf_tensor(f"ptb{i}", [128, 640], BF16)) for i in range(2)]
        negmb = [p4.enter_context(nc.sbuf_tensor(f"negmb{i}", [128, 1], F32)) for i in range(2)]
        sumb = p4.enter_context(nc.sbuf_tensor("sumb", [128, 1], F32))
        rcb = p4.enter_context(nc.sbuf_tensor("rcb", [128, 1], F32))
        pss = [p4.enter_context(nc.psum_tensor(f"pss{i}", [128, 640], F32)) for i in range(2)]
        pspt = [p4.enter_context(nc.psum_tensor(f"pspt{i}", [128, 640], BF16)) for i in range(2)]
        psao = [p4.enter_context(nc.psum_tensor(f"psao{i}", [128, 128], F32)) for i in range(2)]

        NIDX = NT * H

        @block.gpsimd
        def _(g_):
            prol(g_)
            g_.wait_ge(L.sem("p4_mk"), 16)
            i = g_.tensor_copy(out=masks[:].rearrange("p a t -> p (a t)"), in_=masks32[:])
            L.inc("p4_mkc", i, 1)
            for kt in range(NTA):
                if kt >= 6:
                    g_.wait_ge(L.sem("p4_avm"), 16 * (kt - 5))
                sl = kt % 6
                i = g_.dma_start(out=kts[sl][:], in_=kt_d[kt * 128:(kt + 1) * 128, :])
                L.inc(f"p4_kv{sl}", i, 16)
                i = g_.dma_start(out=vs[sl][:], in_=v_d[kt * 128:(kt + 1) * 128, :])
                L.inc(f"p4_kv{sl}", i, 16)

        @block.sync
        def _(sy):
            prol(sy)
            i = sy.dma_start(out=masks32[:], in_=maskh)
            L.inc("p4_mk", i, 16)
            for qt in range(NT):
                if qt >= 3:
                    sy.wait_ge(L.sem("p4_smm"), 16 * (qt - 2))
                i = sy.dma_start(out=qts[qt % 3][:], in_=qt_d[qt * 128:(qt + 1) * 128, :])
                L.inc(f"p4_qt{qt % 3}", i, 16)

        def pe_T(t, j):
            t.wait_ge(L.sem("p4_pan"), j + 1)
            if j >= 2:
                t.wait_ge(L.sem("p4_pte"), j - 1)
            for d in range(5):
                i = t.transpose(out=pspt[j % 2][:, d * 128:(d + 1) * 128],
                                in_=panb[j % 2][:, d * 128:(d + 1) * 128],
                                identity=idenb[:])
            L.inc("p4_tmm", i, 1)

        def pe_AV(t, j):
            qt, h = divmod(j, H)
            t.wait_ge(L.sem("p4_pte"), j + 1)
            if j >= 2:
                t.wait_ge(L.sem("p4_aoe"), j - 1)
            for d in range(5):
                sl = (qt + d) % 6
                i = t.matmul(psao[j % 2][:], vs[sl][:, h * 128:(h + 1) * 128],
                             ptb[j % 2][:, d * 128:(d + 1) * 128],
                             start=(d == 0), stop=(d == 4))
            L.inc("p4_avm", i, 1)

        @block.tensor
        def _(t):
            prol(t)
            for idx in range(NIDX):
                qt, h = divmod(idx, H)
                if idx == 0:
                    t.wait_ge(L.sem("p4_mkc"), 1)
                if h == 0:
                    for kt in range(qt, qt + 5):
                        t.wait_ge(L.sem(f"p4_kv{kt % 6}"), 32 * (kt // 6 + 1))
                    t.wait_ge(L.sem(f"p4_qt{qt % 3}"), 16 * (qt // 3 + 1))
                if idx >= 2:
                    t.wait_ge(L.sem("p4_mx"), idx - 1)
                    t.wait_ge(L.sem("p4_ex"), idx - 1)
                sl = idx % 2
                for d in range(5):
                    ks = (qt + d) % 6
                    t.matmul(pss[sl][:, d * 128:(d + 1) * 128],
                             qts[qt % 3][:, h * 128:(h + 1) * 128],
                             kts[ks][:, h * 128:(h + 1) * 128],
                             start=(d == 0 or d == 4), stop=False)
                dlist = range(5) if qt < 4 else (0, 4)
                tc = qt if qt < 4 else 4
                for d in dlist:
                    last_b0 = (d == max(x for x in dlist if x < 4))
                    i = t.matmul(pss[sl][:, d * 128:(d + 1) * 128], idenb[:],
                                 masks[:, tc * 5 + d, :],
                                 start=False, stop=(last_b0 or d == 4))
                L.inc("p4_smm", i, 1)
                if idx >= 1:
                    pe_T(t, idx - 1)
                    pe_AV(t, idx - 1)
            pe_T(t, NIDX - 1)
            pe_AV(t, NIDX - 1)

        @block.vector
        def _(v):
            prol(v)
            for idx in range(NIDX):
                v.wait_ge(L.sem("p4_smm"), idx + 1)
                i = v.reduce_max(out=negmb[idx % 2][:], in_=pss[idx % 2][:],
                                 axis=AX.X, negate=True)
                L.inc("p4_mx", i, 1)
                if idx >= 1:
                    j = idx - 1
                    v.wait_ge(L.sem("p4_ex"), j + 1)
                    v.reduce_sum(out=sumb[:], in_=pa[j % 2][:], axis=AX.X)
                    v.drain()
                    v.reciprocal(out=rcb[:], in_=sumb[:])
                    v.drain()
                    if j >= 2:
                        v.wait_ge(L.sem("p4_tmm"), j - 1)
                    i = v.tensor_single_scalar(out=panb[j % 2][:], in_=pa[j % 2][:],
                                               scalar=rcb[:, 0:1], op=ALU.mult)
                    L.inc("p4_pan", i, 1)
            j = NIDX - 1
            v.wait_ge(L.sem("p4_ex"), j + 1)
            v.reduce_sum(out=sumb[:], in_=pa[j % 2][:], axis=AX.X)
            v.drain()
            v.reciprocal(out=rcb[:], in_=sumb[:])
            v.drain()
            v.wait_ge(L.sem("p4_tmm"), j - 1)
            i = v.tensor_single_scalar(out=panb[j % 2][:], in_=pa[j % 2][:],
                                       scalar=rcb[:, 0:1], op=ALU.mult)
            L.inc("p4_pan", i, 1)

        def act_tail(s, j):
            qt, h = divmod(j, H)
            s.wait_ge(L.sem("p4_tmm"), j + 1)
            if j >= 2:
                s.wait_ge(L.sem("p4_avm"), j - 1)
            i = s.copy(out=ptb[j % 2][:], in_=pspt[j % 2][:])
            L.inc("p4_pte", i, 1)
            s.wait_ge(L.sem("p4_avm"), j + 1)
            i = s.copy(out=aoT[:, h, qt * 128:(qt + 1) * 128], in_=psao[j % 2][:])
            L.inc("p4_aoe", i, 1)

        @block.scalar
        def _(s):
            prol(s)
            for idx in range(NIDX):
                s.wait_ge(L.sem("p4_mx"), idx + 1)
                if idx >= 2:
                    s.wait_ge(L.sem("p4_pan"), idx - 1)
                i = s.activation(out=pa[idx % 2][:], in_=pss[idx % 2][:], func=AF.Exp,
                                 bias=negmb[idx % 2][:, 0:1])
                L.inc("p4_ex", i, 1)
                if idx >= 1:
                    act_tail(s, idx - 1)
            act_tail(s, NIDX - 1)

        barrier([f"p4_kv{s}" for s in range(6)] + [f"p4_qt{s}" for s in range(3)]
                + ["p4_smm", "p4_mx", "p4_ex", "p4_pan",
                   "p4_tmm", "p4_pte", "p4_avm", "p4_aoe"])

      with ExitStack() as p4w:
        wost = [p4w.enter_context(nc.sbuf_tensor(f"wost{i}", [128, 512], F32)) for i in range(2)]
        wodb = p4w.enter_context(nc.sbuf_tensor("wodb", [128, H, 512], BF16))
        sstg = [p4w.enter_context(nc.sbuf_tensor(f"sstg{i}", [128, 512], BF16)) for i in range(2)]
        pss2 = [p4w.enter_context(nc.psum_tensor(f"pss2_{i}", [128, 512], F32)) for i in range(2)]

        @block.sync
        def _(sy):
            prol(sy)
            for cwo in range(4 * H):
                if cwo >= 2:
                    sy.wait_ge(L.sem("p4w_wc"), cwo - 1)
                do, h2 = divmod(cwo, H)
                src = bass.AP(woT.tensor, h2 * 128 * D + do * 512, [[D, 128], [1, 512]])
                i = sy.dma_start(out=wost[cwo % 2][:], in_=src)
                L.inc(f"p4w_wd{cwo % 2}", i, 16)

        @block.gpsimd
        def _(g_):
            prol(g_)

            def emit_sw(gw):
                do_, tile_ = divmod(gw, NT)
                g_.wait_ge(L.sem("p4w_ev"), gw + 1)
                dsta = bass.AP(s_d.tensor, tile_ * 128 * D + do_ * 512,
                               [[D, 128], [1, 512]])
                i = g_.dma_start(out=dsta, in_=sstg[gw % 2][:])
                L.inc(f"p4w_sw{gw % 2}", i, 16)

            for do in range(4):
                if do >= 1:
                    for gw in range((do - 1) * NT, do * NT):
                        emit_sw(gw)
                for h2 in range(H):
                    cwo = do * H + h2
                    g_.wait_ge(L.sem(f"p4w_wd{cwo % 2}"), 16 * (cwo // 2 + 1))
                    if do >= 1:
                        g_.wait_ge(L.sem("p4w_mm"), NT * do)
                    i = g_.tensor_copy(out=wodb[:, h2, :], in_=wost[cwo % 2][:])
                    L.inc("p4w_wc", i, 1)
            for gw in range(3 * NT, 4 * NT):
                emit_sw(gw)

        @block.tensor
        def _(t):
            prol(t)
            for gw in range(4 * NT):
                do, tile = divmod(gw, NT)
                if gw >= 2:
                    t.wait_ge(L.sem("p4w_ev"), gw - 1)
                for h2 in range(H):
                    if tile == 0:
                        t.wait_ge(L.sem("p4w_wc"), do * H + h2 + 1)
                    i = t.matmul(pss2[gw % 2][:], aoT[:, h2, tile * 128:(tile + 1) * 128],
                                 wodb[:, h2, :], start=(h2 == 0), stop=(h2 == H - 1))
                L.inc("p4w_mm", i, 1)

        @block.scalar
        def _(s):
            prol(s)
            for gw in range(4 * NT):
                s.wait_ge(L.sem("p4w_mm"), gw + 1)
                if gw >= 2:
                    s.wait_ge(L.sem(f"p4w_sw{gw % 2}"), 16 * ((gw - 2) // 2 + 1))
                i = s.copy(out=sstg[gw % 2][:], in_=pss2[gw % 2][:])
                L.inc("p4w_ev", i, 1)

        barrier(["p4w_wd0", "p4w_wd1", "p4w_wc", "p4w_mm", "p4w_ev",
                 "p4w_sw0", "p4w_sw1"])
      aoS.close()

    if _MAX_PHASE >= 5:
      with ExitStack() as p6:
        xb6 = [p6.enter_context(nc.sbuf_tensor(f"xb6_{i}", [128, D], F32)) for i in range(2)]
        ab6 = [p6.enter_context(nc.sbuf_tensor(f"ab6_{i}", [128, D], F32)) for i in range(2)]
        sb6 = [p6.enter_context(nc.sbuf_tensor(f"sb6_{i}", [128, D], BF16)) for i in range(2)]
        xn6 = p6.enter_context(nc.sbuf_tensor("xn6", [128, D], F32))
        ffb = p6.enter_context(nc.sbuf_tensor("ffb", [128, D], F32))
        g6b = p6.enter_context(nc.sbuf_tensor("g6b", [128, D], F32))
        sgb6 = p6.enter_context(nc.sbuf_tensor("sgb6", [128, D], F32))
        outb = [p6.enter_context(nc.sbuf_tensor(f"outb{i}", [128, D], F32)) for i in range(2)]

        @block.sync
        def _(sy):
            prol(sy)
            for tt in range(NT):
                if tt >= 2:
                    sy.wait_ge(L.sem("p6_ot"), tt - 1)
                i = sy.dma_start(out=xb6[tt % 2][:], in_=x_own[tt * 128:(tt + 1) * 128, :])
                L.inc(f"p6_xd{tt % 2}", i, 16)
                i = sy.dma_start(out=ab6[tt % 2][:], in_=acc_d[tt * 128:(tt + 1) * 128, :])
                L.inc(f"p6_ad{tt % 2}", i, 16)

        @block.gpsimd
        def _(g_):
            prol(g_)

            def emit_ow(tt):
                g_.wait_ge(L.sem("p6_ot"), tt + 1)
                i = g_.dma_start(out=outh[tt * 128:(tt + 1) * 128, :], in_=outb[tt % 2][:])
                L.inc(f"p6_ow{tt % 2}", i, 16)

            for tt in range(NT):
                if tt >= 2:
                    g_.wait_ge(L.sem("p6_ot"), tt - 1)
                i = g_.dma_start(out=sb6[tt % 2][:], in_=s_d[tt * 128:(tt + 1) * 128, :])
                L.inc(f"p6_sd{tt % 2}", i, 16)
                if tt >= 2:
                    emit_ow(tt - 2)
            emit_ow(NT - 2)
            emit_ow(NT - 1)
            g_.wait_ge(L.sem("p6_ow0"), 16 * (NT // 2))
            g_.wait_ge(L.sem("p6_ow1"), 16 * (NT // 2))

        @block.vector
        def _(v):
            prol(v)
            for tt in range(NT):
                st = tt + 4
                v.wait_ge(L.sem(f"p6_xd{tt % 2}"), 16 * (tt // 2 + 1))
                v.wait_ge(L.sem(f"p6_ad{tt % 2}"), 16 * (tt // 2 + 1))
                v.tensor_scalar(out=xn6[:], in0=xb6[tt % 2][:],
                                scalar1=stats[:, st, 0:1], scalar2=stats[:, st, 1:2],
                                op0=ALU.subtract, op1=ALU.mult)
                v.drain()
                if tt >= 1:
                    v.wait_ge(L.sem("p6_sl"), tt)
                i = v.tensor_tensor(out=ffb[:], in0=xn6[:], in1=ab6[tt % 2][:], op=ALU.add)
                L.inc("p6_ff", i, 1)
                v.wait_ge(L.sem("p6_sl"), tt + 1)
                v.wait_ge(L.sem(f"p6_sd{tt % 2}"), 16 * (tt // 2 + 1))
                v.tensor_tensor(out=sgb6[:], in0=sb6[tt % 2][:], in1=g6b[:], op=ALU.mult)
                v.drain()
                if tt >= 2:
                    v.wait_ge(L.sem(f"p6_ow{tt % 2}"), 16 * ((tt - 2) // 2 + 1))
                i = v.tensor_tensor(out=outb[tt % 2][:], in0=xb6[tt % 2][:],
                                    in1=sgb6[:], op=ALU.add)
                L.inc("p6_ot", i, 1)

        @block.scalar
        def _(s):
            prol(s)
            for tt in range(NT):
                s.wait_ge(L.sem("p6_ff"), tt + 1)
                i = s.activation(out=g6b[:], in_=ffb[:], func=AF.Silu)
                L.inc("p6_sl", i, 1)

        @block.sync
        def _(sy):
            sy.wait_ge(L.sem("p6_ow0"), 16 * (NT // 2))
            sy.wait_ge(L.sem("p6_ow1"), 16 * (NT // 2))

    gctx.close()
    return nc


def _host_constants(hf):
    P0 = hf * 2048
    pos = np.maximum(P0 - HALO + np.arange(TK), 0).astype(np.float64)
    inv = 10000.0 ** (-(np.arange(0, DH, 2, dtype=np.float64)) / DH)
    ang = pos[None, :] * inv[:, None]
    c = np.cos(ang)
    sn = np.sin(ang)
    cosf = np.empty((128, TK), np.float32)
    sinf = np.empty((128, TK), np.float32)
    cosf[0::2] = c
    cosf[1::2] = c
    sinf[0::2] = -sn
    sinf[1::2] = sn

    ii = np.arange(128)[:, None]
    jj = np.arange(128)[None, :]
    masks = np.zeros((128, 25, 128), np.float32)
    for tc in range(5):
        for d in range(5):
            m = np.zeros((128, 128), np.float32)
            if d == 0:
                m[~(jj > ii) * np.ones((128, 128), bool)] = 0.0
                m = np.where(jj > ii, 0.0, -1e9).astype(np.float32)
            elif d == 4:
                m = np.where(jj <= ii, 0.0, -1e9).astype(np.float32)
            if hf == 0 and tc < 4 and tc + d <= 3:
                m[:] = -1e9
            masks[:, tc * 5 + d, :] = m
    masksb = masks.reshape(128, 25 * 128)
    return cosf, sinf, masksb


def prep_inputs(L0_attn_x, L0_attn_Wq, L0_attn_Wk, L0_attn_Wv, L0_attn_Wo,
                L0_ffn_shared_w1, L0_ffn_shared_b1, L0_ffn_shared_w2, L0_ffn_shared_b2,
                L0_ffn_routed_w1, L0_ffn_routed_b1, L0_ffn_routed_w2, L0_ffn_routed_b2,
                L0_ffn_centroids, L0_ffn_bias):
    x = np.ascontiguousarray(np.asarray(L0_attn_x, dtype=np.float32))
    wqT = np.ascontiguousarray(np.asarray(L0_attn_Wq, np.float32).T)
    wkT = np.ascontiguousarray(np.asarray(L0_attn_Wk, np.float32).T)
    wvT = np.ascontiguousarray(np.asarray(L0_attn_Wv, np.float32).T)
    woT = np.ascontiguousarray(np.asarray(L0_attn_Wo, np.float32).T)
    w1 = np.ascontiguousarray(np.concatenate([np.asarray(L0_ffn_shared_w1, np.float32),
                                              np.asarray(L0_ffn_routed_w1, np.float32)], 0))
    w2 = np.ascontiguousarray(np.concatenate([np.asarray(L0_ffn_shared_w2, np.float32),
                                              np.asarray(L0_ffn_routed_w2, np.float32)], 0))
    b1all = np.concatenate([np.asarray(L0_ffn_shared_b1, np.float32),
                            np.asarray(L0_ffn_routed_b1, np.float32)], 0)
    b1c = np.ascontiguousarray(b1all.reshape(NE, FCN, 128).transpose(2, 0, 1)
                               .reshape(128, NE * FCN))
    b2r = np.ascontiguousarray(np.asarray(L0_ffn_routed_b2, np.float32))
    b2s = np.ascontiguousarray(np.asarray(L0_ffn_shared_b2, np.float32))
    cen = np.asarray(L0_ffn_centroids, np.float32)
    cenT = np.ascontiguousarray(cen.T.reshape(DC, 128, 16).transpose(1, 0, 2)
                                .reshape(128, DC * 16))
    gbias = np.ascontiguousarray(np.asarray(L0_ffn_bias, np.float32).reshape(1, 16))
    p128 = np.zeros((128, 128), np.float32)
    for i in range(64):
        p128[2 * i, 2 * i + 1] = 1.0
        p128[2 * i + 1, 2 * i] = 1.0
    iden = np.eye(128, dtype=np.float32)
    consts = {hf: _host_constants(hf) for hf in range(2)}

    in_maps = []
    for c in range(8):
        b, hf = divmod(c, 2)
        P0 = hf * 2048
        cosf, sinf, masksb = consts[hf]
        x_own = np.ascontiguousarray(x[b, P0:P0 + T])
        x_halo = np.ascontiguousarray(x[b, P0 - HALO:P0]) if hf == 1 else \
            np.zeros((HALO, D), np.float32)
        in_maps.append(dict(
            x_own=x_own, x_halo=x_halo, wqT=wqT, wkT=wkT, wvT=wvT, woT=woT,
            w1=w1, w2=w2, b1c=b1c, b2r=b2r, b2s=b2s, cenT=cenT, gbias=gbias,
            cosf=cosf, sinf=sinf, p128=p128, iden=iden, masksb=masksb))
    return in_maps


def kernel(**inputs):
    in_maps = prep_inputs(**inputs)
    nc = build_program()
    res = run_bass_kernel_spmd(nc, in_maps, core_ids=list(range(8)))
    out = np.empty((4, 4096, D), np.float32)
    for c in range(8):
        b, hf = divmod(c, 2)
        out[b, hf * 2048:(hf + 1) * 2048] = res.results[c]["out"]
    return out


# version 08
# speedup vs baseline: 1.0988x; 1.0117x over previous
import numpy as np
import ml_dtypes
from contextlib import ExitStack

import concourse.bass as bass
import concourse.mybir as mybir
from concourse.bass_utils import run_bass_kernel_spmd

F32 = mybir.dt.float32
BF16 = mybir.dt.bfloat16
AF = mybir.ActivationFunctionType
ALU = mybir.AluOpType
AX = mybir.AxisListType

D = 2048
DC = 16
T = 2048
HALO = 512
TK = 2560
NT = 16
NTA = 20
H = 16
DH = 128
FF = 1408
FCN = 11
NE = 18
SCALE = float(128 ** -0.5)
EPS = 1e-5

_DEBUG_TAPS = set()
_MAX_PHASE = 99
_SKIP_QK = False
_SKIP_V = False
_QK_NCK = 99


class Ledger:
    def __init__(self, nc, ctx):
        self.nc, self.ctx = nc, ctx
        self.h = {}
        self.c = {}

    def sem(self, name):
        if name not in self.h:
            self.h[name] = self.ctx.enter_context(self.nc.semaphore(name))
            self.c[name] = 0
        return self.h[name]

    def inc(self, name, instr, k):
        instr.then_inc(self.sem(name), k)
        self.c[name] += k

    def n(self, name):
        return self.c.get(name, 0)


def build_program():
    nc = bass.Bass("TRN2", target_bir_lowering=False)

    x_own = nc.dram_tensor("x_own", [T, D], F32, kind="ExternalInput").ap()
    x_halo = nc.dram_tensor("x_halo", [HALO, D], F32, kind="ExternalInput").ap()
    wqT = nc.dram_tensor("wqT", [D, D], F32, kind="ExternalInput").ap()
    wkT = nc.dram_tensor("wkT", [D, D], F32, kind="ExternalInput").ap()
    wvT = nc.dram_tensor("wvT", [D, D], F32, kind="ExternalInput").ap()
    woT = nc.dram_tensor("woT", [D, D], F32, kind="ExternalInput").ap()
    w1h = nc.dram_tensor("w1", [NE, D, FF], F32, kind="ExternalInput").ap()
    w2h = nc.dram_tensor("w2", [NE, FF, D], F32, kind="ExternalInput").ap()
    b1ch = nc.dram_tensor("b1c", [128, NE * FCN], F32, kind="ExternalInput").ap()
    b2rh = nc.dram_tensor("b2r", [16, D], F32, kind="ExternalInput").ap()
    b2sh = nc.dram_tensor("b2s", [2, D], F32, kind="ExternalInput").ap()
    cenh = nc.dram_tensor("cenT", [128, DC * 16], F32, kind="ExternalInput").ap()
    gbh = nc.dram_tensor("gbias", [1, 16], F32, kind="ExternalInput").ap()
    cosh = nc.dram_tensor("cosf", [128, TK], F32, kind="ExternalInput").ap()
    sinh = nc.dram_tensor("sinf", [128, TK], F32, kind="ExternalInput").ap()
    p128h = nc.dram_tensor("p128", [128, 128], F32, kind="ExternalInput").ap()
    idh = nc.dram_tensor("iden", [128, 128], F32, kind="ExternalInput").ap()
    maskh = nc.dram_tensor("masksb", [128, 25 * 128], F32, kind="ExternalInput").ap()

    outh = nc.dram_tensor("out", [T, D], F32, kind="ExternalOutput").ap()

    def skind(nm):
        return "ExternalOutput" if nm in _DEBUG_TAPS else "Internal"

    qt_d = nc.dram_tensor("qt_d", [NT * 128, H * 128], BF16, kind=skind("qt_d")).ap()
    kt_d = nc.dram_tensor("kt_d", [NTA * 128, H * 128], BF16, kind=skind("kt_d")).ap()
    v_d = nc.dram_tensor("v_d", [TK, D], BF16, kind=skind("v_d")).ap()
    s_d = nc.dram_tensor("s_d", [T, D], BF16, kind=skind("s_d")).ap()
    acc_d = nc.dram_tensor("acc_d", [T, D], F32, kind=skind("acc_d")).ap()
    xnt_dbg = g_dbg = None
    if "xnt_d" in _DEBUG_TAPS:
        xnt_dbg = nc.dram_tensor("xnt_d", [128, DC * T], BF16, kind="ExternalOutput").ap()
    if "g_d" in _DEBUG_TAPS:
        g_dbg = nc.dram_tensor("g_d", [128, NT * 16], F32, kind="ExternalOutput").ap()
    st_dbg = None
    if "st_d" in _DEBUG_TAPS:
        st_dbg = nc.dram_tensor("st_d", [128, NTA * 2], F32, kind="ExternalOutput").ap()

    gctx = ExitStack()
    L = Ledger(nc, gctx)
    PH = L.sem("ph")
    pno = [0]

    idenb = gctx.enter_context(nc.sbuf_tensor("idenb", [128, 128], BF16))
    p128b = gctx.enter_context(nc.sbuf_tensor("p128b", [128, 128], BF16))
    stats = gctx.enter_context(nc.sbuf_tensor("stats", [128, NTA, 2], F32))
    b1c = gctx.enter_context(nc.sbuf_tensor("b1c_s", [128, NE * FCN], F32))
    gbias = gctx.enter_context(nc.sbuf_tensor("gbias_s", [1, 16], F32))
    bplane = gctx.enter_context(nc.sbuf_tensor("bplane", [128, 16], F32))
    ones2 = gctx.enter_context(nc.sbuf_tensor("ones2", [2, 128], BF16))
    Gm = gctx.enter_context(nc.sbuf_tensor("Gm", [128, NT, 16], F32))
    GTb = gctx.enter_context(nc.sbuf_tensor("GTb", [16, T], BF16))
    b2rb = gctx.enter_context(nc.sbuf_tensor("b2rb", [16, D], BF16))
    b2sb = gctx.enter_context(nc.sbuf_tensor("b2sb", [2, D], BF16))
    epsb = gctx.enter_context(nc.sbuf_tensor("epsb", [128, 1], F32))

    sA = ExitStack()
    xnT = sA.enter_context(nc.sbuf_tensor("xnT", [128, DC, T], BF16))
    sB = ExitStack()
    xnTh = sB.enter_context(nc.sbuf_tensor("xnTh", [128, DC, HALO], BF16))
    cosf = sB.enter_context(nc.sbuf_tensor("cosf_s", [128, TK], F32))
    sinf = sB.enter_context(nc.sbuf_tensor("sinf_s", [128, TK], F32))
    iden32 = sB.enter_context(nc.sbuf_tensor("iden32", [128, 128], F32))
    cenT = sB.enter_context(nc.sbuf_tensor("cenT_s", [128, DC, 16], F32))
    ones1 = sB.enter_context(nc.sbuf_tensor("ones1", [1, 128], F32))

    block = gctx.enter_context(nc.Block())

    def barrier(names):
        pno[0] += 1

        @block.sync
        def _(sy):
            for s in names:
                sy.wait_ge(L.sem(s), L.n(s))
            sy.sem_inc(PH, 1)
        L.c["ph"] += 1

    def prol(eng):
        eng.wait_ge(PH, pno[0])

    with ExitStack() as p0:
        b2r32 = p0.enter_context(nc.sbuf_tensor("b2r32", [16, D], F32))
        b2s32 = p0.enter_context(nc.sbuf_tensor("b2s32", [2, D], F32))
        p128s = p0.enter_context(nc.sbuf_tensor("p128s", [128, 128], F32))
        psbp = p0.enter_context(nc.psum_tensor("psbp", [128, 16], F32))

        @block.sync
        def _(sy):
            for dst, src in [(iden32, idh), (b1c, b1ch),
                             (gbias, gbh), (b2r32, b2rh), (b2s32, b2sh),
                             (p128s, p128h)]:
                i = sy.dma_start(out=dst[:], in_=src)
                L.inc("c_dma", i, 16)
            i = sy.dma_start(out=cenT[:], in_=cenh.rearrange("p (dc j) -> p dc j", dc=DC))
            L.inc("c_dma", i, 16)

        @block.gpsimd
        def _(g):
            g.wait_ge(L.sem("c_dma"), L.n("c_dma"))
            g.memset(ones1[:], 1.0)
            g.memset(ones2[:], 1.0)
            g.memset(epsb[:], EPS)
            g.tensor_copy(out=idenb[:], in_=iden32[:])
            g.tensor_copy(out=b2rb[:], in_=b2r32[:])
            g.tensor_copy(out=b2sb[:], in_=b2s32[:])
            i = g.tensor_copy(out=p128b[:], in_=p128s[:])
            L.inc("c_cast", i, 1)

        @block.tensor
        def _(t):
            t.wait_ge(L.sem("c_cast"), 1)
            i = t.matmul(psbp[:], ones1[:], gbias[:], start=True, stop=True)
            L.inc("c_bp", i, 1)

        @block.scalar
        def _(s):
            s.wait_ge(L.sem("c_bp"), 1)
            i = s.copy(out=bplane[:], in_=psbp[:])
            L.inc("c_bpe", i, 1)

        barrier(["c_dma", "c_cast", "c_bpe"])

    def xnT_sl(tt, dc, lo=0, hi=128):
        if tt < 4:
            return xnTh[:, dc, tt * 128 + lo:tt * 128 + hi]
        return xnT[:, dc, (tt - 4) * 128 + lo:(tt - 4) * 128 + hi]

    def xnT_grp(dc, g):
        if g == 0:
            return xnTh[:, dc, :]
        return xnT[:, dc, (g - 1) * 512:g * 512]

    with ExitStack() as p1:
        xb = [p1.enter_context(nc.sbuf_tensor(f"xb{i}", [128, D], F32)) for i in range(2)]
        xn32 = [p1.enter_context(nc.sbuf_tensor(f"xn32_{i}", [128, D], F32)) for i in range(2)]
        sqd = p1.enter_context(nc.sbuf_tensor("sqd", [128, D], F32))
        s1b = p1.enter_context(nc.sbuf_tensor("s1b", [128, 1], F32))
        s2b = p1.enter_context(nc.sbuf_tensor("s2b", [128, 1], F32))
        ex2b = p1.enter_context(nc.sbuf_tensor("ex2b", [128, 1], F32))
        nvb = p1.enter_context(nc.sbuf_tensor("nvb", [128, 1], F32))
        sdb = p1.enter_context(nc.sbuf_tensor("sdb", [128, 1], F32))
        xnT32 = p1.enter_context(nc.sbuf_tensor("xnT32", [128, DC, 128], F32))
        sgb = [p1.enter_context(nc.sbuf_tensor(f"sgb{i}", [128, 16], F32)) for i in range(2)]
        scb = p1.enter_context(nc.sbuf_tensor("scb", [128, 16], F32))
        scur = [p1.enter_context(nc.sbuf_tensor(f"scur{i}", [128, 16], F32)) for i in range(2)]
        geb = p1.enter_context(nc.sbuf_tensor("geb", [128, 16], F32))
        gmb = p1.enter_context(nc.sbuf_tensor("gmb", [128, 16], F32))
        mrb = [p1.enter_context(nc.sbuf_tensor(f"mrb{i}", [128, 1], F32)) for i in range(4)]
        msb = p1.enter_context(nc.sbuf_tensor("msb", [128, 1], F32))
        recb = p1.enter_context(nc.sbuf_tensor("recb", [128, 1], F32))
        pst = [p1.enter_context(nc.psum_tensor(f"pst{i}", [128, 128], F32)) for i in range(2)]
        pssc = [p1.enter_context(nc.psum_tensor(f"pssc{i}", [128, 16], F32)) for i in range(2)]
        psgt = p1.enter_context(nc.psum_tensor("psgt", [16, 128], F32))

        @block.sync
        def _(sy):
            prol(sy)
            for tt in range(NTA):
                if tt >= 2:
                    sy.wait_ge(L.sem("p1_xn"), tt - 1)
                src = x_halo[tt * 128:(tt + 1) * 128, :] if tt < 4 else \
                    x_own[(tt - 4) * 128:(tt - 3) * 128, :]
                i = sy.dma_start(out=xb[tt % 2][:], in_=src)
                L.inc(f"p1_xd{tt % 2}", i, 16)

        @block.scalar
        def _(s):
            prol(s)
            for tt in range(NTA):
                ot = tt - 4
                s.wait_ge(L.sem(f"p1_xd{tt % 2}"), 16 * (tt // 2 + 1))
                if tt >= 1:
                    s.wait_ge(L.sem("p1_nv"), tt)
                i = s.activation(out=sqd[:], in_=xb[tt % 2][:], func=AF.Square,
                                 accum_out=s2b[:])
                L.inc("p1_sq", i, 1)
                s.wait_ge(L.sem("p1_nv"), tt + 1)
                if tt >= 1:
                    s.wait_ge(L.sem("p1_rc"), tt)
                i = s.activation(out=sdb[:], in_=nvb[:], func=AF.Sqrt,
                                 bias=epsb[:, 0:1], scale=-1.0)
                L.inc("p1_sd", i, 1)
                for dc in range(16):
                    gk = tt * 16 + dc
                    s.wait_ge(L.sem("p1_tr"), gk + 1)
                    if ot >= 0:
                        s.copy(out=xnT_sl(tt, dc), in_=pst[gk % 2][:])
                        i = s.copy(out=xnT32[:, dc, :], in_=pst[gk % 2][:])
                    else:
                        i = s.copy(out=xnT_sl(tt, dc), in_=pst[gk % 2][:])
                    L.inc("p1_ev", i, 1)
                if ot >= 0:
                    s.wait_ge(L.sem("p1_scm"), ot + 1)
                    if ot >= 2:
                        s.wait_ge(L.sem("p1_g"), ot - 1)
                    i = s.activation(out=sgb[ot % 2][:], in_=pssc[ot % 2][:],
                                     func=AF.Sigmoid)
                    L.inc("p1_sg", i, 1)
                if ot >= 1:
                    s.wait_ge(L.sem("p1_gt"), ot)
                    i = s.copy(out=GTb[:, (ot - 1) * 128:ot * 128], in_=psgt[:])
                    L.inc("p1_gte", i, 1)
            s.wait_ge(L.sem("p1_gt"), NT)
            i = s.copy(out=GTb[:, (NT - 1) * 128:NT * 128], in_=psgt[:])
            L.inc("p1_gte", i, 1)

        @block.vector
        def _(v):
            prol(v)
            for tt in range(NTA):
                ot = tt - 4
                v.wait_ge(L.sem(f"p1_xd{tt % 2}"), 16 * (tt // 2 + 1))
                v.reduce_sum(out=s1b[:], in_=xb[tt % 2][:], axis=AX.X)
                v.drain()
                v.tensor_single_scalar(out=stats[:, tt, 0:1], in_=s1b[:],
                                       scalar=1.0 / D, op=ALU.mult)
                v.wait_ge(L.sem("p1_sq"), tt + 1)
                v.tensor_single_scalar(out=ex2b[:], in_=s2b[:], scalar=1.0 / D,
                                       op=ALU.mult)
                v.drain()
                i = v.scalar_tensor_tensor(out=nvb[:], in0=stats[:, tt, 0:1],
                                           scalar=stats[:, tt, 0:1], in1=ex2b[:],
                                           op0=ALU.mult, op1=ALU.subtract)
                L.inc("p1_nv", i, 1)
                v.wait_ge(L.sem("p1_sd"), tt + 1)
                i = v.reciprocal(out=stats[:, tt, 1:2], in_=sdb[:])
                L.inc("p1_rc", i, 1)
                v.drain()
                if tt >= 2:
                    v.wait_ge(L.sem("p1_tr"), 16 * (tt - 1))
                i = v.tensor_scalar(out=xn32[tt % 2][:], in0=xb[tt % 2][:],
                                    scalar1=stats[:, tt, 0:1], scalar2=stats[:, tt, 1:2],
                                    op0=ALU.subtract, op1=ALU.mult)
                L.inc("p1_xn", i, 1)
                if ot >= 0:
                    v.wait_ge(L.sem("p1_sg"), ot + 1)
                    v.tensor_tensor(out=scb[:], in0=sgb[ot % 2][:], in1=bplane[:], op=ALU.add)
                    v.drain()
                    v.tensor_copy(out=scur[0][:], in_=scb[:])
                    v.drain()
                    v.reduce_max(out=mrb[0][:], in_=scur[0][:], axis=AX.X)
                    v.drain()
                    for r in range(1, 4):
                        v.tensor_single_scalar(out=geb[:], in_=scur[(r - 1) % 2][:],
                                               scalar=mrb[r - 1][:, 0:1], op=ALU.is_ge)
                        v.drain()
                        v.scalar_tensor_tensor(out=scur[r % 2][:], in0=geb[:],
                                               scalar=-2e9, in1=scur[(r - 1) % 2][:],
                                               op0=ALU.mult, op1=ALU.add)
                        v.drain()
                        v.reduce_max(out=mrb[r][:], in_=scur[r % 2][:], axis=AX.X)
                        v.drain()
                    v.tensor_tensor(out=msb[:], in0=mrb[0][:], in1=mrb[1][:], op=ALU.add)
                    v.drain()
                    v.tensor_tensor(out=msb[:], in0=msb[:], in1=mrb[2][:], op=ALU.add)
                    v.drain()
                    v.tensor_tensor(out=msb[:], in0=msb[:], in1=mrb[3][:], op=ALU.add)
                    v.drain()
                    v.tensor_single_scalar(out=msb[:], in_=msb[:], scalar=1e-8, op=ALU.max)
                    v.drain()
                    v.reciprocal(out=recb[:], in_=msb[:])
                    v.tensor_single_scalar(out=geb[:], in_=scb[:], scalar=mrb[3][:, 0:1],
                                           op=ALU.is_ge)
                    v.drain()
                    v.tensor_tensor(out=gmb[:], in0=scb[:], in1=geb[:], op=ALU.mult)
                    v.drain()
                    i = v.tensor_single_scalar(out=Gm[:, ot, :], in_=gmb[:],
                                               scalar=recb[:, 0:1], op=ALU.mult)
                    L.inc("p1_g", i, 1)

        def emit_scores(t, ot):
            t.wait_ge(L.sem("p1_ev"), 16 * (ot + 5))
            if ot >= 2:
                t.wait_ge(L.sem("p1_sg"), ot - 1)
            for dc in range(16):
                i = t.matmul(pssc[ot % 2][:], xnT32[:, dc, :], cenT[:, dc, :],
                             start=(dc == 0), stop=(dc == 15))
            L.inc("p1_scm", i, 1)

        def emit_gt(t, j):
            t.wait_ge(L.sem("p1_g"), j + 1)
            if j >= 1:
                t.wait_ge(L.sem("p1_gte"), j)
            i = t.transpose(out=psgt[:], in_=Gm[:, j, :], identity=iden32[:])
            L.inc("p1_gt", i, 1)

        @block.tensor
        def _(t):
            prol(t)
            for tt in range(NTA):
                ot = tt - 4
                t.wait_ge(L.sem("p1_xn"), tt + 1)
                for dc in range(16):
                    gk = tt * 16 + dc
                    if gk >= 2:
                        t.wait_ge(L.sem("p1_ev"), gk - 1)
                    i = t.transpose(out=pst[gk % 2][:],
                                    in_=xn32[tt % 2][:, dc * 128:(dc + 1) * 128],
                                    identity=iden32[:])
                    L.inc("p1_tr", i, 1)
                if ot >= 0:
                    emit_scores(t, ot)
                if ot >= 1:
                    emit_gt(t, ot - 1)
            emit_gt(t, NT - 1)

        names = ["p1_ev", "p1_g", "p1_gte", "p1_xd0", "p1_xd1", "p1_xn",
                 "p1_scm", "p1_gt"]
        if "xnt_d" in _DEBUG_TAPS or "g_d" in _DEBUG_TAPS:
            @block.gpsimd
            def _(g):
                prol(g)
                g.wait_ge(L.sem("p1_ev"), L.n("p1_ev"))
                g.wait_ge(L.sem("p1_g"), L.n("p1_g"))
                if xnt_dbg is not None:
                    i = g.dma_start(out=xnt_dbg,
                                    in_=xnT[:].rearrange("p dc t -> p (dc t)"))
                    L.inc("p1_dbg", i, 16)
                if g_dbg is not None:
                    i = g.dma_start(out=g_dbg, in_=Gm[:].rearrange("p t j -> p (t j)"))
                    L.inc("p1_dbg", i, 16)
                if st_dbg is not None:
                    g.wait_ge(L.sem("p1_xn"), L.n("p1_xn"))
                    i = g.dma_start(out=st_dbg, in_=stats[:].rearrange("p t k -> p (t k)"))
                    L.inc("p1_dbg", i, 16)
                g.wait_ge(L.sem("p1_dbg"), L.n("p1_dbg"))
            names.append("p1_dbg")
        barrier(names)

    if _MAX_PHASE >= 2:
      with ExitStack() as p2:
        wst = [p2.enter_context(nc.sbuf_tensor(f"wst{i}", [128, DC, 128], F32)) for i in range(2)]
        wbf = [p2.enter_context(nc.sbuf_tensor(f"wbf{i}", [128, DC, 128], BF16)) for i in range(2)]
        qa32 = [p2.enter_context(nc.sbuf_tensor(f"qa32_{i}", [128, 512], F32)) for i in range(2)]
        qab = [p2.enter_context(nc.sbuf_tensor(f"qab{i}", [128, 512], BF16)) for i in range(2)]
        t1b = [p2.enter_context(nc.sbuf_tensor(f"t1b{i}", [128, 512], F32)) for i in range(2)]
        t2b = p2.enter_context(nc.sbuf_tensor("t2b", [128, 512], F32))
        qstg = [p2.enter_context(nc.sbuf_tensor(f"qstg{i}", [128, 512], BF16)) for i in range(2)]
        psq = [p2.enter_context(nc.psum_tensor(f"psq{i}", [128, 512], F32)) for i in range(2)]
        pshf = [p2.enter_context(nc.psum_tensor(f"pshf{i}", [128, 512], F32)) for i in range(2)]

        chunks = []
        groups = []
        for h in range(H):
            chunks.append((wqT, h * 128))
            for g in range(4):
                groups.append((h, g, 0))
        for h in range(H):
            chunks.append((wkT, h * 128))
            for g in range(5):
                groups.append((16 + h, g, 1))
        if _QK_NCK < len(chunks):
            chunks = chunks[:_QK_NCK]
            groups = [g for g in groups if g[0] < _QK_NCK]
        NCK = len(chunks)
        lastg = {}
        firstg = {}
        for gi, (ck, g, part) in enumerate(groups):
            lastg[ck] = gi
            firstg.setdefault(ck, gi)

        def grp_meta(gi):
            ck, g, part = groups[gi]
            h = ck if part == 0 else ck - 16
            if part == 0:
                off = HALO + g * 512
                dst = qt_d
                scale = SCALE
                rhs_g = g + 1
            else:
                off = g * 512
                dst = kt_d
                scale = 1.0
                rhs_g = g
            return ck, h, g, off, dst, scale, rhs_g

        @block.sync
        def _(sy):
            if _SKIP_QK:
                return
            prol(sy)
            i = sy.dma_start(out=cosf[:], in_=cosh)
            L.inc("p2_tb", i, 16)
            i = sy.dma_start(out=sinf[:], in_=sinh)
            L.inc("p2_tb", i, 16)
            for ck in range(NCK):
                if ck >= 2:
                    sy.wait_ge(L.sem("p2_wc"), ck - 1)
                wh, col = chunks[ck]
                src = bass.AP(wh.tensor, col,
                              [[D, 128], [128 * D, DC], [1, 128]])
                i = sy.dma_start(out=wst[ck % 2][:], in_=src)
                L.inc(f"p2_wd{ck % 2}", i, 16)

        gbyck = {}
        for gi, (ck, g, part) in enumerate(groups):
            gbyck.setdefault(ck, []).append(gi)

        def emit_qw(g_, gi):
            ck, h, g, off, dst, scale, rhs_g = grp_meta(gi)
            g_.wait_ge(L.sem("p2_rt"), gi + 1)
            t0 = off if dst is kt_d else g * 512
            dsta = bass.AP(dst.tensor, (t0 // 128) * 128 * (H * 128) + h * 128,
                           [[H * 128, 128], [128 * H * 128, 4], [1, 128]])
            i = g_.dma_start(out=dsta,
                             in_=qstg[gi % 2][:].rearrange("p (a t) -> p a t", a=4))
            L.inc(f"p2_qw{gi % 2}", i, 16)

        @block.gpsimd
        def _(g_):
            if _SKIP_QK:
                return
            prol(g_)
            for ck in range(NCK):
                g_.wait_ge(L.sem(f"p2_wd{ck % 2}"), 16 * (ck // 2 + 1))
                if ck >= 2:
                    g_.wait_ge(L.sem("p2_mm"), lastg[ck - 2] + 1)
                i = g_.tensor_copy(out=wbf[ck % 2][:], in_=wst[ck % 2][:])
                L.inc("p2_wc", i, 1)
                if ck >= 1:
                    for gi in gbyck[ck - 1]:
                        emit_qw(g_, gi)
            for gi in gbyck[NCK - 1]:
                emit_qw(g_, gi)

        @block.tensor
        def _(t):
            if _SKIP_QK:
                return
            prol(t)
            for gi in range(len(groups)):
                ck, h, g, off, dst, scale, rhs_g = grp_meta(gi)
                if gi == firstg[ck]:
                    t.wait_ge(L.sem("p2_wc"), ck + 1)
                if gi >= 2:
                    t.wait_ge(L.sem("p2_ea"), gi - 1)
                    t.wait_ge(L.sem("p2_ev"), gi - 1)
                for dc in range(DC):
                    i = t.matmul(psq[gi % 2][:], wbf[ck % 2][:, dc, :],
                                 xnT_grp(dc, rhs_g), start=(dc == 0), stop=(dc == 15))
                L.inc("p2_mm", i, 1)
                if gi >= 1:
                    j = gi - 1
                    t.wait_ge(L.sem("p2_ev"), j + 1)
                    if j >= 2:
                        t.wait_ge(L.sem("p2_t2"), j - 1)
                    i = t.matmul(pshf[j % 2][:], p128b[:], qab[j % 2][:],
                                 start=True, stop=True)
                    L.inc("p2_sh", i, 1)
            j = len(groups) - 1
            t.wait_ge(L.sem("p2_ev"), j + 1)
            t.wait_ge(L.sem("p2_t2"), j - 1)
            i = t.matmul(pshf[j % 2][:], p128b[:], qab[j % 2][:], start=True, stop=True)
            L.inc("p2_sh", i, 1)

        @block.scalar
        def _(s):
            if _SKIP_QK:
                return
            prol(s)
            for gi in range(len(groups)):
                ck, h, g, off, dst, scale, rhs_g = grp_meta(gi)
                s.wait_ge(L.sem("p2_mm"), gi + 1)
                if gi >= 2:
                    s.wait_ge(L.sem("p2_rt"), gi - 1)
                i = s.activation(out=qa32[gi % 2][:], in_=psq[gi % 2][:],
                                 func=AF.Copy, scale=scale)
                L.inc("p2_ea", i, 1)
                if gi >= 2:
                    s.wait_ge(L.sem("p2_sh"), gi - 1)
                i = s.activation(out=qab[gi % 2][:], in_=psq[gi % 2][:],
                                 func=AF.Copy, scale=scale)
                L.inc("p2_ev", i, 1)

        def rope_finish(v, j):
            _, _, _, off, _, _, _ = grp_meta(j)
            v.wait_ge(L.sem("p2_sh"), j + 1)
            i = v.tensor_tensor(out=t2b[:], in0=pshf[j % 2][:],
                                in1=sinf[:, off:off + 512], op=ALU.mult)
            L.inc("p2_t2", i, 1)
            v.wait_ge(L.sem("p2_ea"), j + 1)
            v.tensor_tensor(out=t1b[0][:], in0=qa32[j % 2][:],
                            in1=cosf[:, off:off + 512], op=ALU.mult)
            v.drain()
            if j >= 2:
                v.wait_ge(L.sem(f"p2_qw{j % 2}"), 16 * ((j - 2) // 2 + 1))
            i = v.tensor_tensor(out=qstg[j % 2][:], in0=t1b[0][:],
                                in1=t2b[:], op=ALU.add)
            L.inc("p2_rt", i, 1)
            v.drain()

        @block.vector
        def _(v):
            if _SKIP_QK:
                return
            prol(v)
            v.wait_ge(L.sem("p2_tb"), 32)
            for gi in range(len(groups)):
                if gi >= 1:
                    rope_finish(v, gi - 1)
            rope_finish(v, len(groups) - 1)

        barrier(["p2_tb", "p2_wd0", "p2_wd1", "p2_wc", "p2_mm", "p2_ea",
                 "p2_ev", "p2_sh", "p2_t2", "p2_rt", "p2_qw0", "p2_qw1"])

        vst = [p2.enter_context(nc.sbuf_tensor(f"vst{i}", [128, 512], F32)) for i in range(2)]
        wvbf = p2.enter_context(nc.sbuf_tensor("wvbf", [128, DC, 512], BF16))
        vstg = [p2.enter_context(nc.sbuf_tensor(f"vstg{i}", [128, 512], BF16)) for i in range(2)]
        psv = [p2.enter_context(nc.psum_tensor(f"psv{i}", [128, 512], F32)) for i in range(2)]

        @block.sync
        def _(sy):
            if _SKIP_V:
                return
            prol(sy)
            for vck in range(4 * DC):
                do, dcc = divmod(vck, DC)
                if vck >= 2:
                    sy.wait_ge(L.sem("p2v_wc"), vck - 1)
                src = bass.AP(wvT.tensor, dcc * 128 * D + do * 512,
                              [[D, 128], [1, 512]])
                i = sy.dma_start(out=vst[vck % 2][:], in_=src)
                L.inc(f"p2v_wd{vck % 2}", i, 16)

        @block.gpsimd
        def _(g_):
            if _SKIP_V:
                return
            prol(g_)

            def emit_vw(gv):
                do_, tt_ = divmod(gv, NTA)
                g_.wait_ge(L.sem("p2v_ev"), gv + 1)
                dsta = bass.AP(v_d.tensor, tt_ * 128 * D + do_ * 512,
                               [[D, 128], [1, 512]])
                i = g_.dma_start(out=dsta, in_=vstg[gv % 2][:])
                L.inc(f"p2v_vw{gv % 2}", i, 16)

            for do in range(4):
                if do >= 1:
                    for gv in range((do - 1) * NTA, do * NTA):
                        emit_vw(gv)
                for dcc in range(DC):
                    vck = do * DC + dcc
                    g_.wait_ge(L.sem(f"p2v_wd{vck % 2}"), 16 * (vck // 2 + 1))
                    if do >= 1:
                        g_.wait_ge(L.sem("p2v_mm"), NTA * do)
                    i = g_.tensor_copy(out=wvbf[:, dcc, :], in_=vst[vck % 2][:])
                    L.inc("p2v_wc", i, 1)
            for gv in range(3 * NTA, 4 * NTA):
                emit_vw(gv)

        @block.tensor
        def _(t):
            if _SKIP_V:
                return
            prol(t)
            for gv in range(4 * NTA):
                do, tt = divmod(gv, NTA)
                if gv >= 2:
                    t.wait_ge(L.sem("p2v_ev"), gv - 1)
                for dc in range(DC):
                    if tt == 0:
                        t.wait_ge(L.sem("p2v_wc"), do * DC + dc + 1)
                    i = t.matmul(psv[gv % 2][:], xnT_sl(tt, dc), wvbf[:, dc, :],
                                 start=(dc == 0), stop=(dc == 15))
                L.inc("p2v_mm", i, 1)

        @block.scalar
        def _(s):
            if _SKIP_V:
                return
            prol(s)
            for gv in range(4 * NTA):
                s.wait_ge(L.sem("p2v_mm"), gv + 1)
                if gv >= 2:
                    s.wait_ge(L.sem(f"p2v_vw{gv % 2}"), 16 * ((gv - 2) // 2 + 1))
                i = s.copy(out=vstg[gv % 2][:], in_=psv[gv % 2][:])
                L.inc("p2v_ev", i, 1)

        barrier(["p2v_wd0", "p2v_wd1", "p2v_wc", "p2v_mm", "p2v_ev",
                 "p2v_vw0", "p2v_vw1"])

    sB.close()

    if _MAX_PHASE >= 3:
      with ExitStack() as p5:
        w1st = [p5.enter_context(nc.sbuf_tensor(f"w1st{i}", [128, 8, 128], F32)) for i in range(2)]
        w1bf = [p5.enter_context(nc.sbuf_tensor(f"w1bf{i}", [128, DC, 128], BF16)) for i in range(2)]
        w2st = [p5.enter_context(nc.sbuf_tensor(f"w2st{i}", [128, 512], F32)) for i in range(2)]
        w2bd = [p5.enter_context(nc.sbuf_tensor(f"w2bd{i}", [128, FCN, 512], BF16)) for i in range(2)]
        hT = p5.enter_context(nc.sbuf_tensor("hT", [128, FCN, 1024], BF16))
        acc = p5.enter_context(nc.sbuf_tensor("acc", [128, 8, D], F32))
        psh = [p5.enter_context(nc.psum_tensor(f"psh{i}", [128, 512], F32)) for i in range(4)]
        psy = [p5.enter_context(nc.psum_tensor(f"psy{i}", [128, 512], F32)) for i in range(4)]

        NJ = NE
        n_hgrp = 2 * FCN
        n_ygrp = 4 * 8

        @block.sync
        def _(sy):
            prol(sy)
            for half in range(2):
                for j in range(NJ):
                    jj = half * NJ + j
                    for do in range(4):
                        gd = jj * 4 + do
                        for fc in range(FCN):
                            cw = gd * FCN + fc
                            if cw >= 2:
                                sy.wait_ge(L.sem("p5_w2c"), cw - 1)
                            src = bass.AP(w2h.tensor, j * FF * D + fc * 128 * D + do * 512,
                                          [[D, 128], [1, 512]])
                            i = sy.dma_start(out=w2st[cw % 2][:], in_=src)
                            L.inc(f"p5_w2d{cw % 2}", i, 16)

        @block.vector
        def _(v):
            prol(v)
            for half in range(2):
                for j in range(NJ):
                    jj = half * NJ + j
                    for do in range(4):
                        gd = jj * 4 + do
                        for tile in range(8):
                            gy = gd * 8 + tile
                            v.wait_ge(L.sem("p5_ymm"), gy + 1)
                            if half == 1 and j == 0 and do == 0:
                                v.wait_ge(L.sem("p5_acw"), 128)
                            dsl = acc[:, tile, do * 512:(do + 1) * 512]
                            if j == 0:
                                i = v.tensor_single_scalar(out=dsl, in_=psy[gy % 4][:],
                                                           scalar=1.0, op=ALU.mult)
                            elif j == 1:
                                i = v.scalar_tensor_tensor(out=dsl, in0=psy[gy % 4][:],
                                                           scalar=1.0, in1=dsl,
                                                           op0=ALU.mult, op1=ALU.add)
                            else:
                                gate = Gm[:, half * 8 + tile, j - 2:j - 1]
                                i = v.scalar_tensor_tensor(out=dsl, in0=psy[gy % 4][:],
                                                           scalar=gate, in1=dsl,
                                                           op0=ALU.mult, op1=ALU.add)
                            L.inc("p5_yev", i, 1)

        @block.scalar
        def _(s):
            prol(s)

            def cast_w2(jj2, dos):
                for do in dos:
                    gd = jj2 * 4 + do
                    for fc in range(FCN):
                        cw = gd * FCN + fc
                        s.wait_ge(L.sem(f"p5_w2d{cw % 2}"), 16 * (cw // 2 + 1))
                        if gd >= 2:
                            s.wait_ge(L.sem("p5_ymm"), 8 * (gd - 1))
                        i = s.activation(out=w2bd[gd % 2][:, fc, :],
                                         in_=w2st[cw % 2][:], func=AF.Copy)
                        L.inc("p5_w2c", i, 1)

            for half in range(2):
                for j in range(NJ):
                    jj = half * NJ + j
                    for fc in range(FCN):
                        ck = jj * FCN + fc
                        for g2 in range(2):
                            gh = ck * 2 + g2
                            s.wait_ge(L.sem("p5_hmm"), gh + 1)
                            i = s.activation(out=hT[:, fc, g2 * 512:(g2 + 1) * 512],
                                             in_=psh[gh % 4][:], func=AF.Relu,
                                             bias=b1c[:, j * FCN + fc:j * FCN + fc + 1])
                            L.inc("p5_hev", i, 1)
                        if fc == 0:
                            cast_w2(jj, (0, 1))
                    cast_w2(jj, (2, 3))


        @block.tensor
        def _(t):
            prol(t)
            for half in range(2):
                for j in range(NJ):
                    jj = half * NJ + j
                    for fc in range(FCN):
                        ck = jj * FCN + fc
                        t.wait_ge(L.sem("p5_w1h"), 2 * (ck + 1))
                        for g2 in range(2):
                            gh = ck * 2 + g2
                            if gh >= 4:
                                t.wait_ge(L.sem("p5_hev"), gh - 3)
                            for dc in range(DC):
                                i = t.matmul(psh[gh % 4][:], w1bf[ck % 2][:, dc, :],
                                             xnT[:, dc, half * 1024 + g2 * 512:
                                                 half * 1024 + (g2 + 1) * 512],
                                             start=(dc == 0), stop=(dc == 15))
                            L.inc("p5_hmm", i, 1)
                    for do in range(4):
                        gd = jj * 4 + do
                        for tile in range(8):
                            gy = gd * 8 + tile
                            if gy >= 4:
                                t.wait_ge(L.sem("p5_yev"), gy - 3)
                            nmm = FCN + (2 if j == 0 else 0)
                            k = 0
                            for fc in range(FCN):
                                if tile == 0:
                                    t.wait_ge(L.sem("p5_w2c"), gd * FCN + fc + 1)
                                if do == 0 and tile == 0:
                                    t.wait_ge(L.sem("p5_hev"), 2 * (jj * FCN + fc) + 2)
                                i = t.matmul(psy[gy % 4][:],
                                             hT[:, fc, tile * 128:(tile + 1) * 128],
                                             w2bd[gd % 2][:, fc, :],
                                             start=(fc == 0), stop=(fc == FCN - 1 and j != 0))
                                k += 1
                            if j == 0:
                                tok0 = (half * 8 + tile) * 128
                                t.matmul(psy[gy % 4][:], GTb[:, tok0:tok0 + 128],
                                         b2rb[:, do * 512:(do + 1) * 512],
                                         start=False, stop=False)
                                i = t.matmul(psy[gy % 4][:], ones2[:],
                                             b2sb[:, do * 512:(do + 1) * 512],
                                             start=False, stop=True)
                            L.inc("p5_ymm", i, 1)

        @block.gpsimd
        def _(g_):
            prol(g_)

            NCK2 = 2 * NJ * FCN * 2

            def issue_w1(ck2, after=0):
                if ck2 >= NCK2:
                    return
                if after:
                    g_.wait_ge(L.sem("p5_w1h"), after)
                ck, hp = divmod(ck2, 2)
                jj2, fc = divmod(ck, FCN)
                j = jj2 % NJ
                src = bass.AP(w1h.tensor,
                              j * D * FF + hp * 8 * 128 * FF + fc * 128,
                              [[FF, 128], [128 * FF, 8], [1, 128]])
                i = g_.dma_start(out=w1st[ck2 % 2][:], in_=src)
                L.inc(f"p5_w1d{ck2 % 2}", i, 16)

            def cast_w1(jj2):
                for fc in range(FCN):
                    ck = jj2 * FCN + fc
                    if ck >= 2:
                        g_.wait_ge(L.sem("p5_hmm"), 2 * (ck - 1))
                    for hp in range(2):
                        ck2 = ck * 2 + hp
                        g_.wait_ge(L.sem(f"p5_w1d{ck2 % 2}"), 16 * (ck2 // 2 + 1))
                        i = g_.tensor_copy(out=w1bf[ck % 2][:, hp * 8:(hp + 1) * 8, :],
                                           in_=w1st[ck2 % 2][:])
                        L.inc("p5_w1h", i, 1)
                        issue_w1(ck2 + 2, after=L.n("p5_w1h"))

            def acc_out(half):
                g_.wait_ge(L.sem("p5_yev"), (half + 1) * NJ * 32)
                for tile in range(8):
                    i = g_.dma_start(out=acc_d[(half * 8 + tile) * 128:
                                               (half * 8 + tile + 1) * 128, :],
                                     in_=acc[:, tile, :])
                    L.inc("p5_acw", i, 16)

            issue_w1(0)
            issue_w1(1)
            cast_w1(0)
            for jj2 in range(2 * NJ):
                if jj2 == NJ:
                    acc_out(0)
                if jj2 + 1 < 2 * NJ:
                    cast_w1(jj2 + 1)
            acc_out(1)

        barrier(["p5_w1d0", "p5_w1d1", "p5_w1h", "p5_w2d0",
                 "p5_w2d1", "p5_w2c", "p5_hmm", "p5_hev", "p5_ymm", "p5_yev",
                 "p5_acw"])

    sA.close()

    if _MAX_PHASE >= 4:
      aoS = ExitStack()
      aoT = aoS.enter_context(nc.sbuf_tensor("aoT", [128, H, T], BF16))
      with ExitStack() as p4:
        masks = p4.enter_context(nc.sbuf_tensor("masks", [128, 25, 128], BF16))
        masks32 = p4.enter_context(nc.sbuf_tensor("masks32", [128, 25 * 128], F32))
        kts = [p4.enter_context(nc.sbuf_tensor(f"kts{i}", [128, H * 128], BF16)) for i in range(6)]
        vs = [p4.enter_context(nc.sbuf_tensor(f"vs{i}", [128, D], BF16)) for i in range(6)]
        qts = [p4.enter_context(nc.sbuf_tensor(f"qts{i}", [128, H * 128], BF16)) for i in range(3)]
        pa = [p4.enter_context(nc.sbuf_tensor(f"pa{i}", [128, 640], BF16)) for i in range(2)]
        panb = [p4.enter_context(nc.sbuf_tensor(f"panb{i}", [128, 640], BF16)) for i in range(2)]
        ptb = [p4.enter_context(nc.sbuf_tensor(f"ptb{i}", [128, 640], BF16)) for i in range(2)]
        negmb = [p4.enter_context(nc.sbuf_tensor(f"negmb{i}", [128, 1], F32)) for i in range(2)]
        sumb = p4.enter_context(nc.sbuf_tensor("sumb", [128, 1], F32))
        rcb = p4.enter_context(nc.sbuf_tensor("rcb", [128, 1], F32))
        pss = [p4.enter_context(nc.psum_tensor(f"pss{i}", [128, 640], F32)) for i in range(2)]
        pspt = [p4.enter_context(nc.psum_tensor(f"pspt{i}", [128, 640], BF16)) for i in range(2)]
        psao = [p4.enter_context(nc.psum_tensor(f"psao{i}", [128, 128], F32)) for i in range(2)]

        NIDX = NT * H

        @block.gpsimd
        def _(g_):
            prol(g_)
            g_.wait_ge(L.sem("p4_mk"), 16)
            i = g_.tensor_copy(out=masks[:].rearrange("p a t -> p (a t)"), in_=masks32[:])
            L.inc("p4_mkc", i, 1)
            for kt in range(NTA):
                if kt >= 6:
                    g_.wait_ge(L.sem("p4_avm"), 16 * (kt - 5))
                sl = kt % 6
                i = g_.dma_start(out=kts[sl][:], in_=kt_d[kt * 128:(kt + 1) * 128, :])
                L.inc(f"p4_kv{sl}", i, 16)
                i = g_.dma_start(out=vs[sl][:], in_=v_d[kt * 128:(kt + 1) * 128, :])
                L.inc(f"p4_kv{sl}", i, 16)

        @block.sync
        def _(sy):
            prol(sy)
            i = sy.dma_start(out=masks32[:], in_=maskh)
            L.inc("p4_mk", i, 16)
            for qt in range(NT):
                if qt >= 3:
                    sy.wait_ge(L.sem("p4_smm"), 16 * (qt - 2))
                i = sy.dma_start(out=qts[qt % 3][:], in_=qt_d[qt * 128:(qt + 1) * 128, :])
                L.inc(f"p4_qt{qt % 3}", i, 16)

        def pe_T(t, j):
            t.wait_ge(L.sem("p4_pan"), j + 1)
            if j >= 2:
                t.wait_ge(L.sem("p4_pte"), j - 1)
            for d in range(5):
                i = t.transpose(out=pspt[j % 2][:, d * 128:(d + 1) * 128],
                                in_=panb[j % 2][:, d * 128:(d + 1) * 128],
                                identity=idenb[:])
            L.inc("p4_tmm", i, 1)

        def pe_AV(t, j):
            qt, h = divmod(j, H)
            t.wait_ge(L.sem("p4_pte"), j + 1)
            if j >= 2:
                t.wait_ge(L.sem("p4_aoe"), j - 1)
            for d in range(5):
                sl = (qt + d) % 6
                i = t.matmul(psao[j % 2][:], vs[sl][:, h * 128:(h + 1) * 128],
                             ptb[j % 2][:, d * 128:(d + 1) * 128],
                             start=(d == 0), stop=(d == 4))
            L.inc("p4_avm", i, 1)

        @block.tensor
        def _(t):
            prol(t)
            for idx in range(NIDX):
                qt, h = divmod(idx, H)
                if idx == 0:
                    t.wait_ge(L.sem("p4_mkc"), 1)
                if h == 0:
                    for kt in range(qt, qt + 5):
                        t.wait_ge(L.sem(f"p4_kv{kt % 6}"), 32 * (kt // 6 + 1))
                    t.wait_ge(L.sem(f"p4_qt{qt % 3}"), 16 * (qt // 3 + 1))
                if idx >= 2:
                    t.wait_ge(L.sem("p4_mx"), idx - 1)
                    t.wait_ge(L.sem("p4_ex"), idx - 1)
                sl = idx % 2
                for d in range(5):
                    ks = (qt + d) % 6
                    t.matmul(pss[sl][:, d * 128:(d + 1) * 128],
                             qts[qt % 3][:, h * 128:(h + 1) * 128],
                             kts[ks][:, h * 128:(h + 1) * 128],
                             start=(d == 0 or d == 4), stop=False)
                dlist = range(5) if qt < 4 else (0, 4)
                tc = qt if qt < 4 else 4
                for d in dlist:
                    last_b0 = (d == max(x for x in dlist if x < 4))
                    i = t.matmul(pss[sl][:, d * 128:(d + 1) * 128], idenb[:],
                                 masks[:, tc * 5 + d, :],
                                 start=False, stop=(last_b0 or d == 4))
                L.inc("p4_smm", i, 1)
                if idx >= 2:
                    pe_T(t, idx - 2)
                    pe_AV(t, idx - 2)
            for j in (NIDX - 2, NIDX - 1):
                pe_T(t, j)
                pe_AV(t, j)

        @block.vector
        def _(v):
            prol(v)
            for idx in range(NIDX):
                v.wait_ge(L.sem("p4_smm"), idx + 1)
                i = v.reduce_max(out=negmb[idx % 2][:], in_=pss[idx % 2][:],
                                 axis=AX.X, negate=True)
                L.inc("p4_mx", i, 1)
                if idx >= 1:
                    j = idx - 1
                    v.wait_ge(L.sem("p4_ex"), j + 1)
                    v.reduce_sum(out=sumb[:], in_=pa[j % 2][:], axis=AX.X)
                    v.drain()
                    v.reciprocal(out=rcb[:], in_=sumb[:])
                    v.drain()
                    if j >= 2:
                        v.wait_ge(L.sem("p4_tmm"), j - 1)
                    i = v.tensor_single_scalar(out=panb[j % 2][:], in_=pa[j % 2][:],
                                               scalar=rcb[:, 0:1], op=ALU.mult)
                    L.inc("p4_pan", i, 1)
            j = NIDX - 1
            v.wait_ge(L.sem("p4_ex"), j + 1)
            v.reduce_sum(out=sumb[:], in_=pa[j % 2][:], axis=AX.X)
            v.drain()
            v.reciprocal(out=rcb[:], in_=sumb[:])
            v.drain()
            v.wait_ge(L.sem("p4_tmm"), j - 1)
            i = v.tensor_single_scalar(out=panb[j % 2][:], in_=pa[j % 2][:],
                                       scalar=rcb[:, 0:1], op=ALU.mult)
            L.inc("p4_pan", i, 1)

        def act_tail(s, j):
            qt, h = divmod(j, H)
            s.wait_ge(L.sem("p4_tmm"), j + 1)
            if j >= 2:
                s.wait_ge(L.sem("p4_avm"), j - 1)
            i = s.copy(out=ptb[j % 2][:], in_=pspt[j % 2][:])
            L.inc("p4_pte", i, 1)
            s.wait_ge(L.sem("p4_avm"), j + 1)
            i = s.copy(out=aoT[:, h, qt * 128:(qt + 1) * 128], in_=psao[j % 2][:])
            L.inc("p4_aoe", i, 1)

        @block.scalar
        def _(s):
            prol(s)
            for idx in range(NIDX):
                s.wait_ge(L.sem("p4_mx"), idx + 1)
                if idx >= 2:
                    s.wait_ge(L.sem("p4_pan"), idx - 1)
                i = s.activation(out=pa[idx % 2][:], in_=pss[idx % 2][:], func=AF.Exp,
                                 bias=negmb[idx % 2][:, 0:1])
                L.inc("p4_ex", i, 1)
                if idx >= 2:
                    act_tail(s, idx - 2)
            for j in (NIDX - 2, NIDX - 1):
                act_tail(s, j)

        barrier([f"p4_kv{s}" for s in range(6)] + [f"p4_qt{s}" for s in range(3)]
                + ["p4_smm", "p4_mx", "p4_ex", "p4_pan",
                   "p4_tmm", "p4_pte", "p4_avm", "p4_aoe"])

      with ExitStack() as p4w:
        wost = [p4w.enter_context(nc.sbuf_tensor(f"wost{i}", [128, 512], F32)) for i in range(2)]
        wodb = [p4w.enter_context(nc.sbuf_tensor(f"wodb{i}", [128, H, 512], BF16)) for i in range(2)]
        sstg = [p4w.enter_context(nc.sbuf_tensor(f"sstg{i}", [128, 512], BF16)) for i in range(2)]
        pss2 = [p4w.enter_context(nc.psum_tensor(f"pss2_{i}", [128, 512], F32)) for i in range(2)]

        @block.sync
        def _(sy):
            prol(sy)
            for cwo in range(4 * H):
                if cwo >= 2:
                    sy.wait_ge(L.sem("p4w_wc"), cwo - 1)
                do, h2 = divmod(cwo, H)
                src = bass.AP(woT.tensor, h2 * 128 * D + do * 512, [[D, 128], [1, 512]])
                i = sy.dma_start(out=wost[cwo % 2][:], in_=src)
                L.inc(f"p4w_wd{cwo % 2}", i, 16)

        @block.gpsimd
        def _(g_):
            prol(g_)

            def emit_sw(gw):
                do_, tile_ = divmod(gw, NT)
                g_.wait_ge(L.sem("p4w_ev"), gw + 1)
                dsta = bass.AP(s_d.tensor, tile_ * 128 * D + do_ * 512,
                               [[D, 128], [1, 512]])
                i = g_.dma_start(out=dsta, in_=sstg[gw % 2][:])
                L.inc(f"p4w_sw{gw % 2}", i, 16)

            for do in range(4):
                if do >= 1:
                    for gw in range((do - 1) * NT, do * NT):
                        emit_sw(gw)
            for gw in range(3 * NT, 4 * NT):
                emit_sw(gw)

        @block.tensor
        def _(t):
            prol(t)
            for gw in range(4 * NT):
                do, tile = divmod(gw, NT)
                if gw >= 2:
                    t.wait_ge(L.sem("p4w_ev"), gw - 1)
                for h2 in range(H):
                    if tile == 0:
                        t.wait_ge(L.sem("p4w_wc"), do * H + h2 + 1)
                    i = t.matmul(pss2[gw % 2][:], aoT[:, h2, tile * 128:(tile + 1) * 128],
                                 wodb[do % 2][:, h2, :], start=(h2 == 0), stop=(h2 == H - 1))
                L.inc("p4w_mm", i, 1)

        @block.scalar
        def _(s):
            prol(s)
            for do in range(4):
                for h2 in range(H):
                    cwo = do * H + h2
                    s.wait_ge(L.sem(f"p4w_wd{cwo % 2}"), 16 * (cwo // 2 + 1))
                    if do >= 2:
                        s.wait_ge(L.sem("p4w_mm"), NT * (do - 1))
                    i = s.activation(out=wodb[do % 2][:, h2, :], in_=wost[cwo % 2][:],
                                     func=AF.Copy)
                    L.inc("p4w_wc", i, 1)
                for gw in range(do * NT, (do + 1) * NT):
                    s.wait_ge(L.sem("p4w_mm"), gw + 1)
                    if gw >= 2:
                        s.wait_ge(L.sem(f"p4w_sw{gw % 2}"), 16 * ((gw - 2) // 2 + 1))
                    i = s.copy(out=sstg[gw % 2][:], in_=pss2[gw % 2][:])
                    L.inc("p4w_ev", i, 1)

        barrier(["p4w_wd0", "p4w_wd1", "p4w_wc", "p4w_mm", "p4w_ev",
                 "p4w_sw0", "p4w_sw1"])
      aoS.close()

    if _MAX_PHASE >= 5:
      with ExitStack() as p6:
        xb6 = [p6.enter_context(nc.sbuf_tensor(f"xb6_{i}", [128, D], F32)) for i in range(2)]
        ab6 = [p6.enter_context(nc.sbuf_tensor(f"ab6_{i}", [128, D], F32)) for i in range(2)]
        sb6 = [p6.enter_context(nc.sbuf_tensor(f"sb6_{i}", [128, D], BF16)) for i in range(2)]
        xn6 = p6.enter_context(nc.sbuf_tensor("xn6", [128, D], F32))
        ffb = p6.enter_context(nc.sbuf_tensor("ffb", [128, D], F32))
        g6b = p6.enter_context(nc.sbuf_tensor("g6b", [128, D], F32))
        sgb6 = p6.enter_context(nc.sbuf_tensor("sgb6", [128, D], F32))
        outb = [p6.enter_context(nc.sbuf_tensor(f"outb{i}", [128, D], F32)) for i in range(2)]

        @block.sync
        def _(sy):
            prol(sy)
            for tt in range(NT):
                if tt >= 2:
                    sy.wait_ge(L.sem("p6_ot"), tt - 1)
                i = sy.dma_start(out=xb6[tt % 2][:], in_=x_own[tt * 128:(tt + 1) * 128, :])
                L.inc(f"p6_xd{tt % 2}", i, 16)
                i = sy.dma_start(out=ab6[tt % 2][:], in_=acc_d[tt * 128:(tt + 1) * 128, :])
                L.inc(f"p6_ad{tt % 2}", i, 16)

        @block.gpsimd
        def _(g_):
            prol(g_)

            def emit_ow(tt):
                g_.wait_ge(L.sem("p6_ot"), tt + 1)
                i = g_.dma_start(out=outh[tt * 128:(tt + 1) * 128, :], in_=outb[tt % 2][:])
                L.inc(f"p6_ow{tt % 2}", i, 16)

            for tt in range(NT):
                if tt >= 2:
                    g_.wait_ge(L.sem("p6_ot"), tt - 1)
                i = g_.dma_start(out=sb6[tt % 2][:], in_=s_d[tt * 128:(tt + 1) * 128, :])
                L.inc(f"p6_sd{tt % 2}", i, 16)
                if tt >= 2:
                    emit_ow(tt - 2)
            emit_ow(NT - 2)
            emit_ow(NT - 1)
            g_.wait_ge(L.sem("p6_ow0"), 16 * (NT // 2))
            g_.wait_ge(L.sem("p6_ow1"), 16 * (NT // 2))

        @block.vector
        def _(v):
            prol(v)
            for tt in range(NT):
                st = tt + 4
                v.wait_ge(L.sem(f"p6_xd{tt % 2}"), 16 * (tt // 2 + 1))
                v.wait_ge(L.sem(f"p6_ad{tt % 2}"), 16 * (tt // 2 + 1))
                v.tensor_scalar(out=xn6[:], in0=xb6[tt % 2][:],
                                scalar1=stats[:, st, 0:1], scalar2=stats[:, st, 1:2],
                                op0=ALU.subtract, op1=ALU.mult)
                v.drain()
                if tt >= 1:
                    v.wait_ge(L.sem("p6_sl"), tt)
                i = v.tensor_tensor(out=ffb[:], in0=xn6[:], in1=ab6[tt % 2][:], op=ALU.add)
                L.inc("p6_ff", i, 1)
                v.wait_ge(L.sem("p6_sl"), tt + 1)
                v.wait_ge(L.sem(f"p6_sd{tt % 2}"), 16 * (tt // 2 + 1))
                v.tensor_tensor(out=sgb6[:], in0=sb6[tt % 2][:], in1=g6b[:], op=ALU.mult)
                v.drain()
                if tt >= 2:
                    v.wait_ge(L.sem(f"p6_ow{tt % 2}"), 16 * ((tt - 2) // 2 + 1))
                i = v.tensor_tensor(out=outb[tt % 2][:], in0=xb6[tt % 2][:],
                                    in1=sgb6[:], op=ALU.add)
                L.inc("p6_ot", i, 1)

        @block.scalar
        def _(s):
            prol(s)
            for tt in range(NT):
                s.wait_ge(L.sem("p6_ff"), tt + 1)
                i = s.activation(out=g6b[:], in_=ffb[:], func=AF.Silu)
                L.inc("p6_sl", i, 1)

        @block.sync
        def _(sy):
            sy.wait_ge(L.sem("p6_ow0"), 16 * (NT // 2))
            sy.wait_ge(L.sem("p6_ow1"), 16 * (NT // 2))

    gctx.close()
    return nc


def _host_constants(hf):
    P0 = hf * 2048
    pos = np.maximum(P0 - HALO + np.arange(TK), 0).astype(np.float64)
    inv = 10000.0 ** (-(np.arange(0, DH, 2, dtype=np.float64)) / DH)
    ang = pos[None, :] * inv[:, None]
    c = np.cos(ang)
    sn = np.sin(ang)
    cosf = np.empty((128, TK), np.float32)
    sinf = np.empty((128, TK), np.float32)
    cosf[0::2] = c
    cosf[1::2] = c
    sinf[0::2] = -sn
    sinf[1::2] = sn

    ii = np.arange(128)[:, None]
    jj = np.arange(128)[None, :]
    masks = np.zeros((128, 25, 128), np.float32)
    for tc in range(5):
        for d in range(5):
            m = np.zeros((128, 128), np.float32)
            if d == 0:
                m[~(jj > ii) * np.ones((128, 128), bool)] = 0.0
                m = np.where(jj > ii, 0.0, -1e9).astype(np.float32)
            elif d == 4:
                m = np.where(jj <= ii, 0.0, -1e9).astype(np.float32)
            if hf == 0 and tc < 4 and tc + d <= 3:
                m[:] = -1e9
            masks[:, tc * 5 + d, :] = m
    masksb = masks.reshape(128, 25 * 128)
    return cosf, sinf, masksb


def prep_inputs(L0_attn_x, L0_attn_Wq, L0_attn_Wk, L0_attn_Wv, L0_attn_Wo,
                L0_ffn_shared_w1, L0_ffn_shared_b1, L0_ffn_shared_w2, L0_ffn_shared_b2,
                L0_ffn_routed_w1, L0_ffn_routed_b1, L0_ffn_routed_w2, L0_ffn_routed_b2,
                L0_ffn_centroids, L0_ffn_bias):
    x = np.ascontiguousarray(np.asarray(L0_attn_x, dtype=np.float32))
    wqT = np.ascontiguousarray(np.asarray(L0_attn_Wq, np.float32).T)
    wkT = np.ascontiguousarray(np.asarray(L0_attn_Wk, np.float32).T)
    wvT = np.ascontiguousarray(np.asarray(L0_attn_Wv, np.float32).T)
    woT = np.ascontiguousarray(np.asarray(L0_attn_Wo, np.float32).T)
    w1 = np.ascontiguousarray(np.concatenate([np.asarray(L0_ffn_shared_w1, np.float32),
                                              np.asarray(L0_ffn_routed_w1, np.float32)], 0))
    w2 = np.ascontiguousarray(np.concatenate([np.asarray(L0_ffn_shared_w2, np.float32),
                                              np.asarray(L0_ffn_routed_w2, np.float32)], 0))
    b1all = np.concatenate([np.asarray(L0_ffn_shared_b1, np.float32),
                            np.asarray(L0_ffn_routed_b1, np.float32)], 0)
    b1c = np.ascontiguousarray(b1all.reshape(NE, FCN, 128).transpose(2, 0, 1)
                               .reshape(128, NE * FCN))
    b2r = np.ascontiguousarray(np.asarray(L0_ffn_routed_b2, np.float32))
    b2s = np.ascontiguousarray(np.asarray(L0_ffn_shared_b2, np.float32))
    cen = np.asarray(L0_ffn_centroids, np.float32)
    cenT = np.ascontiguousarray(cen.T.reshape(DC, 128, 16).transpose(1, 0, 2)
                                .reshape(128, DC * 16))
    gbias = np.ascontiguousarray(np.asarray(L0_ffn_bias, np.float32).reshape(1, 16))
    p128 = np.zeros((128, 128), np.float32)
    for i in range(64):
        p128[2 * i, 2 * i + 1] = 1.0
        p128[2 * i + 1, 2 * i] = 1.0
    iden = np.eye(128, dtype=np.float32)
    consts = {hf: _host_constants(hf) for hf in range(2)}

    in_maps = []
    for c in range(8):
        b, hf = divmod(c, 2)
        P0 = hf * 2048
        cosf, sinf, masksb = consts[hf]
        x_own = np.ascontiguousarray(x[b, P0:P0 + T])
        x_halo = np.ascontiguousarray(x[b, P0 - HALO:P0]) if hf == 1 else \
            np.zeros((HALO, D), np.float32)
        in_maps.append(dict(
            x_own=x_own, x_halo=x_halo, wqT=wqT, wkT=wkT, wvT=wvT, woT=woT,
            w1=w1, w2=w2, b1c=b1c, b2r=b2r, b2s=b2s, cenT=cenT, gbias=gbias,
            cosf=cosf, sinf=sinf, p128=p128, iden=iden, masksb=masksb))
    return in_maps


def kernel(**inputs):
    in_maps = prep_inputs(**inputs)
    nc = build_program()
    res = run_bass_kernel_spmd(nc, in_maps, core_ids=list(range(8)))
    out = np.empty((4, 4096, D), np.float32)
    for c in range(8):
        b, hf = divmod(c, 2)
        out[b, hf * 2048:(hf + 1) * 2048] = res.results[c]["out"]
    return out


# version 09
# speedup vs baseline: 1.2935x; 1.1771x over previous
import numpy as np
import ml_dtypes
from contextlib import ExitStack

import concourse.bass as bass
import concourse.mybir as mybir
from concourse.bass_utils import run_bass_kernel_spmd

F32 = mybir.dt.float32
BF16 = mybir.dt.bfloat16
AF = mybir.ActivationFunctionType
ALU = mybir.AluOpType
AX = mybir.AxisListType

D = 2048
DC = 16
T = 2048
HALO = 512
TK = 2560
NT = 16
NTA = 20
H = 16
DH = 128
FF = 1408
FCN = 11
NE = 18
SCALE = float(128 ** -0.5)
EPS = 1e-5

_DEBUG_TAPS = set()
_MAX_PHASE = 99
_SKIP_QK = False
_SKIP_V = False
_QK_NCK = 99


class Ledger:
    def __init__(self, nc, ctx):
        self.nc, self.ctx = nc, ctx
        self.h = {}
        self.c = {}

    def sem(self, name):
        if name not in self.h:
            self.h[name] = self.ctx.enter_context(self.nc.semaphore(name))
            self.c[name] = 0
        return self.h[name]

    def inc(self, name, instr, k):
        instr.then_inc(self.sem(name), k)
        self.c[name] += k

    def n(self, name):
        return self.c.get(name, 0)


def build_program():
    nc = bass.Bass("TRN2", target_bir_lowering=False)

    x_own = nc.dram_tensor("x_own", [T, D], F32, kind="ExternalInput").ap()
    x_halo = nc.dram_tensor("x_halo", [HALO, D], F32, kind="ExternalInput").ap()
    wqT = nc.dram_tensor("wqT", [D, D], F32, kind="ExternalInput").ap()
    wkT = nc.dram_tensor("wkT", [D, D], F32, kind="ExternalInput").ap()
    wvT = nc.dram_tensor("wvT", [D, D], F32, kind="ExternalInput").ap()
    woT = nc.dram_tensor("woT", [D, D], F32, kind="ExternalInput").ap()
    w1h = nc.dram_tensor("w1", [NE, D, FF], F32, kind="ExternalInput").ap()
    w2h = nc.dram_tensor("w2", [NE, FF, D], F32, kind="ExternalInput").ap()
    b1ch = nc.dram_tensor("b1c", [128, NE * FCN], F32, kind="ExternalInput").ap()
    b2rh = nc.dram_tensor("b2r", [16, D], F32, kind="ExternalInput").ap()
    b2sh = nc.dram_tensor("b2s", [2, D], F32, kind="ExternalInput").ap()
    cenh = nc.dram_tensor("cenT", [128, DC * 16], F32, kind="ExternalInput").ap()
    gbh = nc.dram_tensor("gbias", [1, 16], F32, kind="ExternalInput").ap()
    cosh = nc.dram_tensor("cosf", [128, TK], F32, kind="ExternalInput").ap()
    sinh = nc.dram_tensor("sinf", [128, TK], F32, kind="ExternalInput").ap()
    p128h = nc.dram_tensor("p128", [128, 128], F32, kind="ExternalInput").ap()
    idh = nc.dram_tensor("iden", [128, 128], F32, kind="ExternalInput").ap()
    maskh = nc.dram_tensor("masksb", [128, 25 * 128], F32, kind="ExternalInput").ap()

    outh = nc.dram_tensor("out", [T, D], F32, kind="ExternalOutput").ap()

    def skind(nm):
        return "ExternalOutput" if nm in _DEBUG_TAPS else "Internal"

    qt_d = nc.dram_tensor("qt_d", [NT * 128, H * 128], BF16, kind=skind("qt_d")).ap()
    kt_d = nc.dram_tensor("kt_d", [NTA * 128, H * 128], BF16, kind=skind("kt_d")).ap()
    v_d = nc.dram_tensor("v_d", [TK, D], BF16, kind=skind("v_d")).ap()
    s_d = nc.dram_tensor("s_d", [T, D], BF16, kind=skind("s_d")).ap()
    acc_d = nc.dram_tensor("acc_d", [T, D], F32, kind=skind("acc_d")).ap()
    xnt_dbg = g_dbg = None
    if "xnt_d" in _DEBUG_TAPS:
        xnt_dbg = nc.dram_tensor("xnt_d", [128, DC * T], BF16, kind="ExternalOutput").ap()
    if "g_d" in _DEBUG_TAPS:
        g_dbg = nc.dram_tensor("g_d", [128, NT * 16], F32, kind="ExternalOutput").ap()
    st_dbg = None
    if "st_d" in _DEBUG_TAPS:
        st_dbg = nc.dram_tensor("st_d", [128, NTA * 2], F32, kind="ExternalOutput").ap()

    gctx = ExitStack()
    L = Ledger(nc, gctx)
    PH = L.sem("ph")
    pno = [0]

    idenb = gctx.enter_context(nc.sbuf_tensor("idenb", [128, 128], BF16))
    p128b = gctx.enter_context(nc.sbuf_tensor("p128b", [128, 128], BF16))
    stats = gctx.enter_context(nc.sbuf_tensor("stats", [128, NTA, 2], F32))
    b1c = gctx.enter_context(nc.sbuf_tensor("b1c_s", [128, NE * FCN], F32))
    gbias = gctx.enter_context(nc.sbuf_tensor("gbias_s", [1, 16], F32))
    bplane = gctx.enter_context(nc.sbuf_tensor("bplane", [128, 16], F32))
    ones2 = gctx.enter_context(nc.sbuf_tensor("ones2", [2, 128], BF16))
    Gm = gctx.enter_context(nc.sbuf_tensor("Gm", [128, NT, 16], F32))
    GTb = gctx.enter_context(nc.sbuf_tensor("GTb", [16, T], BF16))
    b2rb = gctx.enter_context(nc.sbuf_tensor("b2rb", [16, D], BF16))
    b2sb = gctx.enter_context(nc.sbuf_tensor("b2sb", [2, D], BF16))
    epsb = gctx.enter_context(nc.sbuf_tensor("epsb", [128, 1], F32))

    sA = ExitStack()
    xnT = sA.enter_context(nc.sbuf_tensor("xnT", [128, DC, T], BF16))
    sB = ExitStack()
    xnTh = sB.enter_context(nc.sbuf_tensor("xnTh", [128, DC, HALO], BF16))
    cosf = sB.enter_context(nc.sbuf_tensor("cosf_s", [128, TK], F32))
    sinf = sB.enter_context(nc.sbuf_tensor("sinf_s", [128, TK], F32))
    iden32 = sB.enter_context(nc.sbuf_tensor("iden32", [128, 128], F32))
    cenT = sB.enter_context(nc.sbuf_tensor("cenT_s", [128, DC, 16], F32))
    ones1 = sB.enter_context(nc.sbuf_tensor("ones1", [1, 128], F32))

    block = gctx.enter_context(nc.Block())

    def barrier(names):
        pno[0] += 1

        @block.sync
        def _(sy):
            for s in names:
                sy.wait_ge(L.sem(s), L.n(s))
            sy.sem_inc(PH, 1)
        L.c["ph"] += 1

    def prol(eng):
        eng.wait_ge(PH, pno[0])

    with ExitStack() as p0:
        b2r32 = p0.enter_context(nc.sbuf_tensor("b2r32", [16, D], F32))
        b2s32 = p0.enter_context(nc.sbuf_tensor("b2s32", [2, D], F32))
        p128s = p0.enter_context(nc.sbuf_tensor("p128s", [128, 128], F32))
        psbp = p0.enter_context(nc.psum_tensor("psbp", [128, 16], F32))

        @block.sync
        def _(sy):
            for dst, src in [(iden32, idh), (b1c, b1ch),
                             (gbias, gbh), (b2r32, b2rh), (b2s32, b2sh),
                             (p128s, p128h)]:
                i = sy.dma_start(out=dst[:], in_=src)
                L.inc("c_dma", i, 16)
            i = sy.dma_start(out=cenT[:], in_=cenh.rearrange("p (dc j) -> p dc j", dc=DC))
            L.inc("c_dma", i, 16)

        @block.gpsimd
        def _(g):
            g.wait_ge(L.sem("c_dma"), L.n("c_dma"))
            g.memset(ones1[:], 1.0)
            g.memset(ones2[:], 1.0)
            g.memset(epsb[:], EPS)
            g.tensor_copy(out=idenb[:], in_=iden32[:])
            g.tensor_copy(out=b2rb[:], in_=b2r32[:])
            g.tensor_copy(out=b2sb[:], in_=b2s32[:])
            i = g.tensor_copy(out=p128b[:], in_=p128s[:])
            L.inc("c_cast", i, 1)

        @block.tensor
        def _(t):
            t.wait_ge(L.sem("c_cast"), 1)
            i = t.matmul(psbp[:], ones1[:], gbias[:], start=True, stop=True)
            L.inc("c_bp", i, 1)

        @block.scalar
        def _(s):
            s.wait_ge(L.sem("c_bp"), 1)
            i = s.copy(out=bplane[:], in_=psbp[:])
            L.inc("c_bpe", i, 1)

        barrier(["c_dma", "c_cast", "c_bpe"])

    def xnT_sl(tt, dc, lo=0, hi=128):
        if tt < 4:
            return xnTh[:, dc, tt * 128 + lo:tt * 128 + hi]
        return xnT[:, dc, (tt - 4) * 128 + lo:(tt - 4) * 128 + hi]

    def xnT_grp(dc, g):
        if g == 0:
            return xnTh[:, dc, :]
        return xnT[:, dc, (g - 1) * 512:g * 512]

    with ExitStack() as p1:
        xb = [p1.enter_context(nc.sbuf_tensor(f"xb{i}", [128, D], F32)) for i in range(2)]
        xn32 = [p1.enter_context(nc.sbuf_tensor(f"xn32_{i}", [128, D], F32)) for i in range(2)]
        sqd = p1.enter_context(nc.sbuf_tensor("sqd", [128, D], F32))
        s1b = p1.enter_context(nc.sbuf_tensor("s1b", [128, 1], F32))
        s2b = p1.enter_context(nc.sbuf_tensor("s2b", [128, 1], F32))
        ex2b = p1.enter_context(nc.sbuf_tensor("ex2b", [128, 1], F32))
        nvb = p1.enter_context(nc.sbuf_tensor("nvb", [128, 1], F32))
        sdb = p1.enter_context(nc.sbuf_tensor("sdb", [128, 1], F32))
        xnT32 = p1.enter_context(nc.sbuf_tensor("xnT32", [128, DC, 128], F32))
        sgb = [p1.enter_context(nc.sbuf_tensor(f"sgb{i}", [128, 16], F32)) for i in range(2)]
        scb = p1.enter_context(nc.sbuf_tensor("scb", [128, 16], F32))
        scur = [p1.enter_context(nc.sbuf_tensor(f"scur{i}", [128, 16], F32)) for i in range(2)]
        geb = p1.enter_context(nc.sbuf_tensor("geb", [128, 16], F32))
        gmb = p1.enter_context(nc.sbuf_tensor("gmb", [128, 16], F32))
        mrb = [p1.enter_context(nc.sbuf_tensor(f"mrb{i}", [128, 1], F32)) for i in range(4)]
        msb = p1.enter_context(nc.sbuf_tensor("msb", [128, 1], F32))
        recb = p1.enter_context(nc.sbuf_tensor("recb", [128, 1], F32))
        pst = [p1.enter_context(nc.psum_tensor(f"pst{i}", [128, 128], F32)) for i in range(2)]
        pssc = [p1.enter_context(nc.psum_tensor(f"pssc{i}", [128, 16], F32)) for i in range(2)]
        psgt = p1.enter_context(nc.psum_tensor("psgt", [16, 128], F32))

        @block.sync
        def _(sy):
            prol(sy)
            for tt in range(NTA):
                if tt >= 2:
                    sy.wait_ge(L.sem("p1_xn"), tt - 1)
                src = x_halo[tt * 128:(tt + 1) * 128, :] if tt < 4 else \
                    x_own[(tt - 4) * 128:(tt - 3) * 128, :]
                i = sy.dma_start(out=xb[tt % 2][:], in_=src)
                L.inc(f"p1_xd{tt % 2}", i, 16)

        @block.scalar
        def _(s):
            prol(s)
            for tt in range(NTA):
                ot = tt - 4
                s.wait_ge(L.sem(f"p1_xd{tt % 2}"), 16 * (tt // 2 + 1))
                if tt >= 1:
                    s.wait_ge(L.sem("p1_nv"), tt)
                i = s.activation(out=sqd[:], in_=xb[tt % 2][:], func=AF.Square,
                                 accum_out=s2b[:])
                L.inc("p1_sq", i, 1)
                s.wait_ge(L.sem("p1_nv"), tt + 1)
                if tt >= 1:
                    s.wait_ge(L.sem("p1_rc"), tt)
                i = s.activation(out=sdb[:], in_=nvb[:], func=AF.Sqrt,
                                 bias=epsb[:, 0:1], scale=-1.0)
                L.inc("p1_sd", i, 1)
                for dc in range(16):
                    gk = tt * 16 + dc
                    s.wait_ge(L.sem("p1_tr"), gk + 1)
                    if ot >= 0:
                        s.copy(out=xnT_sl(tt, dc), in_=pst[gk % 2][:])
                        i = s.copy(out=xnT32[:, dc, :], in_=pst[gk % 2][:])
                    else:
                        i = s.copy(out=xnT_sl(tt, dc), in_=pst[gk % 2][:])
                    L.inc("p1_ev", i, 1)
                if ot >= 0:
                    s.wait_ge(L.sem("p1_scm"), ot + 1)
                    if ot >= 2:
                        s.wait_ge(L.sem("p1_g"), ot - 1)
                    i = s.activation(out=sgb[ot % 2][:], in_=pssc[ot % 2][:],
                                     func=AF.Sigmoid)
                    L.inc("p1_sg", i, 1)
                if ot >= 1:
                    s.wait_ge(L.sem("p1_gt"), ot)
                    i = s.copy(out=GTb[:, (ot - 1) * 128:ot * 128], in_=psgt[:])
                    L.inc("p1_gte", i, 1)
            s.wait_ge(L.sem("p1_gt"), NT)
            i = s.copy(out=GTb[:, (NT - 1) * 128:NT * 128], in_=psgt[:])
            L.inc("p1_gte", i, 1)

        @block.vector
        def _(v):
            prol(v)
            for tt in range(NTA):
                ot = tt - 4
                v.wait_ge(L.sem(f"p1_xd{tt % 2}"), 16 * (tt // 2 + 1))
                v.reduce_sum(out=s1b[:], in_=xb[tt % 2][:], axis=AX.X)
                v.drain()
                v.tensor_single_scalar(out=stats[:, tt, 0:1], in_=s1b[:],
                                       scalar=1.0 / D, op=ALU.mult)
                v.wait_ge(L.sem("p1_sq"), tt + 1)
                v.tensor_single_scalar(out=ex2b[:], in_=s2b[:], scalar=1.0 / D,
                                       op=ALU.mult)
                v.drain()
                i = v.scalar_tensor_tensor(out=nvb[:], in0=stats[:, tt, 0:1],
                                           scalar=stats[:, tt, 0:1], in1=ex2b[:],
                                           op0=ALU.mult, op1=ALU.subtract)
                L.inc("p1_nv", i, 1)
                v.wait_ge(L.sem("p1_sd"), tt + 1)
                i = v.reciprocal(out=stats[:, tt, 1:2], in_=sdb[:])
                L.inc("p1_rc", i, 1)
                v.drain()
                if tt >= 2:
                    v.wait_ge(L.sem("p1_tr"), 16 * (tt - 1))
                i = v.tensor_scalar(out=xn32[tt % 2][:], in0=xb[tt % 2][:],
                                    scalar1=stats[:, tt, 0:1], scalar2=stats[:, tt, 1:2],
                                    op0=ALU.subtract, op1=ALU.mult)
                L.inc("p1_xn", i, 1)
                if ot >= 0:
                    v.wait_ge(L.sem("p1_sg"), ot + 1)
                    v.tensor_tensor(out=scb[:], in0=sgb[ot % 2][:], in1=bplane[:], op=ALU.add)
                    v.drain()
                    v.tensor_copy(out=scur[0][:], in_=scb[:])
                    v.drain()
                    v.reduce_max(out=mrb[0][:], in_=scur[0][:], axis=AX.X)
                    v.drain()
                    for r in range(1, 4):
                        v.tensor_single_scalar(out=geb[:], in_=scur[(r - 1) % 2][:],
                                               scalar=mrb[r - 1][:, 0:1], op=ALU.is_ge)
                        v.drain()
                        v.scalar_tensor_tensor(out=scur[r % 2][:], in0=geb[:],
                                               scalar=-2e9, in1=scur[(r - 1) % 2][:],
                                               op0=ALU.mult, op1=ALU.add)
                        v.drain()
                        v.reduce_max(out=mrb[r][:], in_=scur[r % 2][:], axis=AX.X)
                        v.drain()
                    v.tensor_tensor(out=msb[:], in0=mrb[0][:], in1=mrb[1][:], op=ALU.add)
                    v.drain()
                    v.tensor_tensor(out=msb[:], in0=msb[:], in1=mrb[2][:], op=ALU.add)
                    v.drain()
                    v.tensor_tensor(out=msb[:], in0=msb[:], in1=mrb[3][:], op=ALU.add)
                    v.drain()
                    v.tensor_single_scalar(out=msb[:], in_=msb[:], scalar=1e-8, op=ALU.max)
                    v.drain()
                    v.reciprocal(out=recb[:], in_=msb[:])
                    v.tensor_single_scalar(out=geb[:], in_=scb[:], scalar=mrb[3][:, 0:1],
                                           op=ALU.is_ge)
                    v.drain()
                    v.tensor_tensor(out=gmb[:], in0=scb[:], in1=geb[:], op=ALU.mult)
                    v.drain()
                    i = v.tensor_single_scalar(out=Gm[:, ot, :], in_=gmb[:],
                                               scalar=recb[:, 0:1], op=ALU.mult)
                    L.inc("p1_g", i, 1)

        def emit_scores(t, ot):
            t.wait_ge(L.sem("p1_ev"), 16 * (ot + 5))
            if ot >= 2:
                t.wait_ge(L.sem("p1_sg"), ot - 1)
            for dc in range(16):
                i = t.matmul(pssc[ot % 2][:], xnT32[:, dc, :], cenT[:, dc, :],
                             start=(dc == 0), stop=(dc == 15))
            L.inc("p1_scm", i, 1)

        def emit_gt(t, j):
            t.wait_ge(L.sem("p1_g"), j + 1)
            if j >= 1:
                t.wait_ge(L.sem("p1_gte"), j)
            i = t.transpose(out=psgt[:], in_=Gm[:, j, :], identity=iden32[:])
            L.inc("p1_gt", i, 1)

        @block.tensor
        def _(t):
            prol(t)
            for tt in range(NTA):
                ot = tt - 4
                t.wait_ge(L.sem("p1_xn"), tt + 1)
                for dc in range(16):
                    gk = tt * 16 + dc
                    if gk >= 2:
                        t.wait_ge(L.sem("p1_ev"), gk - 1)
                    i = t.transpose(out=pst[gk % 2][:],
                                    in_=xn32[tt % 2][:, dc * 128:(dc + 1) * 128],
                                    identity=iden32[:])
                    L.inc("p1_tr", i, 1)
                if ot >= 0:
                    emit_scores(t, ot)
                if ot >= 1:
                    emit_gt(t, ot - 1)
            emit_gt(t, NT - 1)

        names = ["p1_ev", "p1_g", "p1_gte", "p1_xd0", "p1_xd1", "p1_xn",
                 "p1_scm", "p1_gt"]
        if "xnt_d" in _DEBUG_TAPS or "g_d" in _DEBUG_TAPS:
            @block.gpsimd
            def _(g):
                prol(g)
                g.wait_ge(L.sem("p1_ev"), L.n("p1_ev"))
                g.wait_ge(L.sem("p1_g"), L.n("p1_g"))
                if xnt_dbg is not None:
                    i = g.dma_start(out=xnt_dbg,
                                    in_=xnT[:].rearrange("p dc t -> p (dc t)"))
                    L.inc("p1_dbg", i, 16)
                if g_dbg is not None:
                    i = g.dma_start(out=g_dbg, in_=Gm[:].rearrange("p t j -> p (t j)"))
                    L.inc("p1_dbg", i, 16)
                if st_dbg is not None:
                    g.wait_ge(L.sem("p1_xn"), L.n("p1_xn"))
                    i = g.dma_start(out=st_dbg, in_=stats[:].rearrange("p t k -> p (t k)"))
                    L.inc("p1_dbg", i, 16)
                g.wait_ge(L.sem("p1_dbg"), L.n("p1_dbg"))
            names.append("p1_dbg")
        barrier(names)

    if _MAX_PHASE >= 2:
      with ExitStack() as p2:
        wst = [p2.enter_context(nc.sbuf_tensor(f"wst{i}", [128, DC, 128], F32)) for i in range(2)]
        wbf = [p2.enter_context(nc.sbuf_tensor(f"wbf{i}", [128, DC, 128], BF16)) for i in range(2)]
        qa32 = [p2.enter_context(nc.sbuf_tensor(f"qa32_{i}", [128, 512], F32)) for i in range(2)]
        qab = [p2.enter_context(nc.sbuf_tensor(f"qab{i}", [128, 512], BF16)) for i in range(2)]
        t1b = [p2.enter_context(nc.sbuf_tensor(f"t1b{i}", [128, 512], F32)) for i in range(2)]
        t2b = p2.enter_context(nc.sbuf_tensor("t2b", [128, 512], F32))
        qstg = [p2.enter_context(nc.sbuf_tensor(f"qstg{i}", [128, 512], BF16)) for i in range(2)]
        psq = [p2.enter_context(nc.psum_tensor(f"psq{i}", [128, 512], F32)) for i in range(2)]
        pshf = [p2.enter_context(nc.psum_tensor(f"pshf{i}", [128, 512], F32)) for i in range(2)]

        chunks = []
        groups = []
        for h in range(H):
            chunks.append((wqT, h * 128))
            for g in range(4):
                groups.append((h, g, 0))
        for h in range(H):
            chunks.append((wkT, h * 128))
            for g in range(5):
                groups.append((16 + h, g, 1))
        if _QK_NCK < len(chunks):
            chunks = chunks[:_QK_NCK]
            groups = [g for g in groups if g[0] < _QK_NCK]
        NCK = len(chunks)
        lastg = {}
        firstg = {}
        for gi, (ck, g, part) in enumerate(groups):
            lastg[ck] = gi
            firstg.setdefault(ck, gi)

        def grp_meta(gi):
            ck, g, part = groups[gi]
            h = ck if part == 0 else ck - 16
            if part == 0:
                off = HALO + g * 512
                dst = qt_d
                scale = SCALE
                rhs_g = g + 1
            else:
                off = g * 512
                dst = kt_d
                scale = 1.0
                rhs_g = g
            return ck, h, g, off, dst, scale, rhs_g

        @block.sync
        def _(sy):
            if _SKIP_QK:
                return
            prol(sy)
            i = sy.dma_start(out=cosf[:], in_=cosh)
            L.inc("p2_tb", i, 16)
            i = sy.dma_start(out=sinf[:], in_=sinh)
            L.inc("p2_tb", i, 16)
            for ck in range(NCK):
                if ck >= 2:
                    sy.wait_ge(L.sem("p2_wc"), ck - 1)
                wh, col = chunks[ck]
                src = bass.AP(wh.tensor, col,
                              [[D, 128], [128 * D, DC], [1, 128]])
                i = sy.dma_start(out=wst[ck % 2][:], in_=src)
                L.inc(f"p2_wd{ck % 2}", i, 16)

        gbyck = {}
        for gi, (ck, g, part) in enumerate(groups):
            gbyck.setdefault(ck, []).append(gi)

        def emit_qw(g_, gi):
            ck, h, g, off, dst, scale, rhs_g = grp_meta(gi)
            g_.wait_ge(L.sem("p2_rt"), gi + 1)
            t0 = off if dst is kt_d else g * 512
            dsta = bass.AP(dst.tensor, (t0 // 128) * 128 * (H * 128) + h * 128,
                           [[H * 128, 128], [128 * H * 128, 4], [1, 128]])
            i = g_.dma_start(out=dsta,
                             in_=qstg[gi % 2][:].rearrange("p (a t) -> p a t", a=4))
            L.inc(f"p2_qw{gi % 2}", i, 16)

        @block.gpsimd
        def _(g_):
            if _SKIP_QK:
                return
            prol(g_)
            for ck in range(NCK):
                g_.wait_ge(L.sem(f"p2_wd{ck % 2}"), 16 * (ck // 2 + 1))
                if ck >= 2:
                    g_.wait_ge(L.sem("p2_mm"), lastg[ck - 2] + 1)
                i = g_.tensor_copy(out=wbf[ck % 2][:], in_=wst[ck % 2][:])
                L.inc("p2_wc", i, 1)
                if ck >= 1:
                    for gi in gbyck[ck - 1]:
                        emit_qw(g_, gi)
            for gi in gbyck[NCK - 1]:
                emit_qw(g_, gi)

        @block.tensor
        def _(t):
            if _SKIP_QK:
                return
            prol(t)
            for gi in range(len(groups)):
                ck, h, g, off, dst, scale, rhs_g = grp_meta(gi)
                if gi == firstg[ck]:
                    t.wait_ge(L.sem("p2_wc"), ck + 1)
                if gi >= 2:
                    t.wait_ge(L.sem("p2_ea"), gi - 1)
                    t.wait_ge(L.sem("p2_ev"), gi - 1)
                for dc in range(DC):
                    i = t.matmul(psq[gi % 2][:], wbf[ck % 2][:, dc, :],
                                 xnT_grp(dc, rhs_g), start=(dc == 0), stop=(dc == 15))
                L.inc("p2_mm", i, 1)
                if gi >= 1:
                    j = gi - 1
                    t.wait_ge(L.sem("p2_ev"), j + 1)
                    if j >= 2:
                        t.wait_ge(L.sem("p2_t2"), j - 1)
                    i = t.matmul(pshf[j % 2][:], p128b[:], qab[j % 2][:],
                                 start=True, stop=True)
                    L.inc("p2_sh", i, 1)
            j = len(groups) - 1
            t.wait_ge(L.sem("p2_ev"), j + 1)
            t.wait_ge(L.sem("p2_t2"), j - 1)
            i = t.matmul(pshf[j % 2][:], p128b[:], qab[j % 2][:], start=True, stop=True)
            L.inc("p2_sh", i, 1)

        @block.scalar
        def _(s):
            if _SKIP_QK:
                return
            prol(s)
            for gi in range(len(groups)):
                ck, h, g, off, dst, scale, rhs_g = grp_meta(gi)
                s.wait_ge(L.sem("p2_mm"), gi + 1)
                if gi >= 2:
                    s.wait_ge(L.sem("p2_rt"), gi - 1)
                i = s.activation(out=qa32[gi % 2][:], in_=psq[gi % 2][:],
                                 func=AF.Copy, scale=scale)
                L.inc("p2_ea", i, 1)
                if gi >= 2:
                    s.wait_ge(L.sem("p2_sh"), gi - 1)
                i = s.activation(out=qab[gi % 2][:], in_=psq[gi % 2][:],
                                 func=AF.Copy, scale=scale)
                L.inc("p2_ev", i, 1)

        def rope_finish(v, j):
            _, _, _, off, _, _, _ = grp_meta(j)
            v.wait_ge(L.sem("p2_sh"), j + 1)
            i = v.tensor_tensor(out=t2b[:], in0=pshf[j % 2][:],
                                in1=sinf[:, off:off + 512], op=ALU.mult)
            L.inc("p2_t2", i, 1)
            v.wait_ge(L.sem("p2_ea"), j + 1)
            v.tensor_tensor(out=t1b[0][:], in0=qa32[j % 2][:],
                            in1=cosf[:, off:off + 512], op=ALU.mult)
            v.drain()
            if j >= 2:
                v.wait_ge(L.sem(f"p2_qw{j % 2}"), 16 * ((j - 2) // 2 + 1))
            i = v.tensor_tensor(out=qstg[j % 2][:], in0=t1b[0][:],
                                in1=t2b[:], op=ALU.add)
            L.inc("p2_rt", i, 1)
            v.drain()

        @block.vector
        def _(v):
            if _SKIP_QK:
                return
            prol(v)
            v.wait_ge(L.sem("p2_tb"), 32)
            for gi in range(len(groups)):
                if gi >= 1:
                    rope_finish(v, gi - 1)
            rope_finish(v, len(groups) - 1)

        barrier(["p2_tb", "p2_wd0", "p2_wd1", "p2_wc", "p2_mm", "p2_ea",
                 "p2_ev", "p2_sh", "p2_t2", "p2_rt", "p2_qw0", "p2_qw1"])

        vst = [p2.enter_context(nc.sbuf_tensor(f"vst{i}", [128, 512], F32)) for i in range(2)]
        wvbf = p2.enter_context(nc.sbuf_tensor("wvbf", [128, DC, 512], BF16))
        vstg = [p2.enter_context(nc.sbuf_tensor(f"vstg{i}", [128, 512], BF16)) for i in range(2)]
        psv = [p2.enter_context(nc.psum_tensor(f"psv{i}", [128, 512], F32)) for i in range(2)]

        @block.sync
        def _(sy):
            if _SKIP_V:
                return
            prol(sy)
            for vck in range(4 * DC):
                do, dcc = divmod(vck, DC)
                if vck >= 2:
                    sy.wait_ge(L.sem("p2v_wc"), vck - 1)
                src = bass.AP(wvT.tensor, dcc * 128 * D + do * 512,
                              [[D, 128], [1, 512]])
                i = sy.dma_start(out=vst[vck % 2][:], in_=src)
                L.inc(f"p2v_wd{vck % 2}", i, 16)

        @block.gpsimd
        def _(g_):
            if _SKIP_V:
                return
            prol(g_)

            def emit_vw(gv):
                do_, tt_ = divmod(gv, NTA)
                g_.wait_ge(L.sem("p2v_ev"), gv + 1)
                dsta = bass.AP(v_d.tensor, tt_ * 128 * D + do_ * 512,
                               [[D, 128], [1, 512]])
                i = g_.dma_start(out=dsta, in_=vstg[gv % 2][:])
                L.inc(f"p2v_vw{gv % 2}", i, 16)

            for do in range(4):
                if do >= 1:
                    for gv in range((do - 1) * NTA, do * NTA):
                        emit_vw(gv)
                for dcc in range(DC):
                    vck = do * DC + dcc
                    g_.wait_ge(L.sem(f"p2v_wd{vck % 2}"), 16 * (vck // 2 + 1))
                    if do >= 1:
                        g_.wait_ge(L.sem("p2v_mm"), NTA * do)
                    i = g_.tensor_copy(out=wvbf[:, dcc, :], in_=vst[vck % 2][:])
                    L.inc("p2v_wc", i, 1)
            for gv in range(3 * NTA, 4 * NTA):
                emit_vw(gv)

        @block.tensor
        def _(t):
            if _SKIP_V:
                return
            prol(t)
            for gv in range(4 * NTA):
                do, tt = divmod(gv, NTA)
                if gv >= 2:
                    t.wait_ge(L.sem("p2v_ev"), gv - 1)
                for dc in range(DC):
                    if tt == 0:
                        t.wait_ge(L.sem("p2v_wc"), do * DC + dc + 1)
                    i = t.matmul(psv[gv % 2][:], xnT_sl(tt, dc), wvbf[:, dc, :],
                                 start=(dc == 0), stop=(dc == 15))
                L.inc("p2v_mm", i, 1)

        @block.scalar
        def _(s):
            if _SKIP_V:
                return
            prol(s)
            for gv in range(4 * NTA):
                s.wait_ge(L.sem("p2v_mm"), gv + 1)
                if gv >= 2:
                    s.wait_ge(L.sem(f"p2v_vw{gv % 2}"), 16 * ((gv - 2) // 2 + 1))
                i = s.copy(out=vstg[gv % 2][:], in_=psv[gv % 2][:])
                L.inc("p2v_ev", i, 1)

        barrier(["p2v_wd0", "p2v_wd1", "p2v_wc", "p2v_mm", "p2v_ev",
                 "p2v_vw0", "p2v_vw1"])

    sB.close()

    if _MAX_PHASE >= 3:
      with ExitStack() as p5:
        w1st = [p5.enter_context(nc.sbuf_tensor(f"w1st{i}", [128, 8, 128], F32)) for i in range(2)]
        w1bf = [p5.enter_context(nc.sbuf_tensor(f"w1bf{i}", [128, DC, 128], BF16)) for i in range(2)]
        w2st = [p5.enter_context(nc.sbuf_tensor(f"w2st{i}", [128, 512], F32)) for i in range(2)]
        w2bd = [p5.enter_context(nc.sbuf_tensor(f"w2bd{i}", [128, FCN, 512], BF16)) for i in range(2)]
        hT = p5.enter_context(nc.sbuf_tensor("hT", [128, FCN, 1024], BF16))
        acc = p5.enter_context(nc.sbuf_tensor("acc", [128, 8, D], F32))
        psh = [p5.enter_context(nc.psum_tensor(f"psh{i}", [128, 512], F32)) for i in range(4)]
        psy = [p5.enter_context(nc.psum_tensor(f"psy{i}", [128, 512], F32)) for i in range(4)]

        NJ = NE
        n_hgrp = 2 * FCN
        n_ygrp = 4 * 8

        @block.sync
        def _(sy):
            prol(sy)
            for half in range(2):
                for j in range(NJ):
                    jj = half * NJ + j
                    for fc in range(FCN):
                        ck = jj * FCN + fc
                        for hp in range(2):
                            ck2 = ck * 2 + hp
                            if ck2 >= 2:
                                sy.wait_ge(L.sem("p5_w1h"), ck2 - 1)
                            src = bass.AP(w1h.tensor,
                                          j * D * FF + hp * 8 * 128 * FF + fc * 128,
                                          [[FF, 128], [128 * FF, 8], [1, 128]])
                            i = sy.dma_start(out=w1st[ck2 % 2][:], in_=src)
                            L.inc(f"p5_w1d{ck2 % 2}", i, 16)


        @block.vector
        def _(v):
            prol(v)
            for half in range(2):
                for j in range(NJ):
                    jj = half * NJ + j
                    for do in range(4):
                        gd = jj * 4 + do
                        for tile in range(8):
                            gy = gd * 8 + tile
                            v.wait_ge(L.sem("p5_ymm"), gy + 1)
                            if half == 1 and j == 0 and do == 0:
                                v.wait_ge(L.sem("p5_acw"), 128)
                            dsl = acc[:, tile, do * 512:(do + 1) * 512]
                            if j == 0:
                                i = v.tensor_single_scalar(out=dsl, in_=psy[gy % 4][:],
                                                           scalar=1.0, op=ALU.mult)
                            elif j == 1:
                                i = v.scalar_tensor_tensor(out=dsl, in0=psy[gy % 4][:],
                                                           scalar=1.0, in1=dsl,
                                                           op0=ALU.mult, op1=ALU.add)
                            else:
                                gate = Gm[:, half * 8 + tile, j - 2:j - 1]
                                i = v.scalar_tensor_tensor(out=dsl, in0=psy[gy % 4][:],
                                                           scalar=gate, in1=dsl,
                                                           op0=ALU.mult, op1=ALU.add)
                            L.inc("p5_yev", i, 1)

        @block.scalar
        def _(s):
            prol(s)
            NCKT = 2 * NJ * FCN

            def cast_w1c(ck):
                if ck >= NCKT:
                    return
                if ck >= 2:
                    s.wait_ge(L.sem("p5_hmm"), 2 * (ck - 1))
                for hp in range(2):
                    ck2 = ck * 2 + hp
                    s.wait_ge(L.sem(f"p5_w1d{ck2 % 2}"), 16 * (ck2 // 2 + 1))
                    i = s.activation(out=w1bf[ck % 2][:, hp * 8:(hp + 1) * 8, :],
                                     in_=w1st[ck2 % 2][:], func=AF.Copy)
                    L.inc("p5_w1h", i, 1)

            def cast_w2(jj2, dos):
                for do in dos:
                    gd = jj2 * 4 + do
                    for fc in range(FCN):
                        cw = gd * FCN + fc
                        s.wait_ge(L.sem(f"p5_w2d{cw % 2}"), 16 * (cw // 2 + 1))
                        if gd >= 2:
                            s.wait_ge(L.sem("p5_ymm"), 8 * (gd - 1))
                        i = s.activation(out=w2bd[gd % 2][:, fc, :],
                                         in_=w2st[cw % 2][:], func=AF.Copy)
                        L.inc("p5_w2c", i, 1)

            def cast_w2_one(jj2, k):
                do, fc = divmod(k, FCN)
                gd = jj2 * 4 + do
                cw = gd * FCN + fc
                s.wait_ge(L.sem(f"p5_w2d{cw % 2}"), 16 * (cw // 2 + 1))
                if gd >= 2:
                    s.wait_ge(L.sem("p5_ymm"), 8 * (gd - 1))
                i = s.activation(out=w2bd[gd % 2][:, fc, :],
                                 in_=w2st[cw % 2][:], func=AF.Copy)
                L.inc("p5_w2c", i, 1)

            cast_w1c(0)
            for half in range(2):
                for j in range(NJ):
                    jj = half * NJ + j
                    for fc in range(FCN):
                        ck = jj * FCN + fc
                        cast_w1c(ck + 1)
                        for g2 in range(2):
                            gh = ck * 2 + g2
                            s.wait_ge(L.sem("p5_hmm"), gh + 1)
                            i = s.activation(out=hT[:, fc, g2 * 512:(g2 + 1) * 512],
                                             in_=psh[gh % 4][:], func=AF.Relu,
                                             bias=b1c[:, j * FCN + fc:j * FCN + fc + 1])
                            L.inc("p5_hev", i, 1)
                        for k in (2 * fc, 2 * fc + 1):
                            cast_w2_one(jj, k)
                    cast_w2(jj, (2, 3))


        @block.tensor
        def _(t):
            prol(t)
            for half in range(2):
                for j in range(NJ):
                    jj = half * NJ + j
                    for fc in range(FCN):
                        ck = jj * FCN + fc
                        t.wait_ge(L.sem("p5_w1h"), 2 * (ck + 1))
                        for g2 in range(2):
                            gh = ck * 2 + g2
                            if gh >= 4:
                                t.wait_ge(L.sem("p5_hev"), gh - 3)
                            for dc in range(DC):
                                i = t.matmul(psh[gh % 4][:], w1bf[ck % 2][:, dc, :],
                                             xnT[:, dc, half * 1024 + g2 * 512:
                                                 half * 1024 + (g2 + 1) * 512],
                                             start=(dc == 0), stop=(dc == 15))
                            L.inc("p5_hmm", i, 1)
                    for do in range(4):
                        gd = jj * 4 + do
                        for tile in range(8):
                            gy = gd * 8 + tile
                            if gy >= 4:
                                t.wait_ge(L.sem("p5_yev"), gy - 3)
                            nmm = FCN + (2 if j == 0 else 0)
                            k = 0
                            for fc in range(FCN):
                                if tile == 0:
                                    t.wait_ge(L.sem("p5_w2c"), gd * FCN + fc + 1)
                                if do == 0 and tile == 0:
                                    t.wait_ge(L.sem("p5_hev"), 2 * (jj * FCN + fc) + 2)
                                i = t.matmul(psy[gy % 4][:],
                                             hT[:, fc, tile * 128:(tile + 1) * 128],
                                             w2bd[gd % 2][:, fc, :],
                                             start=(fc == 0), stop=(fc == FCN - 1 and j != 0))
                                k += 1
                            if j == 0:
                                tok0 = (half * 8 + tile) * 128
                                t.matmul(psy[gy % 4][:], GTb[:, tok0:tok0 + 128],
                                         b2rb[:, do * 512:(do + 1) * 512],
                                         start=False, stop=False)
                                i = t.matmul(psy[gy % 4][:], ones2[:],
                                             b2sb[:, do * 512:(do + 1) * 512],
                                             start=False, stop=True)
                            L.inc("p5_ymm", i, 1)

        @block.gpsimd
        def _(g_):
            prol(g_)

            def acc_out(half):
                g_.wait_ge(L.sem("p5_yev"), (half + 1) * NJ * 32)
                for tile in range(8):
                    i = g_.dma_start(out=acc_d[(half * 8 + tile) * 128:
                                               (half * 8 + tile + 1) * 128, :],
                                     in_=acc[:, tile, :])
                    L.inc("p5_acw", i, 16)

            for jj2 in range(2 * NJ):
                j = jj2 % NJ
                if jj2 == NJ:
                    acc_out(0)
                for do in range(4):
                    gd = jj2 * 4 + do
                    for fc in range(FCN):
                        cw = gd * FCN + fc
                        if cw >= 2:
                            g_.wait_ge(L.sem("p5_w2c"), cw - 1)
                        src = bass.AP(w2h.tensor, j * FF * D + fc * 128 * D + do * 512,
                                      [[D, 128], [1, 512]])
                        i = g_.dma_start(out=w2st[cw % 2][:], in_=src)
                        L.inc(f"p5_w2d{cw % 2}", i, 16)
            acc_out(1)

        barrier(["p5_w1d0", "p5_w1d1", "p5_w1h", "p5_w2d0",
                 "p5_w2d1", "p5_w2c", "p5_hmm", "p5_hev", "p5_ymm", "p5_yev",
                 "p5_acw"])

    sA.close()

    if _MAX_PHASE >= 4:
      aoS = ExitStack()
      aoT = aoS.enter_context(nc.sbuf_tensor("aoT", [128, H, T], BF16))
      with ExitStack() as p4:
        masks = p4.enter_context(nc.sbuf_tensor("masks", [128, 25, 128], BF16))
        masks32 = p4.enter_context(nc.sbuf_tensor("masks32", [128, 25 * 128], F32))
        kts = [p4.enter_context(nc.sbuf_tensor(f"kts{i}", [128, H * 128], BF16)) for i in range(6)]
        vs = [p4.enter_context(nc.sbuf_tensor(f"vs{i}", [128, D], BF16)) for i in range(6)]
        qts = [p4.enter_context(nc.sbuf_tensor(f"qts{i}", [128, H * 128], BF16)) for i in range(3)]
        pa = [p4.enter_context(nc.sbuf_tensor(f"pa{i}", [128, 640], BF16)) for i in range(2)]
        panb = [p4.enter_context(nc.sbuf_tensor(f"panb{i}", [128, 640], BF16)) for i in range(2)]
        ptb = [p4.enter_context(nc.sbuf_tensor(f"ptb{i}", [128, 640], BF16)) for i in range(2)]
        negmb = [p4.enter_context(nc.sbuf_tensor(f"negmb{i}", [128, 1], F32)) for i in range(2)]
        sumb = p4.enter_context(nc.sbuf_tensor("sumb", [128, 1], F32))
        rcb = p4.enter_context(nc.sbuf_tensor("rcb", [128, 1], F32))
        pss = [p4.enter_context(nc.psum_tensor(f"pss{i}", [128, 640], F32)) for i in range(2)]
        pspt = [p4.enter_context(nc.psum_tensor(f"pspt{i}", [128, 640], BF16)) for i in range(2)]
        psao = [p4.enter_context(nc.psum_tensor(f"psao{i}", [128, 128], F32)) for i in range(2)]

        NIDX = NT * H

        @block.gpsimd
        def _(g_):
            prol(g_)
            g_.wait_ge(L.sem("p4_mk"), 16)
            i = g_.tensor_copy(out=masks[:].rearrange("p a t -> p (a t)"), in_=masks32[:])
            L.inc("p4_mkc", i, 1)
            for kt in range(NTA):
                if kt >= 6:
                    g_.wait_ge(L.sem("p4_avm"), 16 * (kt - 5))
                sl = kt % 6
                i = g_.dma_start(out=kts[sl][:], in_=kt_d[kt * 128:(kt + 1) * 128, :])
                L.inc(f"p4_kv{sl}", i, 16)
                i = g_.dma_start(out=vs[sl][:], in_=v_d[kt * 128:(kt + 1) * 128, :])
                L.inc(f"p4_kv{sl}", i, 16)

        @block.sync
        def _(sy):
            prol(sy)
            i = sy.dma_start(out=masks32[:], in_=maskh)
            L.inc("p4_mk", i, 16)
            for qt in range(NT):
                if qt >= 3:
                    sy.wait_ge(L.sem("p4_smm"), 16 * (qt - 2))
                i = sy.dma_start(out=qts[qt % 3][:], in_=qt_d[qt * 128:(qt + 1) * 128, :])
                L.inc(f"p4_qt{qt % 3}", i, 16)

        def pe_T(t, j):
            t.wait_ge(L.sem("p4_pan"), j + 1)
            if j >= 2:
                t.wait_ge(L.sem("p4_pte"), j - 1)
            for d in range(5):
                i = t.transpose(out=pspt[j % 2][:, d * 128:(d + 1) * 128],
                                in_=panb[j % 2][:, d * 128:(d + 1) * 128],
                                identity=idenb[:])
            L.inc("p4_tmm", i, 1)

        def pe_AV(t, j):
            qt, h = divmod(j, H)
            t.wait_ge(L.sem("p4_pte"), j + 1)
            if j >= 2:
                t.wait_ge(L.sem("p4_aoe"), j - 1)
            for d in range(5):
                sl = (qt + d) % 6
                i = t.matmul(psao[j % 2][:], vs[sl][:, h * 128:(h + 1) * 128],
                             ptb[j % 2][:, d * 128:(d + 1) * 128],
                             start=(d == 0), stop=(d == 4))
            L.inc("p4_avm", i, 1)

        @block.tensor
        def _(t):
            prol(t)
            for idx in range(NIDX):
                qt, h = divmod(idx, H)
                if idx == 0:
                    t.wait_ge(L.sem("p4_mkc"), 1)
                if h == 0:
                    for kt in range(qt, qt + 5):
                        t.wait_ge(L.sem(f"p4_kv{kt % 6}"), 32 * (kt // 6 + 1))
                    t.wait_ge(L.sem(f"p4_qt{qt % 3}"), 16 * (qt // 3 + 1))
                if idx >= 2:
                    t.wait_ge(L.sem("p4_mx"), idx - 1)
                    t.wait_ge(L.sem("p4_ex"), idx - 1)
                sl = idx % 2
                for d in range(5):
                    ks = (qt + d) % 6
                    t.matmul(pss[sl][:, d * 128:(d + 1) * 128],
                             qts[qt % 3][:, h * 128:(h + 1) * 128],
                             kts[ks][:, h * 128:(h + 1) * 128],
                             start=(d == 0 or d == 4), stop=False)
                dlist = range(5) if qt < 4 else (0, 4)
                tc = qt if qt < 4 else 4
                for d in dlist:
                    last_b0 = (d == max(x for x in dlist if x < 4))
                    i = t.matmul(pss[sl][:, d * 128:(d + 1) * 128], idenb[:],
                                 masks[:, tc * 5 + d, :],
                                 start=False, stop=(last_b0 or d == 4))
                L.inc("p4_smm", i, 1)
                if idx >= 2:
                    pe_T(t, idx - 2)
                    pe_AV(t, idx - 2)
            for j in (NIDX - 2, NIDX - 1):
                pe_T(t, j)
                pe_AV(t, j)

        @block.vector
        def _(v):
            prol(v)
            for idx in range(NIDX):
                v.wait_ge(L.sem("p4_smm"), idx + 1)
                i = v.reduce_max(out=negmb[idx % 2][:], in_=pss[idx % 2][:],
                                 axis=AX.X, negate=True)
                L.inc("p4_mx", i, 1)
                if idx >= 1:
                    j = idx - 1
                    v.wait_ge(L.sem("p4_ex"), j + 1)
                    v.reduce_sum(out=sumb[:], in_=pa[j % 2][:], axis=AX.X)
                    v.drain()
                    v.reciprocal(out=rcb[:], in_=sumb[:])
                    v.drain()
                    if j >= 2:
                        v.wait_ge(L.sem("p4_tmm"), j - 1)
                    i = v.tensor_single_scalar(out=panb[j % 2][:], in_=pa[j % 2][:],
                                               scalar=rcb[:, 0:1], op=ALU.mult)
                    L.inc("p4_pan", i, 1)
            j = NIDX - 1
            v.wait_ge(L.sem("p4_ex"), j + 1)
            v.reduce_sum(out=sumb[:], in_=pa[j % 2][:], axis=AX.X)
            v.drain()
            v.reciprocal(out=rcb[:], in_=sumb[:])
            v.drain()
            v.wait_ge(L.sem("p4_tmm"), j - 1)
            i = v.tensor_single_scalar(out=panb[j % 2][:], in_=pa[j % 2][:],
                                       scalar=rcb[:, 0:1], op=ALU.mult)
            L.inc("p4_pan", i, 1)

        def act_tail(s, j):
            qt, h = divmod(j, H)
            s.wait_ge(L.sem("p4_tmm"), j + 1)
            if j >= 2:
                s.wait_ge(L.sem("p4_avm"), j - 1)
            i = s.copy(out=ptb[j % 2][:], in_=pspt[j % 2][:])
            L.inc("p4_pte", i, 1)
            s.wait_ge(L.sem("p4_avm"), j + 1)
            i = s.copy(out=aoT[:, h, qt * 128:(qt + 1) * 128], in_=psao[j % 2][:])
            L.inc("p4_aoe", i, 1)

        @block.scalar
        def _(s):
            prol(s)
            for idx in range(NIDX):
                s.wait_ge(L.sem("p4_mx"), idx + 1)
                if idx >= 2:
                    s.wait_ge(L.sem("p4_pan"), idx - 1)
                i = s.activation(out=pa[idx % 2][:], in_=pss[idx % 2][:], func=AF.Exp,
                                 bias=negmb[idx % 2][:, 0:1])
                L.inc("p4_ex", i, 1)
                if idx >= 2:
                    act_tail(s, idx - 2)
            for j in (NIDX - 2, NIDX - 1):
                act_tail(s, j)

        barrier([f"p4_kv{s}" for s in range(6)] + [f"p4_qt{s}" for s in range(3)]
                + ["p4_smm", "p4_mx", "p4_ex", "p4_pan",
                   "p4_tmm", "p4_pte", "p4_avm", "p4_aoe"])

      with ExitStack() as p4w:
        wost = [p4w.enter_context(nc.sbuf_tensor(f"wost{i}", [128, 512], F32)) for i in range(2)]
        wodb = [p4w.enter_context(nc.sbuf_tensor(f"wodb{i}", [128, H, 512], BF16)) for i in range(2)]
        sstg = [p4w.enter_context(nc.sbuf_tensor(f"sstg{i}", [128, 512], BF16)) for i in range(2)]
        pss2 = [p4w.enter_context(nc.psum_tensor(f"pss2_{i}", [128, 512], F32)) for i in range(2)]

        @block.sync
        def _(sy):
            prol(sy)
            for cwo in range(4 * H):
                if cwo >= 2:
                    sy.wait_ge(L.sem("p4w_wc"), cwo - 1)
                do, h2 = divmod(cwo, H)
                src = bass.AP(woT.tensor, h2 * 128 * D + do * 512, [[D, 128], [1, 512]])
                i = sy.dma_start(out=wost[cwo % 2][:], in_=src)
                L.inc(f"p4w_wd{cwo % 2}", i, 16)

        @block.gpsimd
        def _(g_):
            prol(g_)

            def emit_sw(gw):
                do_, tile_ = divmod(gw, NT)
                g_.wait_ge(L.sem("p4w_ev"), gw + 1)
                dsta = bass.AP(s_d.tensor, tile_ * 128 * D + do_ * 512,
                               [[D, 128], [1, 512]])
                i = g_.dma_start(out=dsta, in_=sstg[gw % 2][:])
                L.inc(f"p4w_sw{gw % 2}", i, 16)

            for do in range(4):
                if do >= 1:
                    for gw in range((do - 1) * NT, do * NT):
                        emit_sw(gw)
            for gw in range(3 * NT, 4 * NT):
                emit_sw(gw)

        @block.tensor
        def _(t):
            prol(t)
            for gw in range(4 * NT):
                do, tile = divmod(gw, NT)
                if gw >= 2:
                    t.wait_ge(L.sem("p4w_ev"), gw - 1)
                for h2 in range(H):
                    if tile == 0:
                        t.wait_ge(L.sem("p4w_wc"), do * H + h2 + 1)
                    i = t.matmul(pss2[gw % 2][:], aoT[:, h2, tile * 128:(tile + 1) * 128],
                                 wodb[do % 2][:, h2, :], start=(h2 == 0), stop=(h2 == H - 1))
                L.inc("p4w_mm", i, 1)

        @block.scalar
        def _(s):
            prol(s)
            for do in range(4):
                for h2 in range(H):
                    cwo = do * H + h2
                    s.wait_ge(L.sem(f"p4w_wd{cwo % 2}"), 16 * (cwo // 2 + 1))
                    if do >= 2:
                        s.wait_ge(L.sem("p4w_mm"), NT * (do - 1))
                    i = s.activation(out=wodb[do % 2][:, h2, :], in_=wost[cwo % 2][:],
                                     func=AF.Copy)
                    L.inc("p4w_wc", i, 1)
                for gw in range(do * NT, (do + 1) * NT):
                    s.wait_ge(L.sem("p4w_mm"), gw + 1)
                    if gw >= 2:
                        s.wait_ge(L.sem(f"p4w_sw{gw % 2}"), 16 * ((gw - 2) // 2 + 1))
                    i = s.copy(out=sstg[gw % 2][:], in_=pss2[gw % 2][:])
                    L.inc("p4w_ev", i, 1)

        barrier(["p4w_wd0", "p4w_wd1", "p4w_wc", "p4w_mm", "p4w_ev",
                 "p4w_sw0", "p4w_sw1"])
      aoS.close()

    if _MAX_PHASE >= 5:
      with ExitStack() as p6:
        xb6 = [p6.enter_context(nc.sbuf_tensor(f"xb6_{i}", [128, D], F32)) for i in range(2)]
        ab6 = [p6.enter_context(nc.sbuf_tensor(f"ab6_{i}", [128, D], F32)) for i in range(2)]
        sb6 = [p6.enter_context(nc.sbuf_tensor(f"sb6_{i}", [128, D], BF16)) for i in range(2)]
        xn6 = p6.enter_context(nc.sbuf_tensor("xn6", [128, D], F32))
        ffb = p6.enter_context(nc.sbuf_tensor("ffb", [128, D], F32))
        g6b = p6.enter_context(nc.sbuf_tensor("g6b", [128, D], F32))
        sgb6 = p6.enter_context(nc.sbuf_tensor("sgb6", [128, D], F32))
        outb = [p6.enter_context(nc.sbuf_tensor(f"outb{i}", [128, D], F32)) for i in range(2)]

        @block.sync
        def _(sy):
            prol(sy)
            for tt in range(NT):
                if tt >= 2:
                    sy.wait_ge(L.sem("p6_ot"), tt - 1)
                i = sy.dma_start(out=xb6[tt % 2][:], in_=x_own[tt * 128:(tt + 1) * 128, :])
                L.inc(f"p6_xd{tt % 2}", i, 16)
                i = sy.dma_start(out=ab6[tt % 2][:], in_=acc_d[tt * 128:(tt + 1) * 128, :])
                L.inc(f"p6_ad{tt % 2}", i, 16)

        @block.gpsimd
        def _(g_):
            prol(g_)

            def emit_ow(tt):
                g_.wait_ge(L.sem("p6_ot"), tt + 1)
                i = g_.dma_start(out=outh[tt * 128:(tt + 1) * 128, :], in_=outb[tt % 2][:])
                L.inc(f"p6_ow{tt % 2}", i, 16)

            for tt in range(NT):
                if tt >= 2:
                    g_.wait_ge(L.sem("p6_ot"), tt - 1)
                i = g_.dma_start(out=sb6[tt % 2][:], in_=s_d[tt * 128:(tt + 1) * 128, :])
                L.inc(f"p6_sd{tt % 2}", i, 16)
                if tt >= 2:
                    emit_ow(tt - 2)
            emit_ow(NT - 2)
            emit_ow(NT - 1)
            g_.wait_ge(L.sem("p6_ow0"), 16 * (NT // 2))
            g_.wait_ge(L.sem("p6_ow1"), 16 * (NT // 2))

        @block.vector
        def _(v):
            prol(v)
            for tt in range(NT):
                st = tt + 4
                v.wait_ge(L.sem(f"p6_xd{tt % 2}"), 16 * (tt // 2 + 1))
                v.wait_ge(L.sem(f"p6_ad{tt % 2}"), 16 * (tt // 2 + 1))
                v.tensor_scalar(out=xn6[:], in0=xb6[tt % 2][:],
                                scalar1=stats[:, st, 0:1], scalar2=stats[:, st, 1:2],
                                op0=ALU.subtract, op1=ALU.mult)
                v.drain()
                if tt >= 1:
                    v.wait_ge(L.sem("p6_sl"), tt)
                i = v.tensor_tensor(out=ffb[:], in0=xn6[:], in1=ab6[tt % 2][:], op=ALU.add)
                L.inc("p6_ff", i, 1)
                v.wait_ge(L.sem("p6_sl"), tt + 1)
                v.wait_ge(L.sem(f"p6_sd{tt % 2}"), 16 * (tt // 2 + 1))
                v.tensor_tensor(out=sgb6[:], in0=sb6[tt % 2][:], in1=g6b[:], op=ALU.mult)
                v.drain()
                if tt >= 2:
                    v.wait_ge(L.sem(f"p6_ow{tt % 2}"), 16 * ((tt - 2) // 2 + 1))
                i = v.tensor_tensor(out=outb[tt % 2][:], in0=xb6[tt % 2][:],
                                    in1=sgb6[:], op=ALU.add)
                L.inc("p6_ot", i, 1)

        @block.scalar
        def _(s):
            prol(s)
            for tt in range(NT):
                s.wait_ge(L.sem("p6_ff"), tt + 1)
                i = s.activation(out=g6b[:], in_=ffb[:], func=AF.Silu)
                L.inc("p6_sl", i, 1)

        @block.sync
        def _(sy):
            sy.wait_ge(L.sem("p6_ow0"), 16 * (NT // 2))
            sy.wait_ge(L.sem("p6_ow1"), 16 * (NT // 2))

    gctx.close()
    return nc


def _host_constants(hf):
    P0 = hf * 2048
    pos = np.maximum(P0 - HALO + np.arange(TK), 0).astype(np.float64)
    inv = 10000.0 ** (-(np.arange(0, DH, 2, dtype=np.float64)) / DH)
    ang = pos[None, :] * inv[:, None]
    c = np.cos(ang)
    sn = np.sin(ang)
    cosf = np.empty((128, TK), np.float32)
    sinf = np.empty((128, TK), np.float32)
    cosf[0::2] = c
    cosf[1::2] = c
    sinf[0::2] = -sn
    sinf[1::2] = sn

    ii = np.arange(128)[:, None]
    jj = np.arange(128)[None, :]
    masks = np.zeros((128, 25, 128), np.float32)
    for tc in range(5):
        for d in range(5):
            m = np.zeros((128, 128), np.float32)
            if d == 0:
                m[~(jj > ii) * np.ones((128, 128), bool)] = 0.0
                m = np.where(jj > ii, 0.0, -1e9).astype(np.float32)
            elif d == 4:
                m = np.where(jj <= ii, 0.0, -1e9).astype(np.float32)
            if hf == 0 and tc < 4 and tc + d <= 3:
                m[:] = -1e9
            masks[:, tc * 5 + d, :] = m
    masksb = masks.reshape(128, 25 * 128)
    return cosf, sinf, masksb


def prep_inputs(L0_attn_x, L0_attn_Wq, L0_attn_Wk, L0_attn_Wv, L0_attn_Wo,
                L0_ffn_shared_w1, L0_ffn_shared_b1, L0_ffn_shared_w2, L0_ffn_shared_b2,
                L0_ffn_routed_w1, L0_ffn_routed_b1, L0_ffn_routed_w2, L0_ffn_routed_b2,
                L0_ffn_centroids, L0_ffn_bias):
    x = np.ascontiguousarray(np.asarray(L0_attn_x, dtype=np.float32))
    wqT = np.ascontiguousarray(np.asarray(L0_attn_Wq, np.float32).T)
    wkT = np.ascontiguousarray(np.asarray(L0_attn_Wk, np.float32).T)
    wvT = np.ascontiguousarray(np.asarray(L0_attn_Wv, np.float32).T)
    woT = np.ascontiguousarray(np.asarray(L0_attn_Wo, np.float32).T)
    w1 = np.ascontiguousarray(np.concatenate([np.asarray(L0_ffn_shared_w1, np.float32),
                                              np.asarray(L0_ffn_routed_w1, np.float32)], 0))
    w2 = np.ascontiguousarray(np.concatenate([np.asarray(L0_ffn_shared_w2, np.float32),
                                              np.asarray(L0_ffn_routed_w2, np.float32)], 0))
    b1all = np.concatenate([np.asarray(L0_ffn_shared_b1, np.float32),
                            np.asarray(L0_ffn_routed_b1, np.float32)], 0)
    b1c = np.ascontiguousarray(b1all.reshape(NE, FCN, 128).transpose(2, 0, 1)
                               .reshape(128, NE * FCN))
    b2r = np.ascontiguousarray(np.asarray(L0_ffn_routed_b2, np.float32))
    b2s = np.ascontiguousarray(np.asarray(L0_ffn_shared_b2, np.float32))
    cen = np.asarray(L0_ffn_centroids, np.float32)
    cenT = np.ascontiguousarray(cen.T.reshape(DC, 128, 16).transpose(1, 0, 2)
                                .reshape(128, DC * 16))
    gbias = np.ascontiguousarray(np.asarray(L0_ffn_bias, np.float32).reshape(1, 16))
    p128 = np.zeros((128, 128), np.float32)
    for i in range(64):
        p128[2 * i, 2 * i + 1] = 1.0
        p128[2 * i + 1, 2 * i] = 1.0
    iden = np.eye(128, dtype=np.float32)
    consts = {hf: _host_constants(hf) for hf in range(2)}

    in_maps = []
    for c in range(8):
        b, hf = divmod(c, 2)
        P0 = hf * 2048
        cosf, sinf, masksb = consts[hf]
        x_own = np.ascontiguousarray(x[b, P0:P0 + T])
        x_halo = np.ascontiguousarray(x[b, P0 - HALO:P0]) if hf == 1 else \
            np.zeros((HALO, D), np.float32)
        in_maps.append(dict(
            x_own=x_own, x_halo=x_halo, wqT=wqT, wkT=wkT, wvT=wvT, woT=woT,
            w1=w1, w2=w2, b1c=b1c, b2r=b2r, b2s=b2s, cenT=cenT, gbias=gbias,
            cosf=cosf, sinf=sinf, p128=p128, iden=iden, masksb=masksb))
    return in_maps


def kernel(**inputs):
    in_maps = prep_inputs(**inputs)
    nc = build_program()
    res = run_bass_kernel_spmd(nc, in_maps, core_ids=list(range(8)))
    out = np.empty((4, 4096, D), np.float32)
    for c in range(8):
        b, hf = divmod(c, 2)
        out[b, hf * 2048:(hf + 1) * 2048] = res.results[c]["out"]
    return out


# version 10
# speedup vs baseline: 1.3083x; 1.0114x over previous
import numpy as np
import ml_dtypes
from contextlib import ExitStack

import concourse.bass as bass
import concourse.mybir as mybir
from concourse.bass_utils import run_bass_kernel_spmd

F32 = mybir.dt.float32
BF16 = mybir.dt.bfloat16
AF = mybir.ActivationFunctionType
ALU = mybir.AluOpType
AX = mybir.AxisListType

D = 2048
DC = 16
T = 2048
HALO = 512
TK = 2560
NT = 16
NTA = 20
H = 16
DH = 128
FF = 1408
FCN = 11
NE = 18
SCALE = float(128 ** -0.5)
EPS = 1e-5

_DEBUG_TAPS = set()
_MAX_PHASE = 99
_SKIP_QK = False
_SKIP_V = False
_QK_NCK = 99


class Ledger:
    def __init__(self, nc, ctx):
        self.nc, self.ctx = nc, ctx
        self.h = {}
        self.c = {}

    def sem(self, name):
        if name not in self.h:
            self.h[name] = self.ctx.enter_context(self.nc.semaphore(name))
            self.c[name] = 0
        return self.h[name]

    def inc(self, name, instr, k):
        instr.then_inc(self.sem(name), k)
        self.c[name] += k

    def n(self, name):
        return self.c.get(name, 0)


def build_program():
    nc = bass.Bass("TRN2", target_bir_lowering=False)

    x_own = nc.dram_tensor("x_own", [T, D], F32, kind="ExternalInput").ap()
    x_halo = nc.dram_tensor("x_halo", [HALO, D], F32, kind="ExternalInput").ap()
    wqT = nc.dram_tensor("wqT", [D, D], F32, kind="ExternalInput").ap()
    wkT = nc.dram_tensor("wkT", [D, D], F32, kind="ExternalInput").ap()
    wvT = nc.dram_tensor("wvT", [D, D], F32, kind="ExternalInput").ap()
    woT = nc.dram_tensor("woT", [D, D], F32, kind="ExternalInput").ap()
    w1h = nc.dram_tensor("w1", [NE, D, FF], F32, kind="ExternalInput").ap()
    w2h = nc.dram_tensor("w2", [NE, FF, D], F32, kind="ExternalInput").ap()
    b1ch = nc.dram_tensor("b1c", [128, NE * FCN], F32, kind="ExternalInput").ap()
    b2rh = nc.dram_tensor("b2r", [16, D], F32, kind="ExternalInput").ap()
    b2sh = nc.dram_tensor("b2s", [2, D], F32, kind="ExternalInput").ap()
    cenh = nc.dram_tensor("cenT", [128, DC * 16], F32, kind="ExternalInput").ap()
    gbh = nc.dram_tensor("gbias", [1, 16], F32, kind="ExternalInput").ap()
    cosh = nc.dram_tensor("cosf", [128, TK], F32, kind="ExternalInput").ap()
    sinh = nc.dram_tensor("sinf", [128, TK], F32, kind="ExternalInput").ap()
    p128h = nc.dram_tensor("p128", [128, 128], F32, kind="ExternalInput").ap()
    idh = nc.dram_tensor("iden", [128, 128], F32, kind="ExternalInput").ap()
    maskh = nc.dram_tensor("masksb", [128, 25 * 128], F32, kind="ExternalInput").ap()

    outh = nc.dram_tensor("out", [T, D], F32, kind="ExternalOutput").ap()

    def skind(nm):
        return "ExternalOutput" if nm in _DEBUG_TAPS else "Internal"

    qt_d = nc.dram_tensor("qt_d", [NT * 128, H * 128], BF16, kind=skind("qt_d")).ap()
    kt_d = nc.dram_tensor("kt_d", [NTA * 128, H * 128], BF16, kind=skind("kt_d")).ap()
    v_d = nc.dram_tensor("v_d", [TK, D], BF16, kind=skind("v_d")).ap()
    s_d = nc.dram_tensor("s_d", [T, D], BF16, kind=skind("s_d")).ap()
    acc_d = nc.dram_tensor("acc_d", [T, D], F32, kind=skind("acc_d")).ap()
    xnt_dbg = g_dbg = None
    if "xnt_d" in _DEBUG_TAPS:
        xnt_dbg = nc.dram_tensor("xnt_d", [128, DC * T], BF16, kind="ExternalOutput").ap()
    if "g_d" in _DEBUG_TAPS:
        g_dbg = nc.dram_tensor("g_d", [128, NT * 16], F32, kind="ExternalOutput").ap()
    st_dbg = None
    if "st_d" in _DEBUG_TAPS:
        st_dbg = nc.dram_tensor("st_d", [128, NTA * 2], F32, kind="ExternalOutput").ap()

    gctx = ExitStack()
    L = Ledger(nc, gctx)
    PH = L.sem("ph")
    pno = [0]

    idenb = gctx.enter_context(nc.sbuf_tensor("idenb", [128, 128], BF16))
    p128b = gctx.enter_context(nc.sbuf_tensor("p128b", [128, 128], BF16))
    stats = gctx.enter_context(nc.sbuf_tensor("stats", [128, NTA, 2], F32))
    b1c = gctx.enter_context(nc.sbuf_tensor("b1c_s", [128, NE * FCN], F32))
    gbias = gctx.enter_context(nc.sbuf_tensor("gbias_s", [1, 16], F32))
    bplane = gctx.enter_context(nc.sbuf_tensor("bplane", [128, 16], F32))
    ones2 = gctx.enter_context(nc.sbuf_tensor("ones2", [2, 128], BF16))
    Gm = gctx.enter_context(nc.sbuf_tensor("Gm", [128, NT, 16], F32))
    GTb = gctx.enter_context(nc.sbuf_tensor("GTb", [16, T], BF16))
    b2rb = gctx.enter_context(nc.sbuf_tensor("b2rb", [16, D], BF16))
    b2sb = gctx.enter_context(nc.sbuf_tensor("b2sb", [2, D], BF16))
    epsb = gctx.enter_context(nc.sbuf_tensor("epsb", [128, 1], F32))

    sA = ExitStack()
    xnT = sA.enter_context(nc.sbuf_tensor("xnT", [128, DC, T], BF16))
    sB = ExitStack()
    xnTh = sB.enter_context(nc.sbuf_tensor("xnTh", [128, DC, HALO], BF16))
    cosf = sB.enter_context(nc.sbuf_tensor("cosf_s", [128, TK], F32))
    sinf = sB.enter_context(nc.sbuf_tensor("sinf_s", [128, TK], F32))
    iden32 = sB.enter_context(nc.sbuf_tensor("iden32", [128, 128], F32))
    cenT = sB.enter_context(nc.sbuf_tensor("cenT_s", [128, DC, 16], F32))
    ones1 = sB.enter_context(nc.sbuf_tensor("ones1", [1, 128], F32))

    block = gctx.enter_context(nc.Block())

    def barrier(names):
        pno[0] += 1

        @block.sync
        def _(sy):
            for s in names:
                sy.wait_ge(L.sem(s), L.n(s))
            sy.sem_inc(PH, 1)
        L.c["ph"] += 1

    def prol(eng):
        eng.wait_ge(PH, pno[0])

    with ExitStack() as p0:
        b2r32 = p0.enter_context(nc.sbuf_tensor("b2r32", [16, D], F32))
        b2s32 = p0.enter_context(nc.sbuf_tensor("b2s32", [2, D], F32))
        p128s = p0.enter_context(nc.sbuf_tensor("p128s", [128, 128], F32))
        psbp = p0.enter_context(nc.psum_tensor("psbp", [128, 16], F32))

        @block.sync
        def _(sy):
            for dst, src in [(iden32, idh), (b1c, b1ch),
                             (gbias, gbh), (b2r32, b2rh), (b2s32, b2sh),
                             (p128s, p128h)]:
                i = sy.dma_start(out=dst[:], in_=src)
                L.inc("c_dma", i, 16)
            i = sy.dma_start(out=cenT[:], in_=cenh.rearrange("p (dc j) -> p dc j", dc=DC))
            L.inc("c_dma", i, 16)

        @block.gpsimd
        def _(g):
            g.wait_ge(L.sem("c_dma"), L.n("c_dma"))
            g.memset(ones1[:], 1.0)
            g.memset(ones2[:], 1.0)
            g.memset(epsb[:], EPS)
            g.tensor_copy(out=idenb[:], in_=iden32[:])
            g.tensor_copy(out=b2rb[:], in_=b2r32[:])
            g.tensor_copy(out=b2sb[:], in_=b2s32[:])
            i = g.tensor_copy(out=p128b[:], in_=p128s[:])
            L.inc("c_cast", i, 1)

        @block.tensor
        def _(t):
            t.wait_ge(L.sem("c_cast"), 1)
            i = t.matmul(psbp[:], ones1[:], gbias[:], start=True, stop=True)
            L.inc("c_bp", i, 1)

        @block.scalar
        def _(s):
            s.wait_ge(L.sem("c_bp"), 1)
            i = s.copy(out=bplane[:], in_=psbp[:])
            L.inc("c_bpe", i, 1)

        barrier(["c_dma", "c_cast", "c_bpe"])

    def xnT_sl(tt, dc, lo=0, hi=128):
        if tt < 4:
            return xnTh[:, dc, tt * 128 + lo:tt * 128 + hi]
        return xnT[:, dc, (tt - 4) * 128 + lo:(tt - 4) * 128 + hi]

    def xnT_grp(dc, g):
        if g == 0:
            return xnTh[:, dc, :]
        return xnT[:, dc, (g - 1) * 512:g * 512]

    with ExitStack() as p1:
        xb = [p1.enter_context(nc.sbuf_tensor(f"xb{i}", [128, D], F32)) for i in range(2)]
        xn32 = [p1.enter_context(nc.sbuf_tensor(f"xn32_{i}", [128, D], F32)) for i in range(2)]
        sqd = p1.enter_context(nc.sbuf_tensor("sqd", [128, D], F32))
        s1b = p1.enter_context(nc.sbuf_tensor("s1b", [128, 1], F32))
        s2b = p1.enter_context(nc.sbuf_tensor("s2b", [128, 1], F32))
        ex2b = p1.enter_context(nc.sbuf_tensor("ex2b", [128, 1], F32))
        nvb = p1.enter_context(nc.sbuf_tensor("nvb", [128, 1], F32))
        sdb = p1.enter_context(nc.sbuf_tensor("sdb", [128, 1], F32))
        xnT32 = p1.enter_context(nc.sbuf_tensor("xnT32", [128, DC, 128], F32))
        sgb = [p1.enter_context(nc.sbuf_tensor(f"sgb{i}", [128, 16], F32)) for i in range(2)]
        scb = p1.enter_context(nc.sbuf_tensor("scb", [128, 16], F32))
        scur = [p1.enter_context(nc.sbuf_tensor(f"scur{i}", [128, 16], F32)) for i in range(2)]
        geb = p1.enter_context(nc.sbuf_tensor("geb", [128, 16], F32))
        gmb = p1.enter_context(nc.sbuf_tensor("gmb", [128, 16], F32))
        mrb = [p1.enter_context(nc.sbuf_tensor(f"mrb{i}", [128, 1], F32)) for i in range(4)]
        msb = p1.enter_context(nc.sbuf_tensor("msb", [128, 1], F32))
        recb = p1.enter_context(nc.sbuf_tensor("recb", [128, 1], F32))
        pst = [p1.enter_context(nc.psum_tensor(f"pst{i}", [128, 128], F32)) for i in range(4)]
        pssc = [p1.enter_context(nc.psum_tensor(f"pssc{i}", [128, 16], F32)) for i in range(2)]
        psgt = p1.enter_context(nc.psum_tensor("psgt", [16, 128], F32))

        @block.sync
        def _(sy):
            prol(sy)
            for tt in range(NTA):
                if tt >= 2:
                    sy.wait_ge(L.sem("p1_xn"), tt - 1)
                src = x_halo[tt * 128:(tt + 1) * 128, :] if tt < 4 else \
                    x_own[(tt - 4) * 128:(tt - 3) * 128, :]
                i = sy.dma_start(out=xb[tt % 2][:], in_=src)
                L.inc(f"p1_xd{tt % 2}", i, 16)

        @block.scalar
        def _(s):
            prol(s)
            for tt in range(NTA):
                ot = tt - 4
                s.wait_ge(L.sem(f"p1_xd{tt % 2}"), 16 * (tt // 2 + 1))
                if tt >= 1:
                    s.wait_ge(L.sem("p1_nv"), tt)
                i = s.activation(out=sqd[:], in_=xb[tt % 2][:], func=AF.Square,
                                 accum_out=s2b[:])
                L.inc("p1_sq", i, 1)
                s.wait_ge(L.sem("p1_nv"), tt + 1)
                if tt >= 1:
                    s.wait_ge(L.sem("p1_rc"), tt)
                i = s.activation(out=sdb[:], in_=nvb[:], func=AF.Sqrt,
                                 bias=epsb[:, 0:1], scale=-1.0)
                L.inc("p1_sd", i, 1)
                for dc in range(8, 16):
                    gk = tt * 16 + dc
                    s.wait_ge(L.sem("p1_tr"), gk + 1)
                    if ot >= 0:
                        s.copy(out=xnT_sl(tt, dc), in_=pst[gk % 4][:])
                        i = s.copy(out=xnT32[:, dc, :], in_=pst[gk % 4][:])
                    else:
                        i = s.copy(out=xnT_sl(tt, dc), in_=pst[gk % 4][:])
                    L.inc("p1_eva", i, 1)
                if ot >= 0:
                    s.wait_ge(L.sem("p1_scm"), ot + 1)
                    if ot >= 2:
                        s.wait_ge(L.sem("p1_g"), ot - 1)
                    i = s.activation(out=sgb[ot % 2][:], in_=pssc[ot % 2][:],
                                     func=AF.Sigmoid)
                    L.inc("p1_sg", i, 1)
                if ot >= 1:
                    s.wait_ge(L.sem("p1_gt"), ot)
                    i = s.copy(out=GTb[:, (ot - 1) * 128:ot * 128], in_=psgt[:])
                    L.inc("p1_gte", i, 1)
            s.wait_ge(L.sem("p1_gt"), NT)
            i = s.copy(out=GTb[:, (NT - 1) * 128:NT * 128], in_=psgt[:])
            L.inc("p1_gte", i, 1)

        @block.vector
        def _(v):
            prol(v)
            for tt in range(NTA):
                ot = tt - 4
                v.wait_ge(L.sem(f"p1_xd{tt % 2}"), 16 * (tt // 2 + 1))
                v.reduce_sum(out=s1b[:], in_=xb[tt % 2][:], axis=AX.X)
                v.drain()
                v.tensor_single_scalar(out=stats[:, tt, 0:1], in_=s1b[:],
                                       scalar=1.0 / D, op=ALU.mult)
                v.wait_ge(L.sem("p1_sq"), tt + 1)
                v.tensor_single_scalar(out=ex2b[:], in_=s2b[:], scalar=1.0 / D,
                                       op=ALU.mult)
                v.drain()
                i = v.scalar_tensor_tensor(out=nvb[:], in0=stats[:, tt, 0:1],
                                           scalar=stats[:, tt, 0:1], in1=ex2b[:],
                                           op0=ALU.mult, op1=ALU.subtract)
                L.inc("p1_nv", i, 1)
                v.wait_ge(L.sem("p1_sd"), tt + 1)
                i = v.reciprocal(out=stats[:, tt, 1:2], in_=sdb[:])
                L.inc("p1_rc", i, 1)
                v.drain()
                if tt >= 2:
                    v.wait_ge(L.sem("p1_tr"), 16 * (tt - 1))
                i = v.tensor_scalar(out=xn32[tt % 2][:], in0=xb[tt % 2][:],
                                    scalar1=stats[:, tt, 0:1], scalar2=stats[:, tt, 1:2],
                                    op0=ALU.subtract, op1=ALU.mult)
                L.inc("p1_xn", i, 1)
                for dc in range(8):
                    gk = tt * 16 + dc
                    v.wait_ge(L.sem("p1_tr"), gk + 1)
                    if ot >= 0:
                        v.tensor_copy(out=xnT_sl(tt, dc), in_=pst[gk % 4][:])
                        i = v.tensor_copy(out=xnT32[:, dc, :], in_=pst[gk % 4][:])
                    else:
                        i = v.tensor_copy(out=xnT_sl(tt, dc), in_=pst[gk % 4][:])
                    L.inc("p1_evv", i, 1)
                if ot >= 0:
                    v.wait_ge(L.sem("p1_sg"), ot + 1)
                    v.tensor_tensor(out=scb[:], in0=sgb[ot % 2][:], in1=bplane[:], op=ALU.add)
                    v.drain()
                    v.tensor_copy(out=scur[0][:], in_=scb[:])
                    v.drain()
                    v.reduce_max(out=mrb[0][:], in_=scur[0][:], axis=AX.X)
                    v.drain()
                    for r in range(1, 4):
                        v.tensor_single_scalar(out=geb[:], in_=scur[(r - 1) % 2][:],
                                               scalar=mrb[r - 1][:, 0:1], op=ALU.is_ge)
                        v.drain()
                        v.scalar_tensor_tensor(out=scur[r % 2][:], in0=geb[:],
                                               scalar=-2e9, in1=scur[(r - 1) % 2][:],
                                               op0=ALU.mult, op1=ALU.add)
                        v.drain()
                        v.reduce_max(out=mrb[r][:], in_=scur[r % 2][:], axis=AX.X)
                        v.drain()
                    v.tensor_tensor(out=msb[:], in0=mrb[0][:], in1=mrb[1][:], op=ALU.add)
                    v.drain()
                    v.tensor_tensor(out=msb[:], in0=msb[:], in1=mrb[2][:], op=ALU.add)
                    v.drain()
                    v.tensor_tensor(out=msb[:], in0=msb[:], in1=mrb[3][:], op=ALU.add)
                    v.drain()
                    v.tensor_single_scalar(out=msb[:], in_=msb[:], scalar=1e-8, op=ALU.max)
                    v.drain()
                    v.reciprocal(out=recb[:], in_=msb[:])
                    v.tensor_single_scalar(out=geb[:], in_=scb[:], scalar=mrb[3][:, 0:1],
                                           op=ALU.is_ge)
                    v.drain()
                    v.tensor_tensor(out=gmb[:], in0=scb[:], in1=geb[:], op=ALU.mult)
                    v.drain()
                    i = v.tensor_single_scalar(out=Gm[:, ot, :], in_=gmb[:],
                                               scalar=recb[:, 0:1], op=ALU.mult)
                    L.inc("p1_g", i, 1)

        def emit_scores(t, ot):
            t.wait_ge(L.sem("p1_evv"), 8 * (ot + 5))
            t.wait_ge(L.sem("p1_eva"), 8 * (ot + 5))
            if ot >= 2:
                t.wait_ge(L.sem("p1_sg"), ot - 1)
            for dc in range(16):
                i = t.matmul(pssc[ot % 2][:], xnT32[:, dc, :], cenT[:, dc, :],
                             start=(dc == 0), stop=(dc == 15))
            L.inc("p1_scm", i, 1)

        def emit_gt(t, j):
            t.wait_ge(L.sem("p1_g"), j + 1)
            if j >= 1:
                t.wait_ge(L.sem("p1_gte"), j)
            i = t.transpose(out=psgt[:], in_=Gm[:, j, :], identity=iden32[:])
            L.inc("p1_gt", i, 1)

        @block.tensor
        def _(t):
            prol(t)
            for tt in range(NTA):
                ot = tt - 4
                t.wait_ge(L.sem("p1_xn"), tt + 1)
                for dc in range(16):
                    gk = tt * 16 + dc
                    if gk >= 4:
                        gp_ = gk - 4
                        dcp = gp_ % 16
                        if dcp < 8:
                            t.wait_ge(L.sem("p1_evv"), (gp_ // 16) * 8 + dcp + 1)
                        else:
                            t.wait_ge(L.sem("p1_eva"), (gp_ // 16) * 8 + (dcp - 8) + 1)
                    i = t.transpose(out=pst[gk % 4][:],
                                    in_=xn32[tt % 2][:, dc * 128:(dc + 1) * 128],
                                    identity=iden32[:])
                    L.inc("p1_tr", i, 1)
                if ot >= 0:
                    emit_scores(t, ot)
                if ot >= 1:
                    emit_gt(t, ot - 1)
            emit_gt(t, NT - 1)

        names = ["p1_evv", "p1_eva", "p1_g", "p1_gte", "p1_xd0", "p1_xd1",
                 "p1_xn", "p1_scm", "p1_gt"]
        if "xnt_d" in _DEBUG_TAPS or "g_d" in _DEBUG_TAPS:
            @block.gpsimd
            def _(g):
                prol(g)
                g.wait_ge(L.sem("p1_evv"), L.n("p1_evv"))
                g.wait_ge(L.sem("p1_eva"), L.n("p1_eva"))
                g.wait_ge(L.sem("p1_g"), L.n("p1_g"))
                if xnt_dbg is not None:
                    i = g.dma_start(out=xnt_dbg,
                                    in_=xnT[:].rearrange("p dc t -> p (dc t)"))
                    L.inc("p1_dbg", i, 16)
                if g_dbg is not None:
                    i = g.dma_start(out=g_dbg, in_=Gm[:].rearrange("p t j -> p (t j)"))
                    L.inc("p1_dbg", i, 16)
                if st_dbg is not None:
                    g.wait_ge(L.sem("p1_xn"), L.n("p1_xn"))
                    i = g.dma_start(out=st_dbg, in_=stats[:].rearrange("p t k -> p (t k)"))
                    L.inc("p1_dbg", i, 16)
                g.wait_ge(L.sem("p1_dbg"), L.n("p1_dbg"))
            names.append("p1_dbg")
        barrier(names)

    if _MAX_PHASE >= 2:
      with ExitStack() as p2:
        wst = [p2.enter_context(nc.sbuf_tensor(f"wst{i}", [128, DC, 128], F32)) for i in range(2)]
        wbf = [p2.enter_context(nc.sbuf_tensor(f"wbf{i}", [128, DC, 128], BF16)) for i in range(2)]
        qa32 = [p2.enter_context(nc.sbuf_tensor(f"qa32_{i}", [128, 512], F32)) for i in range(2)]
        qab = [p2.enter_context(nc.sbuf_tensor(f"qab{i}", [128, 512], BF16)) for i in range(2)]
        t1b = [p2.enter_context(nc.sbuf_tensor(f"t1b{i}", [128, 512], F32)) for i in range(2)]
        t2b = p2.enter_context(nc.sbuf_tensor("t2b", [128, 512], F32))
        qstg = [p2.enter_context(nc.sbuf_tensor(f"qstg{i}", [128, 512], BF16)) for i in range(2)]
        psq = [p2.enter_context(nc.psum_tensor(f"psq{i}", [128, 512], F32)) for i in range(3)]
        pshf = [p2.enter_context(nc.psum_tensor(f"pshf{i}", [128, 512], F32)) for i in range(2)]

        chunks = []
        groups = []
        for h in range(H):
            chunks.append((wqT, h * 128))
            for g in range(4):
                groups.append((h, g, 0))
        for h in range(H):
            chunks.append((wkT, h * 128))
            for g in range(5):
                groups.append((16 + h, g, 1))
        if _QK_NCK < len(chunks):
            chunks = chunks[:_QK_NCK]
            groups = [g for g in groups if g[0] < _QK_NCK]
        NCK = len(chunks)
        lastg = {}
        firstg = {}
        for gi, (ck, g, part) in enumerate(groups):
            lastg[ck] = gi
            firstg.setdefault(ck, gi)

        def grp_meta(gi):
            ck, g, part = groups[gi]
            h = ck if part == 0 else ck - 16
            if part == 0:
                off = HALO + g * 512
                dst = qt_d
                scale = SCALE
                rhs_g = g + 1
            else:
                off = g * 512
                dst = kt_d
                scale = 1.0
                rhs_g = g
            return ck, h, g, off, dst, scale, rhs_g

        @block.sync
        def _(sy):
            if _SKIP_QK:
                return
            prol(sy)
            i = sy.dma_start(out=cosf[:], in_=cosh)
            L.inc("p2_tb", i, 16)
            i = sy.dma_start(out=sinf[:], in_=sinh)
            L.inc("p2_tb", i, 16)
            for ck in range(NCK):
                if ck >= 2:
                    sy.wait_ge(L.sem("p2_wc"), ck - 1)
                wh, col = chunks[ck]
                src = bass.AP(wh.tensor, col,
                              [[D, 128], [128 * D, DC], [1, 128]])
                i = sy.dma_start(out=wst[ck % 2][:], in_=src)
                L.inc(f"p2_wd{ck % 2}", i, 16)

        gbyck = {}
        for gi, (ck, g, part) in enumerate(groups):
            gbyck.setdefault(ck, []).append(gi)

        def emit_qw(g_, gi):
            ck, h, g, off, dst, scale, rhs_g = grp_meta(gi)
            g_.wait_ge(L.sem("p2_rt"), gi + 1)
            t0 = off if dst is kt_d else g * 512
            dsta = bass.AP(dst.tensor, (t0 // 128) * 128 * (H * 128) + h * 128,
                           [[H * 128, 128], [128 * H * 128, 4], [1, 128]])
            i = g_.dma_start(out=dsta,
                             in_=qstg[gi % 2][:].rearrange("p (a t) -> p a t", a=4))
            L.inc(f"p2_qw{gi % 2}", i, 16)

        @block.gpsimd
        def _(g_):
            if _SKIP_QK:
                return
            prol(g_)
            for ck in range(NCK):
                g_.wait_ge(L.sem(f"p2_wd{ck % 2}"), 16 * (ck // 2 + 1))
                if ck >= 2:
                    g_.wait_ge(L.sem("p2_mm"), lastg[ck - 2] + 1)
                i = g_.tensor_copy(out=wbf[ck % 2][:], in_=wst[ck % 2][:])
                L.inc("p2_wc", i, 1)
                if ck >= 1:
                    for gi in gbyck[ck - 1]:
                        emit_qw(g_, gi)
            for gi in gbyck[NCK - 1]:
                emit_qw(g_, gi)

        @block.tensor
        def _(t):
            if _SKIP_QK:
                return
            prol(t)
            for gi in range(len(groups)):
                ck, h, g, off, dst, scale, rhs_g = grp_meta(gi)
                if gi == firstg[ck]:
                    t.wait_ge(L.sem("p2_wc"), ck + 1)
                if gi >= 3:
                    t.wait_ge(L.sem("p2_ea"), gi - 2)
                    t.wait_ge(L.sem("p2_ev"), gi - 2)
                for dc in range(DC):
                    i = t.matmul(psq[gi % 3][:], wbf[ck % 2][:, dc, :],
                                 xnT_grp(dc, rhs_g), start=(dc == 0), stop=(dc == 15))
                L.inc("p2_mm", i, 1)
                if gi >= 1:
                    j = gi - 1
                    t.wait_ge(L.sem("p2_ev"), j + 1)
                    if j >= 2:
                        t.wait_ge(L.sem("p2_t2"), j - 1)
                    i = t.matmul(pshf[j % 2][:], p128b[:], qab[j % 2][:],
                                 start=True, stop=True)
                    L.inc("p2_sh", i, 1)
            j = len(groups) - 1
            t.wait_ge(L.sem("p2_ev"), j + 1)
            t.wait_ge(L.sem("p2_t2"), j - 1)
            i = t.matmul(pshf[j % 2][:], p128b[:], qab[j % 2][:], start=True, stop=True)
            L.inc("p2_sh", i, 1)

        @block.scalar
        def _(s):
            if _SKIP_QK:
                return
            prol(s)
            for gi in range(len(groups)):
                ck, h, g, off, dst, scale, rhs_g = grp_meta(gi)
                s.wait_ge(L.sem("p2_mm"), gi + 1)
                if gi >= 2:
                    s.wait_ge(L.sem("p2_rt"), gi - 1)
                i = s.activation(out=qa32[gi % 2][:], in_=psq[gi % 3][:],
                                 func=AF.Copy, scale=scale)
                L.inc("p2_ea", i, 1)
                if gi >= 2:
                    s.wait_ge(L.sem("p2_sh"), gi - 1)
                i = s.activation(out=qab[gi % 2][:], in_=psq[gi % 3][:],
                                 func=AF.Copy, scale=scale)
                L.inc("p2_ev", i, 1)

        def rope_finish(v, j):
            _, _, _, off, _, _, _ = grp_meta(j)
            v.wait_ge(L.sem("p2_sh"), j + 1)
            i = v.tensor_tensor(out=t2b[:], in0=pshf[j % 2][:],
                                in1=sinf[:, off:off + 512], op=ALU.mult)
            L.inc("p2_t2", i, 1)
            v.wait_ge(L.sem("p2_ea"), j + 1)
            v.tensor_tensor(out=t1b[0][:], in0=qa32[j % 2][:],
                            in1=cosf[:, off:off + 512], op=ALU.mult)
            v.drain()
            if j >= 2:
                v.wait_ge(L.sem(f"p2_qw{j % 2}"), 16 * ((j - 2) // 2 + 1))
            i = v.tensor_tensor(out=qstg[j % 2][:], in0=t1b[0][:],
                                in1=t2b[:], op=ALU.add)
            L.inc("p2_rt", i, 1)
            v.drain()

        @block.vector
        def _(v):
            if _SKIP_QK:
                return
            prol(v)
            v.wait_ge(L.sem("p2_tb"), 32)
            for gi in range(len(groups)):
                if gi >= 1:
                    rope_finish(v, gi - 1)
            rope_finish(v, len(groups) - 1)

        barrier(["p2_tb", "p2_wd0", "p2_wd1", "p2_wc", "p2_mm", "p2_ea",
                 "p2_ev", "p2_sh", "p2_t2", "p2_rt", "p2_qw0", "p2_qw1"])

        vst = [p2.enter_context(nc.sbuf_tensor(f"vst{i}", [128, 512], F32)) for i in range(2)]
        wvbf = p2.enter_context(nc.sbuf_tensor("wvbf", [128, DC, 512], BF16))
        vstg = [p2.enter_context(nc.sbuf_tensor(f"vstg{i}", [128, 512], BF16)) for i in range(2)]
        psv = [p2.enter_context(nc.psum_tensor(f"psv{i}", [128, 512], F32)) for i in range(2)]

        @block.sync
        def _(sy):
            if _SKIP_V:
                return
            prol(sy)
            for vck in range(4 * DC):
                do, dcc = divmod(vck, DC)
                if vck >= 2:
                    sy.wait_ge(L.sem("p2v_wc"), vck - 1)
                src = bass.AP(wvT.tensor, dcc * 128 * D + do * 512,
                              [[D, 128], [1, 512]])
                i = sy.dma_start(out=vst[vck % 2][:], in_=src)
                L.inc(f"p2v_wd{vck % 2}", i, 16)

        @block.gpsimd
        def _(g_):
            if _SKIP_V:
                return
            prol(g_)

            def emit_vw(gv):
                do_, tt_ = divmod(gv, NTA)
                g_.wait_ge(L.sem("p2v_ev"), gv + 1)
                dsta = bass.AP(v_d.tensor, tt_ * 128 * D + do_ * 512,
                               [[D, 128], [1, 512]])
                i = g_.dma_start(out=dsta, in_=vstg[gv % 2][:])
                L.inc(f"p2v_vw{gv % 2}", i, 16)

            for do in range(4):
                if do >= 1:
                    for gv in range((do - 1) * NTA, do * NTA):
                        emit_vw(gv)
                for dcc in range(DC):
                    vck = do * DC + dcc
                    g_.wait_ge(L.sem(f"p2v_wd{vck % 2}"), 16 * (vck // 2 + 1))
                    if do >= 1:
                        g_.wait_ge(L.sem("p2v_mm"), NTA * do)
                    i = g_.tensor_copy(out=wvbf[:, dcc, :], in_=vst[vck % 2][:])
                    L.inc("p2v_wc", i, 1)
            for gv in range(3 * NTA, 4 * NTA):
                emit_vw(gv)

        @block.tensor
        def _(t):
            if _SKIP_V:
                return
            prol(t)
            for gv in range(4 * NTA):
                do, tt = divmod(gv, NTA)
                if gv >= 2:
                    t.wait_ge(L.sem("p2v_ev"), gv - 1)
                for dc in range(DC):
                    if tt == 0:
                        t.wait_ge(L.sem("p2v_wc"), do * DC + dc + 1)
                    i = t.matmul(psv[gv % 2][:], xnT_sl(tt, dc), wvbf[:, dc, :],
                                 start=(dc == 0), stop=(dc == 15))
                L.inc("p2v_mm", i, 1)

        @block.scalar
        def _(s):
            if _SKIP_V:
                return
            prol(s)
            for gv in range(4 * NTA):
                s.wait_ge(L.sem("p2v_mm"), gv + 1)
                if gv >= 2:
                    s.wait_ge(L.sem(f"p2v_vw{gv % 2}"), 16 * ((gv - 2) // 2 + 1))
                i = s.copy(out=vstg[gv % 2][:], in_=psv[gv % 2][:])
                L.inc("p2v_ev", i, 1)

        barrier(["p2v_wd0", "p2v_wd1", "p2v_wc", "p2v_mm", "p2v_ev",
                 "p2v_vw0", "p2v_vw1"])

    sB.close()

    if _MAX_PHASE >= 3:
      with ExitStack() as p5:
        w1st = [p5.enter_context(nc.sbuf_tensor(f"w1st{i}", [128, 8, 128], F32)) for i in range(2)]
        w1bf = [p5.enter_context(nc.sbuf_tensor(f"w1bf{i}", [128, DC, 128], BF16)) for i in range(2)]
        w2st = [p5.enter_context(nc.sbuf_tensor(f"w2st{i}", [128, 512], F32)) for i in range(2)]
        w2bd = [p5.enter_context(nc.sbuf_tensor(f"w2bd{i}", [128, FCN, 512], BF16)) for i in range(2)]
        hT = p5.enter_context(nc.sbuf_tensor("hT", [128, FCN, 1024], BF16))
        acc = p5.enter_context(nc.sbuf_tensor("acc", [128, 8, D], F32))
        psh = [p5.enter_context(nc.psum_tensor(f"psh{i}", [128, 512], F32)) for i in range(4)]
        psy = [p5.enter_context(nc.psum_tensor(f"psy{i}", [128, 512], F32)) for i in range(4)]

        NJ = NE
        n_hgrp = 2 * FCN
        n_ygrp = 4 * 8

        @block.sync
        def _(sy):
            prol(sy)
            for half in range(2):
                for j in range(NJ):
                    jj = half * NJ + j
                    for fc in range(FCN):
                        ck = jj * FCN + fc
                        for hp in range(2):
                            ck2 = ck * 2 + hp
                            if ck2 >= 2:
                                sy.wait_ge(L.sem("p5_w1h"), ck2 - 1)
                            src = bass.AP(w1h.tensor,
                                          j * D * FF + hp * 8 * 128 * FF + fc * 128,
                                          [[FF, 128], [128 * FF, 8], [1, 128]])
                            i = sy.dma_start(out=w1st[ck2 % 2][:], in_=src)
                            L.inc(f"p5_w1d{ck2 % 2}", i, 16)


        @block.vector
        def _(v):
            prol(v)
            for half in range(2):
                for j in range(NJ):
                    jj = half * NJ + j
                    for do in range(4):
                        gd = jj * 4 + do
                        for tile in range(8):
                            gy = gd * 8 + tile
                            v.wait_ge(L.sem("p5_ymm"), gy + 1)
                            if half == 1 and j == 0 and do == 0:
                                v.wait_ge(L.sem("p5_acw"), 128)
                            dsl = acc[:, tile, do * 512:(do + 1) * 512]
                            if j == 0:
                                i = v.tensor_single_scalar(out=dsl, in_=psy[gy % 4][:],
                                                           scalar=1.0, op=ALU.mult)
                            elif j == 1:
                                i = v.scalar_tensor_tensor(out=dsl, in0=psy[gy % 4][:],
                                                           scalar=1.0, in1=dsl,
                                                           op0=ALU.mult, op1=ALU.add)
                            else:
                                gate = Gm[:, half * 8 + tile, j - 2:j - 1]
                                i = v.scalar_tensor_tensor(out=dsl, in0=psy[gy % 4][:],
                                                           scalar=gate, in1=dsl,
                                                           op0=ALU.mult, op1=ALU.add)
                            L.inc("p5_yev", i, 1)

        @block.scalar
        def _(s):
            prol(s)
            NCKT = 2 * NJ * FCN

            def cast_w1c(ck):
                if ck >= NCKT:
                    return
                if ck >= 2:
                    s.wait_ge(L.sem("p5_hmm"), 2 * (ck - 1))
                for hp in range(2):
                    ck2 = ck * 2 + hp
                    s.wait_ge(L.sem(f"p5_w1d{ck2 % 2}"), 16 * (ck2 // 2 + 1))
                    i = s.activation(out=w1bf[ck % 2][:, hp * 8:(hp + 1) * 8, :],
                                     in_=w1st[ck2 % 2][:], func=AF.Copy)
                    L.inc("p5_w1h", i, 1)

            def cast_w2(jj2, dos):
                for do in dos:
                    gd = jj2 * 4 + do
                    for fc in range(FCN):
                        cw = gd * FCN + fc
                        s.wait_ge(L.sem(f"p5_w2d{cw % 2}"), 16 * (cw // 2 + 1))
                        if gd >= 2:
                            s.wait_ge(L.sem("p5_ymm"), 8 * (gd - 1))
                        i = s.activation(out=w2bd[gd % 2][:, fc, :],
                                         in_=w2st[cw % 2][:], func=AF.Copy)
                        L.inc("p5_w2c", i, 1)

            def cast_w2_one(jj2, k):
                do, fc = divmod(k, FCN)
                gd = jj2 * 4 + do
                cw = gd * FCN + fc
                s.wait_ge(L.sem(f"p5_w2d{cw % 2}"), 16 * (cw // 2 + 1))
                if gd >= 2:
                    s.wait_ge(L.sem("p5_ymm"), 8 * (gd - 1))
                i = s.activation(out=w2bd[gd % 2][:, fc, :],
                                 in_=w2st[cw % 2][:], func=AF.Copy)
                L.inc("p5_w2c", i, 1)

            cast_w1c(0)
            for half in range(2):
                for j in range(NJ):
                    jj = half * NJ + j
                    for fc in range(FCN):
                        ck = jj * FCN + fc
                        cast_w1c(ck + 1)
                        for g2 in range(2):
                            gh = ck * 2 + g2
                            s.wait_ge(L.sem("p5_hmm"), gh + 1)
                            i = s.activation(out=hT[:, fc, g2 * 512:(g2 + 1) * 512],
                                             in_=psh[gh % 4][:], func=AF.Relu,
                                             bias=b1c[:, j * FCN + fc:j * FCN + fc + 1])
                            L.inc("p5_hev", i, 1)
                        for k in (2 * fc, 2 * fc + 1):
                            cast_w2_one(jj, k)
                    cast_w2(jj, (2, 3))


        @block.tensor
        def _(t):
            prol(t)
            for half in range(2):
                for j in range(NJ):
                    jj = half * NJ + j
                    for fc in range(FCN):
                        ck = jj * FCN + fc
                        t.wait_ge(L.sem("p5_w1h"), 2 * (ck + 1))
                        for g2 in range(2):
                            gh = ck * 2 + g2
                            if gh >= 4:
                                t.wait_ge(L.sem("p5_hev"), gh - 3)
                            for dc in range(DC):
                                i = t.matmul(psh[gh % 4][:], w1bf[ck % 2][:, dc, :],
                                             xnT[:, dc, half * 1024 + g2 * 512:
                                                 half * 1024 + (g2 + 1) * 512],
                                             start=(dc == 0), stop=(dc == 15))
                            L.inc("p5_hmm", i, 1)
                    for do in range(4):
                        gd = jj * 4 + do
                        for tile in range(8):
                            gy = gd * 8 + tile
                            if gy >= 4:
                                t.wait_ge(L.sem("p5_yev"), gy - 3)
                            nmm = FCN + (2 if j == 0 else 0)
                            k = 0
                            for fc in range(FCN):
                                if tile == 0:
                                    t.wait_ge(L.sem("p5_w2c"), gd * FCN + fc + 1)
                                if do == 0 and tile == 0:
                                    t.wait_ge(L.sem("p5_hev"), 2 * (jj * FCN + fc) + 2)
                                i = t.matmul(psy[gy % 4][:],
                                             hT[:, fc, tile * 128:(tile + 1) * 128],
                                             w2bd[gd % 2][:, fc, :],
                                             start=(fc == 0), stop=(fc == FCN - 1 and j != 0))
                                k += 1
                            if j == 0:
                                tok0 = (half * 8 + tile) * 128
                                t.matmul(psy[gy % 4][:], GTb[:, tok0:tok0 + 128],
                                         b2rb[:, do * 512:(do + 1) * 512],
                                         start=False, stop=False)
                                i = t.matmul(psy[gy % 4][:], ones2[:],
                                             b2sb[:, do * 512:(do + 1) * 512],
                                             start=False, stop=True)
                            L.inc("p5_ymm", i, 1)

        @block.gpsimd
        def _(g_):
            prol(g_)

            def acc_out(half):
                g_.wait_ge(L.sem("p5_yev"), (half + 1) * NJ * 32)
                for tile in range(8):
                    i = g_.dma_start(out=acc_d[(half * 8 + tile) * 128:
                                               (half * 8 + tile + 1) * 128, :],
                                     in_=acc[:, tile, :])
                    L.inc("p5_acw", i, 16)

            for jj2 in range(2 * NJ):
                j = jj2 % NJ
                if jj2 == NJ:
                    acc_out(0)
                for do in range(4):
                    gd = jj2 * 4 + do
                    for fc in range(FCN):
                        cw = gd * FCN + fc
                        if cw >= 2:
                            g_.wait_ge(L.sem("p5_w2c"), cw - 1)
                        src = bass.AP(w2h.tensor, j * FF * D + fc * 128 * D + do * 512,
                                      [[D, 128], [1, 512]])
                        i = g_.dma_start(out=w2st[cw % 2][:], in_=src)
                        L.inc(f"p5_w2d{cw % 2}", i, 16)
            acc_out(1)

        barrier(["p5_w1d0", "p5_w1d1", "p5_w1h", "p5_w2d0",
                 "p5_w2d1", "p5_w2c", "p5_hmm", "p5_hev", "p5_ymm", "p5_yev",
                 "p5_acw"])

    sA.close()

    if _MAX_PHASE >= 4:
      aoS = ExitStack()
      aoT = aoS.enter_context(nc.sbuf_tensor("aoT", [128, H, T], BF16))
      with ExitStack() as p4:
        masks = p4.enter_context(nc.sbuf_tensor("masks", [128, 25, 128], BF16))
        masks32 = p4.enter_context(nc.sbuf_tensor("masks32", [128, 25 * 128], F32))
        kts = [p4.enter_context(nc.sbuf_tensor(f"kts{i}", [128, H * 128], BF16)) for i in range(6)]
        vs = [p4.enter_context(nc.sbuf_tensor(f"vs{i}", [128, D], BF16)) for i in range(6)]
        qts = [p4.enter_context(nc.sbuf_tensor(f"qts{i}", [128, H * 128], BF16)) for i in range(3)]
        pa = [p4.enter_context(nc.sbuf_tensor(f"pa{i}", [128, 640], BF16)) for i in range(2)]
        panb = [p4.enter_context(nc.sbuf_tensor(f"panb{i}", [128, 640], BF16)) for i in range(2)]
        ptb = [p4.enter_context(nc.sbuf_tensor(f"ptb{i}", [128, 640], BF16)) for i in range(2)]
        negmb = [p4.enter_context(nc.sbuf_tensor(f"negmb{i}", [128, 1], F32)) for i in range(2)]
        sumb = [p4.enter_context(nc.sbuf_tensor(f"sumb{i}", [128, 1], F32)) for i in range(2)]
        rcb = p4.enter_context(nc.sbuf_tensor("rcb", [128, 1], F32))
        pss = [p4.enter_context(nc.psum_tensor(f"pss{i}", [128, 640], F32)) for i in range(2)]
        pspt = [p4.enter_context(nc.psum_tensor(f"pspt{i}", [128, 640], BF16)) for i in range(2)]
        psao = [p4.enter_context(nc.psum_tensor(f"psao{i}", [128, 128], F32)) for i in range(2)]

        NIDX = NT * H

        @block.gpsimd
        def _(g_):
            prol(g_)
            g_.wait_ge(L.sem("p4_mk"), 16)
            i = g_.tensor_copy(out=masks[:].rearrange("p a t -> p (a t)"), in_=masks32[:])
            L.inc("p4_mkc", i, 1)
            for kt in range(NTA):
                if kt >= 6:
                    g_.wait_ge(L.sem("p4_avm"), 16 * (kt - 5))
                sl = kt % 6
                i = g_.dma_start(out=kts[sl][:], in_=kt_d[kt * 128:(kt + 1) * 128, :])
                L.inc(f"p4_kv{sl}", i, 16)
                i = g_.dma_start(out=vs[sl][:], in_=v_d[kt * 128:(kt + 1) * 128, :])
                L.inc(f"p4_kv{sl}", i, 16)

        @block.sync
        def _(sy):
            prol(sy)
            i = sy.dma_start(out=masks32[:], in_=maskh)
            L.inc("p4_mk", i, 16)
            for qt in range(NT):
                if qt >= 3:
                    sy.wait_ge(L.sem("p4_smm"), 16 * (qt - 2))
                i = sy.dma_start(out=qts[qt % 3][:], in_=qt_d[qt * 128:(qt + 1) * 128, :])
                L.inc(f"p4_qt{qt % 3}", i, 16)

        def pe_T(t, j):
            t.wait_ge(L.sem("p4_pan"), j + 1)
            if j >= 2:
                t.wait_ge(L.sem("p4_pte"), j - 1)
            for d in range(5):
                i = t.transpose(out=pspt[j % 2][:, d * 128:(d + 1) * 128],
                                in_=panb[j % 2][:, d * 128:(d + 1) * 128],
                                identity=idenb[:])
            L.inc("p4_tmm", i, 1)

        def pe_AV(t, j):
            qt, h = divmod(j, H)
            t.wait_ge(L.sem("p4_pte"), j + 1)
            if j >= 2:
                t.wait_ge(L.sem("p4_aoe"), j - 1)
            for d in range(5):
                sl = (qt + d) % 6
                i = t.matmul(psao[j % 2][:], vs[sl][:, h * 128:(h + 1) * 128],
                             ptb[j % 2][:, d * 128:(d + 1) * 128],
                             start=(d == 0), stop=(d == 4))
            L.inc("p4_avm", i, 1)

        @block.tensor
        def _(t):
            prol(t)
            for idx in range(NIDX):
                qt, h = divmod(idx, H)
                if idx == 0:
                    t.wait_ge(L.sem("p4_mkc"), 1)
                if h == 0:
                    for kt in range(qt, qt + 5):
                        t.wait_ge(L.sem(f"p4_kv{kt % 6}"), 32 * (kt // 6 + 1))
                    t.wait_ge(L.sem(f"p4_qt{qt % 3}"), 16 * (qt // 3 + 1))
                if idx >= 2:
                    t.wait_ge(L.sem("p4_mx"), idx - 1)
                    t.wait_ge(L.sem("p4_ex"), idx - 1)
                sl = idx % 2
                for d in range(5):
                    ks = (qt + d) % 6
                    t.matmul(pss[sl][:, d * 128:(d + 1) * 128],
                             qts[qt % 3][:, h * 128:(h + 1) * 128],
                             kts[ks][:, h * 128:(h + 1) * 128],
                             start=(d == 0 or d == 4), stop=False)
                dlist = range(5) if qt < 4 else (0, 4)
                tc = qt if qt < 4 else 4
                for d in dlist:
                    last_b0 = (d == max(x for x in dlist if x < 4))
                    i = t.matmul(pss[sl][:, d * 128:(d + 1) * 128], idenb[:],
                                 masks[:, tc * 5 + d, :],
                                 start=False, stop=(last_b0 or d == 4))
                L.inc("p4_smm", i, 1)
                if idx >= 2:
                    pe_T(t, idx - 2)
                    pe_AV(t, idx - 2)
            for j in (NIDX - 2, NIDX - 1):
                pe_T(t, j)
                pe_AV(t, j)

        @block.vector
        def _(v):
            prol(v)

            def dve_tail(j):
                v.wait_ge(L.sem("p4_ex"), j + 1)
                if j >= 1:
                    v.wait_ge(L.sem("p4_pan"), j)
                v.reciprocal(out=rcb[:], in_=sumb[j % 2][:])
                v.drain()
                if j >= 2:
                    v.wait_ge(L.sem("p4_tmm"), j - 1)
                i = v.tensor_single_scalar(out=panb[j % 2][:], in_=pa[j % 2][:],
                                           scalar=rcb[:, 0:1], op=ALU.mult)
                L.inc("p4_pan", i, 1)
                if j >= 1:
                    v.wait_ge(L.sem("p4_tmm"), j)
                    if j >= 3:
                        v.wait_ge(L.sem("p4_avm"), j - 2)
                    i = v.tensor_copy(out=ptb[(j - 1) % 2][:], in_=pspt[(j - 1) % 2][:])
                    L.inc("p4_pte", i, 1)

            for idx in range(NIDX):
                v.wait_ge(L.sem("p4_smm"), idx + 1)
                i = v.reduce_max(out=negmb[idx % 2][:], in_=pss[idx % 2][:],
                                 axis=AX.X, negate=True)
                L.inc("p4_mx", i, 1)
                if idx >= 1:
                    dve_tail(idx - 1)
            dve_tail(NIDX - 1)
            j = NIDX - 1
            v.wait_ge(L.sem("p4_tmm"), j + 1)
            v.wait_ge(L.sem("p4_avm"), j - 1)
            i = v.tensor_copy(out=ptb[j % 2][:], in_=pspt[j % 2][:])
            L.inc("p4_pte", i, 1)

        def act_tail(s, j):
            qt, h = divmod(j, H)
            s.wait_ge(L.sem("p4_avm"), j + 1)
            i = s.copy(out=aoT[:, h, qt * 128:(qt + 1) * 128], in_=psao[j % 2][:])
            L.inc("p4_aoe", i, 1)

        @block.scalar
        def _(s):
            prol(s)
            for idx in range(NIDX):
                s.wait_ge(L.sem("p4_mx"), idx + 1)
                if idx >= 2:
                    s.wait_ge(L.sem("p4_pan"), idx - 1)
                i = s.activation(out=pa[idx % 2][:], in_=pss[idx % 2][:], func=AF.Exp,
                                 bias=negmb[idx % 2][:, 0:1], accum_out=sumb[idx % 2][:])
                L.inc("p4_ex", i, 1)
                if idx >= 2:
                    act_tail(s, idx - 2)
            for j in (NIDX - 2, NIDX - 1):
                act_tail(s, j)

        barrier([f"p4_kv{s}" for s in range(6)] + [f"p4_qt{s}" for s in range(3)]
                + ["p4_smm", "p4_mx", "p4_ex", "p4_pan",
                   "p4_tmm", "p4_pte", "p4_avm", "p4_aoe"])

      with ExitStack() as p4w:
        wost = [p4w.enter_context(nc.sbuf_tensor(f"wost{i}", [128, 512], F32)) for i in range(2)]
        wodb = [p4w.enter_context(nc.sbuf_tensor(f"wodb{i}", [128, H, 512], BF16)) for i in range(2)]
        sstg = [p4w.enter_context(nc.sbuf_tensor(f"sstg{i}", [128, 512], BF16)) for i in range(2)]
        pss2 = [p4w.enter_context(nc.psum_tensor(f"pss2_{i}", [128, 512], F32)) for i in range(2)]

        @block.sync
        def _(sy):
            prol(sy)
            for cwo in range(4 * H):
                if cwo >= 2:
                    sy.wait_ge(L.sem("p4w_wc"), cwo - 1)
                do, h2 = divmod(cwo, H)
                src = bass.AP(woT.tensor, h2 * 128 * D + do * 512, [[D, 128], [1, 512]])
                i = sy.dma_start(out=wost[cwo % 2][:], in_=src)
                L.inc(f"p4w_wd{cwo % 2}", i, 16)

        @block.gpsimd
        def _(g_):
            prol(g_)

            def emit_sw(gw):
                do_, tile_ = divmod(gw, NT)
                g_.wait_ge(L.sem("p4w_ev"), gw + 1)
                dsta = bass.AP(s_d.tensor, tile_ * 128 * D + do_ * 512,
                               [[D, 128], [1, 512]])
                i = g_.dma_start(out=dsta, in_=sstg[gw % 2][:])
                L.inc(f"p4w_sw{gw % 2}", i, 16)

            for do in range(4):
                if do >= 1:
                    for gw in range((do - 1) * NT, do * NT):
                        emit_sw(gw)
            for gw in range(3 * NT, 4 * NT):
                emit_sw(gw)

        @block.tensor
        def _(t):
            prol(t)
            for gw in range(4 * NT):
                do, tile = divmod(gw, NT)
                if gw >= 2:
                    t.wait_ge(L.sem("p4w_ev"), gw - 1)
                for h2 in range(H):
                    if tile == 0:
                        t.wait_ge(L.sem("p4w_wc"), do * H + h2 + 1)
                    i = t.matmul(pss2[gw % 2][:], aoT[:, h2, tile * 128:(tile + 1) * 128],
                                 wodb[do % 2][:, h2, :], start=(h2 == 0), stop=(h2 == H - 1))
                L.inc("p4w_mm", i, 1)

        @block.scalar
        def _(s):
            prol(s)
            for do in range(4):
                for h2 in range(H):
                    cwo = do * H + h2
                    s.wait_ge(L.sem(f"p4w_wd{cwo % 2}"), 16 * (cwo // 2 + 1))
                    if do >= 2:
                        s.wait_ge(L.sem("p4w_mm"), NT * (do - 1))
                    i = s.activation(out=wodb[do % 2][:, h2, :], in_=wost[cwo % 2][:],
                                     func=AF.Copy)
                    L.inc("p4w_wc", i, 1)
                for gw in range(do * NT, (do + 1) * NT):
                    s.wait_ge(L.sem("p4w_mm"), gw + 1)
                    if gw >= 2:
                        s.wait_ge(L.sem(f"p4w_sw{gw % 2}"), 16 * ((gw - 2) // 2 + 1))
                    i = s.copy(out=sstg[gw % 2][:], in_=pss2[gw % 2][:])
                    L.inc("p4w_ev", i, 1)

        barrier(["p4w_wd0", "p4w_wd1", "p4w_wc", "p4w_mm", "p4w_ev",
                 "p4w_sw0", "p4w_sw1"])
      aoS.close()

    if _MAX_PHASE >= 5:
      with ExitStack() as p6:
        xb6 = [p6.enter_context(nc.sbuf_tensor(f"xb6_{i}", [128, D], F32)) for i in range(2)]
        ab6 = [p6.enter_context(nc.sbuf_tensor(f"ab6_{i}", [128, D], F32)) for i in range(2)]
        sb6 = [p6.enter_context(nc.sbuf_tensor(f"sb6_{i}", [128, D], BF16)) for i in range(2)]
        xn6 = p6.enter_context(nc.sbuf_tensor("xn6", [128, D], F32))
        ffb = p6.enter_context(nc.sbuf_tensor("ffb", [128, D], F32))
        g6b = p6.enter_context(nc.sbuf_tensor("g6b", [128, D], F32))
        sgb6 = p6.enter_context(nc.sbuf_tensor("sgb6", [128, D], F32))
        outb = [p6.enter_context(nc.sbuf_tensor(f"outb{i}", [128, D], F32)) for i in range(2)]

        @block.sync
        def _(sy):
            prol(sy)
            for tt in range(NT):
                if tt >= 2:
                    sy.wait_ge(L.sem("p6_ot"), tt - 1)
                i = sy.dma_start(out=xb6[tt % 2][:], in_=x_own[tt * 128:(tt + 1) * 128, :])
                L.inc(f"p6_xd{tt % 2}", i, 16)
                i = sy.dma_start(out=ab6[tt % 2][:], in_=acc_d[tt * 128:(tt + 1) * 128, :])
                L.inc(f"p6_ad{tt % 2}", i, 16)

        @block.gpsimd
        def _(g_):
            prol(g_)

            def emit_ow(tt):
                g_.wait_ge(L.sem("p6_ot"), tt + 1)
                i = g_.dma_start(out=outh[tt * 128:(tt + 1) * 128, :], in_=outb[tt % 2][:])
                L.inc(f"p6_ow{tt % 2}", i, 16)

            for tt in range(NT):
                if tt >= 2:
                    g_.wait_ge(L.sem("p6_ot"), tt - 1)
                i = g_.dma_start(out=sb6[tt % 2][:], in_=s_d[tt * 128:(tt + 1) * 128, :])
                L.inc(f"p6_sd{tt % 2}", i, 16)
                if tt >= 2:
                    emit_ow(tt - 2)
            emit_ow(NT - 2)
            emit_ow(NT - 1)
            g_.wait_ge(L.sem("p6_ow0"), 16 * (NT // 2))
            g_.wait_ge(L.sem("p6_ow1"), 16 * (NT // 2))

        @block.vector
        def _(v):
            prol(v)
            for tt in range(NT):
                st = tt + 4
                v.wait_ge(L.sem(f"p6_xd{tt % 2}"), 16 * (tt // 2 + 1))
                v.wait_ge(L.sem(f"p6_ad{tt % 2}"), 16 * (tt // 2 + 1))
                v.tensor_scalar(out=xn6[:], in0=xb6[tt % 2][:],
                                scalar1=stats[:, st, 0:1], scalar2=stats[:, st, 1:2],
                                op0=ALU.subtract, op1=ALU.mult)
                v.drain()
                if tt >= 1:
                    v.wait_ge(L.sem("p6_sl"), tt)
                i = v.tensor_tensor(out=ffb[:], in0=xn6[:], in1=ab6[tt % 2][:], op=ALU.add)
                L.inc("p6_ff", i, 1)
                v.wait_ge(L.sem("p6_sl"), tt + 1)
                v.wait_ge(L.sem(f"p6_sd{tt % 2}"), 16 * (tt // 2 + 1))
                v.tensor_tensor(out=sgb6[:], in0=sb6[tt % 2][:], in1=g6b[:], op=ALU.mult)
                v.drain()
                if tt >= 2:
                    v.wait_ge(L.sem(f"p6_ow{tt % 2}"), 16 * ((tt - 2) // 2 + 1))
                i = v.tensor_tensor(out=outb[tt % 2][:], in0=xb6[tt % 2][:],
                                    in1=sgb6[:], op=ALU.add)
                L.inc("p6_ot", i, 1)

        @block.scalar
        def _(s):
            prol(s)
            for tt in range(NT):
                s.wait_ge(L.sem("p6_ff"), tt + 1)
                i = s.activation(out=g6b[:], in_=ffb[:], func=AF.Silu)
                L.inc("p6_sl", i, 1)

        @block.sync
        def _(sy):
            sy.wait_ge(L.sem("p6_ow0"), 16 * (NT // 2))
            sy.wait_ge(L.sem("p6_ow1"), 16 * (NT // 2))

    gctx.close()
    return nc


def _host_constants(hf):
    P0 = hf * 2048
    pos = np.maximum(P0 - HALO + np.arange(TK), 0).astype(np.float64)
    inv = 10000.0 ** (-(np.arange(0, DH, 2, dtype=np.float64)) / DH)
    ang = pos[None, :] * inv[:, None]
    c = np.cos(ang)
    sn = np.sin(ang)
    cosf = np.empty((128, TK), np.float32)
    sinf = np.empty((128, TK), np.float32)
    cosf[0::2] = c
    cosf[1::2] = c
    sinf[0::2] = -sn
    sinf[1::2] = sn

    ii = np.arange(128)[:, None]
    jj = np.arange(128)[None, :]
    masks = np.zeros((128, 25, 128), np.float32)
    for tc in range(5):
        for d in range(5):
            m = np.zeros((128, 128), np.float32)
            if d == 0:
                m[~(jj > ii) * np.ones((128, 128), bool)] = 0.0
                m = np.where(jj > ii, 0.0, -1e9).astype(np.float32)
            elif d == 4:
                m = np.where(jj <= ii, 0.0, -1e9).astype(np.float32)
            if hf == 0 and tc < 4 and tc + d <= 3:
                m[:] = -1e9
            masks[:, tc * 5 + d, :] = m
    masksb = masks.reshape(128, 25 * 128)
    return cosf, sinf, masksb


def prep_inputs(L0_attn_x, L0_attn_Wq, L0_attn_Wk, L0_attn_Wv, L0_attn_Wo,
                L0_ffn_shared_w1, L0_ffn_shared_b1, L0_ffn_shared_w2, L0_ffn_shared_b2,
                L0_ffn_routed_w1, L0_ffn_routed_b1, L0_ffn_routed_w2, L0_ffn_routed_b2,
                L0_ffn_centroids, L0_ffn_bias):
    x = np.ascontiguousarray(np.asarray(L0_attn_x, dtype=np.float32))
    wqT = np.ascontiguousarray(np.asarray(L0_attn_Wq, np.float32).T)
    wkT = np.ascontiguousarray(np.asarray(L0_attn_Wk, np.float32).T)
    wvT = np.ascontiguousarray(np.asarray(L0_attn_Wv, np.float32).T)
    woT = np.ascontiguousarray(np.asarray(L0_attn_Wo, np.float32).T)
    w1 = np.ascontiguousarray(np.concatenate([np.asarray(L0_ffn_shared_w1, np.float32),
                                              np.asarray(L0_ffn_routed_w1, np.float32)], 0))
    w2 = np.ascontiguousarray(np.concatenate([np.asarray(L0_ffn_shared_w2, np.float32),
                                              np.asarray(L0_ffn_routed_w2, np.float32)], 0))
    b1all = np.concatenate([np.asarray(L0_ffn_shared_b1, np.float32),
                            np.asarray(L0_ffn_routed_b1, np.float32)], 0)
    b1c = np.ascontiguousarray(b1all.reshape(NE, FCN, 128).transpose(2, 0, 1)
                               .reshape(128, NE * FCN))
    b2r = np.ascontiguousarray(np.asarray(L0_ffn_routed_b2, np.float32))
    b2s = np.ascontiguousarray(np.asarray(L0_ffn_shared_b2, np.float32))
    cen = np.asarray(L0_ffn_centroids, np.float32)
    cenT = np.ascontiguousarray(cen.T.reshape(DC, 128, 16).transpose(1, 0, 2)
                                .reshape(128, DC * 16))
    gbias = np.ascontiguousarray(np.asarray(L0_ffn_bias, np.float32).reshape(1, 16))
    p128 = np.zeros((128, 128), np.float32)
    for i in range(64):
        p128[2 * i, 2 * i + 1] = 1.0
        p128[2 * i + 1, 2 * i] = 1.0
    iden = np.eye(128, dtype=np.float32)
    consts = {hf: _host_constants(hf) for hf in range(2)}

    in_maps = []
    for c in range(8):
        b, hf = divmod(c, 2)
        P0 = hf * 2048
        cosf, sinf, masksb = consts[hf]
        x_own = np.ascontiguousarray(x[b, P0:P0 + T])
        x_halo = np.ascontiguousarray(x[b, P0 - HALO:P0]) if hf == 1 else \
            np.zeros((HALO, D), np.float32)
        in_maps.append(dict(
            x_own=x_own, x_halo=x_halo, wqT=wqT, wkT=wkT, wvT=wvT, woT=woT,
            w1=w1, w2=w2, b1c=b1c, b2r=b2r, b2s=b2s, cenT=cenT, gbias=gbias,
            cosf=cosf, sinf=sinf, p128=p128, iden=iden, masksb=masksb))
    return in_maps


def kernel(**inputs):
    in_maps = prep_inputs(**inputs)
    nc = build_program()
    res = run_bass_kernel_spmd(nc, in_maps, core_ids=list(range(8)))
    out = np.empty((4, 4096, D), np.float32)
    for c in range(8):
        b, hf = divmod(c, 2)
        out[b, hf * 2048:(hf + 1) * 2048] = res.results[c]["out"]
    return out


# version 12 (final)
# speedup vs baseline: 1.3162x; 1.0060x over previous
import numpy as np
import ml_dtypes
from contextlib import ExitStack

import concourse.bass as bass
import concourse.mybir as mybir
from concourse.bass_utils import run_bass_kernel_spmd

F32 = mybir.dt.float32
BF16 = mybir.dt.bfloat16
AF = mybir.ActivationFunctionType
ALU = mybir.AluOpType
AX = mybir.AxisListType

D = 2048
DC = 16
T = 2048
HALO = 512
TK = 2560
NT = 16
NTA = 20
H = 16
DH = 128
FF = 1408
FCN = 11
NE = 18
SCALE = float(128 ** -0.5)
EPS = 1e-5

_DEBUG_TAPS = set()
_MAX_PHASE = 99
_SKIP_QK = False
_SKIP_V = False
_QK_NCK = 99


class Ledger:
    def __init__(self, nc, ctx):
        self.nc, self.ctx = nc, ctx
        self.h = {}
        self.c = {}

    def sem(self, name):
        if name not in self.h:
            self.h[name] = self.ctx.enter_context(self.nc.semaphore(name))
            self.c[name] = 0
        return self.h[name]

    def inc(self, name, instr, k):
        instr.then_inc(self.sem(name), k)
        self.c[name] += k

    def n(self, name):
        return self.c.get(name, 0)


def build_program():
    nc = bass.Bass("TRN2", target_bir_lowering=False)

    x_own = nc.dram_tensor("x_own", [T, D], F32, kind="ExternalInput").ap()
    x_halo = nc.dram_tensor("x_halo", [HALO, D], F32, kind="ExternalInput").ap()
    wqT = nc.dram_tensor("wqT", [D, D], F32, kind="ExternalInput").ap()
    wkT = nc.dram_tensor("wkT", [D, D], F32, kind="ExternalInput").ap()
    wvT = nc.dram_tensor("wvT", [D, D], F32, kind="ExternalInput").ap()
    woT = nc.dram_tensor("woT", [D, D], F32, kind="ExternalInput").ap()
    w1h = nc.dram_tensor("w1", [NE, D, FF], F32, kind="ExternalInput").ap()
    w2h = nc.dram_tensor("w2", [NE, FF, D], F32, kind="ExternalInput").ap()
    b1ch = nc.dram_tensor("b1c", [128, NE * FCN], F32, kind="ExternalInput").ap()
    b2rh = nc.dram_tensor("b2r", [16, D], F32, kind="ExternalInput").ap()
    b2sh = nc.dram_tensor("b2s", [2, D], F32, kind="ExternalInput").ap()
    cenh = nc.dram_tensor("cenT", [128, DC * 16], F32, kind="ExternalInput").ap()
    gbh = nc.dram_tensor("gbias", [1, 16], F32, kind="ExternalInput").ap()
    cosh = nc.dram_tensor("cosf", [128, TK], F32, kind="ExternalInput").ap()
    sinh = nc.dram_tensor("sinf", [128, TK], F32, kind="ExternalInput").ap()
    p128h = nc.dram_tensor("p128", [128, 128], F32, kind="ExternalInput").ap()
    idh = nc.dram_tensor("iden", [128, 128], F32, kind="ExternalInput").ap()
    maskh = nc.dram_tensor("masksb", [128, 25 * 128], F32, kind="ExternalInput").ap()

    outh = nc.dram_tensor("out", [T, D], F32, kind="ExternalOutput").ap()

    def skind(nm):
        return "ExternalOutput" if nm in _DEBUG_TAPS else "Internal"

    qt_d = nc.dram_tensor("qt_d", [NT * 128, H * 128], BF16, kind=skind("qt_d")).ap()
    kt_d = nc.dram_tensor("kt_d", [NTA * 128, H * 128], BF16, kind=skind("kt_d")).ap()
    v_d = nc.dram_tensor("v_d", [TK, D], BF16, kind=skind("v_d")).ap()
    s_d = nc.dram_tensor("s_d", [T, D], BF16, kind=skind("s_d")).ap()
    acc_d = nc.dram_tensor("acc_d", [T, D], F32, kind=skind("acc_d")).ap()
    xnt_dbg = g_dbg = None
    if "xnt_d" in _DEBUG_TAPS:
        xnt_dbg = nc.dram_tensor("xnt_d", [128, DC * T], BF16, kind="ExternalOutput").ap()
    if "g_d" in _DEBUG_TAPS:
        g_dbg = nc.dram_tensor("g_d", [128, NT * 16], F32, kind="ExternalOutput").ap()
    st_dbg = None
    if "st_d" in _DEBUG_TAPS:
        st_dbg = nc.dram_tensor("st_d", [128, NTA * 2], F32, kind="ExternalOutput").ap()

    gctx = ExitStack()
    L = Ledger(nc, gctx)
    PH = L.sem("ph")
    pno = [0]

    idenb = gctx.enter_context(nc.sbuf_tensor("idenb", [128, 128], BF16))
    p128b = gctx.enter_context(nc.sbuf_tensor("p128b", [128, 128], BF16))
    stats = gctx.enter_context(nc.sbuf_tensor("stats", [128, NTA, 2], F32))
    b1c = gctx.enter_context(nc.sbuf_tensor("b1c_s", [128, NE * FCN], F32))
    gbias = gctx.enter_context(nc.sbuf_tensor("gbias_s", [1, 16], F32))
    bplane = gctx.enter_context(nc.sbuf_tensor("bplane", [128, 16], F32))
    ones2 = gctx.enter_context(nc.sbuf_tensor("ones2", [2, 128], BF16))
    Gm = gctx.enter_context(nc.sbuf_tensor("Gm", [128, NT, 16], F32))
    GTb = gctx.enter_context(nc.sbuf_tensor("GTb", [16, T], BF16))
    b2rb = gctx.enter_context(nc.sbuf_tensor("b2rb", [16, D], BF16))
    b2sb = gctx.enter_context(nc.sbuf_tensor("b2sb", [2, D], BF16))
    epsb = gctx.enter_context(nc.sbuf_tensor("epsb", [128, 1], F32))

    sA = ExitStack()
    xnT = sA.enter_context(nc.sbuf_tensor("xnT", [128, DC, T], BF16))
    sB = ExitStack()
    xnTh = sB.enter_context(nc.sbuf_tensor("xnTh", [128, DC, HALO], BF16))
    cosf = sB.enter_context(nc.sbuf_tensor("cosf_s", [128, TK], F32))
    sinf = sB.enter_context(nc.sbuf_tensor("sinf_s", [128, TK], F32))
    iden32 = sB.enter_context(nc.sbuf_tensor("iden32", [128, 128], F32))
    cenT = sB.enter_context(nc.sbuf_tensor("cenT_s", [128, DC, 16], F32))
    ones1 = sB.enter_context(nc.sbuf_tensor("ones1", [1, 128], F32))

    block = gctx.enter_context(nc.Block())

    def barrier(names):
        pno[0] += 1

        @block.sync
        def _(sy):
            for s in names:
                sy.wait_ge(L.sem(s), L.n(s))
            sy.sem_inc(PH, 1)
        L.c["ph"] += 1

    def prol(eng):
        eng.wait_ge(PH, pno[0])

    with ExitStack() as p0:
        b2r32 = p0.enter_context(nc.sbuf_tensor("b2r32", [16, D], F32))
        b2s32 = p0.enter_context(nc.sbuf_tensor("b2s32", [2, D], F32))
        p128s = p0.enter_context(nc.sbuf_tensor("p128s", [128, 128], F32))
        psbp = p0.enter_context(nc.psum_tensor("psbp", [128, 16], F32))

        @block.sync
        def _(sy):
            for dst, src in [(iden32, idh), (b1c, b1ch),
                             (gbias, gbh), (b2r32, b2rh), (b2s32, b2sh),
                             (p128s, p128h)]:
                i = sy.dma_start(out=dst[:], in_=src)
                L.inc("c_dma", i, 16)
            i = sy.dma_start(out=cenT[:], in_=cenh.rearrange("p (dc j) -> p dc j", dc=DC))
            L.inc("c_dma", i, 16)

        @block.gpsimd
        def _(g):
            g.wait_ge(L.sem("c_dma"), L.n("c_dma"))
            g.memset(ones1[:], 1.0)
            g.memset(ones2[:], 1.0)
            g.memset(epsb[:], EPS)
            g.tensor_copy(out=idenb[:], in_=iden32[:])
            g.tensor_copy(out=b2rb[:], in_=b2r32[:])
            g.tensor_copy(out=b2sb[:], in_=b2s32[:])
            i = g.tensor_copy(out=p128b[:], in_=p128s[:])
            L.inc("c_cast", i, 1)

        @block.tensor
        def _(t):
            t.wait_ge(L.sem("c_cast"), 1)
            i = t.matmul(psbp[:], ones1[:], gbias[:], start=True, stop=True)
            L.inc("c_bp", i, 1)

        @block.scalar
        def _(s):
            s.wait_ge(L.sem("c_bp"), 1)
            i = s.copy(out=bplane[:], in_=psbp[:])
            L.inc("c_bpe", i, 1)

        barrier(["c_dma", "c_cast", "c_bpe"])

    def xnT_sl(tt, dc, lo=0, hi=128):
        if tt < 4:
            return xnTh[:, dc, tt * 128 + lo:tt * 128 + hi]
        return xnT[:, dc, (tt - 4) * 128 + lo:(tt - 4) * 128 + hi]

    def xnT_grp(dc, g):
        if g == 0:
            return xnTh[:, dc, :]
        return xnT[:, dc, (g - 1) * 512:g * 512]

    with ExitStack() as p1:
        xb = [p1.enter_context(nc.sbuf_tensor(f"xb{i}", [128, D], F32)) for i in range(2)]
        xn32 = [p1.enter_context(nc.sbuf_tensor(f"xn32_{i}", [128, D], F32)) for i in range(2)]
        sqd = p1.enter_context(nc.sbuf_tensor("sqd", [128, D], F32))
        s1b = p1.enter_context(nc.sbuf_tensor("s1b", [128, 1], F32))
        s2b = p1.enter_context(nc.sbuf_tensor("s2b", [128, 1], F32))
        ex2b = p1.enter_context(nc.sbuf_tensor("ex2b", [128, 1], F32))
        nvb = p1.enter_context(nc.sbuf_tensor("nvb", [128, 1], F32))
        sdb = p1.enter_context(nc.sbuf_tensor("sdb", [128, 1], F32))
        xnT32 = p1.enter_context(nc.sbuf_tensor("xnT32", [128, DC, 128], F32))
        sgb = [p1.enter_context(nc.sbuf_tensor(f"sgb{i}", [128, 16], F32)) for i in range(2)]
        scb = p1.enter_context(nc.sbuf_tensor("scb", [128, 16], F32))
        scur = [p1.enter_context(nc.sbuf_tensor(f"scur{i}", [128, 16], F32)) for i in range(2)]
        geb = p1.enter_context(nc.sbuf_tensor("geb", [128, 16], F32))
        gmb = p1.enter_context(nc.sbuf_tensor("gmb", [128, 16], F32))
        mrb = [p1.enter_context(nc.sbuf_tensor(f"mrb{i}", [128, 1], F32)) for i in range(4)]
        msb = p1.enter_context(nc.sbuf_tensor("msb", [128, 1], F32))
        recb = p1.enter_context(nc.sbuf_tensor("recb", [128, 1], F32))
        pst = [p1.enter_context(nc.psum_tensor(f"pst{i}", [128, 128], F32)) for i in range(4)]
        pssc = [p1.enter_context(nc.psum_tensor(f"pssc{i}", [128, 16], F32)) for i in range(2)]
        psgt = p1.enter_context(nc.psum_tensor("psgt", [16, 128], F32))

        @block.sync
        def _(sy):
            prol(sy)
            for tt in range(NTA):
                if tt >= 2:
                    sy.wait_ge(L.sem("p1_xn"), tt - 1)
                src = x_halo[tt * 128:(tt + 1) * 128, :] if tt < 4 else \
                    x_own[(tt - 4) * 128:(tt - 3) * 128, :]
                i = sy.dma_start(out=xb[tt % 2][:], in_=src)
                L.inc(f"p1_xd{tt % 2}", i, 16)

        @block.scalar
        def _(s):
            prol(s)
            for tt in range(NTA):
                ot = tt - 4
                s.wait_ge(L.sem(f"p1_xd{tt % 2}"), 16 * (tt // 2 + 1))
                if tt >= 1:
                    s.wait_ge(L.sem("p1_nv"), tt)
                i = s.activation(out=sqd[:], in_=xb[tt % 2][:], func=AF.Square,
                                 accum_out=s2b[:])
                L.inc("p1_sq", i, 1)
                s.wait_ge(L.sem("p1_nv"), tt + 1)
                if tt >= 1:
                    s.wait_ge(L.sem("p1_rc"), tt)
                i = s.activation(out=sdb[:], in_=nvb[:], func=AF.Sqrt,
                                 bias=epsb[:, 0:1], scale=-1.0)
                L.inc("p1_sd", i, 1)
                for dc in range(8, 16):
                    gk = tt * 16 + dc
                    s.wait_ge(L.sem("p1_tr"), gk + 1)
                    if ot >= 0:
                        s.copy(out=xnT_sl(tt, dc), in_=pst[gk % 4][:])
                        i = s.copy(out=xnT32[:, dc, :], in_=pst[gk % 4][:])
                    else:
                        i = s.copy(out=xnT_sl(tt, dc), in_=pst[gk % 4][:])
                    L.inc("p1_eva", i, 1)
                if ot >= 0:
                    s.wait_ge(L.sem("p1_scm"), ot + 1)
                    if ot >= 2:
                        s.wait_ge(L.sem("p1_g"), ot - 1)
                    i = s.activation(out=sgb[ot % 2][:], in_=pssc[ot % 2][:],
                                     func=AF.Sigmoid)
                    L.inc("p1_sg", i, 1)
                if ot >= 1:
                    s.wait_ge(L.sem("p1_gt"), ot)
                    i = s.copy(out=GTb[:, (ot - 1) * 128:ot * 128], in_=psgt[:])
                    L.inc("p1_gte", i, 1)
            s.wait_ge(L.sem("p1_gt"), NT)
            i = s.copy(out=GTb[:, (NT - 1) * 128:NT * 128], in_=psgt[:])
            L.inc("p1_gte", i, 1)

        @block.vector
        def _(v):
            prol(v)
            for tt in range(NTA):
                ot = tt - 4
                v.wait_ge(L.sem(f"p1_xd{tt % 2}"), 16 * (tt // 2 + 1))
                v.reduce_sum(out=s1b[:], in_=xb[tt % 2][:], axis=AX.X)
                v.drain()
                v.tensor_single_scalar(out=stats[:, tt, 0:1], in_=s1b[:],
                                       scalar=1.0 / D, op=ALU.mult)
                v.wait_ge(L.sem("p1_sq"), tt + 1)
                v.tensor_single_scalar(out=ex2b[:], in_=s2b[:], scalar=1.0 / D,
                                       op=ALU.mult)
                v.drain()
                i = v.scalar_tensor_tensor(out=nvb[:], in0=stats[:, tt, 0:1],
                                           scalar=stats[:, tt, 0:1], in1=ex2b[:],
                                           op0=ALU.mult, op1=ALU.subtract)
                L.inc("p1_nv", i, 1)
                v.wait_ge(L.sem("p1_sd"), tt + 1)
                i = v.reciprocal(out=stats[:, tt, 1:2], in_=sdb[:])
                L.inc("p1_rc", i, 1)
                v.drain()
                if tt >= 2:
                    v.wait_ge(L.sem("p1_tr"), 16 * (tt - 1))
                i = v.tensor_scalar(out=xn32[tt % 2][:], in0=xb[tt % 2][:],
                                    scalar1=stats[:, tt, 0:1], scalar2=stats[:, tt, 1:2],
                                    op0=ALU.subtract, op1=ALU.mult)
                L.inc("p1_xn", i, 1)
                for dc in range(8):
                    gk = tt * 16 + dc
                    v.wait_ge(L.sem("p1_tr"), gk + 1)
                    if ot >= 0:
                        v.tensor_copy(out=xnT_sl(tt, dc), in_=pst[gk % 4][:])
                        i = v.tensor_copy(out=xnT32[:, dc, :], in_=pst[gk % 4][:])
                    else:
                        i = v.tensor_copy(out=xnT_sl(tt, dc), in_=pst[gk % 4][:])
                    L.inc("p1_evv", i, 1)
                if ot >= 0:
                    v.wait_ge(L.sem("p1_sg"), ot + 1)
                    v.tensor_tensor(out=scb[:], in0=sgb[ot % 2][:], in1=bplane[:], op=ALU.add)
                    v.drain()
                    v.tensor_copy(out=scur[0][:], in_=scb[:])
                    v.drain()
                    v.reduce_max(out=mrb[0][:], in_=scur[0][:], axis=AX.X)
                    v.drain()
                    for r in range(1, 4):
                        v.tensor_single_scalar(out=geb[:], in_=scur[(r - 1) % 2][:],
                                               scalar=mrb[r - 1][:, 0:1], op=ALU.is_ge)
                        v.drain()
                        v.scalar_tensor_tensor(out=scur[r % 2][:], in0=geb[:],
                                               scalar=-2e9, in1=scur[(r - 1) % 2][:],
                                               op0=ALU.mult, op1=ALU.add)
                        v.drain()
                        v.reduce_max(out=mrb[r][:], in_=scur[r % 2][:], axis=AX.X)
                        v.drain()
                    v.tensor_tensor(out=msb[:], in0=mrb[0][:], in1=mrb[1][:], op=ALU.add)
                    v.drain()
                    v.tensor_tensor(out=msb[:], in0=msb[:], in1=mrb[2][:], op=ALU.add)
                    v.drain()
                    v.tensor_tensor(out=msb[:], in0=msb[:], in1=mrb[3][:], op=ALU.add)
                    v.drain()
                    v.tensor_single_scalar(out=msb[:], in_=msb[:], scalar=1e-8, op=ALU.max)
                    v.drain()
                    v.reciprocal(out=recb[:], in_=msb[:])
                    v.tensor_single_scalar(out=geb[:], in_=scb[:], scalar=mrb[3][:, 0:1],
                                           op=ALU.is_ge)
                    v.drain()
                    v.tensor_tensor(out=gmb[:], in0=scb[:], in1=geb[:], op=ALU.mult)
                    v.drain()
                    i = v.tensor_single_scalar(out=Gm[:, ot, :], in_=gmb[:],
                                               scalar=recb[:, 0:1], op=ALU.mult)
                    L.inc("p1_g", i, 1)

        def emit_scores(t, ot):
            t.wait_ge(L.sem("p1_evv"), 8 * (ot + 5))
            t.wait_ge(L.sem("p1_eva"), 8 * (ot + 5))
            if ot >= 2:
                t.wait_ge(L.sem("p1_sg"), ot - 1)
            for dc in range(16):
                i = t.matmul(pssc[ot % 2][:], xnT32[:, dc, :], cenT[:, dc, :],
                             start=(dc == 0), stop=(dc == 15))
            L.inc("p1_scm", i, 1)

        def emit_gt(t, j):
            t.wait_ge(L.sem("p1_g"), j + 1)
            if j >= 1:
                t.wait_ge(L.sem("p1_gte"), j)
            i = t.transpose(out=psgt[:], in_=Gm[:, j, :], identity=iden32[:])
            L.inc("p1_gt", i, 1)

        @block.tensor
        def _(t):
            prol(t)
            for tt in range(NTA):
                ot = tt - 4
                t.wait_ge(L.sem("p1_xn"), tt + 1)
                for dc in range(16):
                    gk = tt * 16 + dc
                    if gk >= 4:
                        gp_ = gk - 4
                        dcp = gp_ % 16
                        if dcp < 8:
                            t.wait_ge(L.sem("p1_evv"), (gp_ // 16) * 8 + dcp + 1)
                        else:
                            t.wait_ge(L.sem("p1_eva"), (gp_ // 16) * 8 + (dcp - 8) + 1)
                    i = t.transpose(out=pst[gk % 4][:],
                                    in_=xn32[tt % 2][:, dc * 128:(dc + 1) * 128],
                                    identity=iden32[:])
                    L.inc("p1_tr", i, 1)
                if ot >= 0:
                    emit_scores(t, ot)
                if ot >= 1:
                    emit_gt(t, ot - 1)
            emit_gt(t, NT - 1)

        names = ["p1_evv", "p1_eva", "p1_g", "p1_gte", "p1_xd0", "p1_xd1",
                 "p1_xn", "p1_scm", "p1_gt"]
        if "xnt_d" in _DEBUG_TAPS or "g_d" in _DEBUG_TAPS:
            @block.gpsimd
            def _(g):
                prol(g)
                g.wait_ge(L.sem("p1_evv"), L.n("p1_evv"))
                g.wait_ge(L.sem("p1_eva"), L.n("p1_eva"))
                g.wait_ge(L.sem("p1_g"), L.n("p1_g"))
                if xnt_dbg is not None:
                    i = g.dma_start(out=xnt_dbg,
                                    in_=xnT[:].rearrange("p dc t -> p (dc t)"))
                    L.inc("p1_dbg", i, 16)
                if g_dbg is not None:
                    i = g.dma_start(out=g_dbg, in_=Gm[:].rearrange("p t j -> p (t j)"))
                    L.inc("p1_dbg", i, 16)
                if st_dbg is not None:
                    g.wait_ge(L.sem("p1_xn"), L.n("p1_xn"))
                    i = g.dma_start(out=st_dbg, in_=stats[:].rearrange("p t k -> p (t k)"))
                    L.inc("p1_dbg", i, 16)
                g.wait_ge(L.sem("p1_dbg"), L.n("p1_dbg"))
            names.append("p1_dbg")
        barrier(names)

    if _MAX_PHASE >= 2:
      with ExitStack() as p2:
        wst = [p2.enter_context(nc.sbuf_tensor(f"wst{i}", [128, DC, 128], F32)) for i in range(2)]
        wbf = [p2.enter_context(nc.sbuf_tensor(f"wbf{i}", [128, DC, 128], BF16)) for i in range(2)]
        qa32 = [p2.enter_context(nc.sbuf_tensor(f"qa32_{i}", [128, 512], F32)) for i in range(2)]
        qab = [p2.enter_context(nc.sbuf_tensor(f"qab{i}", [128, 512], BF16)) for i in range(2)]
        t1b = [p2.enter_context(nc.sbuf_tensor(f"t1b{i}", [128, 512], F32)) for i in range(2)]
        t2b = p2.enter_context(nc.sbuf_tensor("t2b", [128, 512], F32))
        qstg = [p2.enter_context(nc.sbuf_tensor(f"qstg{i}", [128, 512], BF16)) for i in range(2)]
        psq = [p2.enter_context(nc.psum_tensor(f"psq{i}", [128, 512], F32)) for i in range(3)]
        pshf = [p2.enter_context(nc.psum_tensor(f"pshf{i}", [128, 512], F32)) for i in range(2)]

        chunks = []
        groups = []
        for h in range(H):
            chunks.append((wqT, h * 128))
            for g in range(4):
                groups.append((h, g, 0))
        for h in range(H):
            chunks.append((wkT, h * 128))
            for g in range(5):
                groups.append((16 + h, g, 1))
        if _QK_NCK < len(chunks):
            chunks = chunks[:_QK_NCK]
            groups = [g for g in groups if g[0] < _QK_NCK]
        NCK = len(chunks)
        lastg = {}
        firstg = {}
        for gi, (ck, g, part) in enumerate(groups):
            lastg[ck] = gi
            firstg.setdefault(ck, gi)

        def grp_meta(gi):
            ck, g, part = groups[gi]
            h = ck if part == 0 else ck - 16
            if part == 0:
                off = HALO + g * 512
                dst = qt_d
                scale = SCALE
                rhs_g = g + 1
            else:
                off = g * 512
                dst = kt_d
                scale = 1.0
                rhs_g = g
            return ck, h, g, off, dst, scale, rhs_g

        @block.sync
        def _(sy):
            if _SKIP_QK:
                return
            prol(sy)
            i = sy.dma_start(out=cosf[:], in_=cosh)
            L.inc("p2_tb", i, 16)
            i = sy.dma_start(out=sinf[:], in_=sinh)
            L.inc("p2_tb", i, 16)
            for ck in range(NCK):
                if ck >= 2:
                    sy.wait_ge(L.sem("p2_wc"), ck - 1)
                wh, col = chunks[ck]
                src = bass.AP(wh.tensor, col,
                              [[D, 128], [128 * D, DC], [1, 128]])
                i = sy.dma_start(out=wst[ck % 2][:], in_=src)
                L.inc(f"p2_wd{ck % 2}", i, 16)

        gbyck = {}
        for gi, (ck, g, part) in enumerate(groups):
            gbyck.setdefault(ck, []).append(gi)

        def emit_qw(g_, gi):
            ck, h, g, off, dst, scale, rhs_g = grp_meta(gi)
            g_.wait_ge(L.sem("p2_rt"), gi + 1)
            t0 = off if dst is kt_d else g * 512
            dsta = bass.AP(dst.tensor, (t0 // 128) * 128 * (H * 128) + h * 128,
                           [[H * 128, 128], [128 * H * 128, 4], [1, 128]])
            i = g_.dma_start(out=dsta,
                             in_=qstg[gi % 2][:].rearrange("p (a t) -> p a t", a=4))
            L.inc(f"p2_qw{gi % 2}", i, 16)

        @block.gpsimd
        def _(g_):
            if _SKIP_QK:
                return
            prol(g_)
            for ck in range(NCK):
                g_.wait_ge(L.sem(f"p2_wd{ck % 2}"), 16 * (ck // 2 + 1))
                if ck >= 2:
                    g_.wait_ge(L.sem("p2_mm"), lastg[ck - 2] + 1)
                i = g_.tensor_copy(out=wbf[ck % 2][:], in_=wst[ck % 2][:])
                L.inc("p2_wc", i, 1)
                if ck >= 1:
                    for gi in gbyck[ck - 1]:
                        emit_qw(g_, gi)
            for gi in gbyck[NCK - 1]:
                emit_qw(g_, gi)

        @block.tensor
        def _(t):
            if _SKIP_QK:
                return
            prol(t)
            for gi in range(len(groups)):
                ck, h, g, off, dst, scale, rhs_g = grp_meta(gi)
                if gi == firstg[ck]:
                    t.wait_ge(L.sem("p2_wc"), ck + 1)
                if gi >= 3:
                    t.wait_ge(L.sem("p2_ea"), gi - 2)
                    t.wait_ge(L.sem("p2_ev"), gi - 2)
                for dc in range(DC):
                    i = t.matmul(psq[gi % 3][:], wbf[ck % 2][:, dc, :],
                                 xnT_grp(dc, rhs_g), start=(dc == 0), stop=(dc == 15))
                L.inc("p2_mm", i, 1)
                if gi >= 1:
                    j = gi - 1
                    t.wait_ge(L.sem("p2_ev"), j + 1)
                    if j >= 2:
                        t.wait_ge(L.sem("p2_t2"), j - 1)
                    i = t.matmul(pshf[j % 2][:], p128b[:], qab[j % 2][:],
                                 start=True, stop=True)
                    L.inc("p2_sh", i, 1)
            j = len(groups) - 1
            t.wait_ge(L.sem("p2_ev"), j + 1)
            t.wait_ge(L.sem("p2_t2"), j - 1)
            i = t.matmul(pshf[j % 2][:], p128b[:], qab[j % 2][:], start=True, stop=True)
            L.inc("p2_sh", i, 1)

        @block.scalar
        def _(s):
            if _SKIP_QK:
                return
            prol(s)
            for gi in range(len(groups)):
                ck, h, g, off, dst, scale, rhs_g = grp_meta(gi)
                s.wait_ge(L.sem("p2_mm"), gi + 1)
                if gi >= 2:
                    s.wait_ge(L.sem("p2_rt"), gi - 1)
                i = s.activation(out=qa32[gi % 2][:], in_=psq[gi % 3][:],
                                 func=AF.Copy, scale=scale)
                L.inc("p2_ea", i, 1)
                if gi >= 2:
                    s.wait_ge(L.sem("p2_sh"), gi - 1)
                i = s.activation(out=qab[gi % 2][:], in_=psq[gi % 3][:],
                                 func=AF.Copy, scale=scale)
                L.inc("p2_ev", i, 1)

        def rope_finish(v, j):
            _, _, _, off, _, _, _ = grp_meta(j)
            v.wait_ge(L.sem("p2_sh"), j + 1)
            i = v.tensor_tensor(out=t2b[:], in0=pshf[j % 2][:],
                                in1=sinf[:, off:off + 512], op=ALU.mult)
            L.inc("p2_t2", i, 1)
            v.wait_ge(L.sem("p2_ea"), j + 1)
            v.tensor_tensor(out=t1b[0][:], in0=qa32[j % 2][:],
                            in1=cosf[:, off:off + 512], op=ALU.mult)
            v.drain()
            if j >= 2:
                v.wait_ge(L.sem(f"p2_qw{j % 2}"), 16 * ((j - 2) // 2 + 1))
            i = v.tensor_tensor(out=qstg[j % 2][:], in0=t1b[0][:],
                                in1=t2b[:], op=ALU.add)
            L.inc("p2_rt", i, 1)
            v.drain()

        @block.vector
        def _(v):
            if _SKIP_QK:
                return
            prol(v)
            v.wait_ge(L.sem("p2_tb"), 32)
            for gi in range(len(groups)):
                if gi >= 1:
                    rope_finish(v, gi - 1)
            rope_finish(v, len(groups) - 1)

        barrier(["p2_tb", "p2_wd0", "p2_wd1", "p2_wc", "p2_mm", "p2_ea",
                 "p2_ev", "p2_sh", "p2_t2", "p2_rt", "p2_qw0", "p2_qw1"])

        vst = [p2.enter_context(nc.sbuf_tensor(f"vst{i}", [128, 512], F32)) for i in range(2)]
        wvbf = [p2.enter_context(nc.sbuf_tensor(f"wvbf{i}", [128, DC, 512], BF16)) for i in range(2)]
        vstg = [p2.enter_context(nc.sbuf_tensor(f"vstg{i}", [128, 512], BF16)) for i in range(2)]
        psv = [p2.enter_context(nc.psum_tensor(f"psv{i}", [128, 512], F32)) for i in range(2)]

        @block.sync
        def _(sy):
            if _SKIP_V:
                return
            prol(sy)
            for vck in range(4 * DC):
                do, dcc = divmod(vck, DC)
                if vck >= 2:
                    sy.wait_ge(L.sem("p2v_wc"), vck - 1)
                src = bass.AP(wvT.tensor, dcc * 128 * D + do * 512,
                              [[D, 128], [1, 512]])
                i = sy.dma_start(out=vst[vck % 2][:], in_=src)
                L.inc(f"p2v_wd{vck % 2}", i, 16)

        @block.gpsimd
        def _(g_):
            if _SKIP_V:
                return
            prol(g_)

            for gv in range(4 * NTA):
                do_, tt_ = divmod(gv, NTA)
                g_.wait_ge(L.sem("p2v_ev"), gv + 1)
                dsta = bass.AP(v_d.tensor, tt_ * 128 * D + do_ * 512,
                               [[D, 128], [1, 512]])
                i = g_.dma_start(out=dsta, in_=vstg[gv % 2][:])
                L.inc(f"p2v_vw{gv % 2}", i, 16)

        @block.tensor
        def _(t):
            if _SKIP_V:
                return
            prol(t)
            for gv in range(4 * NTA):
                do, tt = divmod(gv, NTA)
                if gv >= 2:
                    t.wait_ge(L.sem("p2v_ev"), gv - 1)
                for dc in range(DC):
                    if tt == 0:
                        t.wait_ge(L.sem("p2v_wc"), do * DC + dc + 1)
                    i = t.matmul(psv[gv % 2][:], xnT_sl(tt, dc), wvbf[do % 2][:, dc, :],
                                 start=(dc == 0), stop=(dc == 15))
                L.inc("p2v_mm", i, 1)

        @block.scalar
        def _(s):
            if _SKIP_V:
                return
            prol(s)

            def vcast(vck):
                do_, dcc = divmod(vck, DC)
                s.wait_ge(L.sem(f"p2v_wd{vck % 2}"), 16 * (vck // 2 + 1))
                if do_ >= 2:
                    s.wait_ge(L.sem("p2v_mm"), NTA * (do_ - 1))
                i = s.activation(out=wvbf[do_ % 2][:, dcc, :], in_=vst[vck % 2][:],
                                 func=AF.Copy)
                L.inc("p2v_wc", i, 1)

            for vck in range(DC):
                vcast(vck)
            for gv in range(4 * NTA):
                do, tt = divmod(gv, NTA)
                s.wait_ge(L.sem("p2v_mm"), gv + 1)
                if gv >= 2:
                    s.wait_ge(L.sem(f"p2v_vw{gv % 2}"), 16 * ((gv - 2) // 2 + 1))
                i = s.copy(out=vstg[gv % 2][:], in_=psv[gv % 2][:])
                L.inc("p2v_ev", i, 1)
                if do < 3 and tt < DC:
                    vcast((do + 1) * DC + tt)

        barrier(["p2v_wd0", "p2v_wd1", "p2v_wc", "p2v_mm", "p2v_ev",
                 "p2v_vw0", "p2v_vw1"])

    sB.close()

    if _MAX_PHASE >= 3:
      with ExitStack() as p5:
        w1st = [p5.enter_context(nc.sbuf_tensor(f"w1st{i}", [128, 8, 128], F32)) for i in range(2)]
        w1bf = [p5.enter_context(nc.sbuf_tensor(f"w1bf{i}", [128, DC, 128], BF16)) for i in range(2)]
        w2st = [p5.enter_context(nc.sbuf_tensor(f"w2st{i}", [128, 512], F32)) for i in range(2)]
        w2bd = [p5.enter_context(nc.sbuf_tensor(f"w2bd{i}", [128, FCN, 512], BF16)) for i in range(2)]
        hT = p5.enter_context(nc.sbuf_tensor("hT", [128, FCN, 1024], BF16))
        acc = p5.enter_context(nc.sbuf_tensor("acc", [128, 8, D], F32))
        psh = [p5.enter_context(nc.psum_tensor(f"psh{i}", [128, 512], F32)) for i in range(4)]
        psy = [p5.enter_context(nc.psum_tensor(f"psy{i}", [128, 512], F32)) for i in range(4)]

        NJ = NE
        n_hgrp = 2 * FCN
        n_ygrp = 4 * 8

        @block.sync
        def _(sy):
            prol(sy)
            for half in range(2):
                for j in range(NJ):
                    jj = half * NJ + j
                    for fc in range(FCN):
                        ck = jj * FCN + fc
                        for hp in range(2):
                            ck2 = ck * 2 + hp
                            if ck2 >= 2:
                                sy.wait_ge(L.sem("p5_w1h"), ck2 - 1)
                            src = bass.AP(w1h.tensor,
                                          j * D * FF + hp * 8 * 128 * FF + fc * 128,
                                          [[FF, 128], [128 * FF, 8], [1, 128]])
                            i = sy.dma_start(out=w1st[ck2 % 2][:], in_=src)
                            L.inc(f"p5_w1d{ck2 % 2}", i, 16)


        @block.vector
        def _(v):
            prol(v)
            for half in range(2):
                for j in range(NJ):
                    jj = half * NJ + j
                    for do in range(4):
                        gd = jj * 4 + do
                        for tile in range(8):
                            gy = gd * 8 + tile
                            v.wait_ge(L.sem("p5_ymm"), gy + 1)
                            if half == 1 and j == 0 and do == 0:
                                v.wait_ge(L.sem("p5_acw"), 128)
                            dsl = acc[:, tile, do * 512:(do + 1) * 512]
                            if j == 0:
                                i = v.tensor_single_scalar(out=dsl, in_=psy[gy % 4][:],
                                                           scalar=1.0, op=ALU.mult)
                            elif j == 1:
                                i = v.scalar_tensor_tensor(out=dsl, in0=psy[gy % 4][:],
                                                           scalar=1.0, in1=dsl,
                                                           op0=ALU.mult, op1=ALU.add)
                            else:
                                gate = Gm[:, half * 8 + tile, j - 2:j - 1]
                                i = v.scalar_tensor_tensor(out=dsl, in0=psy[gy % 4][:],
                                                           scalar=gate, in1=dsl,
                                                           op0=ALU.mult, op1=ALU.add)
                            L.inc("p5_yev", i, 1)

        @block.scalar
        def _(s):
            prol(s)
            NCKT = 2 * NJ * FCN

            def cast_w1c(ck):
                if ck >= NCKT:
                    return
                if ck >= 2:
                    s.wait_ge(L.sem("p5_hmm"), 2 * (ck - 1))
                for hp in range(2):
                    ck2 = ck * 2 + hp
                    s.wait_ge(L.sem(f"p5_w1d{ck2 % 2}"), 16 * (ck2 // 2 + 1))
                    i = s.activation(out=w1bf[ck % 2][:, hp * 8:(hp + 1) * 8, :],
                                     in_=w1st[ck2 % 2][:], func=AF.Copy)
                    L.inc("p5_w1h", i, 1)

            def cast_w2(jj2, dos):
                for do in dos:
                    gd = jj2 * 4 + do
                    for fc in range(FCN):
                        cw = gd * FCN + fc
                        s.wait_ge(L.sem(f"p5_w2d{cw % 2}"), 16 * (cw // 2 + 1))
                        if gd >= 2:
                            s.wait_ge(L.sem("p5_ymm"), 8 * (gd - 1))
                        i = s.activation(out=w2bd[gd % 2][:, fc, :],
                                         in_=w2st[cw % 2][:], func=AF.Copy)
                        L.inc("p5_w2c", i, 1)

            def cast_w2_one(jj2, k):
                do, fc = divmod(k, FCN)
                gd = jj2 * 4 + do
                cw = gd * FCN + fc
                s.wait_ge(L.sem(f"p5_w2d{cw % 2}"), 16 * (cw // 2 + 1))
                if gd >= 2:
                    s.wait_ge(L.sem("p5_ymm"), 8 * (gd - 1))
                i = s.activation(out=w2bd[gd % 2][:, fc, :],
                                 in_=w2st[cw % 2][:], func=AF.Copy)
                L.inc("p5_w2c", i, 1)

            cast_w1c(0)
            for half in range(2):
                for j in range(NJ):
                    jj = half * NJ + j
                    for fc in range(FCN):
                        ck = jj * FCN + fc
                        cast_w1c(ck + 1)
                        for g2 in range(2):
                            gh = ck * 2 + g2
                            s.wait_ge(L.sem("p5_hmm"), gh + 1)
                            i = s.activation(out=hT[:, fc, g2 * 512:(g2 + 1) * 512],
                                             in_=psh[gh % 4][:], func=AF.Relu,
                                             bias=b1c[:, j * FCN + fc:j * FCN + fc + 1])
                            L.inc("p5_hev", i, 1)
                        for k in (2 * fc, 2 * fc + 1):
                            cast_w2_one(jj, k)
                    cast_w2(jj, (2, 3))


        @block.tensor
        def _(t):
            prol(t)
            for half in range(2):
                for j in range(NJ):
                    jj = half * NJ + j
                    for fc in range(FCN):
                        ck = jj * FCN + fc
                        t.wait_ge(L.sem("p5_w1h"), 2 * (ck + 1))
                        for g2 in range(2):
                            gh = ck * 2 + g2
                            if gh >= 4:
                                t.wait_ge(L.sem("p5_hev"), gh - 3)
                            for dc in range(DC):
                                i = t.matmul(psh[gh % 4][:], w1bf[ck % 2][:, dc, :],
                                             xnT[:, dc, half * 1024 + g2 * 512:
                                                 half * 1024 + (g2 + 1) * 512],
                                             start=(dc == 0), stop=(dc == 15))
                            L.inc("p5_hmm", i, 1)
                    for do in range(4):
                        gd = jj * 4 + do
                        for tile in range(8):
                            gy = gd * 8 + tile
                            if gy >= 4:
                                t.wait_ge(L.sem("p5_yev"), gy - 3)
                            nmm = FCN + (2 if j == 0 else 0)
                            k = 0
                            for fc in range(FCN):
                                if tile == 0:
                                    t.wait_ge(L.sem("p5_w2c"), gd * FCN + fc + 1)
                                if do == 0 and tile == 0:
                                    t.wait_ge(L.sem("p5_hev"), 2 * (jj * FCN + fc) + 2)
                                i = t.matmul(psy[gy % 4][:],
                                             hT[:, fc, tile * 128:(tile + 1) * 128],
                                             w2bd[gd % 2][:, fc, :],
                                             start=(fc == 0), stop=(fc == FCN - 1 and j != 0))
                                k += 1
                            if j == 0:
                                tok0 = (half * 8 + tile) * 128
                                t.matmul(psy[gy % 4][:], GTb[:, tok0:tok0 + 128],
                                         b2rb[:, do * 512:(do + 1) * 512],
                                         start=False, stop=False)
                                i = t.matmul(psy[gy % 4][:], ones2[:],
                                             b2sb[:, do * 512:(do + 1) * 512],
                                             start=False, stop=True)
                            L.inc("p5_ymm", i, 1)

        @block.gpsimd
        def _(g_):
            prol(g_)

            def acc_out(half):
                g_.wait_ge(L.sem("p5_yev"), (half + 1) * NJ * 32)
                for tile in range(8):
                    i = g_.dma_start(out=acc_d[(half * 8 + tile) * 128:
                                               (half * 8 + tile + 1) * 128, :],
                                     in_=acc[:, tile, :])
                    L.inc("p5_acw", i, 16)

            for jj2 in range(2 * NJ):
                j = jj2 % NJ
                if jj2 == NJ:
                    acc_out(0)
                for do in range(4):
                    gd = jj2 * 4 + do
                    for fc in range(FCN):
                        cw = gd * FCN + fc
                        if cw >= 2:
                            g_.wait_ge(L.sem("p5_w2c"), cw - 1)
                        src = bass.AP(w2h.tensor, j * FF * D + fc * 128 * D + do * 512,
                                      [[D, 128], [1, 512]])
                        i = g_.dma_start(out=w2st[cw % 2][:], in_=src)
                        L.inc(f"p5_w2d{cw % 2}", i, 16)
            acc_out(1)

        barrier(["p5_w1d0", "p5_w1d1", "p5_w1h", "p5_w2d0",
                 "p5_w2d1", "p5_w2c", "p5_hmm", "p5_hev", "p5_ymm", "p5_yev",
                 "p5_acw"])

    sA.close()

    if _MAX_PHASE >= 4:
      aoS = ExitStack()
      aoT = aoS.enter_context(nc.sbuf_tensor("aoT", [128, H, T], BF16))
      with ExitStack() as p4:
        masks = p4.enter_context(nc.sbuf_tensor("masks", [128, 25, 128], BF16))
        masks32 = p4.enter_context(nc.sbuf_tensor("masks32", [128, 25 * 128], F32))
        kts = [p4.enter_context(nc.sbuf_tensor(f"kts{i}", [128, H * 128], BF16)) for i in range(6)]
        vs = [p4.enter_context(nc.sbuf_tensor(f"vs{i}", [128, D], BF16)) for i in range(6)]
        qts = [p4.enter_context(nc.sbuf_tensor(f"qts{i}", [128, H * 128], BF16)) for i in range(3)]
        pa = [p4.enter_context(nc.sbuf_tensor(f"pa{i}", [128, 640], BF16)) for i in range(2)]
        panb = [p4.enter_context(nc.sbuf_tensor(f"panb{i}", [128, 640], BF16)) for i in range(2)]
        ptb = [p4.enter_context(nc.sbuf_tensor(f"ptb{i}", [128, 640], BF16)) for i in range(2)]
        negmb = [p4.enter_context(nc.sbuf_tensor(f"negmb{i}", [128, 1], F32)) for i in range(2)]
        sumb = [p4.enter_context(nc.sbuf_tensor(f"sumb{i}", [128, 1], F32)) for i in range(2)]
        rcb = p4.enter_context(nc.sbuf_tensor("rcb", [128, 1], F32))
        pss = [p4.enter_context(nc.psum_tensor(f"pss{i}", [128, 640], F32)) for i in range(2)]
        pspt = [p4.enter_context(nc.psum_tensor(f"pspt{i}", [128, 640], BF16)) for i in range(2)]
        psao = [p4.enter_context(nc.psum_tensor(f"psao{i}", [128, 128], F32)) for i in range(2)]

        NIDX = NT * H

        @block.gpsimd
        def _(g_):
            prol(g_)
            g_.wait_ge(L.sem("p4_mk"), 16)
            i = g_.tensor_copy(out=masks[:].rearrange("p a t -> p (a t)"), in_=masks32[:])
            L.inc("p4_mkc", i, 1)
            for kt in range(NTA):
                if kt >= 6:
                    g_.wait_ge(L.sem("p4_avm"), 16 * (kt - 5))
                sl = kt % 6
                i = g_.dma_start(out=kts[sl][:], in_=kt_d[kt * 128:(kt + 1) * 128, :])
                L.inc(f"p4_kv{sl}", i, 16)
                i = g_.dma_start(out=vs[sl][:], in_=v_d[kt * 128:(kt + 1) * 128, :])
                L.inc(f"p4_kv{sl}", i, 16)

        @block.sync
        def _(sy):
            prol(sy)
            i = sy.dma_start(out=masks32[:], in_=maskh)
            L.inc("p4_mk", i, 16)
            for qt in range(NT):
                if qt >= 3:
                    sy.wait_ge(L.sem("p4_smm"), 16 * (qt - 2))
                i = sy.dma_start(out=qts[qt % 3][:], in_=qt_d[qt * 128:(qt + 1) * 128, :])
                L.inc(f"p4_qt{qt % 3}", i, 16)

        def pe_T(t, j):
            t.wait_ge(L.sem("p4_pan"), j + 1)
            if j >= 2:
                t.wait_ge(L.sem("p4_pte"), j - 1)
            for d in range(5):
                i = t.transpose(out=pspt[j % 2][:, d * 128:(d + 1) * 128],
                                in_=panb[j % 2][:, d * 128:(d + 1) * 128],
                                identity=idenb[:])
            L.inc("p4_tmm", i, 1)

        def pe_AV(t, j):
            qt, h = divmod(j, H)
            t.wait_ge(L.sem("p4_pte"), j + 1)
            if j >= 2:
                t.wait_ge(L.sem("p4_aoe"), j - 1)
            for d in range(5):
                sl = (qt + d) % 6
                i = t.matmul(psao[j % 2][:], vs[sl][:, h * 128:(h + 1) * 128],
                             ptb[j % 2][:, d * 128:(d + 1) * 128],
                             start=(d == 0), stop=(d == 4))
            L.inc("p4_avm", i, 1)

        @block.tensor
        def _(t):
            prol(t)
            for idx in range(NIDX):
                qt, h = divmod(idx, H)
                if idx == 0:
                    t.wait_ge(L.sem("p4_mkc"), 1)
                if h == 0:
                    for kt in range(qt, qt + 5):
                        t.wait_ge(L.sem(f"p4_kv{kt % 6}"), 32 * (kt // 6 + 1))
                    t.wait_ge(L.sem(f"p4_qt{qt % 3}"), 16 * (qt // 3 + 1))
                if idx >= 2:
                    t.wait_ge(L.sem("p4_mx"), idx - 1)
                    t.wait_ge(L.sem("p4_ex"), idx - 1)
                sl = idx % 2
                for d in range(5):
                    ks = (qt + d) % 6
                    t.matmul(pss[sl][:, d * 128:(d + 1) * 128],
                             qts[qt % 3][:, h * 128:(h + 1) * 128],
                             kts[ks][:, h * 128:(h + 1) * 128],
                             start=(d == 0 or d == 4), stop=False)
                dlist = range(5) if qt < 4 else (0, 4)
                tc = qt if qt < 4 else 4
                for d in dlist:
                    last_b0 = (d == max(x for x in dlist if x < 4))
                    i = t.matmul(pss[sl][:, d * 128:(d + 1) * 128], idenb[:],
                                 masks[:, tc * 5 + d, :],
                                 start=False, stop=(last_b0 or d == 4))
                L.inc("p4_smm", i, 1)
                if idx >= 2:
                    pe_T(t, idx - 2)
                    pe_AV(t, idx - 2)
            for j in (NIDX - 2, NIDX - 1):
                pe_T(t, j)
                pe_AV(t, j)

        @block.vector
        def _(v):
            prol(v)

            def dve_tail(j):
                v.wait_ge(L.sem("p4_ex"), j + 1)
                if j >= 1:
                    v.wait_ge(L.sem("p4_pan"), j)
                v.reciprocal(out=rcb[:], in_=sumb[j % 2][:])
                v.drain()
                if j >= 2:
                    v.wait_ge(L.sem("p4_tmm"), j - 1)
                i = v.tensor_single_scalar(out=panb[j % 2][:], in_=pa[j % 2][:],
                                           scalar=rcb[:, 0:1], op=ALU.mult)
                L.inc("p4_pan", i, 1)
                if j >= 1:
                    v.wait_ge(L.sem("p4_tmm"), j)
                    if j >= 3:
                        v.wait_ge(L.sem("p4_avm"), j - 2)
                    i = v.tensor_copy(out=ptb[(j - 1) % 2][:], in_=pspt[(j - 1) % 2][:])
                    L.inc("p4_pte", i, 1)

            for idx in range(NIDX):
                v.wait_ge(L.sem("p4_smm"), idx + 1)
                i = v.reduce_max(out=negmb[idx % 2][:], in_=pss[idx % 2][:],
                                 axis=AX.X, negate=True)
                L.inc("p4_mx", i, 1)
                if idx >= 1:
                    dve_tail(idx - 1)
            dve_tail(NIDX - 1)
            j = NIDX - 1
            v.wait_ge(L.sem("p4_tmm"), j + 1)
            v.wait_ge(L.sem("p4_avm"), j - 1)
            i = v.tensor_copy(out=ptb[j % 2][:], in_=pspt[j % 2][:])
            L.inc("p4_pte", i, 1)

        def act_tail(s, j):
            qt, h = divmod(j, H)
            s.wait_ge(L.sem("p4_avm"), j + 1)
            i = s.copy(out=aoT[:, h, qt * 128:(qt + 1) * 128], in_=psao[j % 2][:])
            L.inc("p4_aoe", i, 1)

        @block.scalar
        def _(s):
            prol(s)
            for idx in range(NIDX):
                s.wait_ge(L.sem("p4_mx"), idx + 1)
                if idx >= 2:
                    s.wait_ge(L.sem("p4_pan"), idx - 1)
                i = s.activation(out=pa[idx % 2][:], in_=pss[idx % 2][:], func=AF.Exp,
                                 bias=negmb[idx % 2][:, 0:1], accum_out=sumb[idx % 2][:])
                L.inc("p4_ex", i, 1)
                if idx >= 2:
                    act_tail(s, idx - 2)
            for j in (NIDX - 2, NIDX - 1):
                act_tail(s, j)

        barrier([f"p4_kv{s}" for s in range(6)] + [f"p4_qt{s}" for s in range(3)]
                + ["p4_smm", "p4_mx", "p4_ex", "p4_pan",
                   "p4_tmm", "p4_pte", "p4_avm", "p4_aoe"])

      with ExitStack() as p4w:
        wost = [p4w.enter_context(nc.sbuf_tensor(f"wost{i}", [128, 512], F32)) for i in range(2)]
        wodb = [p4w.enter_context(nc.sbuf_tensor(f"wodb{i}", [128, H, 512], BF16)) for i in range(2)]
        sstg = [p4w.enter_context(nc.sbuf_tensor(f"sstg{i}", [128, 512], BF16)) for i in range(2)]
        pss2 = [p4w.enter_context(nc.psum_tensor(f"pss2_{i}", [128, 512], F32)) for i in range(2)]

        @block.sync
        def _(sy):
            prol(sy)
            for cwo in range(4 * H):
                if cwo >= 2:
                    sy.wait_ge(L.sem("p4w_wc"), cwo - 1)
                do, h2 = divmod(cwo, H)
                src = bass.AP(woT.tensor, h2 * 128 * D + do * 512, [[D, 128], [1, 512]])
                i = sy.dma_start(out=wost[cwo % 2][:], in_=src)
                L.inc(f"p4w_wd{cwo % 2}", i, 16)

        @block.gpsimd
        def _(g_):
            prol(g_)

            def emit_sw(gw):
                do_, tile_ = divmod(gw, NT)
                g_.wait_ge(L.sem("p4w_ev"), gw + 1)
                dsta = bass.AP(s_d.tensor, tile_ * 128 * D + do_ * 512,
                               [[D, 128], [1, 512]])
                i = g_.dma_start(out=dsta, in_=sstg[gw % 2][:])
                L.inc(f"p4w_sw{gw % 2}", i, 16)

            for do in range(4):
                if do >= 1:
                    for gw in range((do - 1) * NT, do * NT):
                        emit_sw(gw)
            for gw in range(3 * NT, 4 * NT):
                emit_sw(gw)

        @block.tensor
        def _(t):
            prol(t)
            for gw in range(4 * NT):
                do, tile = divmod(gw, NT)
                if gw >= 2:
                    t.wait_ge(L.sem("p4w_ev"), gw - 1)
                for h2 in range(H):
                    if tile == 0:
                        t.wait_ge(L.sem("p4w_wc"), do * H + h2 + 1)
                    i = t.matmul(pss2[gw % 2][:], aoT[:, h2, tile * 128:(tile + 1) * 128],
                                 wodb[do % 2][:, h2, :], start=(h2 == 0), stop=(h2 == H - 1))
                L.inc("p4w_mm", i, 1)

        @block.scalar
        def _(s):
            prol(s)

            def wcast(cwo):
                do_, h2 = divmod(cwo, H)
                s.wait_ge(L.sem(f"p4w_wd{cwo % 2}"), 16 * (cwo // 2 + 1))
                if do_ >= 2:
                    s.wait_ge(L.sem("p4w_mm"), NT * (do_ - 1))
                i = s.activation(out=wodb[do_ % 2][:, h2, :], in_=wost[cwo % 2][:],
                                 func=AF.Copy)
                L.inc("p4w_wc", i, 1)

            for cwo in range(H):
                wcast(cwo)
            for gw in range(4 * NT):
                do, tile = divmod(gw, NT)
                s.wait_ge(L.sem("p4w_mm"), gw + 1)
                if gw >= 2:
                    s.wait_ge(L.sem(f"p4w_sw{gw % 2}"), 16 * ((gw - 2) // 2 + 1))
                i = s.copy(out=sstg[gw % 2][:], in_=pss2[gw % 2][:])
                L.inc("p4w_ev", i, 1)
                if do < 3 and tile < H:
                    wcast((do + 1) * H + tile)

        barrier(["p4w_wd0", "p4w_wd1", "p4w_wc", "p4w_mm", "p4w_ev",
                 "p4w_sw0", "p4w_sw1"])
      aoS.close()

    if _MAX_PHASE >= 5:
      with ExitStack() as p6:
        xb6 = [p6.enter_context(nc.sbuf_tensor(f"xb6_{i}", [128, D], F32)) for i in range(2)]
        ab6 = [p6.enter_context(nc.sbuf_tensor(f"ab6_{i}", [128, D], F32)) for i in range(2)]
        sb6 = [p6.enter_context(nc.sbuf_tensor(f"sb6_{i}", [128, D], BF16)) for i in range(2)]
        xn6 = p6.enter_context(nc.sbuf_tensor("xn6", [128, D], F32))
        ffb = p6.enter_context(nc.sbuf_tensor("ffb", [128, D], F32))
        g6b = p6.enter_context(nc.sbuf_tensor("g6b", [128, D], F32))
        sgb6 = p6.enter_context(nc.sbuf_tensor("sgb6", [128, D], F32))
        outb = [p6.enter_context(nc.sbuf_tensor(f"outb{i}", [128, D], F32)) for i in range(2)]

        @block.sync
        def _(sy):
            prol(sy)
            for tt in range(NT):
                if tt >= 2:
                    sy.wait_ge(L.sem("p6_ot"), tt - 1)
                i = sy.dma_start(out=xb6[tt % 2][:], in_=x_own[tt * 128:(tt + 1) * 128, :])
                L.inc(f"p6_xd{tt % 2}", i, 16)
                i = sy.dma_start(out=ab6[tt % 2][:], in_=acc_d[tt * 128:(tt + 1) * 128, :])
                L.inc(f"p6_ad{tt % 2}", i, 16)

        @block.gpsimd
        def _(g_):
            prol(g_)

            def emit_ow(tt):
                g_.wait_ge(L.sem("p6_ot"), tt + 1)
                i = g_.dma_start(out=outh[tt * 128:(tt + 1) * 128, :], in_=outb[tt % 2][:])
                L.inc(f"p6_ow{tt % 2}", i, 16)

            for tt in range(NT):
                if tt >= 2:
                    g_.wait_ge(L.sem("p6_ot"), tt - 1)
                i = g_.dma_start(out=sb6[tt % 2][:], in_=s_d[tt * 128:(tt + 1) * 128, :])
                L.inc(f"p6_sd{tt % 2}", i, 16)
                if tt >= 2:
                    emit_ow(tt - 2)
            emit_ow(NT - 2)
            emit_ow(NT - 1)
            g_.wait_ge(L.sem("p6_ow0"), 16 * (NT // 2))
            g_.wait_ge(L.sem("p6_ow1"), 16 * (NT // 2))

        @block.vector
        def _(v):
            prol(v)
            for tt in range(NT):
                st = tt + 4
                v.wait_ge(L.sem(f"p6_xd{tt % 2}"), 16 * (tt // 2 + 1))
                v.wait_ge(L.sem(f"p6_ad{tt % 2}"), 16 * (tt // 2 + 1))
                v.tensor_scalar(out=xn6[:], in0=xb6[tt % 2][:],
                                scalar1=stats[:, st, 0:1], scalar2=stats[:, st, 1:2],
                                op0=ALU.subtract, op1=ALU.mult)
                v.drain()
                if tt >= 1:
                    v.wait_ge(L.sem("p6_sl"), tt)
                i = v.tensor_tensor(out=ffb[:], in0=xn6[:], in1=ab6[tt % 2][:], op=ALU.add)
                L.inc("p6_ff", i, 1)
                v.wait_ge(L.sem("p6_sl"), tt + 1)
                v.wait_ge(L.sem(f"p6_sd{tt % 2}"), 16 * (tt // 2 + 1))
                v.tensor_tensor(out=sgb6[:], in0=sb6[tt % 2][:], in1=g6b[:], op=ALU.mult)
                v.drain()
                if tt >= 2:
                    v.wait_ge(L.sem(f"p6_ow{tt % 2}"), 16 * ((tt - 2) // 2 + 1))
                i = v.tensor_tensor(out=outb[tt % 2][:], in0=xb6[tt % 2][:],
                                    in1=sgb6[:], op=ALU.add)
                L.inc("p6_ot", i, 1)

        @block.scalar
        def _(s):
            prol(s)
            for tt in range(NT):
                s.wait_ge(L.sem("p6_ff"), tt + 1)
                i = s.activation(out=g6b[:], in_=ffb[:], func=AF.Silu)
                L.inc("p6_sl", i, 1)

        @block.sync
        def _(sy):
            sy.wait_ge(L.sem("p6_ow0"), 16 * (NT // 2))
            sy.wait_ge(L.sem("p6_ow1"), 16 * (NT // 2))

    gctx.close()
    return nc


def _host_constants(hf):
    P0 = hf * 2048
    pos = np.maximum(P0 - HALO + np.arange(TK), 0).astype(np.float64)
    inv = 10000.0 ** (-(np.arange(0, DH, 2, dtype=np.float64)) / DH)
    ang = pos[None, :] * inv[:, None]
    c = np.cos(ang)
    sn = np.sin(ang)
    cosf = np.empty((128, TK), np.float32)
    sinf = np.empty((128, TK), np.float32)
    cosf[0::2] = c
    cosf[1::2] = c
    sinf[0::2] = -sn
    sinf[1::2] = sn

    ii = np.arange(128)[:, None]
    jj = np.arange(128)[None, :]
    masks = np.zeros((128, 25, 128), np.float32)
    for tc in range(5):
        for d in range(5):
            m = np.zeros((128, 128), np.float32)
            if d == 0:
                m[~(jj > ii) * np.ones((128, 128), bool)] = 0.0
                m = np.where(jj > ii, 0.0, -1e9).astype(np.float32)
            elif d == 4:
                m = np.where(jj <= ii, 0.0, -1e9).astype(np.float32)
            if hf == 0 and tc < 4 and tc + d <= 3:
                m[:] = -1e9
            masks[:, tc * 5 + d, :] = m
    masksb = masks.reshape(128, 25 * 128)
    return cosf, sinf, masksb


def prep_inputs(L0_attn_x, L0_attn_Wq, L0_attn_Wk, L0_attn_Wv, L0_attn_Wo,
                L0_ffn_shared_w1, L0_ffn_shared_b1, L0_ffn_shared_w2, L0_ffn_shared_b2,
                L0_ffn_routed_w1, L0_ffn_routed_b1, L0_ffn_routed_w2, L0_ffn_routed_b2,
                L0_ffn_centroids, L0_ffn_bias):
    x = np.ascontiguousarray(np.asarray(L0_attn_x, dtype=np.float32))
    wqT = np.ascontiguousarray(np.asarray(L0_attn_Wq, np.float32).T)
    wkT = np.ascontiguousarray(np.asarray(L0_attn_Wk, np.float32).T)
    wvT = np.ascontiguousarray(np.asarray(L0_attn_Wv, np.float32).T)
    woT = np.ascontiguousarray(np.asarray(L0_attn_Wo, np.float32).T)
    w1 = np.ascontiguousarray(np.concatenate([np.asarray(L0_ffn_shared_w1, np.float32),
                                              np.asarray(L0_ffn_routed_w1, np.float32)], 0))
    w2 = np.ascontiguousarray(np.concatenate([np.asarray(L0_ffn_shared_w2, np.float32),
                                              np.asarray(L0_ffn_routed_w2, np.float32)], 0))
    b1all = np.concatenate([np.asarray(L0_ffn_shared_b1, np.float32),
                            np.asarray(L0_ffn_routed_b1, np.float32)], 0)
    b1c = np.ascontiguousarray(b1all.reshape(NE, FCN, 128).transpose(2, 0, 1)
                               .reshape(128, NE * FCN))
    b2r = np.ascontiguousarray(np.asarray(L0_ffn_routed_b2, np.float32))
    b2s = np.ascontiguousarray(np.asarray(L0_ffn_shared_b2, np.float32))
    cen = np.asarray(L0_ffn_centroids, np.float32)
    cenT = np.ascontiguousarray(cen.T.reshape(DC, 128, 16).transpose(1, 0, 2)
                                .reshape(128, DC * 16))
    gbias = np.ascontiguousarray(np.asarray(L0_ffn_bias, np.float32).reshape(1, 16))
    p128 = np.zeros((128, 128), np.float32)
    for i in range(64):
        p128[2 * i, 2 * i + 1] = 1.0
        p128[2 * i + 1, 2 * i] = 1.0
    iden = np.eye(128, dtype=np.float32)
    consts = {hf: _host_constants(hf) for hf in range(2)}

    in_maps = []
    for c in range(8):
        b, hf = divmod(c, 2)
        P0 = hf * 2048
        cosf, sinf, masksb = consts[hf]
        x_own = np.ascontiguousarray(x[b, P0:P0 + T])
        x_halo = np.ascontiguousarray(x[b, P0 - HALO:P0]) if hf == 1 else \
            np.zeros((HALO, D), np.float32)
        in_maps.append(dict(
            x_own=x_own, x_halo=x_halo, wqT=wqT, wkT=wkT, wvT=wvT, woT=woT,
            w1=w1, w2=w2, b1c=b1c, b2r=b2r, b2s=b2s, cenT=cenT, gbias=gbias,
            cosf=cosf, sinf=sinf, p128=p128, iden=iden, masksb=masksb))
    return in_maps


def kernel(**inputs):
    in_maps = prep_inputs(**inputs)
    nc = build_program()
    res = run_bass_kernel_spmd(nc, in_maps, core_ids=list(range(8)))
    out = np.empty((4, 4096, D), np.float32)
    for c in range(8):
        b, hf = divmod(c, 2)
        out[b, hf * 2048:(hf + 1) * 2048] = res.results[c]["out"]
    return out
